# version 02
# baseline (speedup 1.0000x reference)
import contextlib
import numpy as np

import concourse.bass as bass
import concourse.mybir as mybir
from concourse.bass_utils import run_bass_kernel_spmd

F32 = mybir.dt.float32
F16 = mybir.dt.float16
BF16 = mybir.dt.bfloat16
AF = mybir.ActivationFunctionType
ALU = mybir.AluOpType
AX = mybir.AxisListType

D = 1024
DI = 2048
NDT = 16
NFT = 8
DTR = 64
NS = 16
FF = 3072
NMT = 24
B = 4
LFULL = 4096
TOK = 2048
HALO = 3
TOKH = TOK + 2 * HALO
TC = 512
NCH = 4
CH5 = [(0, 512), (512, 512), (1024, 512), (1536, 512), (2048, 6)]
CH4 = [(i * TC, TC) for i in range(NCH)]

QMAX = 32767.0
WQMAX = 7.0
MAGIC = float(np.float32(1.5 * 2 ** 23))
LN_EPS = 1e-5

SC_X, SC_UP, SC_GATE, SC_GS, SC_T5 = 0, 1, 2, 3, 4
SC_X2, SC_UP2, SC_GATE2, SC_GS2, SC_T52 = 5, 6, 7, 8, 9
N_SCAL = 10
TMPC = 15


class Sem:
    def __init__(self, es, nc, name):
        self.h = es.enter_context(nc.semaphore(name))
        self.v = 0

    def inc(self, instr, n=1):
        instr.then_inc(self.h, n)
        self.v += n

    def dma(self, instr):
        self.inc(instr, 16)


class G:
    pass


def build_nc(taps=()):
    nc = bass.Bass("TRN2", target_bir_lowering=False)
    es = contextlib.ExitStack()
    g = G()
    g.nc = nc
    g.taps = list(taps)

    g.xT = nc.dram_tensor("xT", [D, TOKH], F32, kind="ExternalInput")
    g.inp = {}
    for d in ("A", "B"):
        for nm, shape in (("in_wT", [D, 2 * DI]), ("conv_w", [DI, 4]),
                          ("conv_b", [DI, 1]), ("xproj_wT", [DI, 96]),
                          ("dt_wT", [DTR, DI]), ("dt_b", [DI, 1]),
                          ("Alog", [DI, NS]), ("Dp", [DI, 1]),
                          ("out_wT", [DI, D]), ("projT", [D, D])):
            g.inp[f"{nm}_{d}"] = nc.dram_tensor(f"{nm}_{d}", shape, F32,
                                                kind="ExternalInput").ap()
    g.proj_b = nc.dram_tensor("proj_b", [D, 1], F32, kind="ExternalInput").ap()
    g.ln_w = nc.dram_tensor("ln_w", [D, 1], F32, kind="ExternalInput").ap()
    g.ln_b = nc.dram_tensor("ln_b", [D, 1], F32, kind="ExternalInput").ap()
    g.w_up = nc.dram_tensor("w_up", [FF, D], F32, kind="ExternalInput").ap()
    g.w_gate = nc.dram_tensor("w_gate", [FF, D], F32, kind="ExternalInput").ap()
    g.w_down = nc.dram_tensor("w_down", [D, FF], F32, kind="ExternalInput").ap()
    g.out_h = nc.dram_tensor("out", [D, TOK], F32, kind="ExternalOutput").ap()

    def dram(name, shape, dt):
        if name in g.taps:
            t = nc.dram_tensor(name, shape, dt, kind="ExternalOutput")
        else:
            t = nc.dram_tensor(name, shape, dt)
        setattr(g, name, t)
        return t

    dram("wq_upT", [D, FF], BF16)
    dram("wq_gT", [D, FF], BF16)
    dram("wq_dT", [FF, D], BF16)
    g.wb = {}
    for d in ("A", "B"):
        g.wb[f"in_wT_{d}"] = dram(f"b_in_wT_{d}", [D, 2 * DI], BF16)
        g.wb[f"out_wT_{d}"] = dram(f"b_out_wT_{d}", [DI, D], BF16)
        g.wb[f"projT_{d}"] = dram(f"b_projT_{d}", [D, D], BF16)
    dram("xln_sp", [D, TOKH], F32)
    dram("x1_sp", [D, TOKH], F32)
    dram("x1ln_sp", [D, TOKH], F32)
    dram("x1lnb_sp", [D, TOKH], BF16)
    dram("x1lnbR_sp", [D, TOKH], BF16)
    dram("x2_sp", [D, TOK], F32)
    dram("x2ln_sp", [D, TOK], F32)
    dram("up_sp", [FF, TOKH], F16)
    dram("gate_sp", [FF, TOKH], F16)
    dram("gs_sp", [FF, TOKH], F16)
    dram("t5_sp", [FF, TOKH], F16)
    dram("z_sp", [DI, TOK], F32)
    dram("projA_sp", [D, TOK], F32)
    dram("y0_sp", [DI, TOK], F32)
    g.scal_in = [dram(f"scin{i}", [1, 128], F32) for i in range(N_SCAL)]
    g.scal_out = [dram(f"scout{i}", [1, 128], F32) for i in range(N_SCAL)]
    dram("carry_io", [128, NDT * NS], F32)
    for dn in ("dbg_u", "dbg_dl", "dbg_y"):
        if dn in g.taps:
            dram(dn, [DI, TOK], F32)
    if "dbg_daw" in g.taps:
        dram("dbg_daw", [NDT * NS, 3, TC], F32)
    dram("carry_sum", [128, NDT * NS], F32)

    E = es.enter_context

    def sbuf(name, shape, dt):
        t = E(nc.sbuf_tensor(name, shape, dt))
        setattr(g, name, t)
        return t

    sbuf("ar3", [128, 16512], BF16)
    sbuf("aru", [128, 8192], F32)
    sbuf("arb", [128, 24576], F16)
    sbuf("fc8", [128, NFT, TC], F32)
    sbuf("zdt_s", [128, 2, TC], F32)
    sbuf("ubt_s", [128, 1, TC], BF16)
    sbuf("yt_s", [128, TC], F32)
    sbuf("y0t_s", [128, 2, TC], F32)
    sbuf("dup_s", [128, TC], F32)
    sbuf("ytmp_s", [128, TC], F32)
    sbuf("da_s", [128, 2, TC], F32)
    sbuf("w_sc", [128, 1, TC], F32)
    sbuf("wstr_s", [128, 2, 12, 128], BF16)
    sbuf("st_s", [128, 2, 512], F32)
    sbuf("stb_s", [128, 2, 512], F16)
    sbuf("g16_s", [128, 2, 512], F16)
    sbuf("h16_s", [128, 2, 512], F16)
    sbuf("sg_s", [128, 512], F32)
    sbuf("sg2_s", [128, 512], F32)
    sbuf("dq_s", [128, 96, 3], F32)
    sbuf("xdb_s", [96, 2, TC], F32)
    sbuf("dtb16_s", [DTR, TC], BF16)
    sbuf("bclo_s", [32, 2, TC], BF16)
    sbuf("sel_s", [32, 32, 128], BF16)
    sbuf("xicar_s", [128, NDT, HALO], F32)
    sbuf("ones_s", [1, 128], BF16)
    sbuf("onesf_s", [1, 128], F32)
    sbuf("onescol_s", [128, 1], F32)
    sbuf("ident_s", [128, 128], BF16)
    sbuf("identf_s", [128, 128], F32)
    sbuf("A_s", [128, 2, NDT * NS], F32)
    sbuf("Dp_s", [128, 2, NDT], F32)
    sbuf("cb_s", [128, 2, NDT], F32)
    sbuf("cw_s", [128, 2, NDT, 4], F32)
    sbuf("dtb_s", [128, 2, NDT], F32)
    sbuf("pb_s", [128, NFT], F32)
    sbuf("lnw_s", [128, NFT], F32)
    sbuf("lnb_s", [128, NFT], F32)
    sbuf("xprojT_s", [128, 2, NDT, 96], BF16)
    sbuf("dtwT_s", [DTR, 2, NDT, 128], BF16)
    sbuf("carry_s", [128, 2, NDT * NS], F32)
    sbuf("pcar_s", [128, NDT * NS], F32)
    sbuf("mx_s", [128, 16], F32)
    sbuf("sc_s", [128, N_SCAL], F32)
    sbuf("ss_s", [128, N_SCAL], F32)
    sbuf("t0_s", [1, 128], F32)
    sbuf("eps1_s", [1, 1], F32)

    def psum(name, shape, dt):
        t = E(nc.psum_tensor(name, shape, dt))
        setattr(g, name, t)

    psum("p_mm", [128, 2, 512], F32)
    psum("p_mm2", [128, 2, 512], F32)
    psum("p_bc", [128, 512], F32)
    psum("p_bc2", [128, 512], F32)
    psum("p_xd", [96, TC], F32)
    psum("p_tr", [128, 128], BF16)

    for s in ("s_ld", "s_st", "s_tn", "s_ve", "s_se", "s_ge", "s_cc",
              "s_out", "s_ld2"):
        setattr(g, s, Sem(es, nc, s))

    with nc.Block() as block:
        g.block = block
        prep_constants(g)
        prep_weights(g)
        p1_lnq(g, src="x")
        ffn_mid(g, ver=1)
        p4_mixer(g)
        p1_lnq(g, src="x2")
        ffn_mid(g, ver=2)
        finish(g)
    es.close()
    return nc



def v_qb(g):
    return g.ar3.ap()[:, 0:NFT * TOKH].rearrange("p (f t) -> p f t", f=NFT)


def v_qb2(g):
    return g.ar3.ap()[:, 0:NFT * TOK].rearrange("p (f t) -> p f t", f=NFT)


def v_xi(g):
    return g.ar3.ap().bitcast(F32)[:, 0:NDT * (TC + HALO)].rearrange(
        "p (d t) -> p d t", d=NDT)


def v_du(g):
    return g.ar3.ap().bitcast(F32)[:, 0:NDT * TC].rearrange(
        "p (d t) -> p d t", d=NDT)


def v_gb(g, k=None):
    a = g.ar3.ap()[:, 0:NDT * TC].rearrange("p (d t) -> p d t", d=NDT)
    return a if k is None else a[:, k]


def v_u(g):
    return g.aru.ap().rearrange("p (d t) -> p d t", d=NDT)


def v_sq(g, n):
    return g.aru.ap()[:, 0:NFT * n].rearrange("p (f t) -> p f t", f=NFT)


def v_rows(g):
    return g.aru.ap()[0:1, NFT * TC:NFT * TC + 5 * 512].rearrange(
        "p (r t) -> p r t", r=5)


def v_abl(g):
    o = 2 * (NFT * TC + 5 * 512)
    return g.aru.ap().bitcast(BF16)[0:1, o:o + 4 * 512].rearrange(
        "p (r t) -> p r t", r=4)


def v_brep(g):
    return g.arb.ap().bitcast(F32)[:, 0:8 * TC].rearrange("p (n t) -> p n t", n=8)


def v_crep(g):
    return g.arb.ap().bitcast(F32)[:, 8 * TC:16 * TC]


def v_hall(g):
    return g.arb.ap().bitcast(F32)[:, 16 * TC:24 * TC]


def v_wraw(g):
    return g.arb.ap().bitcast(F32)[:, 0:4096]


def v_wqb(g):
    return g.arb.ap().bitcast(BF16)[:, 8192:8192 + 4096]


def v_lnout(g, n):
    return g.arb.ap().bitcast(F32)[:, 0:NFT * n].rearrange(
        "p (f t) -> p f t", f=NFT)


def v_pp(g):
    return g.arb.ap().bitcast(F32)[:, 16 * TC:16 * TC + NFT * TC].rearrange(
        "p (f t) -> p f t", f=NFT)


def v_x2c(g):
    return g.arb.ap().bitcast(F32)[:, 8 * TC:8 * TC + NFT * TC].rearrange(
        "p (f t) -> p f t", f=NFT)


def v_xlc(g):
    return g.arb.ap().bitcast(BF16)[:, 0:NFT * TC].rearrange(
        "p (f t) -> p f t", f=NFT)


def v_mo(g):
    return g.arb.ap().bitcast(BF16)[:, NFT * TC:2 * NFT * TC].rearrange(
        "p (f t) -> p f t", f=NFT)


def v_lnb16(g):
    return g.arb.ap().bitcast(BF16)[:, 8 * NFT * TC // 4:8 * NFT * TC // 4 + NFT * TC].rearrange(
        "p (f t) -> p f t", f=NFT)


def v_lnb16R(g):
    o = 8 * NFT * TC // 4 + NFT * TC
    return g.arb.ap().bitcast(BF16)[:, o:o + NFT * TC].rearrange(
        "p (f t) -> p f t", f=NFT)


def v_oq(g):
    return g.arb.ap().bitcast(BF16)[:, 0:NMT * 512].rearrange(
        "p (m t) -> p m t", m=NMT)


def hbmT(t):
    return t.ap().rearrange("(f p) t -> p f t", p=128)



def barrier_all(g):
    blk = g.block
    snap = [(g.s_ld, g.s_ld.v), (g.s_st, g.s_st.v), (g.s_tn, g.s_tn.v),
            (g.s_ve, g.s_ve.v), (g.s_se, g.s_se.v), (g.s_ge, g.s_ge.v),
            (g.s_cc, g.s_cc.v), (g.s_ld2, g.s_ld2.v), (g.s_out, g.s_out.v)]
    for eng in ("sync", "vector", "scalar", "tensor", "gpsimd"):
        def body(e, snap=snap):
            for s, v in snap:
                if v:
                    e.wait_ge(s.h, v)
        getattr(blk, eng)(body)


def ln_chunk(g, src_ap, out_ap, n):
    blk = g.block
    sq = v_sq(g, n)
    ve0 = g.s_ve.v

    def se(e, ve0=ve0):
        e.wait_ge(g.s_ve.h, ve0)
        e.activation(out=sq, in_=src_ap, func=AF.Square, bias=0.0, scale=1.0)
        g.s_se.inc(e.drain())
    blk.scalar(se)
    sd = g.s_se.v

    def tn(t, sd=sd, n=n):
        t.wait_ge(g.s_se.h, sd)
        for ft in range(NFT):
            li = t.matmul(g.p_xd[0:1, 0:n], g.onescol_s[:, :], src_ap[:, ft],
                          start=(ft == 0), stop=(ft == NFT - 1))
        g.s_tn.inc(li)
        for ft in range(NFT):
            li = t.matmul(g.p_bc[0:1, 0:n], g.onescol_s[:, :], sq[:, ft],
                          start=(ft == 0), stop=(ft == NFT - 1))
        g.s_tn.inc(li)
    blk.tensor(tn)
    td = g.s_tn.v
    rows = v_rows(g)
    abl = v_abl(g)

    def ve(v, td=td, n=n):
        v.wait_ge(g.s_tn.h, td)
        v.tensor_scalar_mul(out=rows[:, 0, 0:n], in0=g.p_xd[0:1, 0:n],
                            scalar1=1.0 / D)
        v.tensor_scalar_mul(out=rows[:, 1, 0:n], in0=g.p_bc[0:1, 0:n],
                            scalar1=1.0 / D)
        v.drain()
        v.tensor_tensor(out=rows[:, 2, 0:n], in0=rows[:, 0, 0:n],
                        in1=rows[:, 0, 0:n], op=ALU.mult)
        v.drain()
        v.tensor_sub(out=rows[:, 2, 0:n], in0=rows[:, 1, 0:n],
                     in1=rows[:, 2, 0:n])
        g.s_ve.inc(v.drain())
    blk.vector(ve)
    vd = g.s_ve.v

    def se2(e, vd=vd, n=n):
        e.wait_ge(g.s_ve.h, vd)
        g.s_se.inc(e.activation(out=rows[:, 3, 0:n], in_=rows[:, 2, 0:n],
                                func=AF.Sqrt, bias=g.eps1_s[:, :], scale=1.0))
    blk.scalar(se2)
    sd2 = g.s_se.v

    def ve2(v, sd2=sd2, n=n):
        v.wait_ge(g.s_se.h, sd2)
        v.reciprocal(out=rows[:, 3, 0:n], in_=rows[:, 3, 0:n])
        v.drain()
        v.scalar_tensor_tensor(out=rows[:, 4, 0:n], in0=rows[:, 0, 0:n],
                               scalar=-1.0, in1=rows[:, 3, 0:n],
                               op0=ALU.mult, op1=ALU.mult)
        v.drain()
        v.tensor_copy(out=abl[:, 0, 0:n], in_=rows[:, 3, 0:n])
        v.tensor_copy(out=abl[:, 1, 0:n], in_=rows[:, 4, 0:n])
        v.drain()
        v.tensor_sub(out=rows[:, 0, 0:n], in0=rows[:, 3, 0:n], in1=abl[:, 0, 0:n])
        v.tensor_sub(out=rows[:, 1, 0:n], in0=rows[:, 4, 0:n], in1=abl[:, 1, 0:n])
        v.drain()
        v.tensor_copy(out=abl[:, 2, 0:n], in_=rows[:, 0, 0:n])
        v.tensor_copy(out=abl[:, 3, 0:n], in_=rows[:, 1, 0:n])
        g.s_ve.inc(v.drain())
    blk.vector(ve2)
    vd2 = g.s_ve.v

    def tn2(t, vd2=vd2, n=n):
        t.wait_ge(g.s_ve.h, vd2)
        li = t.matmul(g.p_bc[:, 0:n], g.ones_s[:, :], abl[:, 0, 0:n], start=True, stop=False)
        li = t.matmul(g.p_bc[:, 0:n], g.ones_s[:, :], abl[:, 2, 0:n], start=False, stop=True)
        g.s_tn.inc(li)
        li = t.matmul(g.p_bc2[:, 0:n], g.ones_s[:, :], abl[:, 1, 0:n], start=True, stop=False)
        li = t.matmul(g.p_bc2[:, 0:n], g.ones_s[:, :], abl[:, 3, 0:n], start=False, stop=True)
        g.s_tn.inc(li)
    blk.tensor(tn2)
    td2 = g.s_tn.v

    def ve3(v, td2=td2, n=n):
        v.wait_ge(g.s_tn.h, td2)
        for ft in range(NFT):
            v.tensor_tensor(out=out_ap[:, ft], in0=src_ap[:, ft],
                            in1=g.p_bc[:, 0:n], op=ALU.mult)
        v.drain()
        for ft in range(NFT):
            v.tensor_add(out=out_ap[:, ft], in0=out_ap[:, ft],
                         in1=g.p_bc2[:, 0:n])
        g.s_ve.inc(v.drain())
    blk.vector(ve3)
    return g.s_ve.v


def absmax_acc(g, src_ap, slot, axis=AX.X):
    blk = g.block

    def ve(v, slot=slot, axis=axis):
        v.tensor_reduce(out=g.mx_s[:, TMPC:TMPC + 1], in_=src_ap, axis=axis,
                        op=ALU.max, apply_absolute_value=True)
        v.drain()
        v.tensor_max(out=g.mx_s[:, slot:slot + 1], in0=g.mx_s[:, slot:slot + 1],
                     in1=g.mx_s[:, TMPC:TMPC + 1])
        g.s_ve.inc(v.drain())
    blk.vector(ve)
    return g.s_ve.v


def scale_exchange(g, slot, qmax):
    blk = g.block
    ve_prev = g.s_ve.v

    def tn(t, slot=slot, ve_prev=ve_prev):
        t.wait_ge(g.s_ve.h, ve_prev)
        g.s_tn.inc(t.transpose(g.p_bc[0:1, 0:128], g.mx_s[:, slot:slot + 1],
                               g.identf_s[:]))
    blk.tensor(tn)
    td = g.s_tn.v

    def ve(v, td=td):
        v.wait_ge(g.s_tn.h, td)
        v.reduce_max(out=g.t0_s[:, 0:1], in_=g.p_bc[0:1, 0:128], axis=AX.X)
        v.drain()
        g.s_ve.inc(v.tensor_copy(
            out=g.t0_s[:, :], in_=g.t0_s.ap()[:, 0:1].broadcast_to([1, 128])))
    blk.vector(ve)
    vd = g.s_ve.v

    def gp(p, slot=slot, vd=vd):
        p.wait_ge(g.s_ve.h, vd)
        p.wait_ge(g.s_st.h, g.s_st.v)
        g.s_st.dma(p.dma_start(out=g.scal_in[slot].ap(), in_=g.t0_s[:, :]))
        p.wait_ge(g.s_st.h, g.s_st.v)
        g.s_cc.inc(p.collective_compute(
            "AllReduce", ALU.max, replica_groups=[list(range(8))],
            ins=[g.scal_in[slot].ap().opt()], outs=[g.scal_out[slot].ap().opt()]))
        p.wait_ge(g.s_cc.h, g.s_cc.v)
        p.wait_ge(g.s_st.h, g.s_st.v)
        g.s_st.dma(p.dma_start(
            out=g.ss_s[:, slot:slot + 1],
            in_=g.scal_out[slot].ap().rearrange("o p -> (o p)")[:, None]))
    blk.gpsimd(gp)
    std = g.s_st.v

    def ve2(v, slot=slot, std=std, qmax=qmax):
        v.wait_ge(g.s_st.h, std)
        v.tensor_scalar(out=g.ss_s[:, slot:slot + 1], in0=g.ss_s[:, slot:slot + 1],
                        scalar1=1.0 / qmax, scalar2=1e-12, op0=ALU.mult,
                        op1=ALU.max)
        v.drain()
        v.reciprocal(out=g.sc_s[:, slot:slot + 1], in_=g.ss_s[:, slot:slot + 1])
        g.s_ve.inc(v.drain())
    blk.vector(ve2)
    return g.s_ve.v



def prep_constants(g):
    blk = g.block
    nc = g.nc

    def q(qq):
        ctx = nc.allow_non_contiguous_dma(reason="tiny per-channel param loads")
        ctx.__enter__()
        for i, d in enumerate(("A", "B")):
            qq.wait_ge(g.s_ld.h, g.s_ld.v)
            g.s_ld.dma(qq.dma_start(
                out=g.A_s.ap()[:, i].rearrange("p (a n) -> p a n", a=NDT),
                in_=g.inp[f"Alog_{d}"].rearrange("(a p) n -> p a n", p=128)))
            qq.wait_ge(g.s_ld.h, g.s_ld.v)
            g.s_ld.dma(qq.dma_start(
                out=g.Dp_s.ap()[:, i, :, None],
                in_=g.inp[f"Dp_{d}"].rearrange("(a p) o -> p a o", p=128)))
            qq.wait_ge(g.s_ld.h, g.s_ld.v)
            g.s_ld.dma(qq.dma_start(
                out=g.cb_s.ap()[:, i, :, None],
                in_=g.inp[f"conv_b_{d}"].rearrange("(a p) o -> p a o", p=128)))
            qq.wait_ge(g.s_ld.h, g.s_ld.v)
            g.s_ld.dma(qq.dma_start(out=g.cw_s[:, i],
                                    in_=g.inp[f"conv_w_{d}"].rearrange("(a p) k -> p a k", p=128)))
            qq.wait_ge(g.s_ld.h, g.s_ld.v)
            g.s_ld.dma(qq.dma_start(
                out=g.dtb_s.ap()[:, i, :, None],
                in_=g.inp[f"dt_b_{d}"].rearrange("(a p) o -> p a o", p=128)))
        qq.wait_ge(g.s_ld.h, g.s_ld.v)
        g.s_ld.dma(qq.dma_start(out=g.pb_s.ap()[:, :, None],
                                in_=g.proj_b.rearrange("(a p) o -> p a o", p=128)))
        qq.wait_ge(g.s_ld.h, g.s_ld.v)
        g.s_ld.dma(qq.dma_start(out=g.lnw_s.ap()[:, :, None],
                                in_=g.ln_w.rearrange("(a p) o -> p a o", p=128)))
        qq.wait_ge(g.s_ld.h, g.s_ld.v)
        g.s_ld.dma(qq.dma_start(out=g.lnb_s.ap()[:, :, None],
                                in_=g.ln_b.rearrange("(a p) o -> p a o", p=128)))
        ctx.__exit__(None, None, None)
    blk.sync(q)
    ld0 = g.s_ld.v

    def ve(v):
        v.memset(g.ones_s[:], 1.0)
        v.memset(g.onesf_s[:], 1.0)
        v.memset(g.onescol_s[:], 1.0)
        v.memset(g.identf_s[:], 1.0)
        v.memset(g.ident_s[:], 1.0)
        v.memset(g.carry_s[:], 0.0)
        v.memset(g.mx_s[:], 0.0)
        v.memset(g.eps1_s[:], LN_EPS)
        v.memset(g.sel_s[:], 1.0)
        g.s_ve.inc(v.drain())
    blk.vector(ve)
    ve0 = g.s_ve.v

    def gp(p, ve0=ve0):
        p.wait_ge(g.s_ve.h, ve0)
        p.affine_select(out=g.ident_s[:], in_=g.ident_s[:], pattern=[[-1, 128]],
                        compare_op=ALU.is_equal, fill=0.0, base=0,
                        channel_multiplier=1)
        p.affine_select(out=g.identf_s[:], in_=g.identf_s[:],
                        pattern=[[-1, 128]], compare_op=ALU.is_equal,
                        fill=0.0, base=0, channel_multiplier=1)
        for n in range(32):
            p.affine_select(out=g.sel_s[:, n, :], in_=g.sel_s[:, n, :],
                            pattern=[[0, 128]], compare_op=ALU.is_equal,
                            fill=0.0, base=-n, channel_multiplier=1)
        p.sem_inc(g.s_ge.h, 1)
        g.s_ge.v += 1
    blk.gpsimd(gp)

    def se(e, ld0=ld0):
        e.wait_ge(g.s_ld.h, ld0)
        e.activation(out=g.A_s[:], in_=g.A_s[:], func=AF.Exp, bias=0.0, scale=1.0)
        e.drain()
        g.s_se.inc(e.mul(out=g.A_s[:], in_=g.A_s[:], mul=-1.0))
    blk.scalar(se)

    for i, d in enumerate(("A", "B")):
        def q2(qq, d=d, vprev=g.s_ve.v):
            qq.wait_ge(g.s_ve.h, vprev)
            qq.wait_ge(g.s_ld2.h, g.s_ld2.v)
            g.s_ld2.dma(qq.dma_start(
                out=v_wraw(g)[:, 0:NDT * 96].rearrange("p (a m) -> p a m", a=NDT),
                in_=g.inp[f"xproj_wT_{d}"].rearrange("(a p) m -> p a m", p=128)))
            qq.wait_ge(g.s_ld2.h, g.s_ld2.v)
            g.s_ld2.dma(qq.dma_start(
                out=v_wraw(g)[0:DTR, 1536:1536 + NDT * 64],
                in_=g.inp[f"dt_wT_{d}"][:, 0:NDT * 64]))
            qq.wait_ge(g.s_ld2.h, g.s_ld2.v)
            g.s_ld2.dma(qq.dma_start(
                out=v_wraw(g)[64:128, 1536:1536 + NDT * 64],
                in_=g.inp[f"dt_wT_{d}"][:, NDT * 64:NDT * 128]))
        blk.scalar(q2)
        ld = g.s_ld2.v

        def ve2(v, i=i, ld=ld):
            v.wait_ge(g.s_ld2.h, ld)
            v.tensor_copy(
                out=g.xprojT_s.ap().rearrange("p a m k -> p (a m k)")[:, i * NDT * 96:(i + 1) * NDT * 96],
                in_=v_wraw(g)[:, 0:NDT * 96])
            v.tensor_copy(
                out=g.dtwT_s.ap().rearrange("r a m k -> r (a m k)")[:, i * NDT * 128:i * NDT * 128 + NDT * 64],
                in_=v_wraw(g)[0:DTR, 1536:1536 + NDT * 64])
            g.s_ve.inc(v.tensor_copy(
                out=g.dtwT_s.ap().rearrange("r a m k -> r (a m k)")[:, i * NDT * 128 + NDT * 64:(i + 1) * NDT * 128],
                in_=v_wraw(g)[64:128, 1536:1536 + NDT * 64]))
        blk.vector(ve2)


def prep_weights(g):
    blk = g.block
    barrier_all(g)
    jobs = []
    for d in ("A", "B"):
        jobs.append((g.inp[f"in_wT_{d}"], g.wb[f"in_wT_{d}"], D, 2 * DI))
        jobs.append((g.inp[f"out_wT_{d}"], g.wb[f"out_wT_{d}"], DI, D))
        jobs.append((g.inp[f"projT_{d}"], g.wb[f"projT_{d}"], D, D))
    for src, dst, rows, cols in jobs:
        for i in range(rows // 128):
            def q(qq, src=src, i=i, cols=cols, stv=g.s_st.v):
                qq.wait_ge(g.s_st.h, stv)
                qq.wait_ge(g.s_ld.h, g.s_ld.v)
                g.s_ld.dma(qq.dma_start(out=v_wraw(g)[:, 0:cols],
                                        in_=src[i * 128:(i + 1) * 128, :]))
            blk.sync(q)
            ld = g.s_ld.v

            def ve(v, cols=cols, ld=ld):
                v.wait_ge(g.s_ld.h, ld)
                g.s_ve.inc(v.tensor_copy(out=v_wqb(g)[:, 0:cols],
                                         in_=v_wraw(g)[:, 0:cols]))
            blk.vector(ve)
            vd = g.s_ve.v

            def gp(p, dst=dst, i=i, cols=cols, vd=vd):
                p.wait_ge(g.s_ve.h, vd)
                p.wait_ge(g.s_st.h, g.s_st.v)
                g.s_st.dma(p.dma_start(out=dst.ap()[i * 128:(i + 1) * 128, :],
                                       in_=v_wqb(g)[:, 0:cols]))
            blk.gpsimd(gp)

    for src, dstT, O, I in ((g.w_up, g.wq_upT, FF, D),
                            (g.w_gate, g.wq_gT, FF, D),
                            (g.w_down, g.wq_dT, D, FF)):
        nblk = I // 32
        for i in range(O // 128):
            def q(qq, src=src, i=i, I=I, stv=g.s_st.v):
                qq.wait_ge(g.s_st.h, stv)
                qq.wait_ge(g.s_ld.h, g.s_ld.v)
                g.s_ld.dma(qq.dma_start(out=v_wraw(g)[:, 0:I],
                                        in_=src[i * 128:(i + 1) * 128, :]))
            blk.sync(q)
            ld = g.s_ld.v

            def ve(v, I=I, nblk=nblk, ld=ld):
                v.wait_ge(g.s_ld.h, ld)
                w = v_wraw(g)[:, 0:I]
                w3 = v_wraw(g)[:, 0:I].rearrange("p (n k) -> p n k", k=32)
                m = g.dq_s[:, 0:nblk, 0]
                s = g.dq_s[:, 0:nblk, 1]
                r = g.dq_s[:, 0:nblk, 2]
                v.tensor_reduce(out=m, in_=w3, axis=AX.X, op=ALU.max,
                                apply_absolute_value=True)
                v.drain()
                v.tensor_scalar(out=s, in0=m, scalar1=1.0 / WQMAX, scalar2=1e-12,
                                op0=ALU.mult, op1=ALU.max)
                v.drain()
                v.reciprocal(out=r, in_=s)
                v.drain()
                v.tensor_tensor(out=w3, in0=w3,
                                in1=g.dq_s.ap()[:, 0:nblk, 2:3].broadcast_to([128, nblk, 32]),
                                op=ALU.mult)
                v.drain()
                v.tensor_scalar(out=w, in0=w, scalar1=MAGIC, scalar2=MAGIC,
                                op0=ALU.add, op1=ALU.subtract)
                v.drain()
                g.s_ve.inc(v.tensor_tensor(
                    out=v_wqb(g)[:, 0:I].rearrange("p (n k) -> p n k", k=32),
                    in0=w3,
                    in1=g.dq_s.ap()[:, 0:nblk, 1:2].broadcast_to([128, nblk, 32]),
                    op=ALU.mult))
            blk.vector(ve)
            vd = g.s_ve.v

            for j in range(I // 128):
                def tn(t, j=j, vd=vd, gprev=g.s_ge.v, sprev=g.s_se.v):
                    t.wait_ge(g.s_ve.h, vd)
                    t.wait_ge(g.s_ge.h, gprev)
                    t.wait_ge(g.s_se.h, sprev)
                    g.s_tn.inc(t.transpose(g.p_tr[:],
                                           v_wqb(g)[:, j * 128:(j + 1) * 128],
                                           g.ident_s[:]))
                blk.tensor(tn)
                td = g.s_tn.v

                def se(e, td=td, stv=g.s_st.v):
                    e.wait_ge(g.s_tn.h, td)
                    e.wait_ge(g.s_st.h, stv)
                    g.s_se.inc(e.copy(out=g.stb_s.ap().rearrange("p a b -> p (a b)").bitcast(BF16)[:, 0:128],
                                      in_=g.p_tr[:]))
                blk.scalar(se)
                sd = g.s_se.v

                def gp(p, dstT=dstT, i=i, j=j, sd=sd):
                    p.wait_ge(g.s_se.h, sd)
                    p.wait_ge(g.s_st.h, g.s_st.v)
                    g.s_st.dma(p.dma_start(
                        out=dstT.ap()[j * 128:(j + 1) * 128, i * 128:(i + 1) * 128],
                        in_=g.stb_s.ap().rearrange("p a b -> p (a b)").bitcast(BF16)[:, 0:128]))
                blk.gpsimd(gp)


def p1_lnq(g, src):
    blk = g.block
    barrier_all(g)
    if src == "x":
        for c, (o, n) in enumerate(CH5):
            def q(qq, o=o, n=n, vprev=g.s_ve.v):
                qq.wait_ge(g.s_ve.h, vprev)
                qq.wait_ge(g.s_ld.h, g.s_ld.v)
                g.s_ld.dma(qq.dma_start(out=g.fc8[:, :, 0:n],
                                        in_=hbmT(g.xT)[:, :, o:o + n]))
            blk.sync(q)
            ld = g.s_ld.v

            def ve0(v, ld=ld, stv=g.s_st.v):
                v.wait_ge(g.s_ld.h, ld)
                v.wait_ge(g.s_st.h, stv)
                g.s_ve.inc(v.drain())
            blk.vector(ve0)
            ln_chunk(g, g.fc8.ap()[:, :, 0:n], v_lnout(g, n), n)
            absmax_acc(g, v_lnout(g, n), SC_X, axis=AX.XY)
            vd = g.s_ve.v

            def gp(p, o=o, n=n, vd=vd):
                p.wait_ge(g.s_ve.h, vd)
                p.wait_ge(g.s_st.h, g.s_st.v)
                g.s_st.dma(p.dma_start(out=hbmT(g.xln_sp)[:, :, o:o + n],
                                       in_=v_lnout(g, n)[:, :, 0:n]))
            blk.gpsimd(gp)
        scale_exchange(g, SC_X, QMAX)
        slot, lnsrc, chunks, qbv = SC_X, g.xln_sp, CH5, v_qb(g)
    else:
        scale_exchange(g, SC_X2, QMAX)
        slot, lnsrc, chunks, qbv = SC_X2, g.x2ln_sp, CH4, v_qb2(g)

    for c, (o, n) in enumerate(chunks):
        def q(qq, o=o, n=n, lnsrc=lnsrc, vprev=g.s_ve.v, stv=g.s_st.v):
            qq.wait_ge(g.s_ve.h, vprev)
            qq.wait_ge(g.s_st.h, stv)
            qq.wait_ge(g.s_ld.h, g.s_ld.v)
            g.s_ld.dma(qq.dma_start(out=g.fc8[:, :, 0:n],
                                    in_=hbmT(lnsrc)[:, :, o:o + n]))
        blk.sync(q)
        ld = g.s_ld.v

        def ve(v, o=o, n=n, ld=ld, slot=slot, qbv=qbv):
            v.wait_ge(g.s_ld.h, ld)
            v.tensor_scalar(out=g.fc8[:, :, 0:n], in0=g.fc8[:, :, 0:n],
                            scalar1=g.sc_s[:, slot:slot + 1], scalar2=MAGIC,
                            op0=ALU.mult, op1=ALU.add)
            v.drain()
            v.tensor_scalar(out=qbv[:, :, o:o + n], in0=g.fc8[:, :, 0:n],
                            scalar1=MAGIC, scalar2=g.ss_s[:, slot:slot + 1],
                            op0=ALU.subtract, op1=ALU.mult)
            g.s_ve.inc(v.drain())
        blk.vector(ve)


def ffn_mid(g, ver):
    blk = g.block
    barrier_all(g)
    if ver == 1:
        chunks, qbv = CH5, v_qb(g)
        s_up, s_gate, s_gs, s_t5 = SC_UP, SC_GATE, SC_GS, SC_T5
    else:
        chunks, qbv = CH4, v_qb2(g)
        s_up, s_gate, s_gs, s_t5 = SC_UP2, SC_GATE2, SC_GS2, SC_T52

    for wt, spill, slot in ((g.wq_upT, g.up_sp, s_up),
                            (g.wq_gT, g.gate_sp, s_gate)):
        for m in range(NMT):
            bi = m % 2

            def q(qq, wt=wt, m=m, bi=bi, tprev=g.s_tn.v):
                qq.wait_ge(g.s_tn.h, tprev)
                qq.wait_ge(g.s_ld.h, g.s_ld.v)
                g.s_ld.dma(qq.dma_start(
                    out=g.wstr_s[:, bi, 0:NFT, :],
                    in_=wt.ap().rearrange("(k p) f -> p k f", p=128)[:, :, m * 128:(m + 1) * 128]))
            blk.sync(q)
            ld = g.s_ld.v

            for ci, (o, n) in enumerate(chunks):
                pb = ci % 2

                def tn(t, bi=bi, o=o, n=n, pb=pb, ld=ld, sprev=g.s_se.v,
                       qbv=qbv):
                    t.wait_ge(g.s_ld.h, ld)
                    t.wait_ge(g.s_se.h, sprev)
                    for k in range(NFT):
                        li = t.matmul(g.p_mm[:, pb, 0:n], g.wstr_s[:, bi, k, :],
                                 qbv[:, k, o:o + n], start=(k == 0),
                                 stop=(k == NFT - 1))
                    g.s_tn.inc(li)
                blk.tensor(tn)
                td = g.s_tn.v

                def se(e, pb=pb, ci=ci, n=n, td=td, stv=g.s_st.v):
                    e.wait_ge(g.s_tn.h, td)
                    e.wait_ge(g.s_st.h, stv)
                    g.s_se.inc(e.copy(out=g.stb_s[:, ci % 2, 0:n],
                                      in_=g.p_mm[:, pb, 0:n]))
                blk.scalar(se)
                sd = g.s_se.v

                def vw(v, sd=sd):
                    v.wait_ge(g.s_se.h, sd)
                blk.vector(vw)
                absmax_acc(g, g.stb_s[:, ci % 2, 0:n], slot)
                vd = g.s_ve.v

                def gp(p, spill=spill, m=m, o=o, n=n, ci=ci, vd=vd):
                    p.wait_ge(g.s_ve.h, vd)
                    p.wait_ge(g.s_st.h, g.s_st.v)
                    g.s_st.dma(p.dma_start(
                        out=spill.ap()[m * 128:(m + 1) * 128, o:o + n],
                        in_=g.stb_s[:, ci % 2, 0:n]))
                blk.gpsimd(gp)

    scale_exchange(g, s_up, QMAX)
    scale_exchange(g, s_gate, QMAX)

    for m in range(NMT):
        for ci, (o, n) in enumerate(chunks):
            sl = (m * len(chunks) + ci) % 2

            def q(qq, m=m, o=o, n=n, sl=sl, vprev=g.s_ve.v):
                qq.wait_ge(g.s_ve.h, vprev)
                qq.wait_ge(g.s_ld.h, g.s_ld.v)
                g.s_ld.dma(qq.dma_start(
                    out=g.g16_s[:, sl, 0:n],
                    in_=g.gate_sp.ap()[m * 128:(m + 1) * 128, o:o + n]))
            blk.sync(q)
            ld = g.s_ld.v

            def ve(v, sl=sl, n=n, ld=ld, s_gate=s_gate):
                v.wait_ge(g.s_ld.h, ld)
                gq = g.sg_s[:, 0:n]
                v.tensor_scalar(out=gq, in0=g.g16_s[:, sl, 0:n],
                                scalar1=g.sc_s[:, s_gate:s_gate + 1],
                                scalar2=MAGIC, op0=ALU.mult, op1=ALU.add)
                v.drain()
                v.tensor_scalar(out=gq, in0=gq, scalar1=MAGIC,
                                scalar2=g.ss_s[:, s_gate:s_gate + 1],
                                op0=ALU.subtract, op1=ALU.mult)
                g.s_ve.inc(v.drain())
            blk.vector(ve)
            vd = g.s_ve.v

            def se(e, n=n, vd=vd):
                e.wait_ge(g.s_ve.h, vd)
                g.s_se.inc(e.activation(out=g.sg2_s[:, 0:n], in_=g.sg_s[:, 0:n],
                                        func=AF.Sigmoid, bias=0.0, scale=1.0))
            blk.scalar(se)
            sd = g.s_se.v

            def ve2(v, n=n, sl=sl, sd=sd, stv=g.s_st.v):
                v.wait_ge(g.s_se.h, sd)
                v.wait_ge(g.s_st.h, stv)
                sig = g.sg2_s[:, 0:n]
                v.tensor_scalar(out=sig, in0=sig, scalar1=65536.0, scalar2=MAGIC,
                                op0=ALU.mult, op1=ALU.add)
                v.drain()
                v.tensor_scalar(out=sig, in0=sig, scalar1=MAGIC, scalar2=65535.0,
                                op0=ALU.subtract, op1=ALU.min)
                v.drain()
                v.scalar_tensor_tensor(out=g.h16_s[:, sl, 0:n], in0=sig,
                                       scalar=1.0 / 65536.0, in1=g.sg_s[:, 0:n],
                                       op0=ALU.mult, op1=ALU.mult)
                g.s_ve.inc(v.drain())
            blk.vector(ve2)
            absmax_acc(g, g.h16_s[:, sl, 0:n], s_gs)
            vd2 = g.s_ve.v

            def gp(p, m=m, o=o, n=n, sl=sl, vd2=vd2):
                p.wait_ge(g.s_ve.h, vd2)
                p.wait_ge(g.s_st.h, g.s_st.v)
                g.s_st.dma(p.dma_start(
                    out=g.gs_sp.ap()[m * 128:(m + 1) * 128, o:o + n],
                    in_=g.h16_s[:, sl, 0:n]))
            blk.gpsimd(gp)

    scale_exchange(g, s_gs, QMAX)

    for m in range(NMT):
        for ci, (o, n) in enumerate(chunks):
            sl = (m * len(chunks) + ci) % 2

            def q(qq, m=m, o=o, n=n, sl=sl, vprev=g.s_ve.v):
                qq.wait_ge(g.s_ve.h, vprev)
                qq.wait_ge(g.s_ld.h, g.s_ld.v)
                g.s_ld.dma(qq.dma_start(
                    out=g.g16_s[:, sl, 0:n],
                    in_=g.gs_sp.ap()[m * 128:(m + 1) * 128, o:o + n]))
                qq.wait_ge(g.s_ld.h, g.s_ld.v)
                g.s_ld.dma(qq.dma_start(
                    out=g.h16_s[:, sl, 0:n],
                    in_=g.up_sp.ap()[m * 128:(m + 1) * 128, o:o + n]))
            blk.sync(q)
            ld = g.s_ld.v

            def ve(v, n=n, sl=sl, ld=ld, s_gs=s_gs, s_up=s_up, stv=g.s_st.v):
                v.wait_ge(g.s_ld.h, ld)
                v.wait_ge(g.s_st.h, stv)
                g2 = g.sg_s[:, 0:n]
                uq = g.sg2_s[:, 0:n]
                v.tensor_scalar(out=g2, in0=g.g16_s[:, sl, 0:n],
                                scalar1=g.sc_s[:, s_gs:s_gs + 1], scalar2=MAGIC,
                                op0=ALU.mult, op1=ALU.add)
                v.tensor_scalar(out=uq, in0=g.h16_s[:, sl, 0:n],
                                scalar1=g.sc_s[:, s_up:s_up + 1], scalar2=MAGIC,
                                op0=ALU.mult, op1=ALU.add)
                v.drain()
                v.tensor_scalar(out=g2, in0=g2, scalar1=MAGIC,
                                scalar2=g.ss_s[:, s_gs:s_gs + 1],
                                op0=ALU.subtract, op1=ALU.mult)
                v.tensor_scalar(out=uq, in0=uq, scalar1=MAGIC,
                                scalar2=g.ss_s[:, s_up:s_up + 1],
                                op0=ALU.subtract, op1=ALU.mult)
                v.drain()
                v.tensor_mul(out=g.g16_s[:, sl, 0:n], in0=g2, in1=uq)
                g.s_ve.inc(v.drain())
            blk.vector(ve)
            absmax_acc(g, g.g16_s[:, sl, 0:n], s_t5)
            vd = g.s_ve.v

            def gp(p, m=m, o=o, n=n, sl=sl, vd=vd):
                p.wait_ge(g.s_ve.h, vd)
                p.wait_ge(g.s_st.h, g.s_st.v)
                g.s_st.dma(p.dma_start(
                    out=g.t5_sp.ap()[m * 128:(m + 1) * 128, o:o + n],
                    in_=g.g16_s[:, sl, 0:n]))
            blk.gpsimd(gp)

    scale_exchange(g, s_t5, QMAX)

    oq = v_oq(g)
    for ci, (o, n) in enumerate(chunks):
        for m in range(NMT):
            sl = m % 2

            def q(qq, m=m, o=o, n=n, sl=sl, vprev=g.s_ve.v):
                qq.wait_ge(g.s_ve.h, vprev)
                qq.wait_ge(g.s_ld.h, g.s_ld.v)
                g.s_ld.dma(qq.dma_start(
                    out=g.g16_s[:, sl, 0:n],
                    in_=g.t5_sp.ap()[m * 128:(m + 1) * 128, o:o + n]))
            blk.sync(q)
            ld = g.s_ld.v

            def ve(v, m=m, n=n, sl=sl, ld=ld, s_t5=s_t5, oq=oq, tprev=g.s_tn.v):
                v.wait_ge(g.s_ld.h, ld)
                v.wait_ge(g.s_tn.h, tprev)
                v.tensor_scalar(out=g.sg_s[:, 0:n], in0=g.g16_s[:, sl, 0:n],
                                scalar1=g.sc_s[:, s_t5:s_t5 + 1], scalar2=MAGIC,
                                op0=ALU.mult, op1=ALU.add)
                v.drain()
                v.tensor_scalar(out=oq[:, m, 0:n], in0=g.sg_s[:, 0:n],
                                scalar1=MAGIC, scalar2=g.ss_s[:, s_t5:s_t5 + 1],
                                op0=ALU.subtract, op1=ALU.mult)
                g.s_ve.inc(v.drain())
            blk.vector(ve)
        vd_oq = g.s_ve.v

        for ft in range(NFT):
            for half in range(2):
                def q2(qq, ft=ft, half=half, tprev=g.s_tn.v):
                    qq.wait_ge(g.s_tn.h, tprev)
                    qq.wait_ge(g.s_ld.h, g.s_ld.v)
                    g.s_ld.dma(qq.dma_start(
                        out=g.wstr_s[:, half, 0:12, :],
                        in_=g.wq_dT.ap().rearrange("(k p) f -> p k f", p=128)[:, 12 * half:12 * (half + 1), ft * 128:(ft + 1) * 128]))
                blk.sync(q2)
            ld2 = g.s_ld.v

            def tn(t, ft=ft, n=n, vd_oq=vd_oq, ld2=ld2, sprev=g.s_se.v, oq=oq):
                t.wait_ge(g.s_ve.h, vd_oq)
                t.wait_ge(g.s_ld.h, ld2)
                t.wait_ge(g.s_se.h, sprev)
                for k in range(NMT):
                    li = t.matmul(g.p_mm2[:, ft % 2, 0:n],
                             g.wstr_s[:, k // 12, k % 12, :],
                             oq[:, k, 0:n], start=(k == 0), stop=(k == NMT - 1))
                g.s_tn.inc(li)
            blk.tensor(tn)
            td = g.s_tn.v

            if ver == 1:
                def q3(qq, ft=ft, o=o, n=n, vprev=g.s_ve.v, stv=g.s_st.v):
                    qq.wait_ge(g.s_ve.h, vprev)
                    qq.wait_ge(g.s_st.h, stv)
                    qq.wait_ge(g.s_ld.h, g.s_ld.v)
                    g.s_ld.dma(qq.dma_start(out=g.fc8[:, ft, 0:n],
                                            in_=hbmT(g.xT)[:, ft, o:o + n]))
                blk.sync(q3)
                ld3 = g.s_ld.v

                def ve2(v, ft=ft, n=n, td=td, ld3=ld3):
                    v.wait_ge(g.s_tn.h, td)
                    v.wait_ge(g.s_ld.h, ld3)
                    v.scalar_tensor_tensor(out=g.fc8[:, ft, 0:n],
                                           in0=g.p_mm2[:, ft % 2, 0:n],
                                           scalar=0.5, in1=g.fc8[:, ft, 0:n],
                                           op0=ALU.mult, op1=ALU.add)
                    g.s_ve.inc(v.drain())
                blk.vector(ve2)
                vd2 = g.s_ve.v

                def se2(e, vd2=vd2):
                    e.wait_ge(g.s_ve.h, vd2)
                    e.sem_inc(g.s_se.h, 1); g.s_se.v += 1
                blk.scalar(se2)

                def gp(p, ft=ft, o=o, n=n, vd2=vd2):
                    p.wait_ge(g.s_ve.h, vd2)
                    p.wait_ge(g.s_st.h, g.s_st.v)
                    g.s_st.dma(p.dma_start(out=hbmT(g.x1_sp)[:, ft, o:o + n],
                                           in_=g.fc8[:, ft, 0:n]))
                blk.gpsimd(gp)
            else:
                def q3(qq, ft=ft, o=o, n=n, vprev=g.s_ve.v, stv=g.s_st.v):
                    qq.wait_ge(g.s_ve.h, vprev)
                    qq.wait_ge(g.s_st.h, stv)
                    qq.wait_ge(g.s_ld.h, g.s_ld.v)
                    g.s_ld.dma(qq.dma_start(out=g.fc8[:, ft, 0:n],
                                            in_=hbmT(g.x2_sp)[:, ft, o:o + n]))
                blk.sync(q3)
                ld3 = g.s_ld.v

                def ve2(v, ft=ft, n=n, td=td, ld3=ld3, ov=g.s_out.v):
                    v.wait_ge(g.s_tn.h, td)
                    v.wait_ge(g.s_ld.h, ld3)
                    v.wait_ge(g.s_out.h, ov)
                    v.scalar_tensor_tensor(out=g.st_s[:, ft % 2, 0:n],
                                           in0=g.p_mm2[:, ft % 2, 0:n],
                                           scalar=0.5, in1=g.fc8[:, ft, 0:n],
                                           op0=ALU.mult, op1=ALU.add)
                    g.s_ve.inc(v.drain())
                blk.vector(ve2)
                vd2 = g.s_ve.v

                def se2(e, vd2=vd2):
                    e.wait_ge(g.s_ve.h, vd2)
                    e.sem_inc(g.s_se.h, 1); g.s_se.v += 1
                blk.scalar(se2)

                def gp(p, ft=ft, o=o, n=n, vd2=vd2):
                    p.wait_ge(g.s_ve.h, vd2)
                    p.wait_ge(g.s_out.h, g.s_out.v)
                    g.s_out.dma(p.dma_start(
                        out=g.out_h[ft * 128:(ft + 1) * 128, o:o + n],
                        in_=g.st_s[:, ft % 2, 0:n]))
                blk.gpsimd(gp)

    if ver == 1:
        barrier_all(g)
        for c, (o, n) in enumerate(CH5):
            def q(qq, o=o, n=n, vprev=g.s_ve.v, stv=g.s_st.v):
                qq.wait_ge(g.s_ve.h, vprev)
                qq.wait_ge(g.s_st.h, stv)
                qq.wait_ge(g.s_ld.h, g.s_ld.v)
                g.s_ld.dma(qq.dma_start(out=g.fc8[:, :, 0:n],
                                        in_=hbmT(g.x1_sp)[:, :, o:o + n]))
            blk.sync(q)
            ld = g.s_ld.v

            def ve0(v, ld=ld, stv=g.s_st.v):
                v.wait_ge(g.s_ld.h, ld)
                v.wait_ge(g.s_st.h, stv)
                g.s_ve.inc(v.drain())
            blk.vector(ve0)
            ln_chunk(g, g.fc8.ap()[:, :, 0:n], v_lnout(g, n), n)
            vd = g.s_ve.v

            def ve(v, n=n, vd=vd):
                v.wait_ge(g.s_ve.h, vd)
                for ft in range(NFT):
                    v.tensor_copy(out=v_lnb16(g)[:, ft, 0:n],
                                  in_=v_lnout(g, n)[:, ft])
                v.drain()
                for ft in range(NFT):
                    v.tensor_copy(out=v_lnb16R(g)[:, ft, 0:n],
                                  in_=v_lnout(g, n)[:, ft][:, ::-1])
                g.s_ve.inc(v.drain())
            blk.vector(ve)
            vd2 = g.s_ve.v

            def gp(p, o=o, n=n, vd2=vd2):
                p.wait_ge(g.s_ve.h, vd2)
                p.wait_ge(g.s_st.h, g.s_st.v)
                g.s_st.dma(p.dma_start(out=hbmT(g.x1ln_sp)[:, :, o:o + n],
                                       in_=v_lnout(g, n)[:, :, 0:n]))
                p.wait_ge(g.s_st.h, g.s_st.v)
                g.s_st.dma(p.dma_start(out=hbmT(g.x1lnb_sp)[:, :, o:o + n],
                                       in_=v_lnb16(g)[:, :, 0:n]))
                p.wait_ge(g.s_st.h, g.s_st.v)
                g.s_st.dma(p.dma_start(
                    out=hbmT(g.x1lnbR_sp)[:, :, TOKH - o - n:TOKH - o],
                    in_=v_lnb16R(g)[:, :, 0:n]))
            blk.gpsimd(gp)


def p4_mixer(g):
    blk = g.block
    barrier_all(g)
    for di in range(2):
        mixer_direction(g, di)
        if di == 0:
            carry_exchange(g)


def carry_exchange(g):
    blk = g.block
    vd, sd = g.s_ve.v, g.s_se.v

    def gp(p, vd=vd, sd=sd):
        p.wait_ge(g.s_ve.h, vd)
        p.wait_ge(g.s_se.h, sd)
        p.wait_ge(g.s_st.h, g.s_st.v)
        g.s_st.dma(p.dma_start(out=g.carry_io.ap(), in_=g.carry_s[:, 0]))
        p.wait_ge(g.s_st.h, g.s_st.v)
        g.s_cc.inc(p.collective_compute(
            "AllReduce", ALU.add,
            replica_groups=[[0, 1], [2, 3], [4, 5], [6, 7]],
            ins=[g.carry_io.ap().opt()], outs=[g.carry_sum.ap().opt()]))
        p.wait_ge(g.s_cc.h, g.s_cc.v)
        p.wait_ge(g.s_st.h, g.s_st.v)
        g.s_st.dma(p.dma_start(out=g.pcar_s[:, :], in_=g.carry_sum.ap()))
    blk.gpsimd(gp)
    st = g.s_st.v

    def ve(v, st=st):
        v.wait_ge(g.s_st.h, st)
        v.tensor_sub(out=g.carry_s[:, 1], in0=g.pcar_s[:, :], in1=g.carry_s[:, 0])
        g.s_ve.inc(v.drain())
    blk.vector(ve)


def mixer_direction(g, di):
    blk = g.block
    xsrc = g.x1lnb_sp if di == 0 else g.x1lnbR_sp
    wname = "A" if di == 0 else "B"

    def q(qq, xsrc=xsrc, vprev=g.s_ve.v):
        qq.wait_ge(g.s_ve.h, vprev)
        qq.wait_ge(g.s_ld.h, g.s_ld.v)
        g.s_ld.dma(qq.dma_start(out=v_xlc(g)[:, :, 0:HALO],
                                in_=hbmT(xsrc)[:, :, 0:HALO]))
    blk.sync(q)
    ldh = g.s_ld.v
    for dt in range(NDT):
        def q2(qq, dt=dt, wname=wname, tprev=g.s_tn.v):
            qq.wait_ge(g.s_tn.h, tprev)
            qq.wait_ge(g.s_ld.h, g.s_ld.v)
            g.s_ld.dma(qq.dma_start(
                out=g.wstr_s[:, dt % 2, 0:NFT, :],
                in_=g.wb[f"in_wT_{wname}"].ap().rearrange("(k p) f -> p k f", p=128)[:, :, dt * 128:(dt + 1) * 128]))
        blk.sync(q2)
        ld = g.s_ld.v

        def tn(t, dt=dt, ld=ld, ldh=ldh, sprev=g.s_se.v):
            t.wait_ge(g.s_ld.h, max(ld, ldh))
            t.wait_ge(g.s_se.h, sprev)
            for k in range(NFT):
                li = t.matmul(g.p_bc[:, 0:HALO], g.wstr_s[:, dt % 2, k, :],
                         v_xlc(g)[:, k, 0:HALO], start=(k == 0),
                         stop=(k == NFT - 1))
            g.s_tn.inc(li)
        blk.tensor(tn)
        td = g.s_tn.v

        def se(e, dt=dt, td=td):
            e.wait_ge(g.s_tn.h, td)
            g.s_se.inc(e.copy(out=g.xicar_s[:, dt, :], in_=g.p_bc[:, 0:HALO]))
        blk.scalar(se)

    for c in range(NCH):
        mixer_chunk(g, di, c, wname)


def mixer_chunk(g, di, c, wname):
    blk = g.block
    o = HALO + c * TC
    xsrc = g.x1lnb_sp if di == 0 else g.x1lnbR_sp
    corig = c if di == 0 else NCH - 1 - c
    oorig = HALO + corig * TC
    carry = g.carry_s.ap()[:, di]
    xi = v_xi(g)
    u = v_u(g)
    du = v_du(g)

    def vgate(v, stv=g.s_st.v):
        v.wait_ge(g.s_st.h, stv)
        g.s_ve.inc(v.drain())
    blk.vector(vgate)

    def sgate(e, stv=g.s_st.v):
        e.wait_ge(g.s_st.h, stv)
        e.sem_inc(g.s_se.h, 1)
    blk.scalar(sgate)
    g.s_se.v += 1

    def q(qq, o=o, xsrc=xsrc, tprev=g.s_tn.v, sprev=g.s_se.v):
        qq.wait_ge(g.s_tn.h, tprev)
        qq.wait_ge(g.s_se.h, sprev)
        qq.wait_ge(g.s_ld.h, g.s_ld.v)
        g.s_ld.dma(qq.dma_start(out=v_xlc(g)[:, :, 0:TC],
                                in_=hbmT(xsrc)[:, :, o:o + TC]))
    blk.sync(q)
    ld_x = g.s_ld.v

    for m in range(2 * NDT):
        def q2(qq, m=m, wname=wname, tprev=g.s_tn.v):
            qq.wait_ge(g.s_tn.h, tprev)
            qq.wait_ge(g.s_ld.h, g.s_ld.v)
            g.s_ld.dma(qq.dma_start(
                out=g.wstr_s[:, m % 2, 0:NFT, :],
                in_=g.wb[f"in_wT_{wname}"].ap().rearrange("(k p) f -> p k f", p=128)[:, :, m * 128:(m + 1) * 128]))
        blk.sync(q2)
        ld = g.s_ld.v

        def tn(t, m=m, ld=ld, ld_x=ld_x, sprev=g.s_se.v, vprev=g.s_ve.v,
               stprev=g.s_st.v):
            t.wait_ge(g.s_ld.h, max(ld, ld_x))
            t.wait_ge(g.s_se.h, sprev)
            t.wait_ge(g.s_ve.h, vprev)
            t.wait_ge(g.s_st.h, stprev)
            for k in range(NFT):
                li = t.matmul(g.p_mm[:, m % 2, :], g.wstr_s[:, m % 2, k, :],
                         v_xlc(g)[:, k, 0:TC], start=(k == 0), stop=(k == NFT - 1))
            g.s_tn.inc(li)
        blk.tensor(tn)
        td = g.s_tn.v

        if m < NDT:
            def se(e, m=m, td=td):
                e.wait_ge(g.s_tn.h, td)
                g.s_se.inc(e.copy(out=xi[:, m, HALO:HALO + TC],
                                  in_=g.p_mm[:, m % 2, :]))
            blk.scalar(se)
        else:
            def se(e, m=m, td=td, stprev=g.s_st.v):
                e.wait_ge(g.s_tn.h, td)
                e.wait_ge(g.s_st.h, stprev)
                g.s_se.inc(e.copy(out=g.st_s[:, m % 2, :],
                                  in_=g.p_mm[:, m % 2, :]))
            blk.scalar(se)
            sd = g.s_se.v

            def gp(p, m=m, corig=corig, sd=sd):
                p.wait_ge(g.s_se.h, sd)
                p.wait_ge(g.s_st.h, g.s_st.v)
                g.s_st.dma(p.dma_start(
                    out=g.z_sp.ap()[(m - NDT) * 128:(m - NDT + 1) * 128,
                                    corig * TC:(corig + 1) * TC],
                    in_=g.st_s[:, m % 2, :]))
            blk.gpsimd(gp)

    sd_xi = g.s_se.v

    def ve_h(v, sd_xi=sd_xi):
        v.wait_ge(g.s_se.h, sd_xi)
        v.tensor_copy(out=xi[:, :, 0:HALO], in_=g.xicar_s[:, :, :])
        g.s_ve.inc(v.drain())
    blk.vector(ve_h)

    for dt in range(NDT):
        def ve(v, dt=dt, di=di):
            v.tensor_scalar_mul(out=u[:, dt], in0=xi[:, dt, 0:TC],
                                scalar1=g.cw_s[:, di, dt, 0:1])
            v.drain()
            for k in range(1, 4):
                v.scalar_tensor_tensor(out=u[:, dt], in0=xi[:, dt, k:k + TC],
                                       scalar=g.cw_s[:, di, dt, k:k + 1],
                                       in1=u[:, dt], op0=ALU.mult, op1=ALU.add)
                v.drain()
            v.sem_inc(g.s_ve.h, 1); g.s_ve.v += 1
        blk.vector(ve)

    def ve_t(v):
        v.tensor_copy(out=g.xicar_s[:, :, :], in_=xi[:, :, TC:TC + HALO])
        g.s_ve.inc(v.drain())
    blk.vector(ve_t)
    ve_tail = g.s_ve.v

    def se_u(e, ve_tail=ve_tail, di=di):
        e.wait_ge(g.s_ve.h, ve_tail)
        for dt in range(NDT):
            e.activation(out=u[:, dt], in_=u[:, dt], func=AF.Silu,
                         bias=g.cb_s[:, di, dt:dt + 1], scale=1.0)
        g.s_se.inc(e.drain())
    blk.scalar(se_u)
    se_silu = g.s_se.v

    for dt in range(NDT):
        def ve(v, dt=dt, se_silu=se_silu, tprev=g.s_tn.v):
            v.wait_ge(g.s_se.h, se_silu)
            v.wait_ge(g.s_tn.h, tprev)
            g.s_ve.inc(v.tensor_copy(out=g.ubt_s[:, 0, :], in_=u[:, dt]))
        blk.vector(ve)
        vd = g.s_ve.v

        def tn(t, dt=dt, vd=vd, di=di, sprev=g.s_se.v):
            t.wait_ge(g.s_ve.h, vd)
            if dt == 0:
                t.wait_ge(g.s_se.h, sprev)
            li = t.matmul(g.p_xd[:, :], g.xprojT_s[:, di, dt, :],
                     g.ubt_s[:, 0, :],
                     start=(dt == 0), stop=(dt == NDT - 1))
            g.s_tn.inc(li)
        blk.tensor(tn)
    td_xd = g.s_tn.v

    def se_xd(e, td_xd=td_xd):
        e.wait_ge(g.s_tn.h, td_xd)
        e.copy(out=g.dtb16_s[:, :], in_=g.p_xd[0:DTR, :])
        g.s_se.inc(e.copy(out=g.xdb_s[0:32, 0, :], in_=g.p_xd[DTR:96, :]))
    blk.scalar(se_xd)
    sd_xd = g.s_se.v

    for dt in range(NDT):
        def tn(t, dt=dt, sd_xd=sd_xd, di=di, sprev=g.s_se.v):
            t.wait_ge(g.s_se.h, max(sd_xd, sprev if dt >= 2 else 0))
            li = t.matmul(g.p_mm[:, dt % 2, :], g.dtwT_s[:, di, dt, :],
                     g.dtb16_s[:, :], start=True, stop=True)
            g.s_tn.inc(li)
        blk.tensor(tn)
        td = g.s_tn.v

        def se(e, dt=dt, td=td, di=di):
            e.wait_ge(g.s_tn.h, td)
            e.activation(out=du[:, dt], in_=g.p_mm[:, dt % 2, :], func=AF.Exp,
                         bias=g.dtb_s[:, di, dt:dt + 1], scale=1.0)
            e.drain()
            g.s_se.inc(e.activation(out=du[:, dt], in_=du[:, dt], func=AF.Ln,
                                    bias=g.onescol_s[:, :], scale=1.0))
        blk.scalar(se)
    sd_delta = g.s_se.v

    ve_duv = g.s_ve.v

    def ve_bc(v, sd_xd=sd_xd):
        v.wait_ge(g.s_se.h, sd_xd)
        v.tensor_copy(out=g.bclo_s[:, 0, :], in_=g.xdb_s[0:32, 0, :])
        v.drain()
        v.tensor_sub(out=g.xdb_s[0:32, 1, :], in0=g.xdb_s[0:32, 0, :],
                     in1=g.bclo_s[:, 0, :])
        v.drain()
        v.tensor_copy(out=g.bclo_s[:, 1, :], in_=g.xdb_s[0:32, 1, :])
        g.s_ve.inc(v.drain())
    blk.vector(ve_bc)
    ve_bcv = g.s_ve.v

    hall = v_hall(g)
    crep = v_crep(g)
    brep = v_brep(g)
    for gi in range(2):
        for n in range(8):
            def tn(t, gi=gi, n=n, ve_bcv=ve_bcv, sprev=g.s_se.v):
                t.wait_ge(g.s_ve.h, ve_bcv)
                t.wait_ge(g.s_se.h, sprev)
                sn = gi * 8 + n
                li = t.matmul(g.p_bc[:, :], g.sel_s[:, sn, :], g.bclo_s[:, 0, :],
                         start=True, stop=False)
                li = t.matmul(g.p_bc[:, :], g.sel_s[:, sn, :], g.bclo_s[:, 1, :],
                              start=False, stop=True)
                g.s_tn.inc(li)
                li = t.matmul(g.p_bc2[:, :], g.sel_s[:, 16 + sn, :], g.bclo_s[:, 0, :],
                         start=True, stop=False)
                li = t.matmul(g.p_bc2[:, :], g.sel_s[:, 16 + sn, :],
                              g.bclo_s[:, 1, :], start=False, stop=True)
                g.s_tn.inc(li)
            blk.tensor(tn)
            td = g.s_tn.v

            def se(e, n=n, td=td, vprev=g.s_ve.v):
                e.wait_ge(g.s_tn.h, td)
                e.wait_ge(g.s_ve.h, vprev)
                e.copy(out=brep[:, n, :], in_=g.p_bc[:, :])
                g.s_se.inc(e.copy(
                    out=crep.rearrange("p (t n) -> p t n", n=8)[:, :, n],
                    in_=g.p_bc2[:, :]))
            blk.scalar(se)
        sd_bc = g.s_se.v

        for dt in range(NDT):
            def vdup(v, dt=dt, sd_delta=sd_delta):
                v.wait_ge(g.s_se.h, sd_delta)
                v.tensor_mul(out=g.dup_s[:, :], in0=v_du(g)[:, dt],
                             in1=v_u(g)[:, dt])
                g.s_ve.inc(v.drain())
            blk.vector(vdup)

            for n in range(8):
                sn = gi * 8 + n
                slot = n % 2

                def se(e, dt=dt, sn=sn, slot=slot, sd_bc=sd_bc, di=di,
                       vprev=g.s_ve.v):
                    e.wait_ge(g.s_se.h, sd_bc)
                    e.wait_ge(g.s_ve.h, vprev)
                    g.s_se.inc(e.activation(
                        out=g.da_s[:, slot, :], in_=v_du(g)[:, dt],
                        func=AF.Exp, bias=0.0,
                        scale=g.A_s[:, di, dt * NS + sn:dt * NS + sn + 1]))
                blk.scalar(se)
                sd = g.s_se.v

                def ve(v, dt=dt, n=n, sn=sn, slot=slot, sd=sd, carry=carry,
                       ve_duv=ve_duv):
                    v.wait_ge(g.s_se.h, sd)
                    v.wait_ge(g.s_ve.h, ve_duv)
                    v.tensor_mul(out=g.w_sc[:, 0, :], in0=g.dup_s[:, :],
                                 in1=brep[:, n, :])
                    v.drain()
                    v.tensor_tensor_scan(
                        out=hall.rearrange("p (t n) -> p t n", n=8)[:, :, n],
                        data0=g.da_s[:, slot, :], data1=g.w_sc[:, 0, :],
                        initial=carry[:, dt * NS + sn:dt * NS + sn + 1],
                        op0=ALU.mult, op1=ALU.add)
                    g.s_ve.inc(v.drain())
                blk.vector(ve)

                if di == 0 and c == 0 and hasattr(g, "dbg_daw") and dt in (0, 1, 8):
                    def gpw(p, dt=dt, sn=sn, slot=slot, n=n, vprev=g.s_ve.v):
                        p.wait_ge(g.s_ve.h, vprev)
                        p.wait_ge(g.s_st.h, g.s_st.v)
                        row = dt * NS + sn
                        g.s_st.dma(p.dma_start(
                            out=g.dbg_daw.ap()[row, 0, :][None, :],
                            in_=g.da_s[0:1, slot, :]))
                        p.wait_ge(g.s_st.h, g.s_st.v)
                        g.s_st.dma(p.dma_start(
                            out=g.dbg_daw.ap()[row, 1, :][None, :],
                            in_=g.w_sc[0:1, 0, :]))
                        p.wait_ge(g.s_st.h, g.s_st.v)
                        ctx = g.nc.allow_non_contiguous_dma(reason="debug")
                        ctx.__enter__()
                        g.s_st.dma(p.dma_start(
                            out=g.dbg_daw.ap()[row, 2, :][None, :],
                            in_=hall.rearrange("p (t n) -> p t n", n=8)[0:1, :, n]))
                        ctx.__exit__(None, None, None)
                    blk.gpsimd(gpw)
                    def vg(v, stv=g.s_st.v):
                        v.wait_ge(g.s_st.h, stv)
                    blk.vector(vg)

            def ve2(v, dt=dt, gi=gi, carry=carry, stv=g.s_st.v):
                v.wait_ge(g.s_st.h, stv)
                v.tensor_copy(out=carry[:, dt * NS + gi * 8:dt * NS + gi * 8 + 8],
                              in_=hall[:, (TC - 1) * 8:TC * 8])
                v.drain()
                v.tensor_mul(out=hall[:, :], in0=hall[:, :], in1=crep[:, :])
                v.drain()
                if gi == 0:
                    v.tensor_reduce(out=g.yt_s[:, :],
                                    in_=hall.rearrange("p (t n) -> p t n", n=8),
                                    axis=AX.X, op=ALU.add)
                    g.s_ve.inc(v.drain())
                else:
                    v.tensor_reduce(out=g.ytmp_s[:, :],
                                    in_=hall.rearrange("p (t n) -> p t n", n=8),
                                    axis=AX.X, op=ALU.add)
                    g.s_ve.inc(v.drain())
            blk.vector(ve2)

            if gi == 0:
                def gp0(p, dt=dt, corig=corig, vprev=g.s_ve.v):
                    p.wait_ge(g.s_ve.h, vprev)
                    p.wait_ge(g.s_st.h, g.s_st.v)
                    g.s_st.dma(p.dma_start(
                        out=g.y0_sp.ap()[dt * 128:(dt + 1) * 128,
                                         corig * TC:(corig + 1) * TC],
                        in_=g.yt_s[:, :]))
                blk.gpsimd(gp0)

            if gi == 1:
                def q3(qq, dt=dt, corig=corig, vprev=g.s_ve.v, stv=g.s_st.v):
                    qq.wait_ge(g.s_ve.h, vprev)
                    qq.wait_ge(g.s_st.h, stv)
                    qq.wait_ge(g.s_ld.h, g.s_ld.v)
                    g.s_ld.dma(qq.dma_start(
                        out=g.zdt_s[:, dt % 2, :],
                        in_=g.z_sp.ap()[dt * 128:(dt + 1) * 128,
                                        corig * TC:(corig + 1) * TC]))
                    qq.wait_ge(g.s_ld.h, g.s_ld.v)
                    g.s_ld.dma(qq.dma_start(
                        out=g.y0t_s[:, dt % 2, :],
                        in_=g.y0_sp.ap()[dt * 128:(dt + 1) * 128,
                                         corig * TC:(corig + 1) * TC]))
                blk.sync(q3)
                ldz = g.s_ld.v

                def se2(e, dt=dt, ldz=ldz, vprev=g.s_ve.v):
                    e.wait_ge(g.s_ld.h, ldz)
                    e.wait_ge(g.s_ve.h, vprev)
                    g.s_se.inc(e.activation(out=g.zdt_s[:, dt % 2, :],
                                            in_=g.zdt_s[:, dt % 2, :],
                                            func=AF.Silu, bias=0.0, scale=1.0))
                blk.scalar(se2)
                sdz = g.s_se.v

                def ve3(v, dt=dt, di=di, sdz=sdz, ldz=ldz):
                    v.wait_ge(g.s_se.h, sdz)
                    v.wait_ge(g.s_ld.h, ldz)
                    v.tensor_add(out=g.yt_s[:, :], in0=g.ytmp_s[:, :],
                                 in1=g.y0t_s[:, dt % 2, :])
                    v.drain()
                    v.scalar_tensor_tensor(out=g.yt_s[:, :], in0=v_u(g)[:, dt],
                                           scalar=g.Dp_s[:, di, dt:dt + 1],
                                           in1=g.yt_s[:, :], op0=ALU.mult,
                                           op1=ALU.add)
                    v.drain()
                    v.tensor_mul(out=v_gb(g, dt), in0=g.yt_s[:, :],
                                 in1=g.zdt_s[:, dt % 2, :])
                    g.s_ve.inc(v.drain())
                blk.vector(ve3)

                if di == 0 and hasattr(g, "dbg_y"):
                    def gpy(p, dt=dt, corig=corig, vprev=g.s_ve.v):
                        p.wait_ge(g.s_ve.h, vprev)
                        p.wait_ge(g.s_st.h, g.s_st.v)
                        g.s_st.dma(p.dma_start(
                            out=g.dbg_y.ap()[dt * 128:(dt + 1) * 128,
                                             corig * TC:(corig + 1) * TC],
                            in_=g.yt_s[:, :]))
                    blk.gpsimd(gpy)

    ve_gball = g.s_ve.v

    if di == 0:
        for dn, view in (("dbg_u", v_u(g)), ("dbg_dl", v_du(g))):
            if hasattr(g, dn):
                def gpd(p, dn=dn, view=view, corig=corig, vprev=g.s_ve.v):
                    p.wait_ge(g.s_ve.h, vprev)
                    p.wait_ge(g.s_st.h, g.s_st.v)
                    g.s_st.dma(p.dma_start(
                        out=getattr(g, dn).ap().rearrange("(d p) t -> p d t", p=128)[:, :, corig * TC:(corig + 1) * TC],
                        in_=view))
                blk.gpsimd(gpd)

    def q4(qq, oorig=oorig, vprev=g.s_ve.v):
        qq.wait_ge(g.s_ve.h, vprev)
        qq.wait_ge(g.s_ld.h, g.s_ld.v)
        g.s_ld.dma(qq.dma_start(out=g.fc8[:, :, :],
                                in_=hbmT(g.x1ln_sp)[:, :, oorig:oorig + TC]))
    blk.sync(q4)
    ld_xl = g.s_ld.v

    for ft in range(NFT):
        def q5(qq, ft=ft, wname=wname, tprev=g.s_tn.v):
            qq.wait_ge(g.s_tn.h, tprev)
            qq.wait_ge(g.s_ld.h, g.s_ld.v)
            g.s_ld.dma(qq.dma_start(
                out=g.wstr_s[:, ft % 2, 0:12, :],
                in_=g.wb[f"out_wT_{wname}"].ap().rearrange("(k p) f -> p k f", p=128)[:, 0:12, ft * 128:(ft + 1) * 128]))
            qq.wait_ge(g.s_ld.h, g.s_ld.v)
            g.s_ld.dma(qq.dma_start(
                out=g.wstr_s[:, (ft + 1) % 2, 0:4, :],
                in_=g.wb[f"out_wT_{wname}"].ap().rearrange("(k p) f -> p k f", p=128)[:, 12:16, ft * 128:(ft + 1) * 128]))
        blk.sync(q5)
        ld = g.s_ld.v

        def tn(t, ft=ft, ld=ld, ve_gball=ve_gball, vprev=g.s_ve.v):
            t.wait_ge(g.s_ld.h, ld)
            t.wait_ge(g.s_ve.h, ve_gball)
            if ft >= 2:
                t.wait_ge(g.s_ve.h, vprev)
            for k in range(12):
                li = t.matmul(g.p_mm[:, ft % 2, :], g.wstr_s[:, ft % 2, k, :],
                         v_gb(g, k), start=(k == 0), stop=False)
            for k in range(12, 16):
                li = t.matmul(g.p_mm[:, ft % 2, :],
                         g.wstr_s[:, (ft + 1) % 2, k - 12, :],
                         v_gb(g, k), start=False, stop=(k == 15))
            g.s_tn.inc(li)
        blk.tensor(tn)
        td = g.s_tn.v

        def ve(v, ft=ft, di=di, td=td, ld_xl=ld_xl):
            v.wait_ge(g.s_tn.h, td)
            v.wait_ge(g.s_ld.h, ld_xl)
            src = g.p_mm[:, ft % 2, :] if di == 0 else g.p_mm[:, ft % 2, ::-1]
            v.scalar_tensor_tensor(out=v_mo(g)[:, ft, :], in0=src, scalar=1.0,
                                   in1=g.fc8[:, ft, :], op0=ALU.mult,
                                   op1=ALU.add)
            g.s_ve.inc(v.drain())
        blk.vector(ve)
    ve_mo = g.s_ve.v

    for ft in range(NFT):
        def q6(qq, ft=ft, wname=wname, tprev=g.s_tn.v):
            qq.wait_ge(g.s_tn.h, tprev)
            qq.wait_ge(g.s_ld.h, g.s_ld.v)
            g.s_ld.dma(qq.dma_start(
                out=g.wstr_s[:, ft % 2, 0:NFT, :],
                in_=g.wb[f"projT_{wname}"].ap().rearrange("(k p) f -> p k f", p=128)[:, :, ft * 128:(ft + 1) * 128]))
        blk.sync(q6)
        ld = g.s_ld.v

        def tn(t, ft=ft, ld=ld, ve_mo=ve_mo, sprev=g.s_se.v, vprev=g.s_ve.v):
            t.wait_ge(g.s_ld.h, ld)
            t.wait_ge(g.s_ve.h, ve_mo)
            if ft >= 2:
                t.wait_ge(g.s_se.h, sprev)
                t.wait_ge(g.s_ve.h, vprev)
            for k in range(NFT):
                li = t.matmul(g.p_mm2[:, ft % 2, :], g.wstr_s[:, ft % 2, k, :],
                         v_mo(g)[:, k, :], start=(k == 0), stop=(k == NFT - 1))
            g.s_tn.inc(li)
        blk.tensor(tn)
        td = g.s_tn.v

        if di == 0:
            def se(e, ft=ft, td=td, stprev=g.s_st.v):
                e.wait_ge(g.s_tn.h, td)
                e.wait_ge(g.s_st.h, stprev)
                g.s_se.inc(e.copy(out=g.st_s[:, ft % 2, :],
                                  in_=g.p_mm2[:, ft % 2, :]))
            blk.scalar(se)
            sd = g.s_se.v

            def gp(p, ft=ft, corig=corig, sd=sd):
                p.wait_ge(g.s_se.h, sd)
                p.wait_ge(g.s_st.h, g.s_st.v)
                g.s_st.dma(p.dma_start(
                    out=hbmT(g.projA_sp)[:, ft, corig * TC:(corig + 1) * TC],
                    in_=g.st_s[:, ft % 2, :]))
            blk.gpsimd(gp)
        else:
            def q7(qq, ft=ft, corig=corig, vprev=g.s_ve.v):
                qq.wait_ge(g.s_ve.h, vprev)
                qq.wait_ge(g.s_ld.h, g.s_ld.v)
                g.s_ld.dma(qq.dma_start(
                    out=g.st_s[:, ft % 2, :],
                    in_=hbmT(g.projA_sp)[:, ft, corig * TC:(corig + 1) * TC]))
            blk.sync(q7)
            ld2 = g.s_ld.v

            def ve(v, ft=ft, td=td, ld2=ld2):
                v.wait_ge(g.s_tn.h, td)
                v.wait_ge(g.s_ld.h, ld2)
                v.scalar_tensor_tensor(out=v_pp(g)[:, ft, :],
                                       in0=g.p_mm2[:, ft % 2, :],
                                       scalar=g.pb_s[:, ft:ft + 1],
                                       in1=g.st_s[:, ft % 2, :],
                                       op0=ALU.add, op1=ALU.add)
                g.s_ve.inc(v.drain())
            blk.vector(ve)

    if di == 1:
        mixer_epilogue(g, corig, oorig)


def mixer_epilogue(g, corig, oorig):
    blk = g.block
    pp = v_pp(g)
    x2 = v_x2c(g)
    ln_chunk(g, pp, pp, TC)

    def q(qq, oorig=oorig, vprev=g.s_ve.v):
        qq.wait_ge(g.s_ve.h, vprev)
        qq.wait_ge(g.s_ld.h, g.s_ld.v)
        g.s_ld.dma(qq.dma_start(out=g.fc8[:, :, :],
                                in_=hbmT(g.x1_sp)[:, :, oorig:oorig + TC]))
    blk.sync(q)
    ld = g.s_ld.v

    def ve(v, ld=ld):
        v.wait_ge(g.s_ld.h, ld)
        for ft in range(NFT):
            v.tensor_scalar(out=pp[:, ft, :], in0=pp[:, ft, :],
                            scalar1=g.lnw_s[:, ft:ft + 1],
                            scalar2=g.lnb_s[:, ft:ft + 1],
                            op0=ALU.mult, op1=ALU.add)
        v.drain()
        for ft in range(NFT):
            v.tensor_add(out=x2[:, ft, :], in0=pp[:, ft, :], in1=g.fc8[:, ft, :])
        g.s_ve.inc(v.drain())
    blk.vector(ve)
    vd = g.s_ve.v

    def gp(p, corig=corig, vd=vd):
        p.wait_ge(g.s_ve.h, vd)
        p.wait_ge(g.s_st.h, g.s_st.v)
        g.s_st.dma(p.dma_start(
            out=hbmT(g.x2_sp)[:, :, corig * TC:(corig + 1) * TC],
            in_=x2[:, :, :]))
    blk.gpsimd(gp)

    ln_chunk(g, x2, pp, TC)
    absmax_acc(g, pp, SC_X2, axis=AX.XY)
    vd2 = g.s_ve.v

    def gp2(p, corig=corig, vd2=vd2):
        p.wait_ge(g.s_ve.h, vd2)
        p.wait_ge(g.s_st.h, g.s_st.v)
        g.s_st.dma(p.dma_start(
            out=hbmT(g.x2ln_sp)[:, :, corig * TC:(corig + 1) * TC],
            in_=pp[:, :, :]))
    blk.gpsimd(gp2)


def finish(g):
    blk = g.block

    def gp(p):
        p.wait_ge(g.s_out.h, g.s_out.v)
        p.wait_ge(g.s_st.h, g.s_st.v)
    blk.gpsimd(gp)



def kernel(**inputs):
    nc = build_nc()
    in_maps = make_in_maps(inputs)
    res = run_bass_kernel_spmd(nc, in_maps, core_ids=list(range(8)))
    return assemble([r for r in res.results])


def make_in_maps(inputs):
    i = inputs
    x = i["L0_attn_x"]
    w = {}
    for pre in ("fw", "bw"):
        w[f"in_wT_{pre}"] = np.ascontiguousarray(i[f"L0_attn_{pre}_in_w"].T)
        w[f"conv_w_{pre}"] = np.ascontiguousarray(i[f"L0_attn_{pre}_conv_w"][:, 0, :])
        w[f"conv_b_{pre}"] = np.ascontiguousarray(i[f"L0_attn_{pre}_conv_b"].reshape(DI, 1))
        w[f"xproj_wT_{pre}"] = np.ascontiguousarray(i[f"L0_attn_{pre}_xproj_w"].T)
        w[f"dt_wT_{pre}"] = np.ascontiguousarray(i[f"L0_attn_{pre}_dt_w"].T)
        w[f"dt_b_{pre}"] = np.ascontiguousarray(i[f"L0_attn_{pre}_dt_b"].reshape(DI, 1))
        w[f"Alog_{pre}"] = np.ascontiguousarray(i[f"L0_attn_{pre}_Alog"])
        w[f"Dp_{pre}"] = np.ascontiguousarray(i[f"L0_attn_{pre}_Dp"].reshape(DI, 1))
        w[f"out_wT_{pre}"] = np.ascontiguousarray(i[f"L0_attn_{pre}_out_w"].T)
    pT = np.ascontiguousarray(i["L0_attn_proj_w"].T)
    w["projT_fw"] = np.ascontiguousarray(pT[:D, :])
    w["projT_bw"] = np.ascontiguousarray(pT[D:, :])

    shared = dict(
        proj_b=np.ascontiguousarray(i["L0_attn_proj_b"].reshape(D, 1)),
        ln_w=np.ascontiguousarray(i["L0_attn_ln_w"].reshape(D, 1)),
        ln_b=np.ascontiguousarray(i["L0_attn_ln_b"].reshape(D, 1)),
        w_gate=np.ascontiguousarray(i["L0_ffn_w_gate"]),
        w_up=np.ascontiguousarray(i["L0_ffn_w_up"]),
        w_down=np.ascontiguousarray(i["L0_ffn_w_down"]),
    )
    maps = []
    for c in range(8):
        b, h = c // 2, c % 2
        xb = x[b].T
        lo, hi = h * TOK - HALO, h * TOK + TOK + HALO
        pad_l, pad_r = max(0, -lo), max(0, hi - LFULL)
        sl = xb[:, max(0, lo):min(LFULL, hi)]
        if pad_l or pad_r:
            sl = np.pad(sl, ((0, 0), (pad_l, pad_r)))
        if h == 1:
            sl = sl[:, ::-1]
        m = dict(shared)
        m["xT"] = np.ascontiguousarray(sl)
        dirA = "fw" if h == 0 else "bw"
        for key, val in w.items():
            base, pre = key.rsplit("_", 1)
            tgt = "A" if pre == dirA else "B"
            m[f"{base}_{tgt}"] = val
        maps.append(m)
    return maps


def assemble(results):
    out = np.empty((B, LFULL, D), np.float32)
    for c in range(8):
        b, h = c // 2, c % 2
        o = results[c]["out"]
        if h == 1:
            o = o[:, ::-1]
        out[b, h * TOK:(h + 1) * TOK, :] = o.T
    return out


# version 03
# speedup vs baseline: 1.0917x; 1.0917x over previous
import contextlib
import numpy as np

import concourse.bass as bass
import concourse.mybir as mybir
from concourse.bass_utils import run_bass_kernel_spmd

F32 = mybir.dt.float32
F16 = mybir.dt.float16
BF16 = mybir.dt.bfloat16
AF = mybir.ActivationFunctionType
ALU = mybir.AluOpType
AX = mybir.AxisListType

D = 1024
DI = 2048
NDT = 16
NFT = 8
DTR = 64
NS = 16
FF = 3072
NMT = 24
B = 4
LFULL = 4096
TOK = 2048
HALO = 3
TOKH = TOK + 2 * HALO
TC = 512
NCH = 4
CH5 = [(0, 512), (512, 512), (1024, 512), (1536, 512), (2048, 6)]
CH4 = [(i * TC, TC) for i in range(NCH)]

QMAX = 32767.0
WQMAX = 7.0
MAGIC = float(np.float32(1.5 * 2 ** 23))
LN_EPS = 1e-5

SC_X, SC_UP, SC_GATE, SC_GS, SC_T5 = 0, 1, 2, 3, 4
SC_X2, SC_UP2, SC_GATE2, SC_GS2, SC_T52 = 5, 6, 7, 8, 9
N_SCAL = 10
TMPC = 15


class Sem:
    def __init__(self, es, nc, name):
        self.h = es.enter_context(nc.semaphore(name))
        self.v = 0

    def inc(self, instr, n=1):
        instr.then_inc(self.h, n)
        self.v += n

    def dma(self, instr):
        self.inc(instr, 16)


class G:
    pass


def build_nc(taps=()):
    nc = bass.Bass("TRN2", target_bir_lowering=False)
    es = contextlib.ExitStack()
    g = G()
    g.nc = nc
    g.taps = list(taps)

    g.xT = nc.dram_tensor("xT", [D, TOKH], F32, kind="ExternalInput")
    g.inp = {}
    for d in ("A", "B"):
        for nm, shape in (("in_wT", [D, 2 * DI]), ("conv_w", [DI, 4]),
                          ("conv_b", [DI, 1]), ("xproj_wT", [DI, 96]),
                          ("dt_wT", [DTR, DI]), ("dt_b", [DI, 1]),
                          ("Alog", [DI, NS]), ("Dp", [DI, 1]),
                          ("out_wT", [DI, D]), ("projT", [D, D])):
            g.inp[f"{nm}_{d}"] = nc.dram_tensor(f"{nm}_{d}", shape, F32,
                                                kind="ExternalInput").ap()
    g.proj_b = nc.dram_tensor("proj_b", [D, 1], F32, kind="ExternalInput").ap()
    g.ln_w = nc.dram_tensor("ln_w", [D, 1], F32, kind="ExternalInput").ap()
    g.ln_b = nc.dram_tensor("ln_b", [D, 1], F32, kind="ExternalInput").ap()
    g.w_up = nc.dram_tensor("w_up", [FF, D], F32, kind="ExternalInput").ap()
    g.w_gate = nc.dram_tensor("w_gate", [FF, D], F32, kind="ExternalInput").ap()
    g.w_down = nc.dram_tensor("w_down", [D, FF], F32, kind="ExternalInput").ap()
    g.out_h = nc.dram_tensor("out", [D, TOK], F32, kind="ExternalOutput").ap()

    def dram(name, shape, dt):
        if name in g.taps:
            t = nc.dram_tensor(name, shape, dt, kind="ExternalOutput")
        else:
            t = nc.dram_tensor(name, shape, dt)
        setattr(g, name, t)
        return t

    dram("wq_upT", [D, FF], BF16)
    dram("wq_gT", [D, FF], BF16)
    dram("wq_dT", [FF, D], BF16)
    g.wb = {}
    for d in ("A", "B"):
        g.wb[f"in_wT_{d}"] = dram(f"b_in_wT_{d}", [D, 2 * DI], BF16)
        g.wb[f"out_wT_{d}"] = dram(f"b_out_wT_{d}", [DI, D], BF16)
        g.wb[f"projT_{d}"] = dram(f"b_projT_{d}", [D, D], BF16)
    dram("xln_sp", [D, TOKH], F32)
    dram("x1_sp", [D, TOKH], F32)
    dram("x1ln_sp", [D, TOKH], F32)
    dram("x1lnb_sp", [D, TOKH], BF16)
    dram("x1lnbR_sp", [D, TOKH], BF16)
    dram("x2_sp", [D, TOK], F32)
    dram("x2ln_sp", [D, TOK], F32)
    dram("up_sp", [FF, TOKH], F16)
    dram("gate_sp", [FF, TOKH], F16)
    dram("gs_sp", [FF, TOKH], F16)
    dram("t5_sp", [FF, TOKH], F16)
    dram("z_sp", [DI, TOK], F32)
    dram("projA_sp", [D, TOK], F32)
    dram("y0_sp", [DI, TOK], F32)
    g.scal_in = [dram(f"scin{i}", [1, 128], F32) for i in range(N_SCAL)]
    g.scal_out = [dram(f"scout{i}", [1, 128], F32) for i in range(N_SCAL)]
    dram("carry_io", [128, NDT * NS], F32)
    for dn in ("dbg_u", "dbg_dl", "dbg_y"):
        if dn in g.taps:
            dram(dn, [DI, TOK], F32)
    if "dbg_daw" in g.taps:
        dram("dbg_daw", [NDT * NS, 3, TC], F32)
    dram("carry_sum", [128, NDT * NS], F32)

    E = es.enter_context

    def sbuf(name, shape, dt):
        t = E(nc.sbuf_tensor(name, shape, dt))
        setattr(g, name, t)
        return t

    sbuf("ar3", [128, 16512], BF16)
    sbuf("aru", [128, 8192], F32)
    sbuf("arb", [128, 24576], F16)
    sbuf("fc8", [128, NFT, TC], F32)
    sbuf("zdt_s", [128, 2, TC], F32)
    sbuf("ubt_s", [128, 1, TC], BF16)
    sbuf("yt_s", [128, TC], F32)
    sbuf("y0t_s", [128, 2, TC], F32)
    sbuf("dup_s", [128, TC], F32)
    sbuf("ytmp_s", [128, TC], F32)
    sbuf("da_s", [128, 2, TC], F32)
    sbuf("w_sc", [128, 1, TC], F32)
    sbuf("wstr_s", [128, 2, 12, 128], BF16)
    sbuf("st_s", [128, 2, 512], F32)
    sbuf("stb_s", [128, 2, 512], F16)
    sbuf("g16_s", [128, 2, 512], F16)
    sbuf("h16_s", [128, 2, 512], F16)
    sbuf("sg_s", [128, 512], F32)
    sbuf("sg2_s", [128, 512], F32)
    sbuf("dq_s", [128, 96, 3], F32)
    sbuf("xdb_s", [96, 2, TC], F32)
    sbuf("dtb16_s", [DTR, TC], BF16)
    sbuf("bclo_s", [32, 2, TC], BF16)
    sbuf("sel_s", [32, 32, 128], BF16)
    sbuf("xicar_s", [128, NDT, HALO], F32)
    sbuf("ones_s", [1, 128], BF16)
    sbuf("onesf_s", [1, 128], F32)
    sbuf("onescol_s", [128, 1], F32)
    sbuf("ident_s", [128, 128], BF16)
    sbuf("identf_s", [128, 128], F32)
    sbuf("A_s", [128, 2, NDT * NS], F32)
    sbuf("Dp_s", [128, 2, NDT], F32)
    sbuf("cb_s", [128, 2, NDT], F32)
    sbuf("cw_s", [128, 2, NDT, 4], F32)
    sbuf("dtb_s", [128, 2, NDT], F32)
    sbuf("pb_s", [128, NFT], F32)
    sbuf("lnw_s", [128, NFT], F32)
    sbuf("lnb_s", [128, NFT], F32)
    sbuf("xprojT_s", [128, 2, NDT, 96], BF16)
    sbuf("dtwT_s", [DTR, 2, NDT, 128], BF16)
    sbuf("carry_s", [128, 2, NDT * NS], F32)
    sbuf("pcar_s", [128, NDT * NS], F32)
    sbuf("mx_s", [128, 16], F32)
    sbuf("sc_s", [128, N_SCAL], F32)
    sbuf("ss_s", [128, N_SCAL], F32)
    sbuf("t0_s", [1, 128], F32)
    sbuf("eps1_s", [1, 1], F32)

    def psum(name, shape, dt):
        t = E(nc.psum_tensor(name, shape, dt))
        setattr(g, name, t)

    psum("p_mm", [128, 2, 512], F32)
    psum("p_mm2", [128, 2, 512], F32)
    psum("p_bc", [128, 512], F32)
    psum("p_bc2", [128, 512], F32)
    psum("p_xd", [96, TC], F32)
    psum("p_tr", [128, 128], BF16)

    for s in ("s_ld", "s_st", "s_tn", "s_ve", "s_se", "s_ge", "s_cc",
              "s_out", "s_ld2"):
        setattr(g, s, Sem(es, nc, s))
    g.s_lw = [Sem(es, nc, f"s_lw{i}") for i in range(2)]
    g.s_lx = [Sem(es, nc, f"s_lx{i}") for i in range(2)]
    g.s_sw = [Sem(es, nc, f"s_sw{i}") for i in range(2)]

    with nc.Block() as block:
        g.block = block
        prep_constants(g)
        prep_weights(g)
        p1_lnq(g, src="x")
        ffn_mid(g, ver=1)
        p4_mixer(g)
        p1_lnq(g, src="x2")
        ffn_mid(g, ver=2)
        finish(g)
    es.close()
    return nc



def v_qb(g):
    return g.ar3.ap()[:, 0:NFT * TOKH].rearrange("p (f t) -> p f t", f=NFT)


def v_qb2(g):
    return g.ar3.ap()[:, 0:NFT * TOK].rearrange("p (f t) -> p f t", f=NFT)


def v_xi(g):
    return g.ar3.ap().bitcast(F32)[:, 0:NDT * (TC + HALO)].rearrange(
        "p (d t) -> p d t", d=NDT)


def v_du(g):
    return g.ar3.ap().bitcast(F32)[:, 0:NDT * TC].rearrange(
        "p (d t) -> p d t", d=NDT)


def v_gb(g, k=None):
    a = g.ar3.ap()[:, 0:NDT * TC].rearrange("p (d t) -> p d t", d=NDT)
    return a if k is None else a[:, k]


def v_u(g):
    return g.aru.ap().rearrange("p (d t) -> p d t", d=NDT)


def v_sq(g, n):
    return g.aru.ap()[:, 0:NFT * n].rearrange("p (f t) -> p f t", f=NFT)


def v_rows(g):
    return g.aru.ap()[0:1, NFT * TC:NFT * TC + 5 * 512].rearrange(
        "p (r t) -> p r t", r=5)


def v_abl(g):
    o = 2 * (NFT * TC + 5 * 512)
    return g.aru.ap().bitcast(BF16)[0:1, o:o + 4 * 512].rearrange(
        "p (r t) -> p r t", r=4)


def v_brep(g):
    return g.arb.ap().bitcast(F32)[:, 0:8 * TC].rearrange("p (n t) -> p n t", n=8)


def v_crep(g):
    return g.arb.ap().bitcast(F32)[:, 8 * TC:16 * TC]


def v_hall(g):
    return g.arb.ap().bitcast(F32)[:, 16 * TC:24 * TC]


def v_wraw(g):
    return g.arb.ap().bitcast(F32)[:, 0:4096]


def v_wqb(g):
    return g.arb.ap().bitcast(BF16)[:, 8192:8192 + 4096]


def v_lnout(g, n):
    return g.arb.ap().bitcast(F32)[:, 0:NFT * n].rearrange(
        "p (f t) -> p f t", f=NFT)


def v_pp(g):
    return g.arb.ap().bitcast(F32)[:, 16 * TC:16 * TC + NFT * TC].rearrange(
        "p (f t) -> p f t", f=NFT)


def v_x2c(g):
    return g.arb.ap().bitcast(F32)[:, 8 * TC:8 * TC + NFT * TC].rearrange(
        "p (f t) -> p f t", f=NFT)


def v_xlc(g):
    return g.arb.ap().bitcast(BF16)[:, 0:NFT * TC].rearrange(
        "p (f t) -> p f t", f=NFT)


def v_mo(g):
    return g.arb.ap().bitcast(BF16)[:, NFT * TC:2 * NFT * TC].rearrange(
        "p (f t) -> p f t", f=NFT)


def v_lnb16(g):
    return g.arb.ap().bitcast(BF16)[:, 8 * NFT * TC // 4:8 * NFT * TC // 4 + NFT * TC].rearrange(
        "p (f t) -> p f t", f=NFT)


def v_lnb16R(g):
    o = 8 * NFT * TC // 4 + NFT * TC
    return g.arb.ap().bitcast(BF16)[:, o:o + NFT * TC].rearrange(
        "p (f t) -> p f t", f=NFT)


def v_oq(g):
    return g.arb.ap().bitcast(BF16)[:, 0:NMT * 512].rearrange(
        "p (m t) -> p m t", m=NMT)


def hbmT(t):
    return t.ap().rearrange("(f p) t -> p f t", p=128)



def barrier_all(g):
    blk = g.block
    snap = [(g.s_ld, g.s_ld.v), (g.s_st, g.s_st.v), (g.s_tn, g.s_tn.v),
            (g.s_ve, g.s_ve.v), (g.s_se, g.s_se.v), (g.s_ge, g.s_ge.v),
            (g.s_cc, g.s_cc.v), (g.s_ld2, g.s_ld2.v), (g.s_out, g.s_out.v)]
    snap += [(s, s.v) for s in g.s_lw + g.s_lx + g.s_sw]
    for eng in ("sync", "vector", "scalar", "tensor", "gpsimd"):
        def body(e, snap=snap):
            for s, v in snap:
                if v:
                    e.wait_ge(s.h, v)
        getattr(blk, eng)(body)


def ln_chunk(g, src_ap, out_ap, n):
    blk = g.block
    sq = v_sq(g, n)
    ve0 = g.s_ve.v

    def se(e, ve0=ve0):
        e.wait_ge(g.s_ve.h, ve0)
        e.activation(out=sq, in_=src_ap, func=AF.Square, bias=0.0, scale=1.0)
        g.s_se.inc(e.drain())
    blk.scalar(se)
    sd = g.s_se.v

    def tn(t, sd=sd, n=n):
        t.wait_ge(g.s_se.h, sd)
        for ft in range(NFT):
            li = t.matmul(g.p_xd[0:1, 0:n], g.onescol_s[:, :], src_ap[:, ft],
                          start=(ft == 0), stop=(ft == NFT - 1))
        g.s_tn.inc(li)
        for ft in range(NFT):
            li = t.matmul(g.p_bc[0:1, 0:n], g.onescol_s[:, :], sq[:, ft],
                          start=(ft == 0), stop=(ft == NFT - 1))
        g.s_tn.inc(li)
    blk.tensor(tn)
    td = g.s_tn.v
    rows = v_rows(g)
    abl = v_abl(g)

    def ve(v, td=td, n=n):
        v.wait_ge(g.s_tn.h, td)
        v.tensor_scalar_mul(out=rows[:, 0, 0:n], in0=g.p_xd[0:1, 0:n],
                            scalar1=1.0 / D)
        v.tensor_scalar_mul(out=rows[:, 1, 0:n], in0=g.p_bc[0:1, 0:n],
                            scalar1=1.0 / D)
        v.drain()
        v.tensor_tensor(out=rows[:, 2, 0:n], in0=rows[:, 0, 0:n],
                        in1=rows[:, 0, 0:n], op=ALU.mult)
        v.drain()
        v.tensor_sub(out=rows[:, 2, 0:n], in0=rows[:, 1, 0:n],
                     in1=rows[:, 2, 0:n])
        g.s_ve.inc(v.drain())
    blk.vector(ve)
    vd = g.s_ve.v

    def se2(e, vd=vd, n=n):
        e.wait_ge(g.s_ve.h, vd)
        g.s_se.inc(e.activation(out=rows[:, 3, 0:n], in_=rows[:, 2, 0:n],
                                func=AF.Sqrt, bias=g.eps1_s[:, :], scale=1.0))
    blk.scalar(se2)
    sd2 = g.s_se.v

    def ve2(v, sd2=sd2, n=n):
        v.wait_ge(g.s_se.h, sd2)
        v.reciprocal(out=rows[:, 3, 0:n], in_=rows[:, 3, 0:n])
        v.drain()
        v.scalar_tensor_tensor(out=rows[:, 4, 0:n], in0=rows[:, 0, 0:n],
                               scalar=-1.0, in1=rows[:, 3, 0:n],
                               op0=ALU.mult, op1=ALU.mult)
        v.drain()
        v.tensor_copy(out=abl[:, 0, 0:n], in_=rows[:, 3, 0:n])
        v.tensor_copy(out=abl[:, 1, 0:n], in_=rows[:, 4, 0:n])
        v.drain()
        v.tensor_sub(out=rows[:, 0, 0:n], in0=rows[:, 3, 0:n], in1=abl[:, 0, 0:n])
        v.tensor_sub(out=rows[:, 1, 0:n], in0=rows[:, 4, 0:n], in1=abl[:, 1, 0:n])
        v.drain()
        v.tensor_copy(out=abl[:, 2, 0:n], in_=rows[:, 0, 0:n])
        v.tensor_copy(out=abl[:, 3, 0:n], in_=rows[:, 1, 0:n])
        g.s_ve.inc(v.drain())
    blk.vector(ve2)
    vd2 = g.s_ve.v

    def tn2(t, vd2=vd2, n=n):
        t.wait_ge(g.s_ve.h, vd2)
        li = t.matmul(g.p_bc[:, 0:n], g.ones_s[:, :], abl[:, 0, 0:n], start=True, stop=False)
        li = t.matmul(g.p_bc[:, 0:n], g.ones_s[:, :], abl[:, 2, 0:n], start=False, stop=True)
        g.s_tn.inc(li)
        li = t.matmul(g.p_bc2[:, 0:n], g.ones_s[:, :], abl[:, 1, 0:n], start=True, stop=False)
        li = t.matmul(g.p_bc2[:, 0:n], g.ones_s[:, :], abl[:, 3, 0:n], start=False, stop=True)
        g.s_tn.inc(li)
    blk.tensor(tn2)
    td2 = g.s_tn.v

    def ve3(v, td2=td2, n=n):
        v.wait_ge(g.s_tn.h, td2)
        for ft in range(NFT):
            v.tensor_tensor(out=out_ap[:, ft], in0=src_ap[:, ft],
                            in1=g.p_bc[:, 0:n], op=ALU.mult)
        v.drain()
        for ft in range(NFT):
            v.tensor_add(out=out_ap[:, ft], in0=out_ap[:, ft],
                         in1=g.p_bc2[:, 0:n])
        g.s_ve.inc(v.drain())
    blk.vector(ve3)
    return g.s_ve.v


def absmax_acc(g, src_ap, slot, axis=AX.X):
    blk = g.block

    def ve(v, slot=slot, axis=axis):
        v.tensor_reduce(out=g.mx_s[:, TMPC:TMPC + 1], in_=src_ap, axis=axis,
                        op=ALU.max, apply_absolute_value=True)
        v.drain()
        v.tensor_max(out=g.mx_s[:, slot:slot + 1], in0=g.mx_s[:, slot:slot + 1],
                     in1=g.mx_s[:, TMPC:TMPC + 1])
        g.s_ve.inc(v.drain())
    blk.vector(ve)
    return g.s_ve.v


def scale_exchange(g, slot, qmax):
    blk = g.block
    ve_prev = g.s_ve.v

    def tn(t, slot=slot, ve_prev=ve_prev):
        t.wait_ge(g.s_ve.h, ve_prev)
        g.s_tn.inc(t.transpose(g.p_bc[0:1, 0:128], g.mx_s[:, slot:slot + 1],
                               g.identf_s[:]))
    blk.tensor(tn)
    td = g.s_tn.v

    def ve(v, td=td):
        v.wait_ge(g.s_tn.h, td)
        v.reduce_max(out=g.t0_s[:, 0:1], in_=g.p_bc[0:1, 0:128], axis=AX.X)
        v.drain()
        g.s_ve.inc(v.tensor_copy(
            out=g.t0_s[:, :], in_=g.t0_s.ap()[:, 0:1].broadcast_to([1, 128])))
    blk.vector(ve)
    vd = g.s_ve.v

    def gp(p, slot=slot, vd=vd, sw0=g.s_sw[0].v, sw1=g.s_sw[1].v):
        p.wait_ge(g.s_ve.h, vd)
        p.wait_ge(g.s_sw[0].h, sw0)
        p.wait_ge(g.s_sw[1].h, sw1)
        p.wait_ge(g.s_st.h, g.s_st.v)
        g.s_st.dma(p.dma_start(out=g.scal_in[slot].ap(), in_=g.t0_s[:, :]))
        p.wait_ge(g.s_st.h, g.s_st.v)
        g.s_cc.inc(p.collective_compute(
            "AllReduce", ALU.max, replica_groups=[list(range(8))],
            ins=[g.scal_in[slot].ap().opt()], outs=[g.scal_out[slot].ap().opt()]))
        p.wait_ge(g.s_cc.h, g.s_cc.v)
        p.wait_ge(g.s_st.h, g.s_st.v)
        g.s_st.dma(p.dma_start(
            out=g.ss_s[:, slot:slot + 1],
            in_=g.scal_out[slot].ap().rearrange("o p -> (o p)")[:, None]))
    blk.gpsimd(gp)
    std = g.s_st.v

    def ve2(v, slot=slot, std=std, qmax=qmax):
        v.wait_ge(g.s_st.h, std)
        v.tensor_scalar(out=g.ss_s[:, slot:slot + 1], in0=g.ss_s[:, slot:slot + 1],
                        scalar1=1.0 / qmax, scalar2=1e-12, op0=ALU.mult,
                        op1=ALU.max)
        v.drain()
        v.reciprocal(out=g.sc_s[:, slot:slot + 1], in_=g.ss_s[:, slot:slot + 1])
        g.s_ve.inc(v.drain())
    blk.vector(ve2)
    return g.s_ve.v



def prep_constants(g):
    blk = g.block
    nc = g.nc

    def q(qq):
        ctx = nc.allow_non_contiguous_dma(reason="tiny per-channel param loads")
        ctx.__enter__()
        for i, d in enumerate(("A", "B")):
            qq.wait_ge(g.s_ld.h, g.s_ld.v)
            g.s_ld.dma(qq.dma_start(
                out=g.A_s.ap()[:, i].rearrange("p (a n) -> p a n", a=NDT),
                in_=g.inp[f"Alog_{d}"].rearrange("(a p) n -> p a n", p=128)))
            qq.wait_ge(g.s_ld.h, g.s_ld.v)
            g.s_ld.dma(qq.dma_start(
                out=g.Dp_s.ap()[:, i, :, None],
                in_=g.inp[f"Dp_{d}"].rearrange("(a p) o -> p a o", p=128)))
            qq.wait_ge(g.s_ld.h, g.s_ld.v)
            g.s_ld.dma(qq.dma_start(
                out=g.cb_s.ap()[:, i, :, None],
                in_=g.inp[f"conv_b_{d}"].rearrange("(a p) o -> p a o", p=128)))
            qq.wait_ge(g.s_ld.h, g.s_ld.v)
            g.s_ld.dma(qq.dma_start(out=g.cw_s[:, i],
                                    in_=g.inp[f"conv_w_{d}"].rearrange("(a p) k -> p a k", p=128)))
            qq.wait_ge(g.s_ld.h, g.s_ld.v)
            g.s_ld.dma(qq.dma_start(
                out=g.dtb_s.ap()[:, i, :, None],
                in_=g.inp[f"dt_b_{d}"].rearrange("(a p) o -> p a o", p=128)))
        qq.wait_ge(g.s_ld.h, g.s_ld.v)
        g.s_ld.dma(qq.dma_start(out=g.pb_s.ap()[:, :, None],
                                in_=g.proj_b.rearrange("(a p) o -> p a o", p=128)))
        qq.wait_ge(g.s_ld.h, g.s_ld.v)
        g.s_ld.dma(qq.dma_start(out=g.lnw_s.ap()[:, :, None],
                                in_=g.ln_w.rearrange("(a p) o -> p a o", p=128)))
        qq.wait_ge(g.s_ld.h, g.s_ld.v)
        g.s_ld.dma(qq.dma_start(out=g.lnb_s.ap()[:, :, None],
                                in_=g.ln_b.rearrange("(a p) o -> p a o", p=128)))
        ctx.__exit__(None, None, None)
    blk.sync(q)
    ld0 = g.s_ld.v

    def ve(v):
        v.memset(g.ones_s[:], 1.0)
        v.memset(g.onesf_s[:], 1.0)
        v.memset(g.onescol_s[:], 1.0)
        v.memset(g.identf_s[:], 1.0)
        v.memset(g.ident_s[:], 1.0)
        v.memset(g.carry_s[:], 0.0)
        v.memset(g.mx_s[:], 0.0)
        v.memset(g.eps1_s[:], LN_EPS)
        v.memset(g.sel_s[:], 1.0)
        g.s_ve.inc(v.drain())
    blk.vector(ve)
    ve0 = g.s_ve.v

    def gp(p, ve0=ve0):
        p.wait_ge(g.s_ve.h, ve0)
        p.affine_select(out=g.ident_s[:], in_=g.ident_s[:], pattern=[[-1, 128]],
                        compare_op=ALU.is_equal, fill=0.0, base=0,
                        channel_multiplier=1)
        p.affine_select(out=g.identf_s[:], in_=g.identf_s[:],
                        pattern=[[-1, 128]], compare_op=ALU.is_equal,
                        fill=0.0, base=0, channel_multiplier=1)
        for n in range(32):
            p.affine_select(out=g.sel_s[:, n, :], in_=g.sel_s[:, n, :],
                            pattern=[[0, 128]], compare_op=ALU.is_equal,
                            fill=0.0, base=-n, channel_multiplier=1)
        p.sem_inc(g.s_ge.h, 1)
        g.s_ge.v += 1
    blk.gpsimd(gp)

    def se(e, ld0=ld0):
        e.wait_ge(g.s_ld.h, ld0)
        e.activation(out=g.A_s[:], in_=g.A_s[:], func=AF.Exp, bias=0.0, scale=1.0)
        e.drain()
        g.s_se.inc(e.mul(out=g.A_s[:], in_=g.A_s[:], mul=-1.0))
    blk.scalar(se)

    for i, d in enumerate(("A", "B")):
        def q2(qq, d=d, vprev=g.s_ve.v):
            qq.wait_ge(g.s_ve.h, vprev)
            qq.wait_ge(g.s_ld2.h, g.s_ld2.v)
            g.s_ld2.dma(qq.dma_start(
                out=v_wraw(g)[:, 0:NDT * 96].rearrange("p (a m) -> p a m", a=NDT),
                in_=g.inp[f"xproj_wT_{d}"].rearrange("(a p) m -> p a m", p=128)))
            qq.wait_ge(g.s_ld2.h, g.s_ld2.v)
            g.s_ld2.dma(qq.dma_start(
                out=v_wraw(g)[0:DTR, 1536:1536 + NDT * 64],
                in_=g.inp[f"dt_wT_{d}"][:, 0:NDT * 64]))
            qq.wait_ge(g.s_ld2.h, g.s_ld2.v)
            g.s_ld2.dma(qq.dma_start(
                out=v_wraw(g)[64:128, 1536:1536 + NDT * 64],
                in_=g.inp[f"dt_wT_{d}"][:, NDT * 64:NDT * 128]))
        blk.scalar(q2)
        ld = g.s_ld2.v

        def ve2(v, i=i, ld=ld):
            v.wait_ge(g.s_ld2.h, ld)
            v.tensor_copy(
                out=g.xprojT_s.ap().rearrange("p a m k -> p (a m k)")[:, i * NDT * 96:(i + 1) * NDT * 96],
                in_=v_wraw(g)[:, 0:NDT * 96])
            v.tensor_copy(
                out=g.dtwT_s.ap().rearrange("r a m k -> r (a m k)")[:, i * NDT * 128:i * NDT * 128 + NDT * 64],
                in_=v_wraw(g)[0:DTR, 1536:1536 + NDT * 64])
            g.s_ve.inc(v.tensor_copy(
                out=g.dtwT_s.ap().rearrange("r a m k -> r (a m k)")[:, i * NDT * 128 + NDT * 64:(i + 1) * NDT * 128],
                in_=v_wraw(g)[64:128, 1536:1536 + NDT * 64]))
        blk.vector(ve2)


def prep_weights(g):
    blk = g.block
    barrier_all(g)
    jobs = []
    for d in ("A", "B"):
        jobs.append((g.inp[f"in_wT_{d}"], g.wb[f"in_wT_{d}"], D, 2 * DI))
        jobs.append((g.inp[f"out_wT_{d}"], g.wb[f"out_wT_{d}"], DI, D))
        jobs.append((g.inp[f"projT_{d}"], g.wb[f"projT_{d}"], D, D))
    for src, dst, rows, cols in jobs:
        for i in range(rows // 128):
            def q(qq, src=src, i=i, cols=cols, stv=g.s_st.v):
                qq.wait_ge(g.s_st.h, stv)
                qq.wait_ge(g.s_ld.h, g.s_ld.v)
                g.s_ld.dma(qq.dma_start(out=v_wraw(g)[:, 0:cols],
                                        in_=src[i * 128:(i + 1) * 128, :]))
            blk.sync(q)
            ld = g.s_ld.v

            def ve(v, cols=cols, ld=ld):
                v.wait_ge(g.s_ld.h, ld)
                g.s_ve.inc(v.tensor_copy(out=v_wqb(g)[:, 0:cols],
                                         in_=v_wraw(g)[:, 0:cols]))
            blk.vector(ve)
            vd = g.s_ve.v

            def gp(p, dst=dst, i=i, cols=cols, vd=vd):
                p.wait_ge(g.s_ve.h, vd)
                p.wait_ge(g.s_st.h, g.s_st.v)
                g.s_st.dma(p.dma_start(out=dst.ap()[i * 128:(i + 1) * 128, :],
                                       in_=v_wqb(g)[:, 0:cols]))
            blk.gpsimd(gp)

    for src, dstT, O, I in ((g.w_up, g.wq_upT, FF, D),
                            (g.w_gate, g.wq_gT, FF, D),
                            (g.w_down, g.wq_dT, D, FF)):
        nblk = I // 32
        for i in range(O // 128):
            def q(qq, src=src, i=i, I=I, stv=g.s_st.v):
                qq.wait_ge(g.s_st.h, stv)
                qq.wait_ge(g.s_ld.h, g.s_ld.v)
                g.s_ld.dma(qq.dma_start(out=v_wraw(g)[:, 0:I],
                                        in_=src[i * 128:(i + 1) * 128, :]))
            blk.sync(q)
            ld = g.s_ld.v

            def ve(v, I=I, nblk=nblk, ld=ld):
                v.wait_ge(g.s_ld.h, ld)
                w = v_wraw(g)[:, 0:I]
                w3 = v_wraw(g)[:, 0:I].rearrange("p (n k) -> p n k", k=32)
                m = g.dq_s[:, 0:nblk, 0]
                s = g.dq_s[:, 0:nblk, 1]
                r = g.dq_s[:, 0:nblk, 2]
                v.tensor_reduce(out=m, in_=w3, axis=AX.X, op=ALU.max,
                                apply_absolute_value=True)
                v.drain()
                v.tensor_scalar(out=s, in0=m, scalar1=1.0 / WQMAX, scalar2=1e-12,
                                op0=ALU.mult, op1=ALU.max)
                v.drain()
                v.reciprocal(out=r, in_=s)
                v.drain()
                v.tensor_tensor(out=w3, in0=w3,
                                in1=g.dq_s.ap()[:, 0:nblk, 2:3].broadcast_to([128, nblk, 32]),
                                op=ALU.mult)
                v.drain()
                v.tensor_scalar(out=w, in0=w, scalar1=MAGIC, scalar2=MAGIC,
                                op0=ALU.add, op1=ALU.subtract)
                v.drain()
                g.s_ve.inc(v.tensor_tensor(
                    out=v_wqb(g)[:, 0:I].rearrange("p (n k) -> p n k", k=32),
                    in0=w3,
                    in1=g.dq_s.ap()[:, 0:nblk, 1:2].broadcast_to([128, nblk, 32]),
                    op=ALU.mult))
            blk.vector(ve)
            vd = g.s_ve.v

            for j in range(I // 128):
                def tn(t, j=j, vd=vd, gprev=g.s_ge.v, sprev=g.s_se.v):
                    t.wait_ge(g.s_ve.h, vd)
                    t.wait_ge(g.s_ge.h, gprev)
                    t.wait_ge(g.s_se.h, sprev)
                    g.s_tn.inc(t.transpose(g.p_tr[:],
                                           v_wqb(g)[:, j * 128:(j + 1) * 128],
                                           g.ident_s[:]))
                blk.tensor(tn)
                td = g.s_tn.v

                def se(e, td=td, stv=g.s_st.v):
                    e.wait_ge(g.s_tn.h, td)
                    e.wait_ge(g.s_st.h, stv)
                    g.s_se.inc(e.copy(out=g.stb_s.ap().rearrange("p a b -> p (a b)").bitcast(BF16)[:, 0:128],
                                      in_=g.p_tr[:]))
                blk.scalar(se)
                sd = g.s_se.v

                def gp(p, dstT=dstT, i=i, j=j, sd=sd):
                    p.wait_ge(g.s_se.h, sd)
                    p.wait_ge(g.s_st.h, g.s_st.v)
                    g.s_st.dma(p.dma_start(
                        out=dstT.ap()[j * 128:(j + 1) * 128, i * 128:(i + 1) * 128],
                        in_=g.stb_s.ap().rearrange("p a b -> p (a b)").bitcast(BF16)[:, 0:128]))
                blk.gpsimd(gp)


def p1_lnq(g, src):
    blk = g.block
    barrier_all(g)
    if src == "x":
        for c, (o, n) in enumerate(CH5):
            def q(qq, o=o, n=n, vprev=g.s_ve.v):
                qq.wait_ge(g.s_ve.h, vprev)
                qq.wait_ge(g.s_ld.h, g.s_ld.v)
                g.s_ld.dma(qq.dma_start(out=g.fc8[:, :, 0:n],
                                        in_=hbmT(g.xT)[:, :, o:o + n]))
            blk.sync(q)
            ld = g.s_ld.v

            def ve0(v, ld=ld, stv=g.s_st.v):
                v.wait_ge(g.s_ld.h, ld)
                v.wait_ge(g.s_st.h, stv)
                g.s_ve.inc(v.drain())
            blk.vector(ve0)
            ln_chunk(g, g.fc8.ap()[:, :, 0:n], v_lnout(g, n), n)
            absmax_acc(g, v_lnout(g, n), SC_X, axis=AX.XY)
            vd = g.s_ve.v

            def gp(p, o=o, n=n, vd=vd):
                p.wait_ge(g.s_ve.h, vd)
                p.wait_ge(g.s_st.h, g.s_st.v)
                g.s_st.dma(p.dma_start(out=hbmT(g.xln_sp)[:, :, o:o + n],
                                       in_=v_lnout(g, n)[:, :, 0:n]))
            blk.gpsimd(gp)
        scale_exchange(g, SC_X, QMAX)
        slot, lnsrc, chunks, qbv = SC_X, g.xln_sp, CH5, v_qb(g)
    else:
        scale_exchange(g, SC_X2, QMAX)
        slot, lnsrc, chunks, qbv = SC_X2, g.x2ln_sp, CH4, v_qb2(g)

    for c, (o, n) in enumerate(chunks):
        def q(qq, o=o, n=n, lnsrc=lnsrc, vprev=g.s_ve.v, stv=g.s_st.v):
            qq.wait_ge(g.s_ve.h, vprev)
            qq.wait_ge(g.s_st.h, stv)
            qq.wait_ge(g.s_ld.h, g.s_ld.v)
            g.s_ld.dma(qq.dma_start(out=g.fc8[:, :, 0:n],
                                    in_=hbmT(lnsrc)[:, :, o:o + n]))
        blk.sync(q)
        ld = g.s_ld.v

        def ve(v, o=o, n=n, ld=ld, slot=slot, qbv=qbv):
            v.wait_ge(g.s_ld.h, ld)
            v.tensor_scalar(out=g.fc8[:, :, 0:n], in0=g.fc8[:, :, 0:n],
                            scalar1=g.sc_s[:, slot:slot + 1], scalar2=MAGIC,
                            op0=ALU.mult, op1=ALU.add)
            v.drain()
            v.tensor_scalar(out=qbv[:, :, o:o + n], in0=g.fc8[:, :, 0:n],
                            scalar1=MAGIC, scalar2=g.ss_s[:, slot:slot + 1],
                            op0=ALU.subtract, op1=ALU.mult)
            g.s_ve.inc(v.drain())
        blk.vector(ve)


def ffn_mid(g, ver):
    blk = g.block
    barrier_all(g)
    if ver == 1:
        chunks, qbv = CH5, v_qb(g)
        s_up, s_gate, s_gs, s_t5 = SC_UP, SC_GATE, SC_GS, SC_T5
    else:
        chunks, qbv = CH4, v_qb2(g)
        s_up, s_gate, s_gs, s_t5 = SC_UP2, SC_GATE2, SC_GS2, SC_T52

    for wt, spill, slot in ((g.wq_upT, g.up_sp, s_up),
                            (g.wq_gT, g.gate_sp, s_gate)):
        for m in range(NMT):
            bi = m % 2

            def q(qq, wt=wt, m=m, bi=bi, tprev=g.s_tn.v):
                qq.wait_ge(g.s_tn.h, tprev)
                qq.wait_ge(g.s_ld.h, g.s_ld.v)
                g.s_ld.dma(qq.dma_start(
                    out=g.wstr_s[:, bi, 0:NFT, :],
                    in_=wt.ap().rearrange("(k p) f -> p k f", p=128)[:, :, m * 128:(m + 1) * 128]))
            blk.sync(q)
            ld = g.s_ld.v

            for ci, (o, n) in enumerate(chunks):
                pb = ci % 2

                def tn(t, bi=bi, o=o, n=n, pb=pb, ld=ld, sprev=g.s_se.v,
                       qbv=qbv):
                    t.wait_ge(g.s_ld.h, ld)
                    t.wait_ge(g.s_se.h, sprev)
                    for k in range(NFT):
                        li = t.matmul(g.p_mm[:, pb, 0:n], g.wstr_s[:, bi, k, :],
                                 qbv[:, k, o:o + n], start=(k == 0),
                                 stop=(k == NFT - 1))
                    g.s_tn.inc(li)
                blk.tensor(tn)
                td = g.s_tn.v

                def se(e, pb=pb, ci=ci, n=n, td=td, stv=g.s_st.v):
                    e.wait_ge(g.s_tn.h, td)
                    e.wait_ge(g.s_st.h, stv)
                    g.s_se.inc(e.copy(out=g.stb_s[:, ci % 2, 0:n],
                                      in_=g.p_mm[:, pb, 0:n]))
                blk.scalar(se)
                sd = g.s_se.v

                def vw(v, sd=sd):
                    v.wait_ge(g.s_se.h, sd)
                blk.vector(vw)
                absmax_acc(g, g.stb_s[:, ci % 2, 0:n], slot)
                vd = g.s_ve.v

                def gp(p, spill=spill, m=m, o=o, n=n, ci=ci, vd=vd):
                    p.wait_ge(g.s_ve.h, vd)
                    p.wait_ge(g.s_st.h, g.s_st.v)
                    g.s_st.dma(p.dma_start(
                        out=spill.ap()[m * 128:(m + 1) * 128, o:o + n],
                        in_=g.stb_s[:, ci % 2, 0:n]))
                blk.gpsimd(gp)

    scale_exchange(g, s_up, QMAX)
    scale_exchange(g, s_gate, QMAX)

    va_hist = {-2: g.s_ve.v, -1: g.s_ve.v}
    for m in range(NMT):
        for ci, (o, n) in enumerate(chunks):
            it = m * len(chunks) + ci
            sl = it % 2

            def q(qq, m=m, o=o, n=n, sl=sl, vprev=va_hist.get(it - 2, 0)):
                qq.wait_ge(g.s_ve.h, vprev)
                qq.wait_ge(g.s_lx[sl].h, g.s_lx[sl].v)
                g.s_lx[sl].dma(qq.dma_start(
                    out=g.g16_s[:, sl, 0:n],
                    in_=g.gate_sp.ap()[m * 128:(m + 1) * 128, o:o + n]))
            blk.sync(q)
            ld = g.s_lx[sl].v

            def ve(v, sl=sl, n=n, ld=ld, s_gate=s_gate):
                v.wait_ge(g.s_lx[sl].h, ld)
                gq = g.sg_s[:, 0:n]
                v.tensor_scalar(out=gq, in0=g.g16_s[:, sl, 0:n],
                                scalar1=g.sc_s[:, s_gate:s_gate + 1],
                                scalar2=MAGIC, op0=ALU.mult, op1=ALU.add)
                v.drain()
                v.tensor_scalar(out=gq, in0=gq, scalar1=MAGIC,
                                scalar2=g.ss_s[:, s_gate:s_gate + 1],
                                op0=ALU.subtract, op1=ALU.mult)
                g.s_ve.inc(v.drain())
            blk.vector(ve)
            vd = g.s_ve.v

            def se(e, n=n, vd=vd):
                e.wait_ge(g.s_ve.h, vd)
                g.s_se.inc(e.activation(out=g.sg2_s[:, 0:n], in_=g.sg_s[:, 0:n],
                                        func=AF.Sigmoid, bias=0.0, scale=1.0))
            blk.scalar(se)
            sd = g.s_se.v

            def ve2(v, n=n, sl=sl, sd=sd, stv=g.s_sw[sl].v):
                v.wait_ge(g.s_se.h, sd)
                v.wait_ge(g.s_sw[sl].h, stv)
                sig = g.sg2_s[:, 0:n]
                v.tensor_scalar(out=sig, in0=sig, scalar1=65536.0, scalar2=MAGIC,
                                op0=ALU.mult, op1=ALU.add)
                v.drain()
                v.tensor_scalar(out=sig, in0=sig, scalar1=MAGIC, scalar2=65535.0,
                                op0=ALU.subtract, op1=ALU.min)
                v.drain()
                v.scalar_tensor_tensor(out=g.h16_s[:, sl, 0:n], in0=sig,
                                       scalar=1.0 / 65536.0, in1=g.sg_s[:, 0:n],
                                       op0=ALU.mult, op1=ALU.mult)
                g.s_ve.inc(v.drain())
            blk.vector(ve2)
            absmax_acc(g, g.h16_s[:, sl, 0:n], s_gs)
            vd2 = g.s_ve.v

            def gp(p, m=m, o=o, n=n, sl=sl, vd2=vd2):
                p.wait_ge(g.s_ve.h, vd2)
                g.s_sw[sl].dma(p.dma_start(
                    out=g.gs_sp.ap()[m * 128:(m + 1) * 128, o:o + n],
                    in_=g.h16_s[:, sl, 0:n]))
            blk.gpsimd(gp)
            va_hist[it] = g.s_ve.v

    scale_exchange(g, s_gs, QMAX)

    vb_hist = {-2: g.s_ve.v, -1: g.s_ve.v}
    for m in range(NMT):
        for ci, (o, n) in enumerate(chunks):
            it = m * len(chunks) + ci
            sl = it % 2

            def q(qq, m=m, o=o, n=n, sl=sl, vprev=vb_hist.get(it - 2, 0),
                  swv=g.s_sw[sl].v):
                qq.wait_ge(g.s_ve.h, vprev)
                qq.wait_ge(g.s_sw[sl].h, swv)
                qq.wait_ge(g.s_lx[sl].h, g.s_lx[sl].v)
                g.s_lx[sl].dma(qq.dma_start(
                    out=g.g16_s[:, sl, 0:n],
                    in_=g.gs_sp.ap()[m * 128:(m + 1) * 128, o:o + n]))
                qq.wait_ge(g.s_lx[sl].h, g.s_lx[sl].v)
                g.s_lx[sl].dma(qq.dma_start(
                    out=g.h16_s[:, sl, 0:n],
                    in_=g.up_sp.ap()[m * 128:(m + 1) * 128, o:o + n]))
            blk.sync(q)
            ld = g.s_lx[sl].v

            def ve(v, n=n, sl=sl, ld=ld, s_gs=s_gs, s_up=s_up,
                   stv=g.s_sw[sl].v):
                v.wait_ge(g.s_lx[sl].h, ld)
                v.wait_ge(g.s_sw[sl].h, stv)
                g2 = g.sg_s[:, 0:n]
                uq = g.sg2_s[:, 0:n]
                v.tensor_scalar(out=g2, in0=g.g16_s[:, sl, 0:n],
                                scalar1=g.sc_s[:, s_gs:s_gs + 1], scalar2=MAGIC,
                                op0=ALU.mult, op1=ALU.add)
                v.tensor_scalar(out=uq, in0=g.h16_s[:, sl, 0:n],
                                scalar1=g.sc_s[:, s_up:s_up + 1], scalar2=MAGIC,
                                op0=ALU.mult, op1=ALU.add)
                v.drain()
                v.tensor_scalar(out=g2, in0=g2, scalar1=MAGIC,
                                scalar2=g.ss_s[:, s_gs:s_gs + 1],
                                op0=ALU.subtract, op1=ALU.mult)
                v.tensor_scalar(out=uq, in0=uq, scalar1=MAGIC,
                                scalar2=g.ss_s[:, s_up:s_up + 1],
                                op0=ALU.subtract, op1=ALU.mult)
                v.drain()
                v.tensor_mul(out=g.g16_s[:, sl, 0:n], in0=g2, in1=uq)
                g.s_ve.inc(v.drain())
            blk.vector(ve)
            absmax_acc(g, g.g16_s[:, sl, 0:n], s_t5)
            vd = g.s_ve.v

            def gp(p, m=m, o=o, n=n, sl=sl, vd=vd):
                p.wait_ge(g.s_ve.h, vd)
                g.s_sw[sl].dma(p.dma_start(
                    out=g.t5_sp.ap()[m * 128:(m + 1) * 128, o:o + n],
                    in_=g.g16_s[:, sl, 0:n]))
            blk.gpsimd(gp)
            vb_hist[it] = g.s_ve.v

    scale_exchange(g, s_t5, QMAX)

    oq = v_oq(g)
    vc_hist = {-2: g.s_ve.v, -1: g.s_ve.v}
    for ci, (o, n) in enumerate(chunks):
        for m in range(NMT):
            sl = m % 2
            it = ci * NMT + m

            def q(qq, m=m, o=o, n=n, sl=sl, vprev=vc_hist.get(it - 2, 0)):
                qq.wait_ge(g.s_ve.h, vprev)
                qq.wait_ge(g.s_lx[sl].h, g.s_lx[sl].v)
                g.s_lx[sl].dma(qq.dma_start(
                    out=g.g16_s[:, sl, 0:n],
                    in_=g.t5_sp.ap()[m * 128:(m + 1) * 128, o:o + n]))
            blk.sync(q)
            ld = g.s_lx[sl].v

            def ve(v, m=m, n=n, sl=sl, ld=ld, s_t5=s_t5, oq=oq, tprev=g.s_tn.v):
                v.wait_ge(g.s_lx[sl].h, ld)
                v.wait_ge(g.s_tn.h, tprev)
                v.tensor_scalar(out=g.sg_s[:, 0:n], in0=g.g16_s[:, sl, 0:n],
                                scalar1=g.sc_s[:, s_t5:s_t5 + 1], scalar2=MAGIC,
                                op0=ALU.mult, op1=ALU.add)
                v.drain()
                v.tensor_scalar(out=oq[:, m, 0:n], in0=g.sg_s[:, 0:n],
                                scalar1=MAGIC, scalar2=g.ss_s[:, s_t5:s_t5 + 1],
                                op0=ALU.subtract, op1=ALU.mult)
                g.s_ve.inc(v.drain())
            blk.vector(ve)
            vc_hist[it] = g.s_ve.v
        vd_oq = g.s_ve.v

        for ft in range(NFT):
            for half in range(2):
                def q2(qq, ft=ft, half=half, tprev=g.s_tn.v):
                    qq.wait_ge(g.s_tn.h, tprev)
                    qq.wait_ge(g.s_ld.h, g.s_ld.v)
                    g.s_ld.dma(qq.dma_start(
                        out=g.wstr_s[:, half, 0:12, :],
                        in_=g.wq_dT.ap().rearrange("(k p) f -> p k f", p=128)[:, 12 * half:12 * (half + 1), ft * 128:(ft + 1) * 128]))
                blk.sync(q2)
            ld2 = g.s_ld.v

            def tn(t, ft=ft, n=n, vd_oq=vd_oq, ld2=ld2, sprev=g.s_se.v, oq=oq):
                t.wait_ge(g.s_ve.h, vd_oq)
                t.wait_ge(g.s_ld.h, ld2)
                t.wait_ge(g.s_se.h, sprev)
                for k in range(NMT):
                    li = t.matmul(g.p_mm2[:, ft % 2, 0:n],
                             g.wstr_s[:, k // 12, k % 12, :],
                             oq[:, k, 0:n], start=(k == 0), stop=(k == NMT - 1))
                g.s_tn.inc(li)
            blk.tensor(tn)
            td = g.s_tn.v

            if ver == 1:
                def q3(qq, ft=ft, o=o, n=n, vprev=g.s_ve.v, stv=g.s_st.v):
                    qq.wait_ge(g.s_ve.h, vprev)
                    qq.wait_ge(g.s_st.h, stv)
                    qq.wait_ge(g.s_ld.h, g.s_ld.v)
                    g.s_ld.dma(qq.dma_start(out=g.fc8[:, ft, 0:n],
                                            in_=hbmT(g.xT)[:, ft, o:o + n]))
                blk.sync(q3)
                ld3 = g.s_ld.v

                def ve2(v, ft=ft, n=n, td=td, ld3=ld3):
                    v.wait_ge(g.s_tn.h, td)
                    v.wait_ge(g.s_ld.h, ld3)
                    v.scalar_tensor_tensor(out=g.fc8[:, ft, 0:n],
                                           in0=g.p_mm2[:, ft % 2, 0:n],
                                           scalar=0.5, in1=g.fc8[:, ft, 0:n],
                                           op0=ALU.mult, op1=ALU.add)
                    g.s_ve.inc(v.drain())
                blk.vector(ve2)
                vd2 = g.s_ve.v

                def se2(e, vd2=vd2):
                    e.wait_ge(g.s_ve.h, vd2)
                    e.sem_inc(g.s_se.h, 1); g.s_se.v += 1
                blk.scalar(se2)

                def gp(p, ft=ft, o=o, n=n, vd2=vd2):
                    p.wait_ge(g.s_ve.h, vd2)
                    p.wait_ge(g.s_st.h, g.s_st.v)
                    g.s_st.dma(p.dma_start(out=hbmT(g.x1_sp)[:, ft, o:o + n],
                                           in_=g.fc8[:, ft, 0:n]))
                blk.gpsimd(gp)
            else:
                def q3(qq, ft=ft, o=o, n=n, vprev=g.s_ve.v, stv=g.s_st.v):
                    qq.wait_ge(g.s_ve.h, vprev)
                    qq.wait_ge(g.s_st.h, stv)
                    qq.wait_ge(g.s_ld.h, g.s_ld.v)
                    g.s_ld.dma(qq.dma_start(out=g.fc8[:, ft, 0:n],
                                            in_=hbmT(g.x2_sp)[:, ft, o:o + n]))
                blk.sync(q3)
                ld3 = g.s_ld.v

                def ve2(v, ft=ft, n=n, td=td, ld3=ld3, ov=g.s_out.v):
                    v.wait_ge(g.s_tn.h, td)
                    v.wait_ge(g.s_ld.h, ld3)
                    v.wait_ge(g.s_out.h, ov)
                    v.scalar_tensor_tensor(out=g.st_s[:, ft % 2, 0:n],
                                           in0=g.p_mm2[:, ft % 2, 0:n],
                                           scalar=0.5, in1=g.fc8[:, ft, 0:n],
                                           op0=ALU.mult, op1=ALU.add)
                    g.s_ve.inc(v.drain())
                blk.vector(ve2)
                vd2 = g.s_ve.v

                def se2(e, vd2=vd2):
                    e.wait_ge(g.s_ve.h, vd2)
                    e.sem_inc(g.s_se.h, 1); g.s_se.v += 1
                blk.scalar(se2)

                def gp(p, ft=ft, o=o, n=n, vd2=vd2):
                    p.wait_ge(g.s_ve.h, vd2)
                    p.wait_ge(g.s_out.h, g.s_out.v)
                    g.s_out.dma(p.dma_start(
                        out=g.out_h[ft * 128:(ft + 1) * 128, o:o + n],
                        in_=g.st_s[:, ft % 2, 0:n]))
                blk.gpsimd(gp)

    if ver == 1:
        barrier_all(g)
        for c, (o, n) in enumerate(CH5):
            def q(qq, o=o, n=n, vprev=g.s_ve.v, stv=g.s_st.v):
                qq.wait_ge(g.s_ve.h, vprev)
                qq.wait_ge(g.s_st.h, stv)
                qq.wait_ge(g.s_ld.h, g.s_ld.v)
                g.s_ld.dma(qq.dma_start(out=g.fc8[:, :, 0:n],
                                        in_=hbmT(g.x1_sp)[:, :, o:o + n]))
            blk.sync(q)
            ld = g.s_ld.v

            def ve0(v, ld=ld, stv=g.s_st.v):
                v.wait_ge(g.s_ld.h, ld)
                v.wait_ge(g.s_st.h, stv)
                g.s_ve.inc(v.drain())
            blk.vector(ve0)
            ln_chunk(g, g.fc8.ap()[:, :, 0:n], v_lnout(g, n), n)
            vd = g.s_ve.v

            def ve(v, n=n, vd=vd):
                v.wait_ge(g.s_ve.h, vd)
                for ft in range(NFT):
                    v.tensor_copy(out=v_lnb16(g)[:, ft, 0:n],
                                  in_=v_lnout(g, n)[:, ft])
                v.drain()
                for ft in range(NFT):
                    v.tensor_copy(out=v_lnb16R(g)[:, ft, 0:n],
                                  in_=v_lnout(g, n)[:, ft][:, ::-1])
                g.s_ve.inc(v.drain())
            blk.vector(ve)
            vd2 = g.s_ve.v

            def gp(p, o=o, n=n, vd2=vd2):
                p.wait_ge(g.s_ve.h, vd2)
                p.wait_ge(g.s_st.h, g.s_st.v)
                g.s_st.dma(p.dma_start(out=hbmT(g.x1ln_sp)[:, :, o:o + n],
                                       in_=v_lnout(g, n)[:, :, 0:n]))
                p.wait_ge(g.s_st.h, g.s_st.v)
                g.s_st.dma(p.dma_start(out=hbmT(g.x1lnb_sp)[:, :, o:o + n],
                                       in_=v_lnb16(g)[:, :, 0:n]))
                p.wait_ge(g.s_st.h, g.s_st.v)
                g.s_st.dma(p.dma_start(
                    out=hbmT(g.x1lnbR_sp)[:, :, TOKH - o - n:TOKH - o],
                    in_=v_lnb16R(g)[:, :, 0:n]))
            blk.gpsimd(gp)


def p4_mixer(g):
    blk = g.block
    barrier_all(g)
    for di in range(2):
        mixer_direction(g, di)
        if di == 0:
            carry_exchange(g)


def carry_exchange(g):
    blk = g.block
    vd, sd = g.s_ve.v, g.s_se.v

    def gp(p, vd=vd, sd=sd):
        p.wait_ge(g.s_ve.h, vd)
        p.wait_ge(g.s_se.h, sd)
        p.wait_ge(g.s_st.h, g.s_st.v)
        g.s_st.dma(p.dma_start(out=g.carry_io.ap(), in_=g.carry_s[:, 0]))
        p.wait_ge(g.s_st.h, g.s_st.v)
        g.s_cc.inc(p.collective_compute(
            "AllReduce", ALU.add,
            replica_groups=[[0, 1], [2, 3], [4, 5], [6, 7]],
            ins=[g.carry_io.ap().opt()], outs=[g.carry_sum.ap().opt()]))
        p.wait_ge(g.s_cc.h, g.s_cc.v)
        p.wait_ge(g.s_st.h, g.s_st.v)
        g.s_st.dma(p.dma_start(out=g.pcar_s[:, :], in_=g.carry_sum.ap()))
    blk.gpsimd(gp)
    st = g.s_st.v

    def ve(v, st=st):
        v.wait_ge(g.s_st.h, st)
        v.tensor_sub(out=g.carry_s[:, 1], in0=g.pcar_s[:, :], in1=g.carry_s[:, 0])
        g.s_ve.inc(v.drain())
    blk.vector(ve)


def mixer_direction(g, di):
    blk = g.block
    xsrc = g.x1lnb_sp if di == 0 else g.x1lnbR_sp
    wname = "A" if di == 0 else "B"

    def q(qq, xsrc=xsrc, vprev=g.s_ve.v):
        qq.wait_ge(g.s_ve.h, vprev)
        qq.wait_ge(g.s_ld.h, g.s_ld.v)
        g.s_ld.dma(qq.dma_start(out=v_xlc(g)[:, :, 0:HALO],
                                in_=hbmT(xsrc)[:, :, 0:HALO]))
    blk.sync(q)
    ldh = g.s_ld.v
    for dt in range(NDT):
        def q2(qq, dt=dt, wname=wname, tprev=g.s_tn.v):
            qq.wait_ge(g.s_tn.h, tprev)
            qq.wait_ge(g.s_ld.h, g.s_ld.v)
            g.s_ld.dma(qq.dma_start(
                out=g.wstr_s[:, dt % 2, 0:NFT, :],
                in_=g.wb[f"in_wT_{wname}"].ap().rearrange("(k p) f -> p k f", p=128)[:, :, dt * 128:(dt + 1) * 128]))
        blk.sync(q2)
        ld = g.s_ld.v

        def tn(t, dt=dt, ld=ld, ldh=ldh, sprev=g.s_se.v):
            t.wait_ge(g.s_ld.h, max(ld, ldh))
            t.wait_ge(g.s_se.h, sprev)
            for k in range(NFT):
                li = t.matmul(g.p_bc[:, 0:HALO], g.wstr_s[:, dt % 2, k, :],
                         v_xlc(g)[:, k, 0:HALO], start=(k == 0),
                         stop=(k == NFT - 1))
            g.s_tn.inc(li)
        blk.tensor(tn)
        td = g.s_tn.v

        def se(e, dt=dt, td=td):
            e.wait_ge(g.s_tn.h, td)
            g.s_se.inc(e.copy(out=g.xicar_s[:, dt, :], in_=g.p_bc[:, 0:HALO]))
        blk.scalar(se)

    for c in range(NCH):
        mixer_chunk(g, di, c, wname)


def mixer_chunk(g, di, c, wname):
    blk = g.block
    o = HALO + c * TC
    xsrc = g.x1lnb_sp if di == 0 else g.x1lnbR_sp
    corig = c if di == 0 else NCH - 1 - c
    oorig = HALO + corig * TC
    carry = g.carry_s.ap()[:, di]
    xi = v_xi(g)
    u = v_u(g)
    du = v_du(g)

    def vgate(v, stv=g.s_st.v):
        v.wait_ge(g.s_st.h, stv)
        g.s_ve.inc(v.drain())
    blk.vector(vgate)

    def sgate(e, stv=g.s_st.v):
        e.wait_ge(g.s_st.h, stv)
        e.sem_inc(g.s_se.h, 1)
    blk.scalar(sgate)
    g.s_se.v += 1

    def q(qq, o=o, xsrc=xsrc, tprev=g.s_tn.v, sprev=g.s_se.v):
        qq.wait_ge(g.s_tn.h, tprev)
        qq.wait_ge(g.s_se.h, sprev)
        qq.wait_ge(g.s_ld.h, g.s_ld.v)
        g.s_ld.dma(qq.dma_start(out=v_xlc(g)[:, :, 0:TC],
                                in_=hbmT(xsrc)[:, :, o:o + TC]))
    blk.sync(q)
    ld_x = g.s_ld.v

    tn_hist = {-2: g.s_tn.v, -1: g.s_tn.v}
    for m in range(2 * NDT):
        def q2(qq, m=m, wname=wname, tprev=tn_hist.get(m - 2, 0)):
            qq.wait_ge(g.s_tn.h, tprev)
            qq.wait_ge(g.s_lw[m % 2].h, g.s_lw[m % 2].v)
            g.s_lw[m % 2].dma(qq.dma_start(
                out=g.wstr_s[:, m % 2, 0:NFT, :],
                in_=g.wb[f"in_wT_{wname}"].ap().rearrange("(k p) f -> p k f", p=128)[:, :, m * 128:(m + 1) * 128]))
        blk.sync(q2)
        ld = g.s_lw[m % 2].v

        def tn(t, m=m, ld=ld, ld_x=ld_x, sprev=(g.s_se.v if m == 0 else g.s_se.v - 1),
               vprev=g.s_ve.v, stprev=g.s_st.v):
            t.wait_ge(g.s_lw[m % 2].h, ld)
            t.wait_ge(g.s_ld.h, ld_x)
            t.wait_ge(g.s_se.h, max(sprev, 0))
            t.wait_ge(g.s_ve.h, vprev)
            t.wait_ge(g.s_st.h, stprev)
            for k in range(NFT):
                li = t.matmul(g.p_mm[:, m % 2, :], g.wstr_s[:, m % 2, k, :],
                         v_xlc(g)[:, k, 0:TC], start=(k == 0), stop=(k == NFT - 1))
            g.s_tn.inc(li)
        blk.tensor(tn)
        td = g.s_tn.v

        if m < NDT:
            def se(e, m=m, td=td):
                e.wait_ge(g.s_tn.h, td)
                g.s_se.inc(e.copy(out=xi[:, m, HALO:HALO + TC],
                                  in_=g.p_mm[:, m % 2, :]))
            blk.scalar(se)
        else:
            def se(e, m=m, td=td, stprev=g.s_sw[m % 2].v):
                e.wait_ge(g.s_tn.h, td)
                e.wait_ge(g.s_sw[m % 2].h, stprev)
                g.s_se.inc(e.copy(out=g.st_s[:, m % 2, :],
                                  in_=g.p_mm[:, m % 2, :]))
            blk.scalar(se)
            sd = g.s_se.v

            def gp(p, m=m, corig=corig, sd=sd):
                p.wait_ge(g.s_se.h, sd)
                g.s_sw[m % 2].dma(p.dma_start(
                    out=g.z_sp.ap()[(m - NDT) * 128:(m - NDT + 1) * 128,
                                    corig * TC:(corig + 1) * TC],
                    in_=g.st_s[:, m % 2, :]))
            blk.gpsimd(gp)
        tn_hist[m] = g.s_tn.v

    sd_xi = g.s_se.v

    def ve_h(v, sd_xi=sd_xi):
        v.wait_ge(g.s_se.h, sd_xi)
        v.tensor_copy(out=xi[:, :, 0:HALO], in_=g.xicar_s[:, :, :])
        g.s_ve.inc(v.drain())
    blk.vector(ve_h)

    for dt in range(NDT):
        def ve(v, dt=dt, di=di):
            v.tensor_scalar_mul(out=u[:, dt], in0=xi[:, dt, 0:TC],
                                scalar1=g.cw_s[:, di, dt, 0:1])
            v.drain()
            for k in range(1, 4):
                v.scalar_tensor_tensor(out=u[:, dt], in0=xi[:, dt, k:k + TC],
                                       scalar=g.cw_s[:, di, dt, k:k + 1],
                                       in1=u[:, dt], op0=ALU.mult, op1=ALU.add)
                v.drain()
            v.sem_inc(g.s_ve.h, 1); g.s_ve.v += 1
        blk.vector(ve)

    def ve_t(v):
        v.tensor_copy(out=g.xicar_s[:, :, :], in_=xi[:, :, TC:TC + HALO])
        g.s_ve.inc(v.drain())
    blk.vector(ve_t)
    ve_tail = g.s_ve.v

    def se_u(e, ve_tail=ve_tail, di=di):
        e.wait_ge(g.s_ve.h, ve_tail)
        for dt in range(NDT):
            e.activation(out=u[:, dt], in_=u[:, dt], func=AF.Silu,
                         bias=g.cb_s[:, di, dt:dt + 1], scale=1.0)
        g.s_se.inc(e.drain())
    blk.scalar(se_u)
    se_silu = g.s_se.v

    for dt in range(NDT):
        def ve(v, dt=dt, se_silu=se_silu, tprev=g.s_tn.v):
            v.wait_ge(g.s_se.h, se_silu)
            v.wait_ge(g.s_tn.h, tprev)
            g.s_ve.inc(v.tensor_copy(out=g.ubt_s[:, 0, :], in_=u[:, dt]))
        blk.vector(ve)
        vd = g.s_ve.v

        def tn(t, dt=dt, vd=vd, di=di, sprev=g.s_se.v):
            t.wait_ge(g.s_ve.h, vd)
            if dt == 0:
                t.wait_ge(g.s_se.h, sprev)
            li = t.matmul(g.p_xd[:, :], g.xprojT_s[:, di, dt, :],
                     g.ubt_s[:, 0, :],
                     start=(dt == 0), stop=(dt == NDT - 1))
            g.s_tn.inc(li)
        blk.tensor(tn)
    td_xd = g.s_tn.v

    def se_xd(e, td_xd=td_xd):
        e.wait_ge(g.s_tn.h, td_xd)
        e.copy(out=g.dtb16_s[:, :], in_=g.p_xd[0:DTR, :])
        g.s_se.inc(e.copy(out=g.xdb_s[0:32, 0, :], in_=g.p_xd[DTR:96, :]))
    blk.scalar(se_xd)
    sd_xd = g.s_se.v

    for dt in range(NDT):
        def tn(t, dt=dt, sd_xd=sd_xd, di=di, sprev=g.s_se.v):
            t.wait_ge(g.s_se.h, max(sd_xd, sprev if dt >= 2 else 0))
            li = t.matmul(g.p_mm[:, dt % 2, :], g.dtwT_s[:, di, dt, :],
                     g.dtb16_s[:, :], start=True, stop=True)
            g.s_tn.inc(li)
        blk.tensor(tn)
        td = g.s_tn.v

        def se(e, dt=dt, td=td, di=di):
            e.wait_ge(g.s_tn.h, td)
            e.activation(out=du[:, dt], in_=g.p_mm[:, dt % 2, :], func=AF.Exp,
                         bias=g.dtb_s[:, di, dt:dt + 1], scale=1.0)
            e.drain()
            g.s_se.inc(e.activation(out=du[:, dt], in_=du[:, dt], func=AF.Ln,
                                    bias=g.onescol_s[:, :], scale=1.0))
        blk.scalar(se)
    sd_delta = g.s_se.v

    ve_duv = g.s_ve.v

    def ve_bc(v, sd_xd=sd_xd):
        v.wait_ge(g.s_se.h, sd_xd)
        v.tensor_copy(out=g.bclo_s[:, 0, :], in_=g.xdb_s[0:32, 0, :])
        v.drain()
        v.tensor_sub(out=g.xdb_s[0:32, 1, :], in0=g.xdb_s[0:32, 0, :],
                     in1=g.bclo_s[:, 0, :])
        v.drain()
        v.tensor_copy(out=g.bclo_s[:, 1, :], in_=g.xdb_s[0:32, 1, :])
        g.s_ve.inc(v.drain())
    blk.vector(ve_bc)
    ve_bcv = g.s_ve.v

    hall = v_hall(g)
    crep = v_crep(g)
    brep = v_brep(g)
    for gi in range(2):
        for n in range(8):
            def tn(t, gi=gi, n=n, ve_bcv=ve_bcv, sprev=g.s_se.v):
                t.wait_ge(g.s_ve.h, ve_bcv)
                t.wait_ge(g.s_se.h, sprev)
                sn = gi * 8 + n
                li = t.matmul(g.p_bc[:, :], g.sel_s[:, sn, :], g.bclo_s[:, 0, :],
                         start=True, stop=False)
                li = t.matmul(g.p_bc[:, :], g.sel_s[:, sn, :], g.bclo_s[:, 1, :],
                              start=False, stop=True)
                g.s_tn.inc(li)
                li = t.matmul(g.p_bc2[:, :], g.sel_s[:, 16 + sn, :], g.bclo_s[:, 0, :],
                         start=True, stop=False)
                li = t.matmul(g.p_bc2[:, :], g.sel_s[:, 16 + sn, :],
                              g.bclo_s[:, 1, :], start=False, stop=True)
                g.s_tn.inc(li)
            blk.tensor(tn)
            td = g.s_tn.v

            def se(e, n=n, td=td, vprev=g.s_ve.v):
                e.wait_ge(g.s_tn.h, td)
                e.wait_ge(g.s_ve.h, vprev)
                e.copy(out=brep[:, n, :], in_=g.p_bc[:, :])
                g.s_se.inc(e.copy(
                    out=crep.rearrange("p (t n) -> p t n", n=8)[:, :, n],
                    in_=g.p_bc2[:, :]))
            blk.scalar(se)
        sd_bc = g.s_se.v

        for dt in range(NDT):
            def vdup(v, dt=dt, sd_delta=sd_delta):
                v.wait_ge(g.s_se.h, sd_delta)
                v.tensor_mul(out=g.dup_s[:, :], in0=v_du(g)[:, dt],
                             in1=v_u(g)[:, dt])
                g.s_ve.inc(v.drain())
            blk.vector(vdup)

            for n in range(8):
                sn = gi * 8 + n
                slot = n % 2

                def se(e, dt=dt, sn=sn, slot=slot, sd_bc=sd_bc, di=di,
                       vprev=g.s_ve.v):
                    e.wait_ge(g.s_se.h, sd_bc)
                    e.wait_ge(g.s_ve.h, vprev)
                    g.s_se.inc(e.activation(
                        out=g.da_s[:, slot, :], in_=v_du(g)[:, dt],
                        func=AF.Exp, bias=0.0,
                        scale=g.A_s[:, di, dt * NS + sn:dt * NS + sn + 1]))
                blk.scalar(se)
                sd = g.s_se.v

                def ve(v, dt=dt, n=n, sn=sn, slot=slot, sd=sd, carry=carry,
                       ve_duv=ve_duv):
                    v.wait_ge(g.s_se.h, sd)
                    v.wait_ge(g.s_ve.h, ve_duv)
                    v.tensor_mul(out=g.w_sc[:, 0, :], in0=g.dup_s[:, :],
                                 in1=brep[:, n, :])
                    v.drain()
                    v.tensor_tensor_scan(
                        out=hall.rearrange("p (t n) -> p t n", n=8)[:, :, n],
                        data0=g.da_s[:, slot, :], data1=g.w_sc[:, 0, :],
                        initial=carry[:, dt * NS + sn:dt * NS + sn + 1],
                        op0=ALU.mult, op1=ALU.add)
                    g.s_ve.inc(v.drain())
                blk.vector(ve)

                if di == 0 and c == 0 and hasattr(g, "dbg_daw") and dt in (0, 1, 8):
                    def gpw(p, dt=dt, sn=sn, slot=slot, n=n, vprev=g.s_ve.v):
                        p.wait_ge(g.s_ve.h, vprev)
                        p.wait_ge(g.s_st.h, g.s_st.v)
                        row = dt * NS + sn
                        g.s_st.dma(p.dma_start(
                            out=g.dbg_daw.ap()[row, 0, :][None, :],
                            in_=g.da_s[0:1, slot, :]))
                        p.wait_ge(g.s_st.h, g.s_st.v)
                        g.s_st.dma(p.dma_start(
                            out=g.dbg_daw.ap()[row, 1, :][None, :],
                            in_=g.w_sc[0:1, 0, :]))
                        p.wait_ge(g.s_st.h, g.s_st.v)
                        ctx = g.nc.allow_non_contiguous_dma(reason="debug")
                        ctx.__enter__()
                        g.s_st.dma(p.dma_start(
                            out=g.dbg_daw.ap()[row, 2, :][None, :],
                            in_=hall.rearrange("p (t n) -> p t n", n=8)[0:1, :, n]))
                        ctx.__exit__(None, None, None)
                    blk.gpsimd(gpw)
                    def vg(v, stv=g.s_st.v):
                        v.wait_ge(g.s_st.h, stv)
                    blk.vector(vg)

            def ve2(v, dt=dt, gi=gi, carry=carry, stv=g.s_st.v):
                v.wait_ge(g.s_st.h, stv)
                v.tensor_copy(out=carry[:, dt * NS + gi * 8:dt * NS + gi * 8 + 8],
                              in_=hall[:, (TC - 1) * 8:TC * 8])
                v.drain()
                v.tensor_mul(out=hall[:, :], in0=hall[:, :], in1=crep[:, :])
                v.drain()
                if gi == 0:
                    v.tensor_reduce(out=g.yt_s[:, :],
                                    in_=hall.rearrange("p (t n) -> p t n", n=8),
                                    axis=AX.X, op=ALU.add)
                    g.s_ve.inc(v.drain())
                else:
                    v.tensor_reduce(out=g.ytmp_s[:, :],
                                    in_=hall.rearrange("p (t n) -> p t n", n=8),
                                    axis=AX.X, op=ALU.add)
                    g.s_ve.inc(v.drain())
            blk.vector(ve2)

            if gi == 0:
                def gp0(p, dt=dt, corig=corig, vprev=g.s_ve.v):
                    p.wait_ge(g.s_ve.h, vprev)
                    p.wait_ge(g.s_st.h, g.s_st.v)
                    g.s_st.dma(p.dma_start(
                        out=g.y0_sp.ap()[dt * 128:(dt + 1) * 128,
                                         corig * TC:(corig + 1) * TC],
                        in_=g.yt_s[:, :]))
                blk.gpsimd(gp0)

            if gi == 1:
                def q3(qq, dt=dt, corig=corig, vprev=g.s_ve.v, stv=g.s_st.v,
                       sw0=g.s_sw[0].v, sw1=g.s_sw[1].v):
                    qq.wait_ge(g.s_ve.h, vprev)
                    qq.wait_ge(g.s_st.h, stv)
                    qq.wait_ge(g.s_sw[0].h, sw0)
                    qq.wait_ge(g.s_sw[1].h, sw1)
                    qq.wait_ge(g.s_ld.h, g.s_ld.v)
                    g.s_ld.dma(qq.dma_start(
                        out=g.zdt_s[:, dt % 2, :],
                        in_=g.z_sp.ap()[dt * 128:(dt + 1) * 128,
                                        corig * TC:(corig + 1) * TC]))
                    qq.wait_ge(g.s_ld.h, g.s_ld.v)
                    g.s_ld.dma(qq.dma_start(
                        out=g.y0t_s[:, dt % 2, :],
                        in_=g.y0_sp.ap()[dt * 128:(dt + 1) * 128,
                                         corig * TC:(corig + 1) * TC]))
                blk.sync(q3)
                ldz = g.s_ld.v

                def se2(e, dt=dt, ldz=ldz, vprev=g.s_ve.v):
                    e.wait_ge(g.s_ld.h, ldz)
                    e.wait_ge(g.s_ve.h, vprev)
                    g.s_se.inc(e.activation(out=g.zdt_s[:, dt % 2, :],
                                            in_=g.zdt_s[:, dt % 2, :],
                                            func=AF.Silu, bias=0.0, scale=1.0))
                blk.scalar(se2)
                sdz = g.s_se.v

                def ve3(v, dt=dt, di=di, sdz=sdz, ldz=ldz):
                    v.wait_ge(g.s_se.h, sdz)
                    v.wait_ge(g.s_ld.h, ldz)
                    v.tensor_add(out=g.yt_s[:, :], in0=g.ytmp_s[:, :],
                                 in1=g.y0t_s[:, dt % 2, :])
                    v.drain()
                    v.scalar_tensor_tensor(out=g.yt_s[:, :], in0=v_u(g)[:, dt],
                                           scalar=g.Dp_s[:, di, dt:dt + 1],
                                           in1=g.yt_s[:, :], op0=ALU.mult,
                                           op1=ALU.add)
                    v.drain()
                    v.tensor_mul(out=v_gb(g, dt), in0=g.yt_s[:, :],
                                 in1=g.zdt_s[:, dt % 2, :])
                    g.s_ve.inc(v.drain())
                blk.vector(ve3)

                if di == 0 and hasattr(g, "dbg_y"):
                    def gpy(p, dt=dt, corig=corig, vprev=g.s_ve.v):
                        p.wait_ge(g.s_ve.h, vprev)
                        p.wait_ge(g.s_st.h, g.s_st.v)
                        g.s_st.dma(p.dma_start(
                            out=g.dbg_y.ap()[dt * 128:(dt + 1) * 128,
                                             corig * TC:(corig + 1) * TC],
                            in_=g.yt_s[:, :]))
                    blk.gpsimd(gpy)

    ve_gball = g.s_ve.v

    if di == 0:
        for dn, view in (("dbg_u", v_u(g)), ("dbg_dl", v_du(g))):
            if hasattr(g, dn):
                def gpd(p, dn=dn, view=view, corig=corig, vprev=g.s_ve.v):
                    p.wait_ge(g.s_ve.h, vprev)
                    p.wait_ge(g.s_st.h, g.s_st.v)
                    g.s_st.dma(p.dma_start(
                        out=getattr(g, dn).ap().rearrange("(d p) t -> p d t", p=128)[:, :, corig * TC:(corig + 1) * TC],
                        in_=view))
                blk.gpsimd(gpd)

    def q4(qq, oorig=oorig, vprev=g.s_ve.v):
        qq.wait_ge(g.s_ve.h, vprev)
        qq.wait_ge(g.s_ld.h, g.s_ld.v)
        g.s_ld.dma(qq.dma_start(out=g.fc8[:, :, :],
                                in_=hbmT(g.x1ln_sp)[:, :, oorig:oorig + TC]))
    blk.sync(q4)
    ld_xl = g.s_ld.v

    for ft in range(NFT):
        def q5(qq, ft=ft, wname=wname, tprev=g.s_tn.v):
            qq.wait_ge(g.s_tn.h, tprev)
            qq.wait_ge(g.s_ld.h, g.s_ld.v)
            g.s_ld.dma(qq.dma_start(
                out=g.wstr_s[:, ft % 2, 0:12, :],
                in_=g.wb[f"out_wT_{wname}"].ap().rearrange("(k p) f -> p k f", p=128)[:, 0:12, ft * 128:(ft + 1) * 128]))
            qq.wait_ge(g.s_ld.h, g.s_ld.v)
            g.s_ld.dma(qq.dma_start(
                out=g.wstr_s[:, (ft + 1) % 2, 0:4, :],
                in_=g.wb[f"out_wT_{wname}"].ap().rearrange("(k p) f -> p k f", p=128)[:, 12:16, ft * 128:(ft + 1) * 128]))
        blk.sync(q5)
        ld = g.s_ld.v

        def tn(t, ft=ft, ld=ld, ve_gball=ve_gball, vprev=g.s_ve.v):
            t.wait_ge(g.s_ld.h, ld)
            t.wait_ge(g.s_ve.h, ve_gball)
            if ft >= 2:
                t.wait_ge(g.s_ve.h, vprev)
            for k in range(12):
                li = t.matmul(g.p_mm[:, ft % 2, :], g.wstr_s[:, ft % 2, k, :],
                         v_gb(g, k), start=(k == 0), stop=False)
            for k in range(12, 16):
                li = t.matmul(g.p_mm[:, ft % 2, :],
                         g.wstr_s[:, (ft + 1) % 2, k - 12, :],
                         v_gb(g, k), start=False, stop=(k == 15))
            g.s_tn.inc(li)
        blk.tensor(tn)
        td = g.s_tn.v

        def ve(v, ft=ft, di=di, td=td, ld_xl=ld_xl):
            v.wait_ge(g.s_tn.h, td)
            v.wait_ge(g.s_ld.h, ld_xl)
            src = g.p_mm[:, ft % 2, :] if di == 0 else g.p_mm[:, ft % 2, ::-1]
            v.scalar_tensor_tensor(out=v_mo(g)[:, ft, :], in0=src, scalar=1.0,
                                   in1=g.fc8[:, ft, :], op0=ALU.mult,
                                   op1=ALU.add)
            g.s_ve.inc(v.drain())
        blk.vector(ve)
    ve_mo = g.s_ve.v

    for ft in range(NFT):
        def q6(qq, ft=ft, wname=wname, tprev=g.s_tn.v):
            qq.wait_ge(g.s_tn.h, tprev)
            qq.wait_ge(g.s_ld.h, g.s_ld.v)
            g.s_ld.dma(qq.dma_start(
                out=g.wstr_s[:, ft % 2, 0:NFT, :],
                in_=g.wb[f"projT_{wname}"].ap().rearrange("(k p) f -> p k f", p=128)[:, :, ft * 128:(ft + 1) * 128]))
        blk.sync(q6)
        ld = g.s_ld.v

        def tn(t, ft=ft, ld=ld, ve_mo=ve_mo, sprev=g.s_se.v, vprev=g.s_ve.v):
            t.wait_ge(g.s_ld.h, ld)
            t.wait_ge(g.s_ve.h, ve_mo)
            if ft >= 2:
                t.wait_ge(g.s_se.h, sprev)
                t.wait_ge(g.s_ve.h, vprev)
            for k in range(NFT):
                li = t.matmul(g.p_mm2[:, ft % 2, :], g.wstr_s[:, ft % 2, k, :],
                         v_mo(g)[:, k, :], start=(k == 0), stop=(k == NFT - 1))
            g.s_tn.inc(li)
        blk.tensor(tn)
        td = g.s_tn.v

        if di == 0:
            def se(e, ft=ft, td=td, stprev=g.s_st.v):
                e.wait_ge(g.s_tn.h, td)
                e.wait_ge(g.s_st.h, stprev)
                g.s_se.inc(e.copy(out=g.st_s[:, ft % 2, :],
                                  in_=g.p_mm2[:, ft % 2, :]))
            blk.scalar(se)
            sd = g.s_se.v

            def gp(p, ft=ft, corig=corig, sd=sd):
                p.wait_ge(g.s_se.h, sd)
                p.wait_ge(g.s_st.h, g.s_st.v)
                g.s_st.dma(p.dma_start(
                    out=hbmT(g.projA_sp)[:, ft, corig * TC:(corig + 1) * TC],
                    in_=g.st_s[:, ft % 2, :]))
            blk.gpsimd(gp)
        else:
            def q7(qq, ft=ft, corig=corig, vprev=g.s_ve.v):
                qq.wait_ge(g.s_ve.h, vprev)
                qq.wait_ge(g.s_ld.h, g.s_ld.v)
                g.s_ld.dma(qq.dma_start(
                    out=g.st_s[:, ft % 2, :],
                    in_=hbmT(g.projA_sp)[:, ft, corig * TC:(corig + 1) * TC]))
            blk.sync(q7)
            ld2 = g.s_ld.v

            def ve(v, ft=ft, td=td, ld2=ld2):
                v.wait_ge(g.s_tn.h, td)
                v.wait_ge(g.s_ld.h, ld2)
                v.scalar_tensor_tensor(out=v_pp(g)[:, ft, :],
                                       in0=g.p_mm2[:, ft % 2, :],
                                       scalar=g.pb_s[:, ft:ft + 1],
                                       in1=g.st_s[:, ft % 2, :],
                                       op0=ALU.add, op1=ALU.add)
                g.s_ve.inc(v.drain())
            blk.vector(ve)

    if di == 1:
        mixer_epilogue(g, corig, oorig)


def mixer_epilogue(g, corig, oorig):
    blk = g.block
    pp = v_pp(g)
    x2 = v_x2c(g)
    ln_chunk(g, pp, pp, TC)

    def q(qq, oorig=oorig, vprev=g.s_ve.v):
        qq.wait_ge(g.s_ve.h, vprev)
        qq.wait_ge(g.s_ld.h, g.s_ld.v)
        g.s_ld.dma(qq.dma_start(out=g.fc8[:, :, :],
                                in_=hbmT(g.x1_sp)[:, :, oorig:oorig + TC]))
    blk.sync(q)
    ld = g.s_ld.v

    def ve(v, ld=ld):
        v.wait_ge(g.s_ld.h, ld)
        for ft in range(NFT):
            v.tensor_scalar(out=pp[:, ft, :], in0=pp[:, ft, :],
                            scalar1=g.lnw_s[:, ft:ft + 1],
                            scalar2=g.lnb_s[:, ft:ft + 1],
                            op0=ALU.mult, op1=ALU.add)
        v.drain()
        for ft in range(NFT):
            v.tensor_add(out=x2[:, ft, :], in0=pp[:, ft, :], in1=g.fc8[:, ft, :])
        g.s_ve.inc(v.drain())
    blk.vector(ve)
    vd = g.s_ve.v

    def gp(p, corig=corig, vd=vd):
        p.wait_ge(g.s_ve.h, vd)
        p.wait_ge(g.s_st.h, g.s_st.v)
        g.s_st.dma(p.dma_start(
            out=hbmT(g.x2_sp)[:, :, corig * TC:(corig + 1) * TC],
            in_=x2[:, :, :]))
    blk.gpsimd(gp)

    ln_chunk(g, x2, pp, TC)
    absmax_acc(g, pp, SC_X2, axis=AX.XY)
    vd2 = g.s_ve.v

    def gp2(p, corig=corig, vd2=vd2):
        p.wait_ge(g.s_ve.h, vd2)
        p.wait_ge(g.s_st.h, g.s_st.v)
        g.s_st.dma(p.dma_start(
            out=hbmT(g.x2ln_sp)[:, :, corig * TC:(corig + 1) * TC],
            in_=pp[:, :, :]))
    blk.gpsimd(gp2)


def finish(g):
    blk = g.block

    def gp(p):
        p.wait_ge(g.s_out.h, g.s_out.v)
        p.wait_ge(g.s_st.h, g.s_st.v)
    blk.gpsimd(gp)



def kernel(**inputs):
    nc = build_nc()
    in_maps = make_in_maps(inputs)
    res = run_bass_kernel_spmd(nc, in_maps, core_ids=list(range(8)))
    return assemble([r for r in res.results])


def make_in_maps(inputs):
    i = inputs
    x = i["L0_attn_x"]
    w = {}
    for pre in ("fw", "bw"):
        w[f"in_wT_{pre}"] = np.ascontiguousarray(i[f"L0_attn_{pre}_in_w"].T)
        w[f"conv_w_{pre}"] = np.ascontiguousarray(i[f"L0_attn_{pre}_conv_w"][:, 0, :])
        w[f"conv_b_{pre}"] = np.ascontiguousarray(i[f"L0_attn_{pre}_conv_b"].reshape(DI, 1))
        w[f"xproj_wT_{pre}"] = np.ascontiguousarray(i[f"L0_attn_{pre}_xproj_w"].T)
        w[f"dt_wT_{pre}"] = np.ascontiguousarray(i[f"L0_attn_{pre}_dt_w"].T)
        w[f"dt_b_{pre}"] = np.ascontiguousarray(i[f"L0_attn_{pre}_dt_b"].reshape(DI, 1))
        w[f"Alog_{pre}"] = np.ascontiguousarray(i[f"L0_attn_{pre}_Alog"])
        w[f"Dp_{pre}"] = np.ascontiguousarray(i[f"L0_attn_{pre}_Dp"].reshape(DI, 1))
        w[f"out_wT_{pre}"] = np.ascontiguousarray(i[f"L0_attn_{pre}_out_w"].T)
    pT = np.ascontiguousarray(i["L0_attn_proj_w"].T)
    w["projT_fw"] = np.ascontiguousarray(pT[:D, :])
    w["projT_bw"] = np.ascontiguousarray(pT[D:, :])

    shared = dict(
        proj_b=np.ascontiguousarray(i["L0_attn_proj_b"].reshape(D, 1)),
        ln_w=np.ascontiguousarray(i["L0_attn_ln_w"].reshape(D, 1)),
        ln_b=np.ascontiguousarray(i["L0_attn_ln_b"].reshape(D, 1)),
        w_gate=np.ascontiguousarray(i["L0_ffn_w_gate"]),
        w_up=np.ascontiguousarray(i["L0_ffn_w_up"]),
        w_down=np.ascontiguousarray(i["L0_ffn_w_down"]),
    )
    maps = []
    for c in range(8):
        b, h = c // 2, c % 2
        xb = x[b].T
        lo, hi = h * TOK - HALO, h * TOK + TOK + HALO
        pad_l, pad_r = max(0, -lo), max(0, hi - LFULL)
        sl = xb[:, max(0, lo):min(LFULL, hi)]
        if pad_l or pad_r:
            sl = np.pad(sl, ((0, 0), (pad_l, pad_r)))
        if h == 1:
            sl = sl[:, ::-1]
        m = dict(shared)
        m["xT"] = np.ascontiguousarray(sl)
        dirA = "fw" if h == 0 else "bw"
        for key, val in w.items():
            base, pre = key.rsplit("_", 1)
            tgt = "A" if pre == dirA else "B"
            m[f"{base}_{tgt}"] = val
        maps.append(m)
    return maps


def assemble(results):
    out = np.empty((B, LFULL, D), np.float32)
    for c in range(8):
        b, h = c // 2, c % 2
        o = results[c]["out"]
        if h == 1:
            o = o[:, ::-1]
        out[b, h * TOK:(h + 1) * TOK, :] = o.T
    return out


# version 05
# speedup vs baseline: 1.0967x; 1.0046x over previous
import contextlib
import numpy as np

import concourse.bass as bass
import concourse.mybir as mybir
from concourse.bass_utils import run_bass_kernel_spmd

F32 = mybir.dt.float32
F16 = mybir.dt.float16
BF16 = mybir.dt.bfloat16
AF = mybir.ActivationFunctionType
ALU = mybir.AluOpType
AX = mybir.AxisListType

D = 1024
DI = 2048
NDT = 16
NFT = 8
DTR = 64
NS = 16
FF = 3072
NMT = 24
B = 4
LFULL = 4096
TOK = 2048
HALO = 3
TOKH = TOK + 2 * HALO
TC = 512
NCH = 4
CH5 = [(0, 512), (512, 512), (1024, 512), (1536, 512), (2048, 6)]
CH4 = [(i * TC, TC) for i in range(NCH)]

QMAX = 32767.0
WQMAX = 7.0
MAGIC = float(np.float32(1.5 * 2 ** 23))
LN_EPS = 1e-5

SC_X, SC_UP, SC_GATE, SC_GS, SC_T5 = 0, 1, 2, 3, 4
SC_X2, SC_UP2, SC_GATE2, SC_GS2, SC_T52 = 5, 6, 7, 8, 9
N_SCAL = 10
TMPC = 15


class Sem:
    def __init__(self, es, nc, name):
        self.h = es.enter_context(nc.semaphore(name))
        self.v = 0

    def inc(self, instr, n=1):
        instr.then_inc(self.h, n)
        self.v += n

    def dma(self, instr):
        self.inc(instr, 16)


class G:
    pass


def build_nc(taps=()):
    nc = bass.Bass("TRN2", target_bir_lowering=False)
    es = contextlib.ExitStack()
    g = G()
    g.nc = nc
    g.taps = list(taps)

    g.xT = nc.dram_tensor("xT", [D, TOKH], F32, kind="ExternalInput")
    g.inp = {}
    for d in ("A", "B"):
        for nm, shape in (("in_wT", [D, 2 * DI]), ("conv_w", [DI, 4]),
                          ("conv_b", [DI, 1]), ("xproj_wT", [DI, 96]),
                          ("dt_wT", [DTR, DI]), ("dt_b", [DI, 1]),
                          ("Alog", [DI, NS]), ("Dp", [DI, 1]),
                          ("out_wT", [DI, D]), ("projT", [D, D])):
            g.inp[f"{nm}_{d}"] = nc.dram_tensor(f"{nm}_{d}", shape, F32,
                                                kind="ExternalInput").ap()
    g.proj_b = nc.dram_tensor("proj_b", [D, 1], F32, kind="ExternalInput").ap()
    g.ln_w = nc.dram_tensor("ln_w", [D, 1], F32, kind="ExternalInput").ap()
    g.ln_b = nc.dram_tensor("ln_b", [D, 1], F32, kind="ExternalInput").ap()
    g.w_up = nc.dram_tensor("w_up", [FF, D], F32, kind="ExternalInput").ap()
    g.w_gate = nc.dram_tensor("w_gate", [FF, D], F32, kind="ExternalInput").ap()
    g.w_down = nc.dram_tensor("w_down", [D, FF], F32, kind="ExternalInput").ap()
    g.out_h = nc.dram_tensor("out", [D, TOK], F32, kind="ExternalOutput").ap()

    def dram(name, shape, dt):
        if name in g.taps:
            t = nc.dram_tensor(name, shape, dt, kind="ExternalOutput")
        else:
            t = nc.dram_tensor(name, shape, dt)
        setattr(g, name, t)
        return t

    dram("wq_upT", [D, FF], BF16)
    dram("wq_gT", [D, FF], BF16)
    dram("wq_dT", [FF, D], BF16)
    g.wb = {}
    for d in ("A", "B"):
        g.wb[f"in_wT_{d}"] = dram(f"b_in_wT_{d}", [D, 2 * DI], BF16)
        g.wb[f"out_wT_{d}"] = dram(f"b_out_wT_{d}", [DI, D], BF16)
        g.wb[f"projT_{d}"] = dram(f"b_projT_{d}", [D, D], BF16)
    dram("xln_sp", [D, TOKH], F32)
    dram("x1_sp", [D, TOKH], F32)
    dram("x1ln_sp", [D, TOKH], F32)
    dram("x1lnb_sp", [D, TOKH], BF16)
    dram("x1lnbR_sp", [D, TOKH], BF16)
    dram("x2_sp", [D, TOK], F32)
    dram("x2ln_sp", [D, TOK], F32)
    dram("up_sp", [FF, TOKH], F16)
    dram("gate_sp", [FF, TOKH], F16)
    dram("gs_sp", [FF, TOKH], F16)
    dram("t5_sp", [FF, TOKH], F16)
    dram("z_sp", [DI, TOK], F32)
    dram("projA_sp", [D, TOK], F32)
    dram("y0_sp", [DI, TOK], F32)
    g.scal_in = [dram(f"scin{i}", [1, 128], F32) for i in range(N_SCAL)]
    g.scal_out = [dram(f"scout{i}", [1, 128], F32) for i in range(N_SCAL)]
    dram("carry_io", [128, NDT * NS], F32)
    for dn in ("dbg_u", "dbg_dl", "dbg_y"):
        if dn in g.taps:
            dram(dn, [DI, TOK], F32)
    if "dbg_daw" in g.taps:
        dram("dbg_daw", [NDT * NS, 3, TC], F32)
    dram("carry_sum", [128, NDT * NS], F32)

    E = es.enter_context

    def sbuf(name, shape, dt):
        t = E(nc.sbuf_tensor(name, shape, dt))
        setattr(g, name, t)
        return t

    sbuf("ar3", [128, 16512], BF16)
    sbuf("aru", [128, 8192], F32)
    sbuf("arb", [128, 24576], F16)
    sbuf("fc8", [128, NFT, TC], F32)
    sbuf("zdt_s", [128, 2, TC], F32)
    sbuf("ubt_s", [128, 1, TC], BF16)
    sbuf("yt_s", [128, TC], F32)
    sbuf("y0t_s", [128, 2, TC], F32)
    sbuf("dup_s", [128, TC], F32)
    sbuf("ytmp_s", [128, TC], F32)
    sbuf("da_s", [128, 2, TC], F32)
    sbuf("w_sc", [128, 1, TC], F32)
    sbuf("wstr_s", [128, 2, 12, 128], BF16)
    sbuf("st_s", [128, 2, 512], F32)
    sbuf("stb_s", [128, 2, 512], F16)
    sbuf("g16_s", [128, 2, 512], F16)
    sbuf("h16_s", [128, 2, 512], F16)
    sbuf("sg_s", [128, 512], F32)
    sbuf("sg2_s", [128, 512], F32)
    sbuf("dq_s", [128, 96, 3], F32)
    sbuf("xdb_s", [96, 2, TC], F32)
    sbuf("dtb16_s", [DTR, TC], BF16)
    sbuf("bclo_s", [32, 2, TC], BF16)
    sbuf("sel_s", [32, 32, 128], BF16)
    sbuf("xicar_s", [128, NDT, HALO], F32)
    sbuf("ones_s", [1, 128], BF16)
    sbuf("onesf_s", [1, 128], F32)
    sbuf("onescol_s", [128, 1], F32)
    sbuf("ident_s", [128, 128], BF16)
    sbuf("identf_s", [128, 128], F32)
    sbuf("A_s", [128, 2, NDT * NS], F32)
    sbuf("Dp_s", [128, 2, NDT], F32)
    sbuf("cb_s", [128, 2, NDT], F32)
    sbuf("cw_s", [128, 2, NDT, 4], F32)
    sbuf("dtb_s", [128, 2, NDT], F32)
    sbuf("pb_s", [128, NFT], F32)
    sbuf("lnw_s", [128, NFT], F32)
    sbuf("lnb_s", [128, NFT], F32)
    sbuf("xprojT_s", [128, 2, NDT, 96], BF16)
    sbuf("dtwT_s", [DTR, 2, NDT, 128], BF16)
    sbuf("carry_s", [128, 2, NDT * NS], F32)
    sbuf("pcar_s", [128, NDT * NS], F32)
    sbuf("mx_s", [128, 16], F32)
    sbuf("sc_s", [128, N_SCAL], F32)
    sbuf("ss_s", [128, N_SCAL], F32)
    sbuf("t0_s", [1, 128], F32)
    sbuf("eps1_s", [1, 1], F32)

    def psum(name, shape, dt):
        t = E(nc.psum_tensor(name, shape, dt))
        setattr(g, name, t)

    psum("p_mm", [128, 2, 512], F32)
    psum("p_mm2", [128, 2, 512], F32)
    psum("p_bc", [128, 512], F32)
    psum("p_bc2", [128, 512], F32)
    psum("p_xd", [96, TC], F32)
    psum("p_tr", [128, 128], BF16)

    for s in ("s_ld", "s_st", "s_tn", "s_ve", "s_se", "s_ge", "s_cc",
              "s_out", "s_ld2"):
        setattr(g, s, Sem(es, nc, s))
    g.s_lw = [Sem(es, nc, f"s_lw{i}") for i in range(2)]
    g.s_lx = [Sem(es, nc, f"s_lx{i}") for i in range(2)]
    g.s_sw = [Sem(es, nc, f"s_sw{i}") for i in range(2)]

    with nc.Block() as block:
        g.block = block
        prep_constants(g)
        prep_weights(g)
        p1_lnq(g, src="x")
        ffn_mid(g, ver=1)
        p4_mixer(g)
        p1_lnq(g, src="x2")
        ffn_mid(g, ver=2)
        finish(g)
    es.close()
    return nc



def v_qb(g):
    return g.ar3.ap()[:, 0:NFT * TOKH].rearrange("p (f t) -> p f t", f=NFT)


def v_qb2(g):
    return g.ar3.ap()[:, 0:NFT * TOK].rearrange("p (f t) -> p f t", f=NFT)


def v_xi(g):
    return g.ar3.ap().bitcast(F32)[:, 0:NDT * (TC + HALO)].rearrange(
        "p (d t) -> p d t", d=NDT)


def v_du(g):
    return g.ar3.ap().bitcast(F32)[:, 0:NDT * TC].rearrange(
        "p (d t) -> p d t", d=NDT)


def v_gb(g, k=None):
    a = g.ar3.ap()[:, 0:NDT * TC].rearrange("p (d t) -> p d t", d=NDT)
    return a if k is None else a[:, k]


def v_u(g):
    return g.aru.ap().rearrange("p (d t) -> p d t", d=NDT)


def v_sq(g, n):
    return g.aru.ap()[:, 0:NFT * n].rearrange("p (f t) -> p f t", f=NFT)


def v_rows(g):
    return g.aru.ap()[0:1, NFT * TC:NFT * TC + 5 * 512].rearrange(
        "p (r t) -> p r t", r=5)


def v_abl(g):
    o = 2 * (NFT * TC + 5 * 512)
    return g.aru.ap().bitcast(BF16)[0:1, o:o + 4 * 512].rearrange(
        "p (r t) -> p r t", r=4)


def v_brep(g):
    return g.arb.ap().bitcast(F32)[:, 0:8 * TC].rearrange("p (n t) -> p n t", n=8)


def v_crep(g):
    return g.arb.ap().bitcast(F32)[:, 8 * TC:16 * TC]


def v_hall(g):
    return g.arb.ap().bitcast(F32)[:, 16 * TC:24 * TC]


def v_wraw(g):
    return g.arb.ap().bitcast(F32)[:, 0:4096]


def v_wqb(g):
    return g.arb.ap().bitcast(BF16)[:, 8192:8192 + 4096]


def v_lnout(g, n):
    return g.arb.ap().bitcast(F32)[:, 0:NFT * n].rearrange(
        "p (f t) -> p f t", f=NFT)


def v_pp(g):
    return g.arb.ap().bitcast(F32)[:, 16 * TC:16 * TC + NFT * TC].rearrange(
        "p (f t) -> p f t", f=NFT)


def v_x2c(g):
    return g.arb.ap().bitcast(F32)[:, 8 * TC:8 * TC + NFT * TC].rearrange(
        "p (f t) -> p f t", f=NFT)


def v_xlc(g):
    return g.arb.ap().bitcast(BF16)[:, 0:NFT * TC].rearrange(
        "p (f t) -> p f t", f=NFT)


def v_mo(g):
    return g.arb.ap().bitcast(BF16)[:, NFT * TC:2 * NFT * TC].rearrange(
        "p (f t) -> p f t", f=NFT)


def v_lnb16(g):
    return g.arb.ap().bitcast(BF16)[:, 8 * NFT * TC // 4:8 * NFT * TC // 4 + NFT * TC].rearrange(
        "p (f t) -> p f t", f=NFT)


def v_lnb16R(g):
    o = 8 * NFT * TC // 4 + NFT * TC
    return g.arb.ap().bitcast(BF16)[:, o:o + NFT * TC].rearrange(
        "p (f t) -> p f t", f=NFT)


def v_oq(g):
    return g.arb.ap().bitcast(BF16)[:, 0:NMT * 512].rearrange(
        "p (m t) -> p m t", m=NMT)


def hbmT(t):
    return t.ap().rearrange("(f p) t -> p f t", p=128)



def barrier_all(g):
    blk = g.block
    snap = [(g.s_ld, g.s_ld.v), (g.s_st, g.s_st.v), (g.s_tn, g.s_tn.v),
            (g.s_ve, g.s_ve.v), (g.s_se, g.s_se.v), (g.s_ge, g.s_ge.v),
            (g.s_cc, g.s_cc.v), (g.s_ld2, g.s_ld2.v), (g.s_out, g.s_out.v)]
    snap += [(s, s.v) for s in g.s_lw + g.s_lx + g.s_sw]
    for eng in ("sync", "vector", "scalar", "tensor", "gpsimd"):
        def body(e, snap=snap):
            for s, v in snap:
                if v:
                    e.wait_ge(s.h, v)
        getattr(blk, eng)(body)


def ln_chunk(g, src_ap, out_ap, n):
    blk = g.block
    sq = v_sq(g, n)
    ve0 = g.s_ve.v

    def se(e, ve0=ve0):
        e.wait_ge(g.s_ve.h, ve0)
        e.activation(out=sq, in_=src_ap, func=AF.Square, bias=0.0, scale=1.0)
        g.s_se.inc(e.drain())
    blk.scalar(se)
    sd = g.s_se.v

    def tn(t, sd=sd, n=n):
        t.wait_ge(g.s_se.h, sd)
        for ft in range(NFT):
            li = t.matmul(g.p_xd[0:1, 0:n], g.onescol_s[:, :], src_ap[:, ft],
                          start=(ft == 0), stop=(ft == NFT - 1))
        g.s_tn.inc(li)
        for ft in range(NFT):
            li = t.matmul(g.p_bc[0:1, 0:n], g.onescol_s[:, :], sq[:, ft],
                          start=(ft == 0), stop=(ft == NFT - 1))
        g.s_tn.inc(li)
    blk.tensor(tn)
    td = g.s_tn.v
    rows = v_rows(g)
    abl = v_abl(g)

    def ve(v, td=td, n=n):
        v.wait_ge(g.s_tn.h, td)
        v.tensor_scalar_mul(out=rows[:, 0, 0:n], in0=g.p_xd[0:1, 0:n],
                            scalar1=1.0 / D)
        v.tensor_scalar_mul(out=rows[:, 1, 0:n], in0=g.p_bc[0:1, 0:n],
                            scalar1=1.0 / D)
        v.drain()
        v.tensor_tensor(out=rows[:, 2, 0:n], in0=rows[:, 0, 0:n],
                        in1=rows[:, 0, 0:n], op=ALU.mult)
        v.drain()
        v.tensor_sub(out=rows[:, 2, 0:n], in0=rows[:, 1, 0:n],
                     in1=rows[:, 2, 0:n])
        g.s_ve.inc(v.drain())
    blk.vector(ve)
    vd = g.s_ve.v

    def se2(e, vd=vd, n=n):
        e.wait_ge(g.s_ve.h, vd)
        g.s_se.inc(e.activation(out=rows[:, 3, 0:n], in_=rows[:, 2, 0:n],
                                func=AF.Sqrt, bias=g.eps1_s[:, :], scale=1.0))
    blk.scalar(se2)
    sd2 = g.s_se.v

    def ve2(v, sd2=sd2, n=n):
        v.wait_ge(g.s_se.h, sd2)
        v.reciprocal(out=rows[:, 3, 0:n], in_=rows[:, 3, 0:n])
        v.drain()
        v.scalar_tensor_tensor(out=rows[:, 4, 0:n], in0=rows[:, 0, 0:n],
                               scalar=-1.0, in1=rows[:, 3, 0:n],
                               op0=ALU.mult, op1=ALU.mult)
        v.drain()
        v.tensor_copy(out=abl[:, 0, 0:n], in_=rows[:, 3, 0:n])
        v.tensor_copy(out=abl[:, 1, 0:n], in_=rows[:, 4, 0:n])
        v.drain()
        v.tensor_sub(out=rows[:, 0, 0:n], in0=rows[:, 3, 0:n], in1=abl[:, 0, 0:n])
        v.tensor_sub(out=rows[:, 1, 0:n], in0=rows[:, 4, 0:n], in1=abl[:, 1, 0:n])
        v.drain()
        v.tensor_copy(out=abl[:, 2, 0:n], in_=rows[:, 0, 0:n])
        v.tensor_copy(out=abl[:, 3, 0:n], in_=rows[:, 1, 0:n])
        g.s_ve.inc(v.drain())
    blk.vector(ve2)
    vd2 = g.s_ve.v

    def tn2(t, vd2=vd2, n=n):
        t.wait_ge(g.s_ve.h, vd2)
        li = t.matmul(g.p_bc[:, 0:n], g.ones_s[:, :], abl[:, 0, 0:n], start=True, stop=False)
        li = t.matmul(g.p_bc[:, 0:n], g.ones_s[:, :], abl[:, 2, 0:n], start=False, stop=True)
        g.s_tn.inc(li)
        li = t.matmul(g.p_bc2[:, 0:n], g.ones_s[:, :], abl[:, 1, 0:n], start=True, stop=False)
        li = t.matmul(g.p_bc2[:, 0:n], g.ones_s[:, :], abl[:, 3, 0:n], start=False, stop=True)
        g.s_tn.inc(li)
    blk.tensor(tn2)
    td2 = g.s_tn.v

    def ve3(v, td2=td2, n=n):
        v.wait_ge(g.s_tn.h, td2)
        v.tensor_tensor(out=out_ap, in0=src_ap,
                        in1=g.p_bc.ap()[:, None, 0:n].broadcast_to([128, NFT, n]),
                        op=ALU.mult)
        v.drain()
        v.tensor_tensor(out=out_ap, in0=out_ap,
                        in1=g.p_bc2.ap()[:, None, 0:n].broadcast_to([128, NFT, n]),
                        op=ALU.add)
        g.s_ve.inc(v.drain())
    blk.vector(ve3)
    return g.s_ve.v


def absmax_acc(g, src_ap, slot, axis=AX.X, wait_se=None):
    blk = g.block

    def ve(v, slot=slot, axis=axis):
        if wait_se is not None:
            v.wait_ge(g.s_se.h, wait_se)
        v.tensor_reduce(out=g.mx_s[:, TMPC:TMPC + 1], in_=src_ap, axis=axis,
                        op=ALU.max, apply_absolute_value=True)
        v.drain()
        v.tensor_max(out=g.mx_s[:, slot:slot + 1], in0=g.mx_s[:, slot:slot + 1],
                     in1=g.mx_s[:, TMPC:TMPC + 1])
        g.s_ve.inc(v.drain())
    blk.vector(ve)
    return g.s_ve.v


def scale_exchange(g, slot, qmax):
    blk = g.block
    ve_prev = g.s_ve.v

    def tn(t, slot=slot, ve_prev=ve_prev):
        t.wait_ge(g.s_ve.h, ve_prev)
        g.s_tn.inc(t.transpose(g.p_bc[0:1, 0:128], g.mx_s[:, slot:slot + 1],
                               g.identf_s[:]))
    blk.tensor(tn)
    td = g.s_tn.v

    def ve(v, td=td):
        v.wait_ge(g.s_tn.h, td)
        v.reduce_max(out=g.t0_s[:, 0:1], in_=g.p_bc[0:1, 0:128], axis=AX.X)
        v.drain()
        g.s_ve.inc(v.tensor_copy(
            out=g.t0_s[:, :], in_=g.t0_s.ap()[:, 0:1].broadcast_to([1, 128])))
    blk.vector(ve)
    vd = g.s_ve.v

    def gp(p, slot=slot, vd=vd, sw0=g.s_sw[0].v, sw1=g.s_sw[1].v):
        p.wait_ge(g.s_ve.h, vd)
        p.wait_ge(g.s_sw[0].h, sw0)
        p.wait_ge(g.s_sw[1].h, sw1)
        p.wait_ge(g.s_st.h, g.s_st.v)
        g.s_st.dma(p.dma_start(out=g.scal_in[slot].ap(), in_=g.t0_s[:, :]))
        p.wait_ge(g.s_st.h, g.s_st.v)
        g.s_cc.inc(p.collective_compute(
            "AllReduce", ALU.max, replica_groups=[list(range(8))],
            ins=[g.scal_in[slot].ap().opt()], outs=[g.scal_out[slot].ap().opt()]))
        p.wait_ge(g.s_cc.h, g.s_cc.v)
        p.wait_ge(g.s_st.h, g.s_st.v)
        g.s_st.dma(p.dma_start(
            out=g.ss_s[:, slot:slot + 1],
            in_=g.scal_out[slot].ap().rearrange("o p -> (o p)")[:, None]))
    blk.gpsimd(gp)
    std = g.s_st.v

    def ve2(v, slot=slot, std=std, qmax=qmax):
        v.wait_ge(g.s_st.h, std)
        v.tensor_scalar(out=g.ss_s[:, slot:slot + 1], in0=g.ss_s[:, slot:slot + 1],
                        scalar1=1.0 / qmax, scalar2=1e-12, op0=ALU.mult,
                        op1=ALU.max)
        v.drain()
        v.reciprocal(out=g.sc_s[:, slot:slot + 1], in_=g.ss_s[:, slot:slot + 1])
        g.s_ve.inc(v.drain())
    blk.vector(ve2)
    return g.s_ve.v



def prep_constants(g):
    blk = g.block
    nc = g.nc

    def q(qq):
        ctx = nc.allow_non_contiguous_dma(reason="tiny per-channel param loads")
        ctx.__enter__()
        for i, d in enumerate(("A", "B")):
            qq.wait_ge(g.s_ld.h, g.s_ld.v)
            g.s_ld.dma(qq.dma_start(
                out=g.A_s.ap()[:, i].rearrange("p (a n) -> p a n", a=NDT),
                in_=g.inp[f"Alog_{d}"].rearrange("(a p) n -> p a n", p=128)))
            qq.wait_ge(g.s_ld.h, g.s_ld.v)
            g.s_ld.dma(qq.dma_start(
                out=g.Dp_s.ap()[:, i, :, None],
                in_=g.inp[f"Dp_{d}"].rearrange("(a p) o -> p a o", p=128)))
            qq.wait_ge(g.s_ld.h, g.s_ld.v)
            g.s_ld.dma(qq.dma_start(
                out=g.cb_s.ap()[:, i, :, None],
                in_=g.inp[f"conv_b_{d}"].rearrange("(a p) o -> p a o", p=128)))
            qq.wait_ge(g.s_ld.h, g.s_ld.v)
            g.s_ld.dma(qq.dma_start(out=g.cw_s[:, i],
                                    in_=g.inp[f"conv_w_{d}"].rearrange("(a p) k -> p a k", p=128)))
            qq.wait_ge(g.s_ld.h, g.s_ld.v)
            g.s_ld.dma(qq.dma_start(
                out=g.dtb_s.ap()[:, i, :, None],
                in_=g.inp[f"dt_b_{d}"].rearrange("(a p) o -> p a o", p=128)))
        qq.wait_ge(g.s_ld.h, g.s_ld.v)
        g.s_ld.dma(qq.dma_start(out=g.pb_s.ap()[:, :, None],
                                in_=g.proj_b.rearrange("(a p) o -> p a o", p=128)))
        qq.wait_ge(g.s_ld.h, g.s_ld.v)
        g.s_ld.dma(qq.dma_start(out=g.lnw_s.ap()[:, :, None],
                                in_=g.ln_w.rearrange("(a p) o -> p a o", p=128)))
        qq.wait_ge(g.s_ld.h, g.s_ld.v)
        g.s_ld.dma(qq.dma_start(out=g.lnb_s.ap()[:, :, None],
                                in_=g.ln_b.rearrange("(a p) o -> p a o", p=128)))
        ctx.__exit__(None, None, None)
    blk.sync(q)
    ld0 = g.s_ld.v

    def ve(v):
        v.memset(g.ones_s[:], 1.0)
        v.memset(g.onesf_s[:], 1.0)
        v.memset(g.onescol_s[:], 1.0)
        v.memset(g.identf_s[:], 1.0)
        v.memset(g.ident_s[:], 1.0)
        v.memset(g.carry_s[:], 0.0)
        v.memset(g.mx_s[:], 0.0)
        v.memset(g.eps1_s[:], LN_EPS)
        v.memset(g.sel_s[:], 1.0)
        g.s_ve.inc(v.drain())
    blk.vector(ve)
    ve0 = g.s_ve.v

    def gp(p, ve0=ve0):
        p.wait_ge(g.s_ve.h, ve0)
        p.affine_select(out=g.ident_s[:], in_=g.ident_s[:], pattern=[[-1, 128]],
                        compare_op=ALU.is_equal, fill=0.0, base=0,
                        channel_multiplier=1)
        p.affine_select(out=g.identf_s[:], in_=g.identf_s[:],
                        pattern=[[-1, 128]], compare_op=ALU.is_equal,
                        fill=0.0, base=0, channel_multiplier=1)
        for n in range(32):
            p.affine_select(out=g.sel_s[:, n, :], in_=g.sel_s[:, n, :],
                            pattern=[[0, 128]], compare_op=ALU.is_equal,
                            fill=0.0, base=-n, channel_multiplier=1)
        p.sem_inc(g.s_ge.h, 1)
        g.s_ge.v += 1
    blk.gpsimd(gp)

    def se(e, ld0=ld0):
        e.wait_ge(g.s_ld.h, ld0)
        e.activation(out=g.A_s[:], in_=g.A_s[:], func=AF.Exp, bias=0.0, scale=1.0)
        e.drain()
        g.s_se.inc(e.mul(out=g.A_s[:], in_=g.A_s[:], mul=-1.0))
    blk.scalar(se)

    for i, d in enumerate(("A", "B")):
        def q2(qq, d=d, vprev=g.s_ve.v):
            qq.wait_ge(g.s_ve.h, vprev)
            qq.wait_ge(g.s_ld2.h, g.s_ld2.v)
            g.s_ld2.dma(qq.dma_start(
                out=v_wraw(g)[:, 0:NDT * 96].rearrange("p (a m) -> p a m", a=NDT),
                in_=g.inp[f"xproj_wT_{d}"].rearrange("(a p) m -> p a m", p=128)))
            qq.wait_ge(g.s_ld2.h, g.s_ld2.v)
            g.s_ld2.dma(qq.dma_start(
                out=v_wraw(g)[0:DTR, 1536:1536 + NDT * 64],
                in_=g.inp[f"dt_wT_{d}"][:, 0:NDT * 64]))
            qq.wait_ge(g.s_ld2.h, g.s_ld2.v)
            g.s_ld2.dma(qq.dma_start(
                out=v_wraw(g)[64:128, 1536:1536 + NDT * 64],
                in_=g.inp[f"dt_wT_{d}"][:, NDT * 64:NDT * 128]))
        blk.scalar(q2)
        ld = g.s_ld2.v

        def ve2(v, i=i, ld=ld):
            v.wait_ge(g.s_ld2.h, ld)
            v.tensor_copy(
                out=g.xprojT_s.ap().rearrange("p a m k -> p (a m k)")[:, i * NDT * 96:(i + 1) * NDT * 96],
                in_=v_wraw(g)[:, 0:NDT * 96])
            v.tensor_copy(
                out=g.dtwT_s.ap().rearrange("r a m k -> r (a m k)")[:, i * NDT * 128:i * NDT * 128 + NDT * 64],
                in_=v_wraw(g)[0:DTR, 1536:1536 + NDT * 64])
            g.s_ve.inc(v.tensor_copy(
                out=g.dtwT_s.ap().rearrange("r a m k -> r (a m k)")[:, i * NDT * 128 + NDT * 64:(i + 1) * NDT * 128],
                in_=v_wraw(g)[64:128, 1536:1536 + NDT * 64]))
        blk.vector(ve2)


def prep_weights(g):
    blk = g.block
    barrier_all(g)
    jobs = []
    for d in ("A", "B"):
        jobs.append((g.inp[f"in_wT_{d}"], g.wb[f"in_wT_{d}"], D, 2 * DI))
        jobs.append((g.inp[f"out_wT_{d}"], g.wb[f"out_wT_{d}"], DI, D))
        jobs.append((g.inp[f"projT_{d}"], g.wb[f"projT_{d}"], D, D))
    for src, dst, rows, cols in jobs:
        for i in range(rows // 128):
            def q(qq, src=src, i=i, cols=cols, stv=g.s_st.v):
                qq.wait_ge(g.s_st.h, stv)
                qq.wait_ge(g.s_ld.h, g.s_ld.v)
                g.s_ld.dma(qq.dma_start(out=v_wraw(g)[:, 0:cols],
                                        in_=src[i * 128:(i + 1) * 128, :]))
            blk.sync(q)
            ld = g.s_ld.v

            def ve(v, cols=cols, ld=ld):
                v.wait_ge(g.s_ld.h, ld)
                g.s_ve.inc(v.tensor_copy(out=v_wqb(g)[:, 0:cols],
                                         in_=v_wraw(g)[:, 0:cols]))
            blk.vector(ve)
            vd = g.s_ve.v

            def gp(p, dst=dst, i=i, cols=cols, vd=vd):
                p.wait_ge(g.s_ve.h, vd)
                p.wait_ge(g.s_st.h, g.s_st.v)
                g.s_st.dma(p.dma_start(out=dst.ap()[i * 128:(i + 1) * 128, :],
                                       in_=v_wqb(g)[:, 0:cols]))
            blk.gpsimd(gp)

    for src, dstT, O, I in ((g.w_up, g.wq_upT, FF, D),
                            (g.w_gate, g.wq_gT, FF, D),
                            (g.w_down, g.wq_dT, D, FF)):
        nblk = I // 32
        for i in range(O // 128):
            def q(qq, src=src, i=i, I=I, stv=g.s_st.v):
                qq.wait_ge(g.s_st.h, stv)
                qq.wait_ge(g.s_ld.h, g.s_ld.v)
                g.s_ld.dma(qq.dma_start(out=v_wraw(g)[:, 0:I],
                                        in_=src[i * 128:(i + 1) * 128, :]))
            blk.sync(q)
            ld = g.s_ld.v

            def ve(v, I=I, nblk=nblk, ld=ld):
                v.wait_ge(g.s_ld.h, ld)
                w = v_wraw(g)[:, 0:I]
                w3 = v_wraw(g)[:, 0:I].rearrange("p (n k) -> p n k", k=32)
                m = g.dq_s[:, 0:nblk, 0]
                s = g.dq_s[:, 0:nblk, 1]
                r = g.dq_s[:, 0:nblk, 2]
                v.tensor_reduce(out=m, in_=w3, axis=AX.X, op=ALU.max,
                                apply_absolute_value=True)
                v.drain()
                v.tensor_scalar(out=s, in0=m, scalar1=1.0 / WQMAX, scalar2=1e-12,
                                op0=ALU.mult, op1=ALU.max)
                v.drain()
                v.reciprocal(out=r, in_=s)
                v.drain()
                v.tensor_tensor(out=w3, in0=w3,
                                in1=g.dq_s.ap()[:, 0:nblk, 2:3].broadcast_to([128, nblk, 32]),
                                op=ALU.mult)
                v.drain()
                v.tensor_scalar(out=w, in0=w, scalar1=MAGIC, scalar2=MAGIC,
                                op0=ALU.add, op1=ALU.subtract)
                v.drain()
                g.s_ve.inc(v.tensor_tensor(
                    out=v_wqb(g)[:, 0:I].rearrange("p (n k) -> p n k", k=32),
                    in0=w3,
                    in1=g.dq_s.ap()[:, 0:nblk, 1:2].broadcast_to([128, nblk, 32]),
                    op=ALU.mult))
            blk.vector(ve)
            vd = g.s_ve.v

            for j in range(I // 128):
                def tn(t, j=j, vd=vd, gprev=g.s_ge.v, sprev=g.s_se.v):
                    t.wait_ge(g.s_ve.h, vd)
                    t.wait_ge(g.s_ge.h, gprev)
                    t.wait_ge(g.s_se.h, sprev)
                    g.s_tn.inc(t.transpose(g.p_tr[:],
                                           v_wqb(g)[:, j * 128:(j + 1) * 128],
                                           g.ident_s[:]))
                blk.tensor(tn)
                td = g.s_tn.v

                def se(e, td=td, stv=g.s_st.v):
                    e.wait_ge(g.s_tn.h, td)
                    e.wait_ge(g.s_st.h, stv)
                    g.s_se.inc(e.copy(out=g.stb_s.ap().rearrange("p a b -> p (a b)").bitcast(BF16)[:, 0:128],
                                      in_=g.p_tr[:]))
                blk.scalar(se)
                sd = g.s_se.v

                def gp(p, dstT=dstT, i=i, j=j, sd=sd):
                    p.wait_ge(g.s_se.h, sd)
                    p.wait_ge(g.s_st.h, g.s_st.v)
                    g.s_st.dma(p.dma_start(
                        out=dstT.ap()[j * 128:(j + 1) * 128, i * 128:(i + 1) * 128],
                        in_=g.stb_s.ap().rearrange("p a b -> p (a b)").bitcast(BF16)[:, 0:128]))
                blk.gpsimd(gp)


def p1_lnq(g, src):
    blk = g.block
    barrier_all(g)
    if src == "x":
        for c, (o, n) in enumerate(CH5):
            def q(qq, o=o, n=n, vprev=g.s_ve.v):
                qq.wait_ge(g.s_ve.h, vprev)
                qq.wait_ge(g.s_ld.h, g.s_ld.v)
                g.s_ld.dma(qq.dma_start(out=g.fc8[:, :, 0:n],
                                        in_=hbmT(g.xT)[:, :, o:o + n]))
            blk.sync(q)
            ld = g.s_ld.v

            def ve0(v, ld=ld, stv=g.s_st.v):
                v.wait_ge(g.s_ld.h, ld)
                v.wait_ge(g.s_st.h, stv)
                g.s_ve.inc(v.drain())
            blk.vector(ve0)
            ln_chunk(g, g.fc8.ap()[:, :, 0:n], v_lnout(g, n), n)
            absmax_acc(g, v_lnout(g, n), SC_X, axis=AX.XY)
            vd = g.s_ve.v

            def gp(p, o=o, n=n, vd=vd):
                p.wait_ge(g.s_ve.h, vd)
                p.wait_ge(g.s_st.h, g.s_st.v)
                g.s_st.dma(p.dma_start(out=hbmT(g.xln_sp)[:, :, o:o + n],
                                       in_=v_lnout(g, n)[:, :, 0:n]))
            blk.gpsimd(gp)
        scale_exchange(g, SC_X, QMAX)
        slot, lnsrc, chunks, qbv = SC_X, g.xln_sp, CH5, v_qb(g)
    else:
        scale_exchange(g, SC_X2, QMAX)
        slot, lnsrc, chunks, qbv = SC_X2, g.x2ln_sp, CH4, v_qb2(g)

    for c, (o, n) in enumerate(chunks):
        def q(qq, o=o, n=n, lnsrc=lnsrc, vprev=g.s_ve.v, stv=g.s_st.v):
            qq.wait_ge(g.s_ve.h, vprev)
            qq.wait_ge(g.s_st.h, stv)
            qq.wait_ge(g.s_ld.h, g.s_ld.v)
            g.s_ld.dma(qq.dma_start(out=g.fc8[:, :, 0:n],
                                    in_=hbmT(lnsrc)[:, :, o:o + n]))
        blk.sync(q)
        ld = g.s_ld.v

        def ve(v, o=o, n=n, ld=ld, slot=slot, qbv=qbv):
            v.wait_ge(g.s_ld.h, ld)
            v.tensor_scalar(out=g.fc8[:, :, 0:n], in0=g.fc8[:, :, 0:n],
                            scalar1=g.sc_s[:, slot:slot + 1], scalar2=MAGIC,
                            op0=ALU.mult, op1=ALU.add)
            v.drain()
            v.tensor_scalar(out=qbv[:, :, o:o + n], in0=g.fc8[:, :, 0:n],
                            scalar1=MAGIC, scalar2=g.ss_s[:, slot:slot + 1],
                            op0=ALU.subtract, op1=ALU.mult)
            g.s_ve.inc(v.drain())
        blk.vector(ve)


def ffn_mid(g, ver):
    blk = g.block
    barrier_all(g)
    if ver == 1:
        chunks, qbv = CH5, v_qb(g)
        s_up, s_gate, s_gs, s_t5 = SC_UP, SC_GATE, SC_GS, SC_T5
    else:
        chunks, qbv = CH4, v_qb2(g)
        s_up, s_gate, s_gs, s_t5 = SC_UP2, SC_GATE2, SC_GS2, SC_T52

    for wt, spill, slot in ((g.wq_upT, g.up_sp, s_up),
                            (g.wq_gT, g.gate_sp, s_gate)):
        for m in range(NMT):
            bi = m % 2

            def q(qq, wt=wt, m=m, bi=bi, tprev=g.s_tn.v):
                qq.wait_ge(g.s_tn.h, tprev)
                qq.wait_ge(g.s_ld.h, g.s_ld.v)
                g.s_ld.dma(qq.dma_start(
                    out=g.wstr_s[:, bi, 0:NFT, :],
                    in_=wt.ap().rearrange("(k p) f -> p k f", p=128)[:, :, m * 128:(m + 1) * 128]))
            blk.sync(q)
            ld = g.s_ld.v

            for ci, (o, n) in enumerate(chunks):
                pb = ci % 2

                def tn(t, bi=bi, o=o, n=n, pb=pb, ld=ld, sprev=g.s_se.v,
                       qbv=qbv):
                    t.wait_ge(g.s_ld.h, ld)
                    t.wait_ge(g.s_se.h, sprev)
                    for k in range(NFT):
                        li = t.matmul(g.p_mm[:, pb, 0:n], g.wstr_s[:, bi, k, :],
                                 qbv[:, k, o:o + n], start=(k == 0),
                                 stop=(k == NFT - 1))
                    g.s_tn.inc(li)
                blk.tensor(tn)
                td = g.s_tn.v

                def se(e, pb=pb, ci=ci, n=n, td=td, stv=g.s_st.v):
                    e.wait_ge(g.s_tn.h, td)
                    e.wait_ge(g.s_st.h, stv)
                    g.s_se.inc(e.copy(out=g.stb_s[:, ci % 2, 0:n],
                                      in_=g.p_mm[:, pb, 0:n]))
                blk.scalar(se)
                sd = g.s_se.v

                absmax_acc(g, g.stb_s[:, ci % 2, 0:n], slot, wait_se=sd)
                vd = g.s_ve.v

                def gp(p, spill=spill, m=m, o=o, n=n, ci=ci, vd=vd):
                    p.wait_ge(g.s_ve.h, vd)
                    p.wait_ge(g.s_st.h, g.s_st.v)
                    g.s_st.dma(p.dma_start(
                        out=spill.ap()[m * 128:(m + 1) * 128, o:o + n],
                        in_=g.stb_s[:, ci % 2, 0:n]))
                blk.gpsimd(gp)

    scale_exchange(g, s_up, QMAX)
    scale_exchange(g, s_gate, QMAX)

    va_hist = {-2: g.s_ve.v, -1: g.s_ve.v}
    for m in range(NMT):
        for ci, (o, n) in enumerate(chunks):
            it = m * len(chunks) + ci
            sl = it % 2

            def q(qq, m=m, o=o, n=n, sl=sl, vprev=va_hist.get(it - 2, 0)):
                qq.wait_ge(g.s_ve.h, vprev)
                qq.wait_ge(g.s_lx[sl].h, g.s_lx[sl].v)
                g.s_lx[sl].dma(qq.dma_start(
                    out=g.g16_s[:, sl, 0:n],
                    in_=g.gate_sp.ap()[m * 128:(m + 1) * 128, o:o + n]))
            blk.sync(q)
            ld = g.s_lx[sl].v

            def ve(v, sl=sl, n=n, ld=ld, s_gate=s_gate):
                v.wait_ge(g.s_lx[sl].h, ld)
                gq = g.sg_s[:, 0:n]
                v.tensor_scalar(out=gq, in0=g.g16_s[:, sl, 0:n],
                                scalar1=g.sc_s[:, s_gate:s_gate + 1],
                                scalar2=MAGIC, op0=ALU.mult, op1=ALU.add)
                v.drain()
                v.tensor_scalar(out=gq, in0=gq, scalar1=MAGIC,
                                scalar2=g.ss_s[:, s_gate:s_gate + 1],
                                op0=ALU.subtract, op1=ALU.mult)
                g.s_ve.inc(v.drain())
            blk.vector(ve)
            vd = g.s_ve.v

            def se(e, n=n, vd=vd):
                e.wait_ge(g.s_ve.h, vd)
                g.s_se.inc(e.activation(out=g.sg2_s[:, 0:n], in_=g.sg_s[:, 0:n],
                                        func=AF.Sigmoid, bias=0.0, scale=1.0))
            blk.scalar(se)
            sd = g.s_se.v

            def ve2(v, n=n, sl=sl, sd=sd, stv=g.s_sw[sl].v):
                v.wait_ge(g.s_se.h, sd)
                v.wait_ge(g.s_sw[sl].h, stv)
                sig = g.sg2_s[:, 0:n]
                v.tensor_scalar(out=sig, in0=sig, scalar1=65536.0, scalar2=MAGIC,
                                op0=ALU.mult, op1=ALU.add)
                v.drain()
                v.tensor_scalar(out=sig, in0=sig, scalar1=MAGIC, scalar2=65535.0,
                                op0=ALU.subtract, op1=ALU.min)
                v.drain()
                v.scalar_tensor_tensor(out=g.h16_s[:, sl, 0:n], in0=sig,
                                       scalar=1.0 / 65536.0, in1=g.sg_s[:, 0:n],
                                       op0=ALU.mult, op1=ALU.mult)
                g.s_ve.inc(v.drain())
            blk.vector(ve2)
            absmax_acc(g, g.h16_s[:, sl, 0:n], s_gs)
            vd2 = g.s_ve.v

            def gp(p, m=m, o=o, n=n, sl=sl, vd2=vd2):
                p.wait_ge(g.s_ve.h, vd2)
                g.s_sw[sl].dma(p.dma_start(
                    out=g.gs_sp.ap()[m * 128:(m + 1) * 128, o:o + n],
                    in_=g.h16_s[:, sl, 0:n]))
            blk.gpsimd(gp)
            va_hist[it] = g.s_ve.v

    scale_exchange(g, s_gs, QMAX)

    vb_hist = {-2: g.s_ve.v, -1: g.s_ve.v}
    for m in range(NMT):
        for ci, (o, n) in enumerate(chunks):
            it = m * len(chunks) + ci
            sl = it % 2

            def q(qq, m=m, o=o, n=n, sl=sl, vprev=vb_hist.get(it - 2, 0),
                  swv=g.s_sw[sl].v):
                qq.wait_ge(g.s_ve.h, vprev)
                qq.wait_ge(g.s_sw[sl].h, swv)
                qq.wait_ge(g.s_lx[sl].h, g.s_lx[sl].v)
                g.s_lx[sl].dma(qq.dma_start(
                    out=g.g16_s[:, sl, 0:n],
                    in_=g.gs_sp.ap()[m * 128:(m + 1) * 128, o:o + n]))
                qq.wait_ge(g.s_lx[sl].h, g.s_lx[sl].v)
                g.s_lx[sl].dma(qq.dma_start(
                    out=g.h16_s[:, sl, 0:n],
                    in_=g.up_sp.ap()[m * 128:(m + 1) * 128, o:o + n]))
            blk.sync(q)
            ld = g.s_lx[sl].v

            def ve(v, n=n, sl=sl, ld=ld, s_gs=s_gs, s_up=s_up,
                   stv=g.s_sw[sl].v):
                v.wait_ge(g.s_lx[sl].h, ld)
                v.wait_ge(g.s_sw[sl].h, stv)
                g2 = g.sg_s[:, 0:n]
                uq = g.sg2_s[:, 0:n]
                v.tensor_scalar(out=g2, in0=g.g16_s[:, sl, 0:n],
                                scalar1=g.sc_s[:, s_gs:s_gs + 1], scalar2=MAGIC,
                                op0=ALU.mult, op1=ALU.add)
                v.tensor_scalar(out=uq, in0=g.h16_s[:, sl, 0:n],
                                scalar1=g.sc_s[:, s_up:s_up + 1], scalar2=MAGIC,
                                op0=ALU.mult, op1=ALU.add)
                v.drain()
                v.tensor_scalar(out=g2, in0=g2, scalar1=MAGIC,
                                scalar2=g.ss_s[:, s_gs:s_gs + 1],
                                op0=ALU.subtract, op1=ALU.mult)
                v.tensor_scalar(out=uq, in0=uq, scalar1=MAGIC,
                                scalar2=g.ss_s[:, s_up:s_up + 1],
                                op0=ALU.subtract, op1=ALU.mult)
                v.drain()
                v.tensor_mul(out=g.g16_s[:, sl, 0:n], in0=g2, in1=uq)
                g.s_ve.inc(v.drain())
            blk.vector(ve)
            absmax_acc(g, g.g16_s[:, sl, 0:n], s_t5)
            vd = g.s_ve.v

            def gp(p, m=m, o=o, n=n, sl=sl, vd=vd):
                p.wait_ge(g.s_ve.h, vd)
                g.s_sw[sl].dma(p.dma_start(
                    out=g.t5_sp.ap()[m * 128:(m + 1) * 128, o:o + n],
                    in_=g.g16_s[:, sl, 0:n]))
            blk.gpsimd(gp)
            vb_hist[it] = g.s_ve.v

    scale_exchange(g, s_t5, QMAX)

    oq = v_oq(g)
    vc_hist = {-2: g.s_ve.v, -1: g.s_ve.v}
    for ci, (o, n) in enumerate(chunks):
        for m in range(NMT):
            sl = m % 2
            it = ci * NMT + m

            def q(qq, m=m, o=o, n=n, sl=sl, vprev=vc_hist.get(it - 2, 0)):
                qq.wait_ge(g.s_ve.h, vprev)
                qq.wait_ge(g.s_lx[sl].h, g.s_lx[sl].v)
                g.s_lx[sl].dma(qq.dma_start(
                    out=g.g16_s[:, sl, 0:n],
                    in_=g.t5_sp.ap()[m * 128:(m + 1) * 128, o:o + n]))
            blk.sync(q)
            ld = g.s_lx[sl].v

            def ve(v, m=m, n=n, sl=sl, ld=ld, s_t5=s_t5, oq=oq, tprev=g.s_tn.v):
                v.wait_ge(g.s_lx[sl].h, ld)
                v.wait_ge(g.s_tn.h, tprev)
                v.tensor_scalar(out=g.sg_s[:, 0:n], in0=g.g16_s[:, sl, 0:n],
                                scalar1=g.sc_s[:, s_t5:s_t5 + 1], scalar2=MAGIC,
                                op0=ALU.mult, op1=ALU.add)
                v.drain()
                v.tensor_scalar(out=oq[:, m, 0:n], in0=g.sg_s[:, 0:n],
                                scalar1=MAGIC, scalar2=g.ss_s[:, s_t5:s_t5 + 1],
                                op0=ALU.subtract, op1=ALU.mult)
                g.s_ve.inc(v.drain())
            blk.vector(ve)
            vc_hist[it] = g.s_ve.v
        vd_oq = g.s_ve.v

        for ft in range(NFT):
            for half in range(2):
                def q2(qq, ft=ft, half=half, tprev=g.s_tn.v):
                    qq.wait_ge(g.s_tn.h, tprev)
                    qq.wait_ge(g.s_ld.h, g.s_ld.v)
                    g.s_ld.dma(qq.dma_start(
                        out=g.wstr_s[:, half, 0:12, :],
                        in_=g.wq_dT.ap().rearrange("(k p) f -> p k f", p=128)[:, 12 * half:12 * (half + 1), ft * 128:(ft + 1) * 128]))
                blk.sync(q2)
            ld2 = g.s_ld.v

            def tn(t, ft=ft, n=n, vd_oq=vd_oq, ld2=ld2, sprev=g.s_se.v, oq=oq):
                t.wait_ge(g.s_ve.h, vd_oq)
                t.wait_ge(g.s_ld.h, ld2)
                t.wait_ge(g.s_se.h, sprev)
                for k in range(NMT):
                    li = t.matmul(g.p_mm2[:, ft % 2, 0:n],
                             g.wstr_s[:, k // 12, k % 12, :],
                             oq[:, k, 0:n], start=(k == 0), stop=(k == NMT - 1))
                g.s_tn.inc(li)
            blk.tensor(tn)
            td = g.s_tn.v

            if ver == 1:
                def q3(qq, ft=ft, o=o, n=n, vprev=g.s_ve.v, stv=g.s_st.v):
                    qq.wait_ge(g.s_ve.h, vprev)
                    qq.wait_ge(g.s_st.h, stv)
                    qq.wait_ge(g.s_ld.h, g.s_ld.v)
                    g.s_ld.dma(qq.dma_start(out=g.fc8[:, ft, 0:n],
                                            in_=hbmT(g.xT)[:, ft, o:o + n]))
                blk.sync(q3)
                ld3 = g.s_ld.v

                def ve2(v, ft=ft, n=n, td=td, ld3=ld3):
                    v.wait_ge(g.s_tn.h, td)
                    v.wait_ge(g.s_ld.h, ld3)
                    v.scalar_tensor_tensor(out=g.fc8[:, ft, 0:n],
                                           in0=g.p_mm2[:, ft % 2, 0:n],
                                           scalar=0.5, in1=g.fc8[:, ft, 0:n],
                                           op0=ALU.mult, op1=ALU.add)
                    g.s_ve.inc(v.drain())
                blk.vector(ve2)
                vd2 = g.s_ve.v

                def se2(e, vd2=vd2):
                    e.wait_ge(g.s_ve.h, vd2)
                    e.sem_inc(g.s_se.h, 1); g.s_se.v += 1
                blk.scalar(se2)

                def gp(p, ft=ft, o=o, n=n, vd2=vd2):
                    p.wait_ge(g.s_ve.h, vd2)
                    p.wait_ge(g.s_st.h, g.s_st.v)
                    g.s_st.dma(p.dma_start(out=hbmT(g.x1_sp)[:, ft, o:o + n],
                                           in_=g.fc8[:, ft, 0:n]))
                blk.gpsimd(gp)
            else:
                def q3(qq, ft=ft, o=o, n=n, vprev=g.s_ve.v, stv=g.s_st.v):
                    qq.wait_ge(g.s_ve.h, vprev)
                    qq.wait_ge(g.s_st.h, stv)
                    qq.wait_ge(g.s_ld.h, g.s_ld.v)
                    g.s_ld.dma(qq.dma_start(out=g.fc8[:, ft, 0:n],
                                            in_=hbmT(g.x2_sp)[:, ft, o:o + n]))
                blk.sync(q3)
                ld3 = g.s_ld.v

                def ve2(v, ft=ft, n=n, td=td, ld3=ld3, ov=g.s_out.v):
                    v.wait_ge(g.s_tn.h, td)
                    v.wait_ge(g.s_ld.h, ld3)
                    v.wait_ge(g.s_out.h, ov)
                    v.scalar_tensor_tensor(out=g.st_s[:, ft % 2, 0:n],
                                           in0=g.p_mm2[:, ft % 2, 0:n],
                                           scalar=0.5, in1=g.fc8[:, ft, 0:n],
                                           op0=ALU.mult, op1=ALU.add)
                    g.s_ve.inc(v.drain())
                blk.vector(ve2)
                vd2 = g.s_ve.v

                def se2(e, vd2=vd2):
                    e.wait_ge(g.s_ve.h, vd2)
                    e.sem_inc(g.s_se.h, 1); g.s_se.v += 1
                blk.scalar(se2)

                def gp(p, ft=ft, o=o, n=n, vd2=vd2):
                    p.wait_ge(g.s_ve.h, vd2)
                    p.wait_ge(g.s_out.h, g.s_out.v)
                    g.s_out.dma(p.dma_start(
                        out=g.out_h[ft * 128:(ft + 1) * 128, o:o + n],
                        in_=g.st_s[:, ft % 2, 0:n]))
                blk.gpsimd(gp)

    if ver == 1:
        barrier_all(g)
        for c, (o, n) in enumerate(CH5):
            def q(qq, o=o, n=n, vprev=g.s_ve.v, stv=g.s_st.v):
                qq.wait_ge(g.s_ve.h, vprev)
                qq.wait_ge(g.s_st.h, stv)
                qq.wait_ge(g.s_ld.h, g.s_ld.v)
                g.s_ld.dma(qq.dma_start(out=g.fc8[:, :, 0:n],
                                        in_=hbmT(g.x1_sp)[:, :, o:o + n]))
            blk.sync(q)
            ld = g.s_ld.v

            def ve0(v, ld=ld, stv=g.s_st.v):
                v.wait_ge(g.s_ld.h, ld)
                v.wait_ge(g.s_st.h, stv)
                g.s_ve.inc(v.drain())
            blk.vector(ve0)
            ln_chunk(g, g.fc8.ap()[:, :, 0:n], v_lnout(g, n), n)
            vd = g.s_ve.v

            def ve(v, n=n, vd=vd):
                v.wait_ge(g.s_ve.h, vd)
                for ft in range(NFT):
                    v.tensor_copy(out=v_lnb16(g)[:, ft, 0:n],
                                  in_=v_lnout(g, n)[:, ft])
                v.drain()
                for ft in range(NFT):
                    v.tensor_copy(out=v_lnb16R(g)[:, ft, 0:n],
                                  in_=v_lnout(g, n)[:, ft][:, ::-1])
                g.s_ve.inc(v.drain())
            blk.vector(ve)
            vd2 = g.s_ve.v

            def gp(p, o=o, n=n, vd2=vd2):
                p.wait_ge(g.s_ve.h, vd2)
                p.wait_ge(g.s_st.h, g.s_st.v)
                g.s_st.dma(p.dma_start(out=hbmT(g.x1ln_sp)[:, :, o:o + n],
                                       in_=v_lnout(g, n)[:, :, 0:n]))
                p.wait_ge(g.s_st.h, g.s_st.v)
                g.s_st.dma(p.dma_start(out=hbmT(g.x1lnb_sp)[:, :, o:o + n],
                                       in_=v_lnb16(g)[:, :, 0:n]))
                p.wait_ge(g.s_st.h, g.s_st.v)
                g.s_st.dma(p.dma_start(
                    out=hbmT(g.x1lnbR_sp)[:, :, TOKH - o - n:TOKH - o],
                    in_=v_lnb16R(g)[:, :, 0:n]))
            blk.gpsimd(gp)


def p4_mixer(g):
    blk = g.block
    barrier_all(g)
    for di in range(2):
        mixer_direction(g, di)
        if di == 0:
            carry_exchange(g)


def carry_exchange(g):
    blk = g.block
    vd, sd = g.s_ve.v, g.s_se.v

    def gp(p, vd=vd, sd=sd):
        p.wait_ge(g.s_ve.h, vd)
        p.wait_ge(g.s_se.h, sd)
        p.wait_ge(g.s_st.h, g.s_st.v)
        g.s_st.dma(p.dma_start(out=g.carry_io.ap(), in_=g.carry_s[:, 0]))
        p.wait_ge(g.s_st.h, g.s_st.v)
        g.s_cc.inc(p.collective_compute(
            "AllReduce", ALU.add,
            replica_groups=[[0, 1], [2, 3], [4, 5], [6, 7]],
            ins=[g.carry_io.ap().opt()], outs=[g.carry_sum.ap().opt()]))
        p.wait_ge(g.s_cc.h, g.s_cc.v)
        p.wait_ge(g.s_st.h, g.s_st.v)
        g.s_st.dma(p.dma_start(out=g.pcar_s[:, :], in_=g.carry_sum.ap()))
    blk.gpsimd(gp)
    st = g.s_st.v

    def ve(v, st=st):
        v.wait_ge(g.s_st.h, st)
        v.tensor_sub(out=g.carry_s[:, 1], in0=g.pcar_s[:, :], in1=g.carry_s[:, 0])
        g.s_ve.inc(v.drain())
    blk.vector(ve)


def mixer_direction(g, di):
    blk = g.block
    xsrc = g.x1lnb_sp if di == 0 else g.x1lnbR_sp
    wname = "A" if di == 0 else "B"

    def q(qq, xsrc=xsrc, vprev=g.s_ve.v):
        qq.wait_ge(g.s_ve.h, vprev)
        qq.wait_ge(g.s_ld.h, g.s_ld.v)
        g.s_ld.dma(qq.dma_start(out=v_xlc(g)[:, :, 0:HALO],
                                in_=hbmT(xsrc)[:, :, 0:HALO]))
    blk.sync(q)
    ldh = g.s_ld.v
    for dt in range(NDT):
        def q2(qq, dt=dt, wname=wname, tprev=g.s_tn.v):
            qq.wait_ge(g.s_tn.h, tprev)
            qq.wait_ge(g.s_ld.h, g.s_ld.v)
            g.s_ld.dma(qq.dma_start(
                out=g.wstr_s[:, dt % 2, 0:NFT, :],
                in_=g.wb[f"in_wT_{wname}"].ap().rearrange("(k p) f -> p k f", p=128)[:, :, dt * 128:(dt + 1) * 128]))
        blk.sync(q2)
        ld = g.s_ld.v

        def tn(t, dt=dt, ld=ld, ldh=ldh, sprev=g.s_se.v):
            t.wait_ge(g.s_ld.h, max(ld, ldh))
            t.wait_ge(g.s_se.h, sprev)
            for k in range(NFT):
                li = t.matmul(g.p_bc[:, 0:HALO], g.wstr_s[:, dt % 2, k, :],
                         v_xlc(g)[:, k, 0:HALO], start=(k == 0),
                         stop=(k == NFT - 1))
            g.s_tn.inc(li)
        blk.tensor(tn)
        td = g.s_tn.v

        def se(e, dt=dt, td=td):
            e.wait_ge(g.s_tn.h, td)
            g.s_se.inc(e.copy(out=g.xicar_s[:, dt, :], in_=g.p_bc[:, 0:HALO]))
        blk.scalar(se)

    for c in range(NCH):
        mixer_chunk(g, di, c, wname)


def mixer_chunk(g, di, c, wname):
    blk = g.block
    o = HALO + c * TC
    xsrc = g.x1lnb_sp if di == 0 else g.x1lnbR_sp
    corig = c if di == 0 else NCH - 1 - c
    oorig = HALO + corig * TC
    carry = g.carry_s.ap()[:, di]
    xi = v_xi(g)
    u = v_u(g)
    du = v_du(g)

    def vgate(v, stv=g.s_st.v):
        v.wait_ge(g.s_st.h, stv)
        g.s_ve.inc(v.drain())
    blk.vector(vgate)

    def sgate(e, stv=g.s_st.v):
        e.wait_ge(g.s_st.h, stv)
        e.sem_inc(g.s_se.h, 1)
    blk.scalar(sgate)
    g.s_se.v += 1

    def q(qq, o=o, xsrc=xsrc, tprev=g.s_tn.v, sprev=g.s_se.v):
        qq.wait_ge(g.s_tn.h, tprev)
        qq.wait_ge(g.s_se.h, sprev)
        qq.wait_ge(g.s_ld.h, g.s_ld.v)
        g.s_ld.dma(qq.dma_start(out=v_xlc(g)[:, :, 0:TC],
                                in_=hbmT(xsrc)[:, :, o:o + TC]))
    blk.sync(q)
    ld_x = g.s_ld.v

    tn_hist = {-2: g.s_tn.v, -1: g.s_tn.v}
    for m in range(2 * NDT):
        def q2(qq, m=m, wname=wname, tprev=tn_hist.get(m - 2, 0)):
            qq.wait_ge(g.s_tn.h, tprev)
            qq.wait_ge(g.s_lw[m % 2].h, g.s_lw[m % 2].v)
            g.s_lw[m % 2].dma(qq.dma_start(
                out=g.wstr_s[:, m % 2, 0:NFT, :],
                in_=g.wb[f"in_wT_{wname}"].ap().rearrange("(k p) f -> p k f", p=128)[:, :, m * 128:(m + 1) * 128]))
        blk.sync(q2)
        ld = g.s_lw[m % 2].v

        def tn(t, m=m, ld=ld, ld_x=ld_x, sprev=(g.s_se.v if m == 0 else g.s_se.v - 1),
               vprev=g.s_ve.v, stprev=g.s_st.v):
            t.wait_ge(g.s_lw[m % 2].h, ld)
            t.wait_ge(g.s_ld.h, ld_x)
            t.wait_ge(g.s_se.h, max(sprev, 0))
            t.wait_ge(g.s_ve.h, vprev)
            t.wait_ge(g.s_st.h, stprev)
            for k in range(NFT):
                li = t.matmul(g.p_mm[:, m % 2, :], g.wstr_s[:, m % 2, k, :],
                         v_xlc(g)[:, k, 0:TC], start=(k == 0), stop=(k == NFT - 1))
            g.s_tn.inc(li)
        blk.tensor(tn)
        td = g.s_tn.v

        if m < NDT:
            def se(e, m=m, td=td):
                e.wait_ge(g.s_tn.h, td)
                g.s_se.inc(e.copy(out=xi[:, m, HALO:HALO + TC],
                                  in_=g.p_mm[:, m % 2, :]))
            blk.scalar(se)
        else:
            def se(e, m=m, td=td, stprev=g.s_sw[m % 2].v):
                e.wait_ge(g.s_tn.h, td)
                e.wait_ge(g.s_sw[m % 2].h, stprev)
                g.s_se.inc(e.copy(out=g.st_s[:, m % 2, :],
                                  in_=g.p_mm[:, m % 2, :]))
            blk.scalar(se)
            sd = g.s_se.v

            def gp(p, m=m, corig=corig, sd=sd):
                p.wait_ge(g.s_se.h, sd)
                g.s_sw[m % 2].dma(p.dma_start(
                    out=g.z_sp.ap()[(m - NDT) * 128:(m - NDT + 1) * 128,
                                    corig * TC:(corig + 1) * TC],
                    in_=g.st_s[:, m % 2, :]))
            blk.gpsimd(gp)
        tn_hist[m] = g.s_tn.v

    sd_xi = g.s_se.v

    def ve_h(v, sd_xi=sd_xi):
        v.wait_ge(g.s_se.h, sd_xi)
        v.tensor_copy(out=xi[:, :, 0:HALO], in_=g.xicar_s[:, :, :])
        g.s_ve.inc(v.drain())
    blk.vector(ve_h)

    for dt in range(NDT):
        def ve(v, dt=dt, di=di):
            v.tensor_scalar_mul(out=u[:, dt], in0=xi[:, dt, 0:TC],
                                scalar1=g.cw_s[:, di, dt, 0:1])
            v.drain()
            for k in range(1, 4):
                v.scalar_tensor_tensor(out=u[:, dt], in0=xi[:, dt, k:k + TC],
                                       scalar=g.cw_s[:, di, dt, k:k + 1],
                                       in1=u[:, dt], op0=ALU.mult, op1=ALU.add)
                v.drain()
            v.sem_inc(g.s_ve.h, 1); g.s_ve.v += 1
        blk.vector(ve)

    def ve_t(v):
        v.tensor_copy(out=g.xicar_s[:, :, :], in_=xi[:, :, TC:TC + HALO])
        g.s_ve.inc(v.drain())
    blk.vector(ve_t)
    ve_tail = g.s_ve.v

    def se_u(e, ve_tail=ve_tail, di=di):
        e.wait_ge(g.s_ve.h, ve_tail)
        for dt in range(NDT):
            e.activation(out=u[:, dt], in_=u[:, dt], func=AF.Silu,
                         bias=g.cb_s[:, di, dt:dt + 1], scale=1.0)
        g.s_se.inc(e.drain())
    blk.scalar(se_u)
    se_silu = g.s_se.v

    for dt in range(NDT):
        def ve(v, dt=dt, se_silu=se_silu, tprev=g.s_tn.v):
            v.wait_ge(g.s_se.h, se_silu)
            v.wait_ge(g.s_tn.h, tprev)
            g.s_ve.inc(v.tensor_copy(out=g.ubt_s[:, 0, :], in_=u[:, dt]))
        blk.vector(ve)
        vd = g.s_ve.v

        def tn(t, dt=dt, vd=vd, di=di, sprev=g.s_se.v):
            t.wait_ge(g.s_ve.h, vd)
            if dt == 0:
                t.wait_ge(g.s_se.h, sprev)
            li = t.matmul(g.p_xd[:, :], g.xprojT_s[:, di, dt, :],
                     g.ubt_s[:, 0, :],
                     start=(dt == 0), stop=(dt == NDT - 1))
            g.s_tn.inc(li)
        blk.tensor(tn)
    td_xd = g.s_tn.v

    def se_xd(e, td_xd=td_xd):
        e.wait_ge(g.s_tn.h, td_xd)
        e.copy(out=g.dtb16_s[:, :], in_=g.p_xd[0:DTR, :])
        g.s_se.inc(e.copy(out=g.xdb_s[0:32, 0, :], in_=g.p_xd[DTR:96, :]))
    blk.scalar(se_xd)
    sd_xd = g.s_se.v

    for dt in range(NDT):
        def tn(t, dt=dt, sd_xd=sd_xd, di=di, sprev=g.s_se.v):
            t.wait_ge(g.s_se.h, max(sd_xd, sprev if dt >= 2 else 0))
            li = t.matmul(g.p_mm[:, dt % 2, :], g.dtwT_s[:, di, dt, :],
                     g.dtb16_s[:, :], start=True, stop=True)
            g.s_tn.inc(li)
        blk.tensor(tn)
        td = g.s_tn.v

        def se(e, dt=dt, td=td, di=di):
            e.wait_ge(g.s_tn.h, td)
            e.activation(out=du[:, dt], in_=g.p_mm[:, dt % 2, :], func=AF.Exp,
                         bias=g.dtb_s[:, di, dt:dt + 1], scale=1.0)
            e.drain()
            g.s_se.inc(e.activation(out=du[:, dt], in_=du[:, dt], func=AF.Ln,
                                    bias=g.onescol_s[:, :], scale=1.0))
        blk.scalar(se)
    sd_delta = g.s_se.v

    ve_duv = g.s_ve.v

    def ve_bc(v, sd_xd=sd_xd):
        v.wait_ge(g.s_se.h, sd_xd)
        v.tensor_copy(out=g.bclo_s[:, 0, :], in_=g.xdb_s[0:32, 0, :])
        v.drain()
        v.tensor_sub(out=g.xdb_s[0:32, 1, :], in0=g.xdb_s[0:32, 0, :],
                     in1=g.bclo_s[:, 0, :])
        v.drain()
        v.tensor_copy(out=g.bclo_s[:, 1, :], in_=g.xdb_s[0:32, 1, :])
        g.s_ve.inc(v.drain())
    blk.vector(ve_bc)
    ve_bcv = g.s_ve.v

    hall = v_hall(g)
    crep = v_crep(g)
    brep = v_brep(g)
    for gi in range(2):
        for n in range(8):
            def tn(t, gi=gi, n=n, ve_bcv=ve_bcv, sprev=g.s_se.v):
                t.wait_ge(g.s_ve.h, ve_bcv)
                t.wait_ge(g.s_se.h, sprev)
                sn = gi * 8 + n
                li = t.matmul(g.p_bc[:, :], g.sel_s[:, sn, :], g.bclo_s[:, 0, :],
                         start=True, stop=False)
                li = t.matmul(g.p_bc[:, :], g.sel_s[:, sn, :], g.bclo_s[:, 1, :],
                              start=False, stop=True)
                g.s_tn.inc(li)
                li = t.matmul(g.p_bc2[:, :], g.sel_s[:, 16 + sn, :], g.bclo_s[:, 0, :],
                         start=True, stop=False)
                li = t.matmul(g.p_bc2[:, :], g.sel_s[:, 16 + sn, :],
                              g.bclo_s[:, 1, :], start=False, stop=True)
                g.s_tn.inc(li)
            blk.tensor(tn)
            td = g.s_tn.v

            def se(e, n=n, td=td, vprev=g.s_ve.v):
                e.wait_ge(g.s_tn.h, td)
                e.wait_ge(g.s_ve.h, vprev)
                e.copy(out=brep[:, n, :], in_=g.p_bc[:, :])
                g.s_se.inc(e.copy(
                    out=crep.rearrange("p (t n) -> p t n", n=8)[:, :, n],
                    in_=g.p_bc2[:, :]))
            blk.scalar(se)
        sd_bc = g.s_se.v

        for dt in range(NDT):
            def vdup(v, dt=dt, sd_delta=sd_delta):
                v.wait_ge(g.s_se.h, sd_delta)
                v.tensor_mul(out=g.dup_s[:, :], in0=v_du(g)[:, dt],
                             in1=v_u(g)[:, dt])
                g.s_ve.inc(v.drain())
            blk.vector(vdup)

            for n in range(8):
                sn = gi * 8 + n
                slot = n % 2

                def se(e, dt=dt, sn=sn, slot=slot, sd_bc=sd_bc, di=di,
                       vprev=g.s_ve.v):
                    e.wait_ge(g.s_se.h, sd_bc)
                    e.wait_ge(g.s_ve.h, vprev)
                    g.s_se.inc(e.activation(
                        out=g.da_s[:, slot, :], in_=v_du(g)[:, dt],
                        func=AF.Exp, bias=0.0,
                        scale=g.A_s[:, di, dt * NS + sn:dt * NS + sn + 1]))
                blk.scalar(se)
                sd = g.s_se.v

                def ve(v, dt=dt, n=n, sn=sn, slot=slot, sd=sd, carry=carry,
                       ve_duv=ve_duv):
                    v.wait_ge(g.s_se.h, sd)
                    v.wait_ge(g.s_ve.h, ve_duv)
                    v.tensor_mul(out=g.w_sc[:, 0, :], in0=g.dup_s[:, :],
                                 in1=brep[:, n, :])
                    v.drain()
                    v.tensor_tensor_scan(
                        out=hall.rearrange("p (t n) -> p t n", n=8)[:, :, n],
                        data0=g.da_s[:, slot, :], data1=g.w_sc[:, 0, :],
                        initial=carry[:, dt * NS + sn:dt * NS + sn + 1],
                        op0=ALU.mult, op1=ALU.add)
                    g.s_ve.inc(v.drain())
                blk.vector(ve)

                if di == 0 and c == 0 and hasattr(g, "dbg_daw") and dt in (0, 1, 8):
                    def gpw(p, dt=dt, sn=sn, slot=slot, n=n, vprev=g.s_ve.v):
                        p.wait_ge(g.s_ve.h, vprev)
                        p.wait_ge(g.s_st.h, g.s_st.v)
                        row = dt * NS + sn
                        g.s_st.dma(p.dma_start(
                            out=g.dbg_daw.ap()[row, 0, :][None, :],
                            in_=g.da_s[0:1, slot, :]))
                        p.wait_ge(g.s_st.h, g.s_st.v)
                        g.s_st.dma(p.dma_start(
                            out=g.dbg_daw.ap()[row, 1, :][None, :],
                            in_=g.w_sc[0:1, 0, :]))
                        p.wait_ge(g.s_st.h, g.s_st.v)
                        ctx = g.nc.allow_non_contiguous_dma(reason="debug")
                        ctx.__enter__()
                        g.s_st.dma(p.dma_start(
                            out=g.dbg_daw.ap()[row, 2, :][None, :],
                            in_=hall.rearrange("p (t n) -> p t n", n=8)[0:1, :, n]))
                        ctx.__exit__(None, None, None)
                    blk.gpsimd(gpw)
                    def vg(v, stv=g.s_st.v):
                        v.wait_ge(g.s_st.h, stv)
                    blk.vector(vg)

            def ve2(v, dt=dt, gi=gi, carry=carry, stv=g.s_st.v):
                v.wait_ge(g.s_st.h, stv)
                v.tensor_copy(out=carry[:, dt * NS + gi * 8:dt * NS + gi * 8 + 8],
                              in_=hall[:, (TC - 1) * 8:TC * 8])
                v.drain()
                v.tensor_mul(out=hall[:, :], in0=hall[:, :], in1=crep[:, :])
                v.drain()
                if gi == 0:
                    v.tensor_reduce(out=g.yt_s[:, :],
                                    in_=hall.rearrange("p (t n) -> p t n", n=8),
                                    axis=AX.X, op=ALU.add)
                    g.s_ve.inc(v.drain())
                else:
                    v.tensor_reduce(out=g.ytmp_s[:, :],
                                    in_=hall.rearrange("p (t n) -> p t n", n=8),
                                    axis=AX.X, op=ALU.add)
                    g.s_ve.inc(v.drain())
            blk.vector(ve2)

            if gi == 0:
                def gp0(p, dt=dt, corig=corig, vprev=g.s_ve.v):
                    p.wait_ge(g.s_ve.h, vprev)
                    p.wait_ge(g.s_st.h, g.s_st.v)
                    g.s_st.dma(p.dma_start(
                        out=g.y0_sp.ap()[dt * 128:(dt + 1) * 128,
                                         corig * TC:(corig + 1) * TC],
                        in_=g.yt_s[:, :]))
                blk.gpsimd(gp0)

            if gi == 1:
                def q3(qq, dt=dt, corig=corig, vprev=g.s_ve.v, stv=g.s_st.v,
                       sw0=g.s_sw[0].v, sw1=g.s_sw[1].v):
                    qq.wait_ge(g.s_ve.h, vprev)
                    qq.wait_ge(g.s_st.h, stv)
                    qq.wait_ge(g.s_sw[0].h, sw0)
                    qq.wait_ge(g.s_sw[1].h, sw1)
                    qq.wait_ge(g.s_ld.h, g.s_ld.v)
                    g.s_ld.dma(qq.dma_start(
                        out=g.zdt_s[:, dt % 2, :],
                        in_=g.z_sp.ap()[dt * 128:(dt + 1) * 128,
                                        corig * TC:(corig + 1) * TC]))
                    qq.wait_ge(g.s_ld.h, g.s_ld.v)
                    g.s_ld.dma(qq.dma_start(
                        out=g.y0t_s[:, dt % 2, :],
                        in_=g.y0_sp.ap()[dt * 128:(dt + 1) * 128,
                                         corig * TC:(corig + 1) * TC]))
                blk.sync(q3)
                ldz = g.s_ld.v

                def se2(e, dt=dt, ldz=ldz, vprev=g.s_ve.v):
                    e.wait_ge(g.s_ld.h, ldz)
                    e.wait_ge(g.s_ve.h, vprev)
                    g.s_se.inc(e.activation(out=g.zdt_s[:, dt % 2, :],
                                            in_=g.zdt_s[:, dt % 2, :],
                                            func=AF.Silu, bias=0.0, scale=1.0))
                blk.scalar(se2)
                sdz = g.s_se.v

                def ve3(v, dt=dt, di=di, sdz=sdz, ldz=ldz):
                    v.wait_ge(g.s_se.h, sdz)
                    v.wait_ge(g.s_ld.h, ldz)
                    v.tensor_add(out=g.yt_s[:, :], in0=g.ytmp_s[:, :],
                                 in1=g.y0t_s[:, dt % 2, :])
                    v.drain()
                    v.scalar_tensor_tensor(out=g.yt_s[:, :], in0=v_u(g)[:, dt],
                                           scalar=g.Dp_s[:, di, dt:dt + 1],
                                           in1=g.yt_s[:, :], op0=ALU.mult,
                                           op1=ALU.add)
                    v.drain()
                    v.tensor_mul(out=v_gb(g, dt), in0=g.yt_s[:, :],
                                 in1=g.zdt_s[:, dt % 2, :])
                    g.s_ve.inc(v.drain())
                blk.vector(ve3)

                if di == 0 and hasattr(g, "dbg_y"):
                    def gpy(p, dt=dt, corig=corig, vprev=g.s_ve.v):
                        p.wait_ge(g.s_ve.h, vprev)
                        p.wait_ge(g.s_st.h, g.s_st.v)
                        g.s_st.dma(p.dma_start(
                            out=g.dbg_y.ap()[dt * 128:(dt + 1) * 128,
                                             corig * TC:(corig + 1) * TC],
                            in_=g.yt_s[:, :]))
                    blk.gpsimd(gpy)

    ve_gball = g.s_ve.v

    if di == 0:
        for dn, view in (("dbg_u", v_u(g)), ("dbg_dl", v_du(g))):
            if hasattr(g, dn):
                def gpd(p, dn=dn, view=view, corig=corig, vprev=g.s_ve.v):
                    p.wait_ge(g.s_ve.h, vprev)
                    p.wait_ge(g.s_st.h, g.s_st.v)
                    g.s_st.dma(p.dma_start(
                        out=getattr(g, dn).ap().rearrange("(d p) t -> p d t", p=128)[:, :, corig * TC:(corig + 1) * TC],
                        in_=view))
                blk.gpsimd(gpd)

    def q4(qq, oorig=oorig, vprev=g.s_ve.v):
        qq.wait_ge(g.s_ve.h, vprev)
        qq.wait_ge(g.s_ld.h, g.s_ld.v)
        g.s_ld.dma(qq.dma_start(out=g.fc8[:, :, :],
                                in_=hbmT(g.x1ln_sp)[:, :, oorig:oorig + TC]))
    blk.sync(q4)
    ld_xl = g.s_ld.v

    for ft in range(NFT):
        def q5(qq, ft=ft, wname=wname, tprev=g.s_tn.v):
            qq.wait_ge(g.s_tn.h, tprev)
            qq.wait_ge(g.s_ld.h, g.s_ld.v)
            g.s_ld.dma(qq.dma_start(
                out=g.wstr_s[:, ft % 2, 0:12, :],
                in_=g.wb[f"out_wT_{wname}"].ap().rearrange("(k p) f -> p k f", p=128)[:, 0:12, ft * 128:(ft + 1) * 128]))
            qq.wait_ge(g.s_ld.h, g.s_ld.v)
            g.s_ld.dma(qq.dma_start(
                out=g.wstr_s[:, (ft + 1) % 2, 0:4, :],
                in_=g.wb[f"out_wT_{wname}"].ap().rearrange("(k p) f -> p k f", p=128)[:, 12:16, ft * 128:(ft + 1) * 128]))
        blk.sync(q5)
        ld = g.s_ld.v

        def tn(t, ft=ft, ld=ld, ve_gball=ve_gball, vprev=g.s_ve.v):
            t.wait_ge(g.s_ld.h, ld)
            t.wait_ge(g.s_ve.h, ve_gball)
            if ft >= 2:
                t.wait_ge(g.s_ve.h, vprev)
            for k in range(12):
                li = t.matmul(g.p_mm[:, ft % 2, :], g.wstr_s[:, ft % 2, k, :],
                         v_gb(g, k), start=(k == 0), stop=False)
            for k in range(12, 16):
                li = t.matmul(g.p_mm[:, ft % 2, :],
                         g.wstr_s[:, (ft + 1) % 2, k - 12, :],
                         v_gb(g, k), start=False, stop=(k == 15))
            g.s_tn.inc(li)
        blk.tensor(tn)
        td = g.s_tn.v

        def ve(v, ft=ft, di=di, td=td, ld_xl=ld_xl):
            v.wait_ge(g.s_tn.h, td)
            v.wait_ge(g.s_ld.h, ld_xl)
            src = g.p_mm[:, ft % 2, :] if di == 0 else g.p_mm[:, ft % 2, ::-1]
            v.scalar_tensor_tensor(out=v_mo(g)[:, ft, :], in0=src, scalar=1.0,
                                   in1=g.fc8[:, ft, :], op0=ALU.mult,
                                   op1=ALU.add)
            g.s_ve.inc(v.drain())
        blk.vector(ve)
    ve_mo = g.s_ve.v

    for ft in range(NFT):
        def q6(qq, ft=ft, wname=wname, tprev=g.s_tn.v):
            qq.wait_ge(g.s_tn.h, tprev)
            qq.wait_ge(g.s_ld.h, g.s_ld.v)
            g.s_ld.dma(qq.dma_start(
                out=g.wstr_s[:, ft % 2, 0:NFT, :],
                in_=g.wb[f"projT_{wname}"].ap().rearrange("(k p) f -> p k f", p=128)[:, :, ft * 128:(ft + 1) * 128]))
        blk.sync(q6)
        ld = g.s_ld.v

        def tn(t, ft=ft, ld=ld, ve_mo=ve_mo, sprev=g.s_se.v, vprev=g.s_ve.v):
            t.wait_ge(g.s_ld.h, ld)
            t.wait_ge(g.s_ve.h, ve_mo)
            if ft >= 2:
                t.wait_ge(g.s_se.h, sprev)
                t.wait_ge(g.s_ve.h, vprev)
            for k in range(NFT):
                li = t.matmul(g.p_mm2[:, ft % 2, :], g.wstr_s[:, ft % 2, k, :],
                         v_mo(g)[:, k, :], start=(k == 0), stop=(k == NFT - 1))
            g.s_tn.inc(li)
        blk.tensor(tn)
        td = g.s_tn.v

        if di == 0:
            def se(e, ft=ft, td=td, stprev=g.s_st.v):
                e.wait_ge(g.s_tn.h, td)
                e.wait_ge(g.s_st.h, stprev)
                g.s_se.inc(e.copy(out=g.st_s[:, ft % 2, :],
                                  in_=g.p_mm2[:, ft % 2, :]))
            blk.scalar(se)
            sd = g.s_se.v

            def gp(p, ft=ft, corig=corig, sd=sd):
                p.wait_ge(g.s_se.h, sd)
                p.wait_ge(g.s_st.h, g.s_st.v)
                g.s_st.dma(p.dma_start(
                    out=hbmT(g.projA_sp)[:, ft, corig * TC:(corig + 1) * TC],
                    in_=g.st_s[:, ft % 2, :]))
            blk.gpsimd(gp)
        else:
            def q7(qq, ft=ft, corig=corig, vprev=g.s_ve.v):
                qq.wait_ge(g.s_ve.h, vprev)
                qq.wait_ge(g.s_ld.h, g.s_ld.v)
                g.s_ld.dma(qq.dma_start(
                    out=g.st_s[:, ft % 2, :],
                    in_=hbmT(g.projA_sp)[:, ft, corig * TC:(corig + 1) * TC]))
            blk.sync(q7)
            ld2 = g.s_ld.v

            def ve(v, ft=ft, td=td, ld2=ld2):
                v.wait_ge(g.s_tn.h, td)
                v.wait_ge(g.s_ld.h, ld2)
                v.scalar_tensor_tensor(out=v_pp(g)[:, ft, :],
                                       in0=g.p_mm2[:, ft % 2, :],
                                       scalar=g.pb_s[:, ft:ft + 1],
                                       in1=g.st_s[:, ft % 2, :],
                                       op0=ALU.add, op1=ALU.add)
                g.s_ve.inc(v.drain())
            blk.vector(ve)

    if di == 1:
        mixer_epilogue(g, corig, oorig)


def mixer_epilogue(g, corig, oorig):
    blk = g.block
    pp = v_pp(g)
    x2 = v_x2c(g)
    ln_chunk(g, pp, pp, TC)

    def q(qq, oorig=oorig, vprev=g.s_ve.v):
        qq.wait_ge(g.s_ve.h, vprev)
        qq.wait_ge(g.s_ld.h, g.s_ld.v)
        g.s_ld.dma(qq.dma_start(out=g.fc8[:, :, :],
                                in_=hbmT(g.x1_sp)[:, :, oorig:oorig + TC]))
    blk.sync(q)
    ld = g.s_ld.v

    def ve(v, ld=ld):
        v.wait_ge(g.s_ld.h, ld)
        for ft in range(NFT):
            v.tensor_scalar(out=pp[:, ft, :], in0=pp[:, ft, :],
                            scalar1=g.lnw_s[:, ft:ft + 1],
                            scalar2=g.lnb_s[:, ft:ft + 1],
                            op0=ALU.mult, op1=ALU.add)
        v.drain()
        v.tensor_tensor(out=x2, in0=pp, in1=g.fc8.ap()[:, :, :],
                        op=ALU.add)
        g.s_ve.inc(v.drain())
    blk.vector(ve)
    vd = g.s_ve.v

    def gp(p, corig=corig, vd=vd):
        p.wait_ge(g.s_ve.h, vd)
        p.wait_ge(g.s_st.h, g.s_st.v)
        g.s_st.dma(p.dma_start(
            out=hbmT(g.x2_sp)[:, :, corig * TC:(corig + 1) * TC],
            in_=x2[:, :, :]))
    blk.gpsimd(gp)

    ln_chunk(g, x2, pp, TC)
    absmax_acc(g, pp, SC_X2, axis=AX.XY)
    vd2 = g.s_ve.v

    def gp2(p, corig=corig, vd2=vd2):
        p.wait_ge(g.s_ve.h, vd2)
        p.wait_ge(g.s_st.h, g.s_st.v)
        g.s_st.dma(p.dma_start(
            out=hbmT(g.x2ln_sp)[:, :, corig * TC:(corig + 1) * TC],
            in_=pp[:, :, :]))
    blk.gpsimd(gp2)


def finish(g):
    blk = g.block

    def gp(p):
        p.wait_ge(g.s_out.h, g.s_out.v)
        p.wait_ge(g.s_st.h, g.s_st.v)
    blk.gpsimd(gp)



def kernel(**inputs):
    nc = build_nc()
    in_maps = make_in_maps(inputs)
    res = run_bass_kernel_spmd(nc, in_maps, core_ids=list(range(8)))
    return assemble([r for r in res.results])


def make_in_maps(inputs):
    i = inputs
    x = i["L0_attn_x"]
    w = {}
    for pre in ("fw", "bw"):
        w[f"in_wT_{pre}"] = np.ascontiguousarray(i[f"L0_attn_{pre}_in_w"].T)
        w[f"conv_w_{pre}"] = np.ascontiguousarray(i[f"L0_attn_{pre}_conv_w"][:, 0, :])
        w[f"conv_b_{pre}"] = np.ascontiguousarray(i[f"L0_attn_{pre}_conv_b"].reshape(DI, 1))
        w[f"xproj_wT_{pre}"] = np.ascontiguousarray(i[f"L0_attn_{pre}_xproj_w"].T)
        w[f"dt_wT_{pre}"] = np.ascontiguousarray(i[f"L0_attn_{pre}_dt_w"].T)
        w[f"dt_b_{pre}"] = np.ascontiguousarray(i[f"L0_attn_{pre}_dt_b"].reshape(DI, 1))
        w[f"Alog_{pre}"] = np.ascontiguousarray(i[f"L0_attn_{pre}_Alog"])
        w[f"Dp_{pre}"] = np.ascontiguousarray(i[f"L0_attn_{pre}_Dp"].reshape(DI, 1))
        w[f"out_wT_{pre}"] = np.ascontiguousarray(i[f"L0_attn_{pre}_out_w"].T)
    pT = np.ascontiguousarray(i["L0_attn_proj_w"].T)
    w["projT_fw"] = np.ascontiguousarray(pT[:D, :])
    w["projT_bw"] = np.ascontiguousarray(pT[D:, :])

    shared = dict(
        proj_b=np.ascontiguousarray(i["L0_attn_proj_b"].reshape(D, 1)),
        ln_w=np.ascontiguousarray(i["L0_attn_ln_w"].reshape(D, 1)),
        ln_b=np.ascontiguousarray(i["L0_attn_ln_b"].reshape(D, 1)),
        w_gate=np.ascontiguousarray(i["L0_ffn_w_gate"]),
        w_up=np.ascontiguousarray(i["L0_ffn_w_up"]),
        w_down=np.ascontiguousarray(i["L0_ffn_w_down"]),
    )
    maps = []
    for c in range(8):
        b, h = c // 2, c % 2
        xb = x[b].T
        lo, hi = h * TOK - HALO, h * TOK + TOK + HALO
        pad_l, pad_r = max(0, -lo), max(0, hi - LFULL)
        sl = xb[:, max(0, lo):min(LFULL, hi)]
        if pad_l or pad_r:
            sl = np.pad(sl, ((0, 0), (pad_l, pad_r)))
        if h == 1:
            sl = sl[:, ::-1]
        m = dict(shared)
        m["xT"] = np.ascontiguousarray(sl)
        dirA = "fw" if h == 0 else "bw"
        for key, val in w.items():
            base, pre = key.rsplit("_", 1)
            tgt = "A" if pre == dirA else "B"
            m[f"{base}_{tgt}"] = val
        maps.append(m)
    return maps


def assemble(results):
    out = np.empty((B, LFULL, D), np.float32)
    for c in range(8):
        b, h = c // 2, c % 2
        o = results[c]["out"]
        if h == 1:
            o = o[:, ::-1]
        out[b, h * TOK:(h + 1) * TOK, :] = o.T
    return out


# version 06
# speedup vs baseline: 1.2500x; 1.1398x over previous
import contextlib
import numpy as np

import concourse.bass as bass
import concourse.mybir as mybir
from concourse.bass_utils import run_bass_kernel_spmd

F32 = mybir.dt.float32
F16 = mybir.dt.float16
BF16 = mybir.dt.bfloat16
AF = mybir.ActivationFunctionType
ALU = mybir.AluOpType
AX = mybir.AxisListType

D = 1024
DI = 2048
NDT = 16
NFT = 8
DTR = 64
NS = 16
FF = 3072
NMT = 24
B = 4
LFULL = 4096
TOK = 2048
HALO = 3
TOKH = TOK + 2 * HALO
TC = 512
NCH = 4
CH5 = [(0, 512), (512, 512), (1024, 512), (1536, 512), (2048, 6)]
CH4 = [(i * TC, TC) for i in range(NCH)]

QMAX = 32767.0
WQMAX = 7.0
MAGIC = float(np.float32(1.5 * 2 ** 23))
LN_EPS = 1e-5

SC_X, SC_UP, SC_GATE, SC_GS, SC_T5 = 0, 1, 2, 3, 4
SC_X2, SC_UP2, SC_GATE2, SC_GS2, SC_T52 = 5, 6, 7, 8, 9
N_SCAL = 10
TMPC = 15


class Sem:
    def __init__(self, es, nc, name):
        self.h = es.enter_context(nc.semaphore(name))
        self.v = 0

    def inc(self, instr, n=1):
        instr.then_inc(self.h, n)
        self.v += n

    def dma(self, instr):
        self.inc(instr, 16)


class G:
    pass


def build_nc(taps=()):
    nc = bass.Bass("TRN2", target_bir_lowering=False,
                   detect_race_conditions=False)
    es = contextlib.ExitStack()
    g = G()
    g.nc = nc
    g.taps = list(taps)

    g.xT = nc.dram_tensor("xT", [D, TOKH], F32, kind="ExternalInput")
    g.inp = {}
    for d in ("A", "B"):
        for nm, shape in (("in_wT", [D, 2 * DI]), ("conv_w", [DI, 4]),
                          ("conv_b", [DI, 1]), ("xproj_wT", [DI, 96]),
                          ("dt_wT", [DTR, DI]), ("dt_b", [DI, 1]),
                          ("Alog", [DI, NS]), ("Dp", [DI, 1]),
                          ("out_wT", [DI, D]), ("projT", [D, D])):
            g.inp[f"{nm}_{d}"] = nc.dram_tensor(f"{nm}_{d}", shape, F32,
                                                kind="ExternalInput").ap()
    g.proj_b = nc.dram_tensor("proj_b", [D, 1], F32, kind="ExternalInput").ap()
    g.ln_w = nc.dram_tensor("ln_w", [D, 1], F32, kind="ExternalInput").ap()
    g.ln_b = nc.dram_tensor("ln_b", [D, 1], F32, kind="ExternalInput").ap()
    g.w_up = nc.dram_tensor("w_up", [FF, D], F32, kind="ExternalInput").ap()
    g.w_gate = nc.dram_tensor("w_gate", [FF, D], F32, kind="ExternalInput").ap()
    g.w_down = nc.dram_tensor("w_down", [D, FF], F32, kind="ExternalInput").ap()
    g.out_h = nc.dram_tensor("out", [D, TOK], F32, kind="ExternalOutput").ap()

    def dram(name, shape, dt):
        if name in g.taps:
            t = nc.dram_tensor(name, shape, dt, kind="ExternalOutput")
        else:
            t = nc.dram_tensor(name, shape, dt)
        setattr(g, name, t)
        return t

    dram("wq_upT", [D, FF], BF16)
    dram("wq_gT", [D, FF], BF16)
    dram("wq_dT", [FF, D], BF16)
    g.wb = {}
    for d in ("A", "B"):
        g.wb[f"in_wT_{d}"] = dram(f"b_in_wT_{d}", [D, 2 * DI], BF16)
        g.wb[f"out_wT_{d}"] = dram(f"b_out_wT_{d}", [DI, D], BF16)
        g.wb[f"projT_{d}"] = dram(f"b_projT_{d}", [D, D], BF16)
    dram("xln_sp", [D, TOKH], F32)
    dram("x1_sp", [D, TOKH], F32)
    dram("x1ln_sp", [D, TOKH], F32)
    dram("x1lnb_sp", [D, TOKH], BF16)
    dram("x1lnbR_sp", [D, TOKH], BF16)
    dram("x2_sp", [D, TOK], F32)
    dram("x2ln_sp", [D, TOK], F32)
    dram("up_sp", [FF, TOKH], F16)
    dram("gate_sp", [FF, TOKH], F16)
    dram("gs_sp", [FF, TOKH], F16)
    dram("t5_sp", [FF, TOKH], F16)
    dram("z_sp", [DI, TOK], F32)
    dram("projA_sp", [D, TOK], F32)
    dram("y0_sp", [DI, TOK], F32)
    g.scal_in = [dram(f"scin{i}", [1, 128], F32) for i in range(N_SCAL)]
    g.scal_out = [dram(f"scout{i}", [1, 128], F32) for i in range(N_SCAL)]
    dram("carry_io", [128, NDT * NS], F32)
    for dn in ("dbg_u", "dbg_dl", "dbg_y"):
        if dn in g.taps:
            dram(dn, [DI, TOK], F32)
    if "dbg_daw" in g.taps:
        dram("dbg_daw", [NDT * NS, 3, TC], F32)
    dram("carry_sum", [128, NDT * NS], F32)

    E = es.enter_context

    def sbuf(name, shape, dt):
        t = E(nc.sbuf_tensor(name, shape, dt))
        setattr(g, name, t)
        return t

    sbuf("ar3", [128, 16512], BF16)
    sbuf("aru", [128, 8192], F32)
    sbuf("arb", [128, 24576], F16)
    sbuf("fc8", [128, NFT, TC], F32)
    sbuf("zdt_s", [128, 2, TC], F32)
    sbuf("ubt_s", [128, 1, TC], BF16)
    sbuf("yt_s", [128, TC], F32)
    sbuf("y0t_s", [128, 2, TC], F32)
    sbuf("dup_s", [128, TC], F32)
    sbuf("ytmp_s", [128, TC], F32)
    sbuf("da_s", [128, 2, TC], F32)
    sbuf("w_sc", [128, 1, TC], F32)
    sbuf("wstr_s", [128, 2, 12, 128], BF16)
    sbuf("st_s", [128, 2, 512], F32)
    sbuf("stb_s", [128, 2, 512], F16)
    sbuf("g16_s", [128, 2, 512], F16)
    sbuf("h16_s", [128, 2, 512], F16)
    sbuf("sg_s", [128, 512], F32)
    sbuf("sg2_s", [128, 512], F32)
    sbuf("dq_s", [128, 96, 3], F32)
    sbuf("xdb_s", [96, 2, TC], F32)
    sbuf("dtb16_s", [DTR, TC], BF16)
    sbuf("bclo_s", [32, 2, TC], BF16)
    sbuf("sel_s", [32, 32, 128], BF16)
    sbuf("xicar_s", [128, NDT, HALO], F32)
    sbuf("ones_s", [1, 128], BF16)
    sbuf("onesf_s", [1, 128], F32)
    sbuf("onescol_s", [128, 1], F32)
    sbuf("ident_s", [128, 128], BF16)
    sbuf("identf_s", [128, 128], F32)
    sbuf("A_s", [128, 2, NDT * NS], F32)
    sbuf("Dp_s", [128, 2, NDT], F32)
    sbuf("cb_s", [128, 2, NDT], F32)
    sbuf("cw_s", [128, 2, NDT, 4], F32)
    sbuf("dtb_s", [128, 2, NDT], F32)
    sbuf("pb_s", [128, NFT], F32)
    sbuf("lnw_s", [128, NFT], F32)
    sbuf("lnb_s", [128, NFT], F32)
    sbuf("xprojT_s", [128, 2, NDT, 96], BF16)
    sbuf("dtwT_s", [DTR, 2, NDT, 128], BF16)
    sbuf("carry_s", [128, 2, NDT * NS], F32)
    sbuf("pcar_s", [128, NDT * NS], F32)
    sbuf("mx_s", [128, 16], F32)
    sbuf("sc_s", [128, N_SCAL], F32)
    sbuf("ss_s", [128, N_SCAL], F32)
    sbuf("t0_s", [1, 128], F32)
    sbuf("eps1_s", [1, 1], F32)

    def psum(name, shape, dt):
        t = E(nc.psum_tensor(name, shape, dt))
        setattr(g, name, t)

    psum("p_mm", [128, 2, 512], F32)
    psum("p_mm2", [128, 2, 512], F32)
    psum("p_bc", [128, 512], F32)
    psum("p_bc2", [128, 512], F32)
    psum("p_xd", [96, TC], F32)
    psum("p_tr", [128, 128], BF16)

    for s in ("s_ld", "s_st", "s_tn", "s_ve", "s_se", "s_ge", "s_cc",
              "s_out", "s_ld2"):
        setattr(g, s, Sem(es, nc, s))
    g.s_lw = [Sem(es, nc, f"s_lw{i}") for i in range(2)]
    g.s_lx = [Sem(es, nc, f"s_lx{i}") for i in range(2)]
    g.s_sw = [Sem(es, nc, f"s_sw{i}") for i in range(2)]

    with nc.Block() as block:
        g.block = block
        prep_constants(g)
        prep_weights(g)
        p1_lnq(g, src="x")
        ffn_mid(g, ver=1)
        p4_mixer(g)
        p1_lnq(g, src="x2")
        ffn_mid(g, ver=2)
        finish(g)
    es.close()
    return nc



def v_qb(g):
    return g.ar3.ap()[:, 0:NFT * TOKH].rearrange("p (f t) -> p f t", f=NFT)


def v_qb2(g):
    return g.ar3.ap()[:, 0:NFT * TOK].rearrange("p (f t) -> p f t", f=NFT)


def v_xi(g):
    return g.ar3.ap().bitcast(F32)[:, 0:NDT * (TC + HALO)].rearrange(
        "p (d t) -> p d t", d=NDT)


def v_du(g):
    return g.ar3.ap().bitcast(F32)[:, 0:NDT * TC].rearrange(
        "p (d t) -> p d t", d=NDT)


def v_gb(g, k=None):
    a = g.ar3.ap()[:, 0:NDT * TC].rearrange("p (d t) -> p d t", d=NDT)
    return a if k is None else a[:, k]


def v_u(g):
    return g.aru.ap().rearrange("p (d t) -> p d t", d=NDT)


def v_sq(g, n):
    return g.aru.ap()[:, 0:NFT * n].rearrange("p (f t) -> p f t", f=NFT)


def v_rows(g):
    return g.aru.ap()[0:1, NFT * TC:NFT * TC + 5 * 512].rearrange(
        "p (r t) -> p r t", r=5)


def v_abl(g):
    o = 2 * (NFT * TC + 5 * 512)
    return g.aru.ap().bitcast(BF16)[0:1, o:o + 4 * 512].rearrange(
        "p (r t) -> p r t", r=4)


def v_brep(g):
    return g.arb.ap().bitcast(F32)[:, 0:8 * TC].rearrange("p (n t) -> p n t", n=8)


def v_crep(g):
    return g.arb.ap().bitcast(F32)[:, 8 * TC:16 * TC]


def v_hall(g):
    return g.arb.ap().bitcast(F32)[:, 16 * TC:24 * TC]


def v_wraw(g):
    return g.arb.ap().bitcast(F32)[:, 0:4096]


def v_wqb(g):
    return g.arb.ap().bitcast(BF16)[:, 8192:8192 + 4096]


def v_lnout(g, n):
    return g.arb.ap().bitcast(F32)[:, 0:NFT * n].rearrange(
        "p (f t) -> p f t", f=NFT)


def v_pp(g):
    return g.arb.ap().bitcast(F32)[:, 16 * TC:16 * TC + NFT * TC].rearrange(
        "p (f t) -> p f t", f=NFT)


def v_x2c(g):
    return g.arb.ap().bitcast(F32)[:, 8 * TC:8 * TC + NFT * TC].rearrange(
        "p (f t) -> p f t", f=NFT)


def v_xlc(g):
    return g.arb.ap().bitcast(BF16)[:, 0:NFT * TC].rearrange(
        "p (f t) -> p f t", f=NFT)


def v_mo(g):
    return g.arb.ap().bitcast(BF16)[:, NFT * TC:2 * NFT * TC].rearrange(
        "p (f t) -> p f t", f=NFT)


def v_lnb16(g):
    return g.arb.ap().bitcast(BF16)[:, 8 * NFT * TC // 4:8 * NFT * TC // 4 + NFT * TC].rearrange(
        "p (f t) -> p f t", f=NFT)


def v_lnb16R(g):
    o = 8 * NFT * TC // 4 + NFT * TC
    return g.arb.ap().bitcast(BF16)[:, o:o + NFT * TC].rearrange(
        "p (f t) -> p f t", f=NFT)


def v_oq(g):
    return g.arb.ap().bitcast(BF16)[:, 0:NMT * 512].rearrange(
        "p (m t) -> p m t", m=NMT)


def hbmT(t):
    return t.ap().rearrange("(f p) t -> p f t", p=128)



def barrier_all(g):
    blk = g.block
    snap = [(g.s_ld, g.s_ld.v), (g.s_st, g.s_st.v), (g.s_tn, g.s_tn.v),
            (g.s_ve, g.s_ve.v), (g.s_se, g.s_se.v), (g.s_ge, g.s_ge.v),
            (g.s_cc, g.s_cc.v), (g.s_ld2, g.s_ld2.v), (g.s_out, g.s_out.v)]
    snap += [(s, s.v) for s in g.s_lw + g.s_lx + g.s_sw]
    for eng in ("sync", "vector", "scalar", "tensor", "gpsimd"):
        def body(e, snap=snap):
            for s, v in snap:
                if v:
                    e.wait_ge(s.h, v)
        getattr(blk, eng)(body)


def ln_chunk(g, src_ap, out_ap, n):
    blk = g.block
    sq = v_sq(g, n)
    ve0 = g.s_ve.v

    def se(e, ve0=ve0):
        e.wait_ge(g.s_ve.h, ve0)
        e.activation(out=sq, in_=src_ap, func=AF.Square, bias=0.0, scale=1.0)
        g.s_se.inc(e.drain())
    blk.scalar(se)
    sd = g.s_se.v

    def tn(t, sd=sd, n=n):
        t.wait_ge(g.s_se.h, sd)
        for ft in range(NFT):
            li = t.matmul(g.p_xd[0:1, 0:n], g.onescol_s[:, :], src_ap[:, ft],
                          start=(ft == 0), stop=(ft == NFT - 1))
        g.s_tn.inc(li)
        for ft in range(NFT):
            li = t.matmul(g.p_bc[0:1, 0:n], g.onescol_s[:, :], sq[:, ft],
                          start=(ft == 0), stop=(ft == NFT - 1))
        g.s_tn.inc(li)
    blk.tensor(tn)
    td = g.s_tn.v
    rows = v_rows(g)
    abl = v_abl(g)

    def ve(v, td=td, n=n):
        v.wait_ge(g.s_tn.h, td)
        v.tensor_scalar_mul(out=rows[:, 0, 0:n], in0=g.p_xd[0:1, 0:n],
                            scalar1=1.0 / D)
        v.tensor_scalar_mul(out=rows[:, 1, 0:n], in0=g.p_bc[0:1, 0:n],
                            scalar1=1.0 / D)
        v.drain()
        v.tensor_tensor(out=rows[:, 2, 0:n], in0=rows[:, 0, 0:n],
                        in1=rows[:, 0, 0:n], op=ALU.mult)
        v.drain()
        v.tensor_sub(out=rows[:, 2, 0:n], in0=rows[:, 1, 0:n],
                     in1=rows[:, 2, 0:n])
        g.s_ve.inc(v.drain())
    blk.vector(ve)
    vd = g.s_ve.v

    def se2(e, vd=vd, n=n):
        e.wait_ge(g.s_ve.h, vd)
        g.s_se.inc(e.activation(out=rows[:, 3, 0:n], in_=rows[:, 2, 0:n],
                                func=AF.Sqrt, bias=g.eps1_s[:, :], scale=1.0))
    blk.scalar(se2)
    sd2 = g.s_se.v

    def ve2(v, sd2=sd2, n=n):
        v.wait_ge(g.s_se.h, sd2)
        v.reciprocal(out=rows[:, 3, 0:n], in_=rows[:, 3, 0:n])
        v.drain()
        v.scalar_tensor_tensor(out=rows[:, 4, 0:n], in0=rows[:, 0, 0:n],
                               scalar=-1.0, in1=rows[:, 3, 0:n],
                               op0=ALU.mult, op1=ALU.mult)
        v.drain()
        v.tensor_copy(out=abl[:, 0, 0:n], in_=rows[:, 3, 0:n])
        v.tensor_copy(out=abl[:, 1, 0:n], in_=rows[:, 4, 0:n])
        v.drain()
        v.tensor_sub(out=rows[:, 0, 0:n], in0=rows[:, 3, 0:n], in1=abl[:, 0, 0:n])
        v.tensor_sub(out=rows[:, 1, 0:n], in0=rows[:, 4, 0:n], in1=abl[:, 1, 0:n])
        v.drain()
        v.tensor_copy(out=abl[:, 2, 0:n], in_=rows[:, 0, 0:n])
        v.tensor_copy(out=abl[:, 3, 0:n], in_=rows[:, 1, 0:n])
        g.s_ve.inc(v.drain())
    blk.vector(ve2)
    vd2 = g.s_ve.v

    def tn2(t, vd2=vd2, n=n):
        t.wait_ge(g.s_ve.h, vd2)
        li = t.matmul(g.p_bc[:, 0:n], g.ones_s[:, :], abl[:, 0, 0:n], start=True, stop=False)
        li = t.matmul(g.p_bc[:, 0:n], g.ones_s[:, :], abl[:, 2, 0:n], start=False, stop=True)
        g.s_tn.inc(li)
        li = t.matmul(g.p_bc2[:, 0:n], g.ones_s[:, :], abl[:, 1, 0:n], start=True, stop=False)
        li = t.matmul(g.p_bc2[:, 0:n], g.ones_s[:, :], abl[:, 3, 0:n], start=False, stop=True)
        g.s_tn.inc(li)
    blk.tensor(tn2)
    td2 = g.s_tn.v

    def ve3(v, td2=td2, n=n):
        v.wait_ge(g.s_tn.h, td2)
        v.tensor_tensor(out=out_ap, in0=src_ap,
                        in1=g.p_bc.ap()[:, None, 0:n].broadcast_to([128, NFT, n]),
                        op=ALU.mult)
        v.drain()
        v.tensor_tensor(out=out_ap, in0=out_ap,
                        in1=g.p_bc2.ap()[:, None, 0:n].broadcast_to([128, NFT, n]),
                        op=ALU.add)
        g.s_ve.inc(v.drain())
    blk.vector(ve3)
    return g.s_ve.v


def absmax_acc(g, src_ap, slot, axis=AX.X, wait_se=None):
    blk = g.block

    def ve(v, slot=slot, axis=axis):
        if wait_se is not None:
            v.wait_ge(g.s_se.h, wait_se)
        v.tensor_reduce(out=g.mx_s[:, TMPC:TMPC + 1], in_=src_ap, axis=axis,
                        op=ALU.max, apply_absolute_value=True)
        v.drain()
        v.tensor_max(out=g.mx_s[:, slot:slot + 1], in0=g.mx_s[:, slot:slot + 1],
                     in1=g.mx_s[:, TMPC:TMPC + 1])
        g.s_ve.inc(v.drain())
    blk.vector(ve)
    return g.s_ve.v


def scale_exchange(g, slot, qmax):
    blk = g.block
    ve_prev = g.s_ve.v

    def tn(t, slot=slot, ve_prev=ve_prev):
        t.wait_ge(g.s_ve.h, ve_prev)
        g.s_tn.inc(t.transpose(g.p_bc[0:1, 0:128], g.mx_s[:, slot:slot + 1],
                               g.identf_s[:]))
    blk.tensor(tn)
    td = g.s_tn.v

    def ve(v, td=td):
        v.wait_ge(g.s_tn.h, td)
        v.reduce_max(out=g.t0_s[:, 0:1], in_=g.p_bc[0:1, 0:128], axis=AX.X)
        v.drain()
        g.s_ve.inc(v.tensor_copy(
            out=g.t0_s[:, :], in_=g.t0_s.ap()[:, 0:1].broadcast_to([1, 128])))
    blk.vector(ve)
    vd = g.s_ve.v

    def gp(p, slot=slot, vd=vd, sw0=g.s_sw[0].v, sw1=g.s_sw[1].v):
        p.wait_ge(g.s_ve.h, vd)
        p.wait_ge(g.s_sw[0].h, sw0)
        p.wait_ge(g.s_sw[1].h, sw1)
        p.wait_ge(g.s_st.h, g.s_st.v)
        g.s_st.dma(p.dma_start(out=g.scal_in[slot].ap(), in_=g.t0_s[:, :]))
        p.wait_ge(g.s_st.h, g.s_st.v)
        g.s_cc.inc(p.collective_compute(
            "AllReduce", ALU.max, replica_groups=[list(range(8))],
            ins=[g.scal_in[slot].ap().opt()], outs=[g.scal_out[slot].ap().opt()]))
        p.wait_ge(g.s_cc.h, g.s_cc.v)
        p.wait_ge(g.s_st.h, g.s_st.v)
        g.s_st.dma(p.dma_start(
            out=g.ss_s[:, slot:slot + 1],
            in_=g.scal_out[slot].ap().rearrange("o p -> (o p)")[:, None]))
    blk.gpsimd(gp)
    std = g.s_st.v

    def ve2(v, slot=slot, std=std, qmax=qmax):
        v.wait_ge(g.s_st.h, std)
        v.tensor_scalar(out=g.ss_s[:, slot:slot + 1], in0=g.ss_s[:, slot:slot + 1],
                        scalar1=1.0 / qmax, scalar2=1e-12, op0=ALU.mult,
                        op1=ALU.max)
        v.drain()
        v.reciprocal(out=g.sc_s[:, slot:slot + 1], in_=g.ss_s[:, slot:slot + 1])
        g.s_ve.inc(v.drain())
    blk.vector(ve2)
    return g.s_ve.v



def prep_constants(g):
    blk = g.block
    nc = g.nc

    def q(qq):
        ctx = nc.allow_non_contiguous_dma(reason="tiny per-channel param loads")
        ctx.__enter__()
        for i, d in enumerate(("A", "B")):
            qq.wait_ge(g.s_ld.h, g.s_ld.v)
            g.s_ld.dma(qq.dma_start(
                out=g.A_s.ap()[:, i].rearrange("p (a n) -> p a n", a=NDT),
                in_=g.inp[f"Alog_{d}"].rearrange("(a p) n -> p a n", p=128)))
            qq.wait_ge(g.s_ld.h, g.s_ld.v)
            g.s_ld.dma(qq.dma_start(
                out=g.Dp_s.ap()[:, i, :, None],
                in_=g.inp[f"Dp_{d}"].rearrange("(a p) o -> p a o", p=128)))
            qq.wait_ge(g.s_ld.h, g.s_ld.v)
            g.s_ld.dma(qq.dma_start(
                out=g.cb_s.ap()[:, i, :, None],
                in_=g.inp[f"conv_b_{d}"].rearrange("(a p) o -> p a o", p=128)))
            qq.wait_ge(g.s_ld.h, g.s_ld.v)
            g.s_ld.dma(qq.dma_start(out=g.cw_s[:, i],
                                    in_=g.inp[f"conv_w_{d}"].rearrange("(a p) k -> p a k", p=128)))
            qq.wait_ge(g.s_ld.h, g.s_ld.v)
            g.s_ld.dma(qq.dma_start(
                out=g.dtb_s.ap()[:, i, :, None],
                in_=g.inp[f"dt_b_{d}"].rearrange("(a p) o -> p a o", p=128)))
        qq.wait_ge(g.s_ld.h, g.s_ld.v)
        g.s_ld.dma(qq.dma_start(out=g.pb_s.ap()[:, :, None],
                                in_=g.proj_b.rearrange("(a p) o -> p a o", p=128)))
        qq.wait_ge(g.s_ld.h, g.s_ld.v)
        g.s_ld.dma(qq.dma_start(out=g.lnw_s.ap()[:, :, None],
                                in_=g.ln_w.rearrange("(a p) o -> p a o", p=128)))
        qq.wait_ge(g.s_ld.h, g.s_ld.v)
        g.s_ld.dma(qq.dma_start(out=g.lnb_s.ap()[:, :, None],
                                in_=g.ln_b.rearrange("(a p) o -> p a o", p=128)))
        ctx.__exit__(None, None, None)
    blk.sync(q)
    ld0 = g.s_ld.v

    def ve(v):
        v.memset(g.ones_s[:], 1.0)
        v.memset(g.onesf_s[:], 1.0)
        v.memset(g.onescol_s[:], 1.0)
        v.memset(g.identf_s[:], 1.0)
        v.memset(g.ident_s[:], 1.0)
        v.memset(g.carry_s[:], 0.0)
        v.memset(g.mx_s[:], 0.0)
        v.memset(g.eps1_s[:], LN_EPS)
        v.memset(g.sel_s[:], 1.0)
        g.s_ve.inc(v.drain())
    blk.vector(ve)
    ve0 = g.s_ve.v

    def gp(p, ve0=ve0):
        p.wait_ge(g.s_ve.h, ve0)
        p.affine_select(out=g.ident_s[:], in_=g.ident_s[:], pattern=[[-1, 128]],
                        compare_op=ALU.is_equal, fill=0.0, base=0,
                        channel_multiplier=1)
        p.affine_select(out=g.identf_s[:], in_=g.identf_s[:],
                        pattern=[[-1, 128]], compare_op=ALU.is_equal,
                        fill=0.0, base=0, channel_multiplier=1)
        for n in range(32):
            p.affine_select(out=g.sel_s[:, n, :], in_=g.sel_s[:, n, :],
                            pattern=[[0, 128]], compare_op=ALU.is_equal,
                            fill=0.0, base=-n, channel_multiplier=1)
        p.sem_inc(g.s_ge.h, 1)
        g.s_ge.v += 1
    blk.gpsimd(gp)

    def se(e, ld0=ld0):
        e.wait_ge(g.s_ld.h, ld0)
        e.activation(out=g.A_s[:], in_=g.A_s[:], func=AF.Exp, bias=0.0, scale=1.0)
        e.drain()
        g.s_se.inc(e.mul(out=g.A_s[:], in_=g.A_s[:], mul=-1.0))
    blk.scalar(se)

    for i, d in enumerate(("A", "B")):
        def q2(qq, d=d, vprev=g.s_ve.v):
            qq.wait_ge(g.s_ve.h, vprev)
            qq.wait_ge(g.s_ld2.h, g.s_ld2.v)
            g.s_ld2.dma(qq.dma_start(
                out=v_wraw(g)[:, 0:NDT * 96].rearrange("p (a m) -> p a m", a=NDT),
                in_=g.inp[f"xproj_wT_{d}"].rearrange("(a p) m -> p a m", p=128)))
            qq.wait_ge(g.s_ld2.h, g.s_ld2.v)
            g.s_ld2.dma(qq.dma_start(
                out=v_wraw(g)[0:DTR, 1536:1536 + NDT * 64],
                in_=g.inp[f"dt_wT_{d}"][:, 0:NDT * 64]))
            qq.wait_ge(g.s_ld2.h, g.s_ld2.v)
            g.s_ld2.dma(qq.dma_start(
                out=v_wraw(g)[64:128, 1536:1536 + NDT * 64],
                in_=g.inp[f"dt_wT_{d}"][:, NDT * 64:NDT * 128]))
        blk.scalar(q2)
        ld = g.s_ld2.v

        def ve2(v, i=i, ld=ld):
            v.wait_ge(g.s_ld2.h, ld)
            v.tensor_copy(
                out=g.xprojT_s.ap().rearrange("p a m k -> p (a m k)")[:, i * NDT * 96:(i + 1) * NDT * 96],
                in_=v_wraw(g)[:, 0:NDT * 96])
            v.tensor_copy(
                out=g.dtwT_s.ap().rearrange("r a m k -> r (a m k)")[:, i * NDT * 128:i * NDT * 128 + NDT * 64],
                in_=v_wraw(g)[0:DTR, 1536:1536 + NDT * 64])
            g.s_ve.inc(v.tensor_copy(
                out=g.dtwT_s.ap().rearrange("r a m k -> r (a m k)")[:, i * NDT * 128 + NDT * 64:(i + 1) * NDT * 128],
                in_=v_wraw(g)[64:128, 1536:1536 + NDT * 64]))
        blk.vector(ve2)


def prep_weights(g):
    blk = g.block
    barrier_all(g)
    jobs = []
    for d in ("A", "B"):
        jobs.append((g.inp[f"in_wT_{d}"], g.wb[f"in_wT_{d}"], D, 2 * DI))
        jobs.append((g.inp[f"out_wT_{d}"], g.wb[f"out_wT_{d}"], DI, D))
        jobs.append((g.inp[f"projT_{d}"], g.wb[f"projT_{d}"], D, D))
    for src, dst, rows, cols in jobs:
        for i in range(rows // 128):
            def q(qq, src=src, i=i, cols=cols, stv=g.s_st.v):
                qq.wait_ge(g.s_st.h, stv)
                qq.wait_ge(g.s_ld.h, g.s_ld.v)
                g.s_ld.dma(qq.dma_start(out=v_wraw(g)[:, 0:cols],
                                        in_=src[i * 128:(i + 1) * 128, :]))
            blk.sync(q)
            ld = g.s_ld.v

            def ve(v, cols=cols, ld=ld):
                v.wait_ge(g.s_ld.h, ld)
                g.s_ve.inc(v.tensor_copy(out=v_wqb(g)[:, 0:cols],
                                         in_=v_wraw(g)[:, 0:cols]))
            blk.vector(ve)
            vd = g.s_ve.v

            def gp(p, dst=dst, i=i, cols=cols, vd=vd):
                p.wait_ge(g.s_ve.h, vd)
                p.wait_ge(g.s_st.h, g.s_st.v)
                g.s_st.dma(p.dma_start(out=dst.ap()[i * 128:(i + 1) * 128, :],
                                       in_=v_wqb(g)[:, 0:cols]))
            blk.gpsimd(gp)

    for src, dstT, O, I in ((g.w_up, g.wq_upT, FF, D),
                            (g.w_gate, g.wq_gT, FF, D),
                            (g.w_down, g.wq_dT, D, FF)):
        nblk = I // 32
        for i in range(O // 128):
            def q(qq, src=src, i=i, I=I, stv=g.s_st.v):
                qq.wait_ge(g.s_st.h, stv)
                qq.wait_ge(g.s_ld.h, g.s_ld.v)
                g.s_ld.dma(qq.dma_start(out=v_wraw(g)[:, 0:I],
                                        in_=src[i * 128:(i + 1) * 128, :]))
            blk.sync(q)
            ld = g.s_ld.v

            def ve(v, I=I, nblk=nblk, ld=ld):
                v.wait_ge(g.s_ld.h, ld)
                w = v_wraw(g)[:, 0:I]
                w3 = v_wraw(g)[:, 0:I].rearrange("p (n k) -> p n k", k=32)
                m = g.dq_s[:, 0:nblk, 0]
                s = g.dq_s[:, 0:nblk, 1]
                r = g.dq_s[:, 0:nblk, 2]
                v.tensor_reduce(out=m, in_=w3, axis=AX.X, op=ALU.max,
                                apply_absolute_value=True)
                v.drain()
                v.tensor_scalar(out=s, in0=m, scalar1=1.0 / WQMAX, scalar2=1e-12,
                                op0=ALU.mult, op1=ALU.max)
                v.drain()
                v.reciprocal(out=r, in_=s)
                v.drain()
                v.tensor_tensor(out=w3, in0=w3,
                                in1=g.dq_s.ap()[:, 0:nblk, 2:3].broadcast_to([128, nblk, 32]),
                                op=ALU.mult)
                v.drain()
                v.tensor_scalar(out=w, in0=w, scalar1=MAGIC, scalar2=MAGIC,
                                op0=ALU.add, op1=ALU.subtract)
                v.drain()
                g.s_ve.inc(v.tensor_tensor(
                    out=v_wqb(g)[:, 0:I].rearrange("p (n k) -> p n k", k=32),
                    in0=w3,
                    in1=g.dq_s.ap()[:, 0:nblk, 1:2].broadcast_to([128, nblk, 32]),
                    op=ALU.mult))
            blk.vector(ve)
            vd = g.s_ve.v

            for j in range(I // 128):
                def tn(t, j=j, vd=vd, gprev=g.s_ge.v, sprev=g.s_se.v):
                    t.wait_ge(g.s_ve.h, vd)
                    t.wait_ge(g.s_ge.h, gprev)
                    t.wait_ge(g.s_se.h, sprev)
                    g.s_tn.inc(t.transpose(g.p_tr[:],
                                           v_wqb(g)[:, j * 128:(j + 1) * 128],
                                           g.ident_s[:]))
                blk.tensor(tn)
                td = g.s_tn.v

                def se(e, td=td, stv=g.s_st.v):
                    e.wait_ge(g.s_tn.h, td)
                    e.wait_ge(g.s_st.h, stv)
                    g.s_se.inc(e.copy(out=g.stb_s.ap().rearrange("p a b -> p (a b)").bitcast(BF16)[:, 0:128],
                                      in_=g.p_tr[:]))
                blk.scalar(se)
                sd = g.s_se.v

                def gp(p, dstT=dstT, i=i, j=j, sd=sd):
                    p.wait_ge(g.s_se.h, sd)
                    p.wait_ge(g.s_st.h, g.s_st.v)
                    g.s_st.dma(p.dma_start(
                        out=dstT.ap()[j * 128:(j + 1) * 128, i * 128:(i + 1) * 128],
                        in_=g.stb_s.ap().rearrange("p a b -> p (a b)").bitcast(BF16)[:, 0:128]))
                blk.gpsimd(gp)


def p1_lnq(g, src):
    blk = g.block
    barrier_all(g)
    if src == "x":
        for c, (o, n) in enumerate(CH5):
            def q(qq, o=o, n=n, vprev=g.s_ve.v):
                qq.wait_ge(g.s_ve.h, vprev)
                qq.wait_ge(g.s_ld.h, g.s_ld.v)
                g.s_ld.dma(qq.dma_start(out=g.fc8[:, :, 0:n],
                                        in_=hbmT(g.xT)[:, :, o:o + n]))
            blk.sync(q)
            ld = g.s_ld.v

            def ve0(v, ld=ld, stv=g.s_st.v):
                v.wait_ge(g.s_ld.h, ld)
                v.wait_ge(g.s_st.h, stv)
                g.s_ve.inc(v.drain())
            blk.vector(ve0)
            ln_chunk(g, g.fc8.ap()[:, :, 0:n], v_lnout(g, n), n)
            absmax_acc(g, v_lnout(g, n), SC_X, axis=AX.XY)
            vd = g.s_ve.v

            def gp(p, o=o, n=n, vd=vd):
                p.wait_ge(g.s_ve.h, vd)
                p.wait_ge(g.s_st.h, g.s_st.v)
                g.s_st.dma(p.dma_start(out=hbmT(g.xln_sp)[:, :, o:o + n],
                                       in_=v_lnout(g, n)[:, :, 0:n]))
            blk.gpsimd(gp)
        scale_exchange(g, SC_X, QMAX)
        slot, lnsrc, chunks, qbv = SC_X, g.xln_sp, CH5, v_qb(g)
    else:
        scale_exchange(g, SC_X2, QMAX)
        slot, lnsrc, chunks, qbv = SC_X2, g.x2ln_sp, CH4, v_qb2(g)

    for c, (o, n) in enumerate(chunks):
        def q(qq, o=o, n=n, lnsrc=lnsrc, vprev=g.s_ve.v, stv=g.s_st.v):
            qq.wait_ge(g.s_ve.h, vprev)
            qq.wait_ge(g.s_st.h, stv)
            qq.wait_ge(g.s_ld.h, g.s_ld.v)
            g.s_ld.dma(qq.dma_start(out=g.fc8[:, :, 0:n],
                                    in_=hbmT(lnsrc)[:, :, o:o + n]))
        blk.sync(q)
        ld = g.s_ld.v

        def ve(v, o=o, n=n, ld=ld, slot=slot, qbv=qbv):
            v.wait_ge(g.s_ld.h, ld)
            v.tensor_scalar(out=g.fc8[:, :, 0:n], in0=g.fc8[:, :, 0:n],
                            scalar1=g.sc_s[:, slot:slot + 1], scalar2=MAGIC,
                            op0=ALU.mult, op1=ALU.add)
            v.drain()
            v.tensor_scalar(out=qbv[:, :, o:o + n], in0=g.fc8[:, :, 0:n],
                            scalar1=MAGIC, scalar2=g.ss_s[:, slot:slot + 1],
                            op0=ALU.subtract, op1=ALU.mult)
            g.s_ve.inc(v.drain())
        blk.vector(ve)


def ffn_mid(g, ver):
    blk = g.block
    barrier_all(g)
    if ver == 1:
        chunks, qbv = CH5, v_qb(g)
        s_up, s_gate, s_gs, s_t5 = SC_UP, SC_GATE, SC_GS, SC_T5
    else:
        chunks, qbv = CH4, v_qb2(g)
        s_up, s_gate, s_gs, s_t5 = SC_UP2, SC_GATE2, SC_GS2, SC_T52

    for wt, spill, slot in ((g.wq_upT, g.up_sp, s_up),
                            (g.wq_gT, g.gate_sp, s_gate)):
        for m in range(NMT):
            bi = m % 2

            def q(qq, wt=wt, m=m, bi=bi, tprev=g.s_tn.v):
                qq.wait_ge(g.s_tn.h, tprev)
                qq.wait_ge(g.s_ld.h, g.s_ld.v)
                g.s_ld.dma(qq.dma_start(
                    out=g.wstr_s[:, bi, 0:NFT, :],
                    in_=wt.ap().rearrange("(k p) f -> p k f", p=128)[:, :, m * 128:(m + 1) * 128]))
            blk.sync(q)
            ld = g.s_ld.v

            for ci, (o, n) in enumerate(chunks):
                pb = ci % 2

                def tn(t, bi=bi, o=o, n=n, pb=pb, ld=ld, sprev=g.s_se.v,
                       qbv=qbv):
                    t.wait_ge(g.s_ld.h, ld)
                    t.wait_ge(g.s_se.h, sprev)
                    for k in range(NFT):
                        li = t.matmul(g.p_mm[:, pb, 0:n], g.wstr_s[:, bi, k, :],
                                 qbv[:, k, o:o + n], start=(k == 0),
                                 stop=(k == NFT - 1))
                    g.s_tn.inc(li)
                blk.tensor(tn)
                td = g.s_tn.v

                def se(e, pb=pb, ci=ci, n=n, td=td, stv=g.s_st.v):
                    e.wait_ge(g.s_tn.h, td)
                    e.wait_ge(g.s_st.h, stv)
                    g.s_se.inc(e.copy(out=g.stb_s[:, ci % 2, 0:n],
                                      in_=g.p_mm[:, pb, 0:n]))
                blk.scalar(se)
                sd = g.s_se.v

                absmax_acc(g, g.stb_s[:, ci % 2, 0:n], slot, wait_se=sd)
                vd = g.s_ve.v

                def gp(p, spill=spill, m=m, o=o, n=n, ci=ci, vd=vd):
                    p.wait_ge(g.s_ve.h, vd)
                    p.wait_ge(g.s_st.h, g.s_st.v)
                    g.s_st.dma(p.dma_start(
                        out=spill.ap()[m * 128:(m + 1) * 128, o:o + n],
                        in_=g.stb_s[:, ci % 2, 0:n]))
                blk.gpsimd(gp)

    scale_exchange(g, s_up, QMAX)
    scale_exchange(g, s_gate, QMAX)

    va_hist = {-2: g.s_ve.v, -1: g.s_ve.v}
    for m in range(NMT):
        for ci, (o, n) in enumerate(chunks):
            it = m * len(chunks) + ci
            sl = it % 2

            def q(qq, m=m, o=o, n=n, sl=sl, vprev=va_hist.get(it - 2, 0)):
                qq.wait_ge(g.s_ve.h, vprev)
                qq.wait_ge(g.s_lx[sl].h, g.s_lx[sl].v)
                g.s_lx[sl].dma(qq.dma_start(
                    out=g.g16_s[:, sl, 0:n],
                    in_=g.gate_sp.ap()[m * 128:(m + 1) * 128, o:o + n]))
            blk.sync(q)
            ld = g.s_lx[sl].v

            def ve(v, sl=sl, n=n, ld=ld, s_gate=s_gate):
                v.wait_ge(g.s_lx[sl].h, ld)
                gq = g.sg_s[:, 0:n]
                v.tensor_scalar(out=gq, in0=g.g16_s[:, sl, 0:n],
                                scalar1=g.sc_s[:, s_gate:s_gate + 1],
                                scalar2=MAGIC, op0=ALU.mult, op1=ALU.add)
                v.drain()
                v.tensor_scalar(out=gq, in0=gq, scalar1=MAGIC,
                                scalar2=g.ss_s[:, s_gate:s_gate + 1],
                                op0=ALU.subtract, op1=ALU.mult)
                g.s_ve.inc(v.drain())
            blk.vector(ve)
            vd = g.s_ve.v

            def se(e, n=n, vd=vd):
                e.wait_ge(g.s_ve.h, vd)
                g.s_se.inc(e.activation(out=g.sg2_s[:, 0:n], in_=g.sg_s[:, 0:n],
                                        func=AF.Sigmoid, bias=0.0, scale=1.0))
            blk.scalar(se)
            sd = g.s_se.v

            def ve2(v, n=n, sl=sl, sd=sd, stv=g.s_sw[sl].v):
                v.wait_ge(g.s_se.h, sd)
                v.wait_ge(g.s_sw[sl].h, stv)
                sig = g.sg2_s[:, 0:n]
                v.tensor_scalar(out=sig, in0=sig, scalar1=65536.0, scalar2=MAGIC,
                                op0=ALU.mult, op1=ALU.add)
                v.drain()
                v.tensor_scalar(out=sig, in0=sig, scalar1=MAGIC, scalar2=65535.0,
                                op0=ALU.subtract, op1=ALU.min)
                v.drain()
                v.scalar_tensor_tensor(out=g.h16_s[:, sl, 0:n], in0=sig,
                                       scalar=1.0 / 65536.0, in1=g.sg_s[:, 0:n],
                                       op0=ALU.mult, op1=ALU.mult)
                g.s_ve.inc(v.drain())
            blk.vector(ve2)
            absmax_acc(g, g.h16_s[:, sl, 0:n], s_gs)
            vd2 = g.s_ve.v

            def gp(p, m=m, o=o, n=n, sl=sl, vd2=vd2):
                p.wait_ge(g.s_ve.h, vd2)
                g.s_sw[sl].dma(p.dma_start(
                    out=g.gs_sp.ap()[m * 128:(m + 1) * 128, o:o + n],
                    in_=g.h16_s[:, sl, 0:n]))
            blk.gpsimd(gp)
            va_hist[it] = g.s_ve.v

    scale_exchange(g, s_gs, QMAX)

    vb_hist = {-2: g.s_ve.v, -1: g.s_ve.v}
    for m in range(NMT):
        for ci, (o, n) in enumerate(chunks):
            it = m * len(chunks) + ci
            sl = it % 2

            def q(qq, m=m, o=o, n=n, sl=sl, vprev=vb_hist.get(it - 2, 0),
                  swv=g.s_sw[sl].v):
                qq.wait_ge(g.s_ve.h, vprev)
                qq.wait_ge(g.s_sw[sl].h, swv)
                qq.wait_ge(g.s_lx[sl].h, g.s_lx[sl].v)
                g.s_lx[sl].dma(qq.dma_start(
                    out=g.g16_s[:, sl, 0:n],
                    in_=g.gs_sp.ap()[m * 128:(m + 1) * 128, o:o + n]))
                qq.wait_ge(g.s_lx[sl].h, g.s_lx[sl].v)
                g.s_lx[sl].dma(qq.dma_start(
                    out=g.h16_s[:, sl, 0:n],
                    in_=g.up_sp.ap()[m * 128:(m + 1) * 128, o:o + n]))
            blk.sync(q)
            ld = g.s_lx[sl].v

            def ve(v, n=n, sl=sl, ld=ld, s_gs=s_gs, s_up=s_up,
                   stv=g.s_sw[sl].v):
                v.wait_ge(g.s_lx[sl].h, ld)
                v.wait_ge(g.s_sw[sl].h, stv)
                g2 = g.sg_s[:, 0:n]
                uq = g.sg2_s[:, 0:n]
                v.tensor_scalar(out=g2, in0=g.g16_s[:, sl, 0:n],
                                scalar1=g.sc_s[:, s_gs:s_gs + 1], scalar2=MAGIC,
                                op0=ALU.mult, op1=ALU.add)
                v.tensor_scalar(out=uq, in0=g.h16_s[:, sl, 0:n],
                                scalar1=g.sc_s[:, s_up:s_up + 1], scalar2=MAGIC,
                                op0=ALU.mult, op1=ALU.add)
                v.drain()
                v.tensor_scalar(out=g2, in0=g2, scalar1=MAGIC,
                                scalar2=g.ss_s[:, s_gs:s_gs + 1],
                                op0=ALU.subtract, op1=ALU.mult)
                v.tensor_scalar(out=uq, in0=uq, scalar1=MAGIC,
                                scalar2=g.ss_s[:, s_up:s_up + 1],
                                op0=ALU.subtract, op1=ALU.mult)
                v.drain()
                v.tensor_mul(out=g.g16_s[:, sl, 0:n], in0=g2, in1=uq)
                g.s_ve.inc(v.drain())
            blk.vector(ve)
            absmax_acc(g, g.g16_s[:, sl, 0:n], s_t5)
            vd = g.s_ve.v

            def gp(p, m=m, o=o, n=n, sl=sl, vd=vd):
                p.wait_ge(g.s_ve.h, vd)
                g.s_sw[sl].dma(p.dma_start(
                    out=g.t5_sp.ap()[m * 128:(m + 1) * 128, o:o + n],
                    in_=g.g16_s[:, sl, 0:n]))
            blk.gpsimd(gp)
            vb_hist[it] = g.s_ve.v

    scale_exchange(g, s_t5, QMAX)

    oq = v_oq(g)
    vc_hist = {-2: g.s_ve.v, -1: g.s_ve.v}
    for ci, (o, n) in enumerate(chunks):
        for m in range(NMT):
            sl = m % 2
            it = ci * NMT + m

            def q(qq, m=m, o=o, n=n, sl=sl, vprev=vc_hist.get(it - 2, 0)):
                qq.wait_ge(g.s_ve.h, vprev)
                qq.wait_ge(g.s_lx[sl].h, g.s_lx[sl].v)
                g.s_lx[sl].dma(qq.dma_start(
                    out=g.g16_s[:, sl, 0:n],
                    in_=g.t5_sp.ap()[m * 128:(m + 1) * 128, o:o + n]))
            blk.sync(q)
            ld = g.s_lx[sl].v

            def ve(v, m=m, n=n, sl=sl, ld=ld, s_t5=s_t5, oq=oq, tprev=g.s_tn.v):
                v.wait_ge(g.s_lx[sl].h, ld)
                v.wait_ge(g.s_tn.h, tprev)
                v.tensor_scalar(out=g.sg_s[:, 0:n], in0=g.g16_s[:, sl, 0:n],
                                scalar1=g.sc_s[:, s_t5:s_t5 + 1], scalar2=MAGIC,
                                op0=ALU.mult, op1=ALU.add)
                v.drain()
                v.tensor_scalar(out=oq[:, m, 0:n], in0=g.sg_s[:, 0:n],
                                scalar1=MAGIC, scalar2=g.ss_s[:, s_t5:s_t5 + 1],
                                op0=ALU.subtract, op1=ALU.mult)
                g.s_ve.inc(v.drain())
            blk.vector(ve)
            vc_hist[it] = g.s_ve.v
        vd_oq = g.s_ve.v

        for ft in range(NFT):
            for half in range(2):
                def q2(qq, ft=ft, half=half, tprev=g.s_tn.v):
                    qq.wait_ge(g.s_tn.h, tprev)
                    qq.wait_ge(g.s_ld.h, g.s_ld.v)
                    g.s_ld.dma(qq.dma_start(
                        out=g.wstr_s[:, half, 0:12, :],
                        in_=g.wq_dT.ap().rearrange("(k p) f -> p k f", p=128)[:, 12 * half:12 * (half + 1), ft * 128:(ft + 1) * 128]))
                blk.sync(q2)
            ld2 = g.s_ld.v

            def tn(t, ft=ft, n=n, vd_oq=vd_oq, ld2=ld2, sprev=g.s_se.v, oq=oq):
                t.wait_ge(g.s_ve.h, vd_oq)
                t.wait_ge(g.s_ld.h, ld2)
                t.wait_ge(g.s_se.h, sprev)
                for k in range(NMT):
                    li = t.matmul(g.p_mm2[:, ft % 2, 0:n],
                             g.wstr_s[:, k // 12, k % 12, :],
                             oq[:, k, 0:n], start=(k == 0), stop=(k == NMT - 1))
                g.s_tn.inc(li)
            blk.tensor(tn)
            td = g.s_tn.v

            if ver == 1:
                def q3(qq, ft=ft, o=o, n=n, vprev=g.s_ve.v, stv=g.s_st.v):
                    qq.wait_ge(g.s_ve.h, vprev)
                    qq.wait_ge(g.s_st.h, stv)
                    qq.wait_ge(g.s_ld.h, g.s_ld.v)
                    g.s_ld.dma(qq.dma_start(out=g.fc8[:, ft, 0:n],
                                            in_=hbmT(g.xT)[:, ft, o:o + n]))
                blk.sync(q3)
                ld3 = g.s_ld.v

                def ve2(v, ft=ft, n=n, td=td, ld3=ld3):
                    v.wait_ge(g.s_tn.h, td)
                    v.wait_ge(g.s_ld.h, ld3)
                    v.scalar_tensor_tensor(out=g.fc8[:, ft, 0:n],
                                           in0=g.p_mm2[:, ft % 2, 0:n],
                                           scalar=0.5, in1=g.fc8[:, ft, 0:n],
                                           op0=ALU.mult, op1=ALU.add)
                    g.s_ve.inc(v.drain())
                blk.vector(ve2)
                vd2 = g.s_ve.v

                def se2(e, vd2=vd2):
                    e.wait_ge(g.s_ve.h, vd2)
                    e.sem_inc(g.s_se.h, 1); g.s_se.v += 1
                blk.scalar(se2)

                def gp(p, ft=ft, o=o, n=n, vd2=vd2):
                    p.wait_ge(g.s_ve.h, vd2)
                    p.wait_ge(g.s_st.h, g.s_st.v)
                    g.s_st.dma(p.dma_start(out=hbmT(g.x1_sp)[:, ft, o:o + n],
                                           in_=g.fc8[:, ft, 0:n]))
                blk.gpsimd(gp)
            else:
                def q3(qq, ft=ft, o=o, n=n, vprev=g.s_ve.v, stv=g.s_st.v):
                    qq.wait_ge(g.s_ve.h, vprev)
                    qq.wait_ge(g.s_st.h, stv)
                    qq.wait_ge(g.s_ld.h, g.s_ld.v)
                    g.s_ld.dma(qq.dma_start(out=g.fc8[:, ft, 0:n],
                                            in_=hbmT(g.x2_sp)[:, ft, o:o + n]))
                blk.sync(q3)
                ld3 = g.s_ld.v

                def ve2(v, ft=ft, n=n, td=td, ld3=ld3, ov=g.s_out.v):
                    v.wait_ge(g.s_tn.h, td)
                    v.wait_ge(g.s_ld.h, ld3)
                    v.wait_ge(g.s_out.h, ov)
                    v.scalar_tensor_tensor(out=g.st_s[:, ft % 2, 0:n],
                                           in0=g.p_mm2[:, ft % 2, 0:n],
                                           scalar=0.5, in1=g.fc8[:, ft, 0:n],
                                           op0=ALU.mult, op1=ALU.add)
                    g.s_ve.inc(v.drain())
                blk.vector(ve2)
                vd2 = g.s_ve.v

                def se2(e, vd2=vd2):
                    e.wait_ge(g.s_ve.h, vd2)
                    e.sem_inc(g.s_se.h, 1); g.s_se.v += 1
                blk.scalar(se2)

                def gp(p, ft=ft, o=o, n=n, vd2=vd2):
                    p.wait_ge(g.s_ve.h, vd2)
                    p.wait_ge(g.s_out.h, g.s_out.v)
                    g.s_out.dma(p.dma_start(
                        out=g.out_h[ft * 128:(ft + 1) * 128, o:o + n],
                        in_=g.st_s[:, ft % 2, 0:n]))
                blk.gpsimd(gp)

    if ver == 1:
        barrier_all(g)
        for c, (o, n) in enumerate(CH5):
            def q(qq, o=o, n=n, vprev=g.s_ve.v, stv=g.s_st.v):
                qq.wait_ge(g.s_ve.h, vprev)
                qq.wait_ge(g.s_st.h, stv)
                qq.wait_ge(g.s_ld.h, g.s_ld.v)
                g.s_ld.dma(qq.dma_start(out=g.fc8[:, :, 0:n],
                                        in_=hbmT(g.x1_sp)[:, :, o:o + n]))
            blk.sync(q)
            ld = g.s_ld.v

            def ve0(v, ld=ld, stv=g.s_st.v):
                v.wait_ge(g.s_ld.h, ld)
                v.wait_ge(g.s_st.h, stv)
                g.s_ve.inc(v.drain())
            blk.vector(ve0)
            ln_chunk(g, g.fc8.ap()[:, :, 0:n], v_lnout(g, n), n)
            vd = g.s_ve.v

            def ve(v, n=n, vd=vd):
                v.wait_ge(g.s_ve.h, vd)
                for ft in range(NFT):
                    v.tensor_copy(out=v_lnb16(g)[:, ft, 0:n],
                                  in_=v_lnout(g, n)[:, ft])
                v.drain()
                for ft in range(NFT):
                    v.tensor_copy(out=v_lnb16R(g)[:, ft, 0:n],
                                  in_=v_lnout(g, n)[:, ft][:, ::-1])
                g.s_ve.inc(v.drain())
            blk.vector(ve)
            vd2 = g.s_ve.v

            def gp(p, o=o, n=n, vd2=vd2):
                p.wait_ge(g.s_ve.h, vd2)
                p.wait_ge(g.s_st.h, g.s_st.v)
                g.s_st.dma(p.dma_start(out=hbmT(g.x1ln_sp)[:, :, o:o + n],
                                       in_=v_lnout(g, n)[:, :, 0:n]))
                p.wait_ge(g.s_st.h, g.s_st.v)
                g.s_st.dma(p.dma_start(out=hbmT(g.x1lnb_sp)[:, :, o:o + n],
                                       in_=v_lnb16(g)[:, :, 0:n]))
                p.wait_ge(g.s_st.h, g.s_st.v)
                g.s_st.dma(p.dma_start(
                    out=hbmT(g.x1lnbR_sp)[:, :, TOKH - o - n:TOKH - o],
                    in_=v_lnb16R(g)[:, :, 0:n]))
            blk.gpsimd(gp)


def p4_mixer(g):
    blk = g.block
    barrier_all(g)
    for di in range(2):
        mixer_direction(g, di)
        if di == 0:
            carry_exchange(g)


def carry_exchange(g):
    blk = g.block
    vd, sd = g.s_ve.v, g.s_se.v

    def gp(p, vd=vd, sd=sd):
        p.wait_ge(g.s_ve.h, vd)
        p.wait_ge(g.s_se.h, sd)
        p.wait_ge(g.s_st.h, g.s_st.v)
        g.s_st.dma(p.dma_start(out=g.carry_io.ap(), in_=g.carry_s[:, 0]))
        p.wait_ge(g.s_st.h, g.s_st.v)
        g.s_cc.inc(p.collective_compute(
            "AllReduce", ALU.add,
            replica_groups=[[0, 1], [2, 3], [4, 5], [6, 7]],
            ins=[g.carry_io.ap().opt()], outs=[g.carry_sum.ap().opt()]))
        p.wait_ge(g.s_cc.h, g.s_cc.v)
        p.wait_ge(g.s_st.h, g.s_st.v)
        g.s_st.dma(p.dma_start(out=g.pcar_s[:, :], in_=g.carry_sum.ap()))
    blk.gpsimd(gp)
    st = g.s_st.v

    def ve(v, st=st):
        v.wait_ge(g.s_st.h, st)
        v.tensor_sub(out=g.carry_s[:, 1], in0=g.pcar_s[:, :], in1=g.carry_s[:, 0])
        g.s_ve.inc(v.drain())
    blk.vector(ve)


def mixer_direction(g, di):
    blk = g.block
    xsrc = g.x1lnb_sp if di == 0 else g.x1lnbR_sp
    wname = "A" if di == 0 else "B"

    def q(qq, xsrc=xsrc, vprev=g.s_ve.v):
        qq.wait_ge(g.s_ve.h, vprev)
        qq.wait_ge(g.s_ld.h, g.s_ld.v)
        g.s_ld.dma(qq.dma_start(out=v_xlc(g)[:, :, 0:HALO],
                                in_=hbmT(xsrc)[:, :, 0:HALO]))
    blk.sync(q)
    ldh = g.s_ld.v
    for dt in range(NDT):
        def q2(qq, dt=dt, wname=wname, tprev=g.s_tn.v):
            qq.wait_ge(g.s_tn.h, tprev)
            qq.wait_ge(g.s_ld.h, g.s_ld.v)
            g.s_ld.dma(qq.dma_start(
                out=g.wstr_s[:, dt % 2, 0:NFT, :],
                in_=g.wb[f"in_wT_{wname}"].ap().rearrange("(k p) f -> p k f", p=128)[:, :, dt * 128:(dt + 1) * 128]))
        blk.sync(q2)
        ld = g.s_ld.v

        def tn(t, dt=dt, ld=ld, ldh=ldh, sprev=g.s_se.v):
            t.wait_ge(g.s_ld.h, max(ld, ldh))
            t.wait_ge(g.s_se.h, sprev)
            for k in range(NFT):
                li = t.matmul(g.p_bc[:, 0:HALO], g.wstr_s[:, dt % 2, k, :],
                         v_xlc(g)[:, k, 0:HALO], start=(k == 0),
                         stop=(k == NFT - 1))
            g.s_tn.inc(li)
        blk.tensor(tn)
        td = g.s_tn.v

        def se(e, dt=dt, td=td):
            e.wait_ge(g.s_tn.h, td)
            g.s_se.inc(e.copy(out=g.xicar_s[:, dt, :], in_=g.p_bc[:, 0:HALO]))
        blk.scalar(se)

    for c in range(NCH):
        mixer_chunk(g, di, c, wname)


def mixer_chunk(g, di, c, wname):
    blk = g.block
    o = HALO + c * TC
    xsrc = g.x1lnb_sp if di == 0 else g.x1lnbR_sp
    corig = c if di == 0 else NCH - 1 - c
    oorig = HALO + corig * TC
    carry = g.carry_s.ap()[:, di]
    xi = v_xi(g)
    u = v_u(g)
    du = v_du(g)

    def vgate(v, stv=g.s_st.v):
        v.wait_ge(g.s_st.h, stv)
        g.s_ve.inc(v.drain())
    blk.vector(vgate)

    def sgate(e, stv=g.s_st.v):
        e.wait_ge(g.s_st.h, stv)
        e.sem_inc(g.s_se.h, 1)
    blk.scalar(sgate)
    g.s_se.v += 1

    def q(qq, o=o, xsrc=xsrc, tprev=g.s_tn.v, sprev=g.s_se.v):
        qq.wait_ge(g.s_tn.h, tprev)
        qq.wait_ge(g.s_se.h, sprev)
        qq.wait_ge(g.s_ld.h, g.s_ld.v)
        g.s_ld.dma(qq.dma_start(out=v_xlc(g)[:, :, 0:TC],
                                in_=hbmT(xsrc)[:, :, o:o + TC]))
    blk.sync(q)
    ld_x = g.s_ld.v

    tn_hist = {-2: g.s_tn.v, -1: g.s_tn.v}
    for m in range(2 * NDT):
        def q2(qq, m=m, wname=wname, tprev=tn_hist.get(m - 2, 0)):
            qq.wait_ge(g.s_tn.h, tprev)
            qq.wait_ge(g.s_lw[m % 2].h, g.s_lw[m % 2].v)
            g.s_lw[m % 2].dma(qq.dma_start(
                out=g.wstr_s[:, m % 2, 0:NFT, :],
                in_=g.wb[f"in_wT_{wname}"].ap().rearrange("(k p) f -> p k f", p=128)[:, :, m * 128:(m + 1) * 128]))
        blk.sync(q2)
        ld = g.s_lw[m % 2].v

        def tn(t, m=m, ld=ld, ld_x=ld_x, sprev=(g.s_se.v if m == 0 else g.s_se.v - 1),
               vprev=g.s_ve.v, stprev=g.s_st.v):
            t.wait_ge(g.s_lw[m % 2].h, ld)
            t.wait_ge(g.s_ld.h, ld_x)
            t.wait_ge(g.s_se.h, max(sprev, 0))
            if m == 0:
                t.wait_ge(g.s_ve.h, vprev)
                t.wait_ge(g.s_st.h, stprev)
            for k in range(NFT):
                li = t.matmul(g.p_mm[:, m % 2, :], g.wstr_s[:, m % 2, k, :],
                         v_xlc(g)[:, k, 0:TC], start=(k == 0), stop=(k == NFT - 1))
            g.s_tn.inc(li)
        blk.tensor(tn)
        td = g.s_tn.v

        if m < NDT:
            def se(e, m=m, td=td):
                e.wait_ge(g.s_tn.h, td)
                g.s_se.inc(e.copy(out=xi[:, m, HALO:HALO + TC],
                                  in_=g.p_mm[:, m % 2, :]))
            blk.scalar(se)
        else:
            def se(e, m=m, td=td, stprev=g.s_sw[m % 2].v):
                e.wait_ge(g.s_tn.h, td)
                e.wait_ge(g.s_sw[m % 2].h, stprev)
                g.s_se.inc(e.copy(out=g.st_s[:, m % 2, :],
                                  in_=g.p_mm[:, m % 2, :]))
            blk.scalar(se)
            sd = g.s_se.v

            def gp(p, m=m, corig=corig, sd=sd):
                p.wait_ge(g.s_se.h, sd)
                g.s_sw[m % 2].dma(p.dma_start(
                    out=g.z_sp.ap()[(m - NDT) * 128:(m - NDT + 1) * 128,
                                    corig * TC:(corig + 1) * TC],
                    in_=g.st_s[:, m % 2, :]))
            blk.gpsimd(gp)
        tn_hist[m] = g.s_tn.v

    sd_xi = g.s_se.v

    def ve_h(v, sd_xi=sd_xi):
        v.wait_ge(g.s_se.h, sd_xi)
        v.tensor_copy(out=xi[:, :, 0:HALO], in_=g.xicar_s[:, :, :])
        g.s_ve.inc(v.drain())
    blk.vector(ve_h)

    for dt in range(NDT):
        def ve(v, dt=dt, di=di):
            v.tensor_scalar_mul(out=u[:, dt], in0=xi[:, dt, 0:TC],
                                scalar1=g.cw_s[:, di, dt, 0:1])
            v.drain()
            for k in range(1, 4):
                v.scalar_tensor_tensor(out=u[:, dt], in0=xi[:, dt, k:k + TC],
                                       scalar=g.cw_s[:, di, dt, k:k + 1],
                                       in1=u[:, dt], op0=ALU.mult, op1=ALU.add)
                v.drain()
            v.sem_inc(g.s_ve.h, 1); g.s_ve.v += 1
        blk.vector(ve)

    def ve_t(v):
        v.tensor_copy(out=g.xicar_s[:, :, :], in_=xi[:, :, TC:TC + HALO])
        g.s_ve.inc(v.drain())
    blk.vector(ve_t)
    ve_tail = g.s_ve.v

    def se_u(e, ve_tail=ve_tail, di=di):
        e.wait_ge(g.s_ve.h, ve_tail)
        for dt in range(NDT):
            e.activation(out=u[:, dt], in_=u[:, dt], func=AF.Silu,
                         bias=g.cb_s[:, di, dt:dt + 1], scale=1.0)
        g.s_se.inc(e.drain())
    blk.scalar(se_u)
    se_silu = g.s_se.v

    for dt in range(NDT):
        def ve(v, dt=dt, se_silu=se_silu, tprev=g.s_tn.v):
            v.wait_ge(g.s_se.h, se_silu)
            v.wait_ge(g.s_tn.h, tprev)
            g.s_ve.inc(v.tensor_copy(out=g.ubt_s[:, 0, :], in_=u[:, dt]))
        blk.vector(ve)
        vd = g.s_ve.v

        def tn(t, dt=dt, vd=vd, di=di, sprev=g.s_se.v):
            t.wait_ge(g.s_ve.h, vd)
            if dt == 0:
                t.wait_ge(g.s_se.h, sprev)
            li = t.matmul(g.p_xd[:, :], g.xprojT_s[:, di, dt, :],
                     g.ubt_s[:, 0, :],
                     start=(dt == 0), stop=(dt == NDT - 1))
            g.s_tn.inc(li)
        blk.tensor(tn)
    td_xd = g.s_tn.v

    def se_xd(e, td_xd=td_xd):
        e.wait_ge(g.s_tn.h, td_xd)
        e.copy(out=g.dtb16_s[:, :], in_=g.p_xd[0:DTR, :])
        g.s_se.inc(e.copy(out=g.xdb_s[0:32, 0, :], in_=g.p_xd[DTR:96, :]))
    blk.scalar(se_xd)
    sd_xd = g.s_se.v

    for dt in range(NDT):
        def tn(t, dt=dt, sd_xd=sd_xd, di=di, sprev=g.s_se.v):
            t.wait_ge(g.s_se.h, max(sd_xd, sprev if dt >= 2 else 0))
            li = t.matmul(g.p_mm[:, dt % 2, :], g.dtwT_s[:, di, dt, :],
                     g.dtb16_s[:, :], start=True, stop=True)
            g.s_tn.inc(li)
        blk.tensor(tn)
        td = g.s_tn.v

        def se(e, dt=dt, td=td, di=di):
            e.wait_ge(g.s_tn.h, td)
            e.activation(out=du[:, dt], in_=g.p_mm[:, dt % 2, :], func=AF.Exp,
                         bias=g.dtb_s[:, di, dt:dt + 1], scale=1.0)
            e.drain()
            g.s_se.inc(e.activation(out=du[:, dt], in_=du[:, dt], func=AF.Ln,
                                    bias=g.onescol_s[:, :], scale=1.0))
        blk.scalar(se)
    sd_delta = g.s_se.v

    ve_duv = g.s_ve.v

    def ve_bc(v, sd_xd=sd_xd):
        v.wait_ge(g.s_se.h, sd_xd)
        v.tensor_copy(out=g.bclo_s[:, 0, :], in_=g.xdb_s[0:32, 0, :])
        v.drain()
        v.tensor_sub(out=g.xdb_s[0:32, 1, :], in0=g.xdb_s[0:32, 0, :],
                     in1=g.bclo_s[:, 0, :])
        v.drain()
        v.tensor_copy(out=g.bclo_s[:, 1, :], in_=g.xdb_s[0:32, 1, :])
        g.s_ve.inc(v.drain())
    blk.vector(ve_bc)
    ve_bcv = g.s_ve.v

    hall = v_hall(g)
    crep = v_crep(g)
    brep = v_brep(g)
    scan_hist = {"base": g.s_ve.v}
    for gi in range(2):
        for n in range(8):
            def tn(t, gi=gi, n=n, ve_bcv=ve_bcv, sprev=g.s_se.v):
                t.wait_ge(g.s_ve.h, ve_bcv)
                t.wait_ge(g.s_se.h, sprev)
                sn = gi * 8 + n
                li = t.matmul(g.p_bc[:, :], g.sel_s[:, sn, :], g.bclo_s[:, 0, :],
                         start=True, stop=False)
                li = t.matmul(g.p_bc[:, :], g.sel_s[:, sn, :], g.bclo_s[:, 1, :],
                              start=False, stop=True)
                g.s_tn.inc(li)
                li = t.matmul(g.p_bc2[:, :], g.sel_s[:, 16 + sn, :], g.bclo_s[:, 0, :],
                         start=True, stop=False)
                li = t.matmul(g.p_bc2[:, :], g.sel_s[:, 16 + sn, :],
                              g.bclo_s[:, 1, :], start=False, stop=True)
                g.s_tn.inc(li)
            blk.tensor(tn)
            td = g.s_tn.v

            def se(e, n=n, td=td, vprev=g.s_ve.v):
                e.wait_ge(g.s_tn.h, td)
                e.wait_ge(g.s_ve.h, vprev)
                e.copy(out=brep[:, n, :], in_=g.p_bc[:, :])
                g.s_se.inc(e.copy(
                    out=crep.rearrange("p (t n) -> p t n", n=8)[:, :, n],
                    in_=g.p_bc2[:, :]))
            blk.scalar(se)
        sd_bc = g.s_se.v

        for dt in range(NDT):
            def vdup(v, dt=dt, sd_delta=sd_delta):
                v.wait_ge(g.s_se.h, sd_delta)
                v.tensor_mul(out=g.dup_s[:, :], in0=v_du(g)[:, dt],
                             in1=v_u(g)[:, dt])
                g.s_ve.inc(v.drain())
            blk.vector(vdup)

            for n in range(8):
                sn = gi * 8 + n
                slot = n % 2

                def se(e, dt=dt, sn=sn, slot=slot, sd_bc=sd_bc, di=di,
                       vprev=scan_hist.get((gi, dt, n - 2),
                                           scan_hist.get((gi, dt - 1, 7),
                                           scan_hist.get((gi - 1, NDT - 1, 7),
                                           scan_hist["base"])))):
                    e.wait_ge(g.s_se.h, sd_bc)
                    e.wait_ge(g.s_ve.h, vprev)
                    g.s_se.inc(e.activation(
                        out=g.da_s[:, slot, :], in_=v_du(g)[:, dt],
                        func=AF.Exp, bias=0.0,
                        scale=g.A_s[:, di, dt * NS + sn:dt * NS + sn + 1]))
                blk.scalar(se)
                sd = g.s_se.v

                def ve(v, dt=dt, n=n, sn=sn, slot=slot, sd=sd, carry=carry,
                       ve_duv=ve_duv):
                    v.wait_ge(g.s_se.h, sd)
                    v.wait_ge(g.s_ve.h, ve_duv)
                    v.tensor_mul(out=g.w_sc[:, 0, :], in0=g.dup_s[:, :],
                                 in1=brep[:, n, :])
                    v.drain()
                    v.tensor_tensor_scan(
                        out=hall.rearrange("p (t n) -> p t n", n=8)[:, :, n],
                        data0=g.da_s[:, slot, :], data1=g.w_sc[:, 0, :],
                        initial=carry[:, dt * NS + sn:dt * NS + sn + 1],
                        op0=ALU.mult, op1=ALU.add)
                    g.s_ve.inc(v.drain())
                blk.vector(ve)

                if di == 0 and c == 0 and hasattr(g, "dbg_daw") and dt in (0, 1, 8):
                    def gpw(p, dt=dt, sn=sn, slot=slot, n=n, vprev=g.s_ve.v):
                        p.wait_ge(g.s_ve.h, vprev)
                        p.wait_ge(g.s_st.h, g.s_st.v)
                        row = dt * NS + sn
                        g.s_st.dma(p.dma_start(
                            out=g.dbg_daw.ap()[row, 0, :][None, :],
                            in_=g.da_s[0:1, slot, :]))
                        p.wait_ge(g.s_st.h, g.s_st.v)
                        g.s_st.dma(p.dma_start(
                            out=g.dbg_daw.ap()[row, 1, :][None, :],
                            in_=g.w_sc[0:1, 0, :]))
                        p.wait_ge(g.s_st.h, g.s_st.v)
                        ctx = g.nc.allow_non_contiguous_dma(reason="debug")
                        ctx.__enter__()
                        g.s_st.dma(p.dma_start(
                            out=g.dbg_daw.ap()[row, 2, :][None, :],
                            in_=hall.rearrange("p (t n) -> p t n", n=8)[0:1, :, n]))
                        ctx.__exit__(None, None, None)
                    blk.gpsimd(gpw)
                    def vg(v, stv=g.s_st.v):
                        v.wait_ge(g.s_st.h, stv)
                    blk.vector(vg)

            def ve2(v, dt=dt, gi=gi, carry=carry, stv=g.s_st.v):
                v.wait_ge(g.s_st.h, stv)
                v.tensor_copy(out=carry[:, dt * NS + gi * 8:dt * NS + gi * 8 + 8],
                              in_=hall[:, (TC - 1) * 8:TC * 8])
                v.drain()
                v.tensor_mul(out=hall[:, :], in0=hall[:, :], in1=crep[:, :])
                v.drain()
                if gi == 0:
                    v.tensor_reduce(out=g.yt_s[:, :],
                                    in_=hall.rearrange("p (t n) -> p t n", n=8),
                                    axis=AX.X, op=ALU.add)
                    g.s_ve.inc(v.drain())
                else:
                    v.tensor_reduce(out=g.ytmp_s[:, :],
                                    in_=hall.rearrange("p (t n) -> p t n", n=8),
                                    axis=AX.X, op=ALU.add)
                    g.s_ve.inc(v.drain())
            blk.vector(ve2)

            if gi == 0:
                def gp0(p, dt=dt, corig=corig, vprev=g.s_ve.v):
                    p.wait_ge(g.s_ve.h, vprev)
                    p.wait_ge(g.s_st.h, g.s_st.v)
                    g.s_st.dma(p.dma_start(
                        out=g.y0_sp.ap()[dt * 128:(dt + 1) * 128,
                                         corig * TC:(corig + 1) * TC],
                        in_=g.yt_s[:, :]))
                blk.gpsimd(gp0)

            if gi == 1:
                def q3(qq, dt=dt, corig=corig, vprev=g.s_ve.v, stv=g.s_st.v,
                       sw0=g.s_sw[0].v, sw1=g.s_sw[1].v):
                    qq.wait_ge(g.s_ve.h, vprev)
                    qq.wait_ge(g.s_st.h, stv)
                    qq.wait_ge(g.s_sw[0].h, sw0)
                    qq.wait_ge(g.s_sw[1].h, sw1)
                    qq.wait_ge(g.s_ld.h, g.s_ld.v)
                    g.s_ld.dma(qq.dma_start(
                        out=g.zdt_s[:, dt % 2, :],
                        in_=g.z_sp.ap()[dt * 128:(dt + 1) * 128,
                                        corig * TC:(corig + 1) * TC]))
                    qq.wait_ge(g.s_ld.h, g.s_ld.v)
                    g.s_ld.dma(qq.dma_start(
                        out=g.y0t_s[:, dt % 2, :],
                        in_=g.y0_sp.ap()[dt * 128:(dt + 1) * 128,
                                         corig * TC:(corig + 1) * TC]))
                blk.sync(q3)
                ldz = g.s_ld.v

                def se2(e, dt=dt, ldz=ldz, vprev=g.s_ve.v):
                    e.wait_ge(g.s_ld.h, ldz)
                    e.wait_ge(g.s_ve.h, vprev)
                    g.s_se.inc(e.activation(out=g.zdt_s[:, dt % 2, :],
                                            in_=g.zdt_s[:, dt % 2, :],
                                            func=AF.Silu, bias=0.0, scale=1.0))
                blk.scalar(se2)
                sdz = g.s_se.v

                def ve3(v, dt=dt, di=di, sdz=sdz, ldz=ldz):
                    v.wait_ge(g.s_se.h, sdz)
                    v.wait_ge(g.s_ld.h, ldz)
                    v.tensor_add(out=g.yt_s[:, :], in0=g.ytmp_s[:, :],
                                 in1=g.y0t_s[:, dt % 2, :])
                    v.drain()
                    v.scalar_tensor_tensor(out=g.yt_s[:, :], in0=v_u(g)[:, dt],
                                           scalar=g.Dp_s[:, di, dt:dt + 1],
                                           in1=g.yt_s[:, :], op0=ALU.mult,
                                           op1=ALU.add)
                    v.drain()
                    v.tensor_mul(out=v_gb(g, dt), in0=g.yt_s[:, :],
                                 in1=g.zdt_s[:, dt % 2, :])
                    g.s_ve.inc(v.drain())
                blk.vector(ve3)

                if di == 0 and hasattr(g, "dbg_y"):
                    def gpy(p, dt=dt, corig=corig, vprev=g.s_ve.v):
                        p.wait_ge(g.s_ve.h, vprev)
                        p.wait_ge(g.s_st.h, g.s_st.v)
                        g.s_st.dma(p.dma_start(
                            out=g.dbg_y.ap()[dt * 128:(dt + 1) * 128,
                                             corig * TC:(corig + 1) * TC],
                            in_=g.yt_s[:, :]))
                    blk.gpsimd(gpy)

    ve_gball = g.s_ve.v

    if di == 0:
        for dn, view in (("dbg_u", v_u(g)), ("dbg_dl", v_du(g))):
            if hasattr(g, dn):
                def gpd(p, dn=dn, view=view, corig=corig, vprev=g.s_ve.v):
                    p.wait_ge(g.s_ve.h, vprev)
                    p.wait_ge(g.s_st.h, g.s_st.v)
                    g.s_st.dma(p.dma_start(
                        out=getattr(g, dn).ap().rearrange("(d p) t -> p d t", p=128)[:, :, corig * TC:(corig + 1) * TC],
                        in_=view))
                blk.gpsimd(gpd)

    def q4(qq, oorig=oorig, vprev=g.s_ve.v):
        qq.wait_ge(g.s_ve.h, vprev)
        qq.wait_ge(g.s_ld.h, g.s_ld.v)
        g.s_ld.dma(qq.dma_start(out=g.fc8[:, :, :],
                                in_=hbmT(g.x1ln_sp)[:, :, oorig:oorig + TC]))
    blk.sync(q4)
    ld_xl = g.s_ld.v

    for ft in range(NFT):
        def q5(qq, ft=ft, wname=wname, tprev=g.s_tn.v):
            qq.wait_ge(g.s_tn.h, tprev)
            qq.wait_ge(g.s_ld.h, g.s_ld.v)
            g.s_ld.dma(qq.dma_start(
                out=g.wstr_s[:, ft % 2, 0:12, :],
                in_=g.wb[f"out_wT_{wname}"].ap().rearrange("(k p) f -> p k f", p=128)[:, 0:12, ft * 128:(ft + 1) * 128]))
            qq.wait_ge(g.s_ld.h, g.s_ld.v)
            g.s_ld.dma(qq.dma_start(
                out=g.wstr_s[:, (ft + 1) % 2, 0:4, :],
                in_=g.wb[f"out_wT_{wname}"].ap().rearrange("(k p) f -> p k f", p=128)[:, 12:16, ft * 128:(ft + 1) * 128]))
        blk.sync(q5)
        ld = g.s_ld.v

        def tn(t, ft=ft, ld=ld, ve_gball=ve_gball, vprev=g.s_ve.v):
            t.wait_ge(g.s_ld.h, ld)
            t.wait_ge(g.s_ve.h, ve_gball)
            if ft >= 2:
                t.wait_ge(g.s_ve.h, vprev)
            for k in range(12):
                li = t.matmul(g.p_mm[:, ft % 2, :], g.wstr_s[:, ft % 2, k, :],
                         v_gb(g, k), start=(k == 0), stop=False)
            for k in range(12, 16):
                li = t.matmul(g.p_mm[:, ft % 2, :],
                         g.wstr_s[:, (ft + 1) % 2, k - 12, :],
                         v_gb(g, k), start=False, stop=(k == 15))
            g.s_tn.inc(li)
        blk.tensor(tn)
        td = g.s_tn.v

        def ve(v, ft=ft, di=di, td=td, ld_xl=ld_xl):
            v.wait_ge(g.s_tn.h, td)
            v.wait_ge(g.s_ld.h, ld_xl)
            src = g.p_mm[:, ft % 2, :] if di == 0 else g.p_mm[:, ft % 2, ::-1]
            v.scalar_tensor_tensor(out=v_mo(g)[:, ft, :], in0=src, scalar=1.0,
                                   in1=g.fc8[:, ft, :], op0=ALU.mult,
                                   op1=ALU.add)
            g.s_ve.inc(v.drain())
        blk.vector(ve)
    ve_mo = g.s_ve.v

    for ft in range(NFT):
        def q6(qq, ft=ft, wname=wname, tprev=g.s_tn.v):
            qq.wait_ge(g.s_tn.h, tprev)
            qq.wait_ge(g.s_ld.h, g.s_ld.v)
            g.s_ld.dma(qq.dma_start(
                out=g.wstr_s[:, ft % 2, 0:NFT, :],
                in_=g.wb[f"projT_{wname}"].ap().rearrange("(k p) f -> p k f", p=128)[:, :, ft * 128:(ft + 1) * 128]))
        blk.sync(q6)
        ld = g.s_ld.v

        def tn(t, ft=ft, ld=ld, ve_mo=ve_mo, sprev=g.s_se.v, vprev=g.s_ve.v):
            t.wait_ge(g.s_ld.h, ld)
            t.wait_ge(g.s_ve.h, ve_mo)
            if ft >= 2:
                t.wait_ge(g.s_se.h, sprev)
                t.wait_ge(g.s_ve.h, vprev)
            for k in range(NFT):
                li = t.matmul(g.p_mm2[:, ft % 2, :], g.wstr_s[:, ft % 2, k, :],
                         v_mo(g)[:, k, :], start=(k == 0), stop=(k == NFT - 1))
            g.s_tn.inc(li)
        blk.tensor(tn)
        td = g.s_tn.v

        if di == 0:
            def se(e, ft=ft, td=td, stprev=g.s_st.v):
                e.wait_ge(g.s_tn.h, td)
                e.wait_ge(g.s_st.h, stprev)
                g.s_se.inc(e.copy(out=g.st_s[:, ft % 2, :],
                                  in_=g.p_mm2[:, ft % 2, :]))
            blk.scalar(se)
            sd = g.s_se.v

            def gp(p, ft=ft, corig=corig, sd=sd):
                p.wait_ge(g.s_se.h, sd)
                p.wait_ge(g.s_st.h, g.s_st.v)
                g.s_st.dma(p.dma_start(
                    out=hbmT(g.projA_sp)[:, ft, corig * TC:(corig + 1) * TC],
                    in_=g.st_s[:, ft % 2, :]))
            blk.gpsimd(gp)
        else:
            def q7(qq, ft=ft, corig=corig, vprev=g.s_ve.v):
                qq.wait_ge(g.s_ve.h, vprev)
                qq.wait_ge(g.s_ld.h, g.s_ld.v)
                g.s_ld.dma(qq.dma_start(
                    out=g.st_s[:, ft % 2, :],
                    in_=hbmT(g.projA_sp)[:, ft, corig * TC:(corig + 1) * TC]))
            blk.sync(q7)
            ld2 = g.s_ld.v

            def ve(v, ft=ft, td=td, ld2=ld2):
                v.wait_ge(g.s_tn.h, td)
                v.wait_ge(g.s_ld.h, ld2)
                v.scalar_tensor_tensor(out=v_pp(g)[:, ft, :],
                                       in0=g.p_mm2[:, ft % 2, :],
                                       scalar=g.pb_s[:, ft:ft + 1],
                                       in1=g.st_s[:, ft % 2, :],
                                       op0=ALU.add, op1=ALU.add)
                g.s_ve.inc(v.drain())
            blk.vector(ve)

    if di == 1:
        mixer_epilogue(g, corig, oorig)


def mixer_epilogue(g, corig, oorig):
    blk = g.block
    pp = v_pp(g)
    x2 = v_x2c(g)
    ln_chunk(g, pp, pp, TC)

    def q(qq, oorig=oorig, vprev=g.s_ve.v):
        qq.wait_ge(g.s_ve.h, vprev)
        qq.wait_ge(g.s_ld.h, g.s_ld.v)
        g.s_ld.dma(qq.dma_start(out=g.fc8[:, :, :],
                                in_=hbmT(g.x1_sp)[:, :, oorig:oorig + TC]))
    blk.sync(q)
    ld = g.s_ld.v

    def ve(v, ld=ld):
        v.wait_ge(g.s_ld.h, ld)
        for ft in range(NFT):
            v.tensor_scalar(out=pp[:, ft, :], in0=pp[:, ft, :],
                            scalar1=g.lnw_s[:, ft:ft + 1],
                            scalar2=g.lnb_s[:, ft:ft + 1],
                            op0=ALU.mult, op1=ALU.add)
        v.drain()
        v.tensor_tensor(out=x2, in0=pp, in1=g.fc8.ap()[:, :, :],
                        op=ALU.add)
        g.s_ve.inc(v.drain())
    blk.vector(ve)
    vd = g.s_ve.v

    def gp(p, corig=corig, vd=vd):
        p.wait_ge(g.s_ve.h, vd)
        p.wait_ge(g.s_st.h, g.s_st.v)
        g.s_st.dma(p.dma_start(
            out=hbmT(g.x2_sp)[:, :, corig * TC:(corig + 1) * TC],
            in_=x2[:, :, :]))
    blk.gpsimd(gp)

    ln_chunk(g, x2, pp, TC)
    absmax_acc(g, pp, SC_X2, axis=AX.XY)
    vd2 = g.s_ve.v

    def gp2(p, corig=corig, vd2=vd2):
        p.wait_ge(g.s_ve.h, vd2)
        p.wait_ge(g.s_st.h, g.s_st.v)
        g.s_st.dma(p.dma_start(
            out=hbmT(g.x2ln_sp)[:, :, corig * TC:(corig + 1) * TC],
            in_=pp[:, :, :]))
    blk.gpsimd(gp2)


def finish(g):
    blk = g.block

    def gp(p):
        p.wait_ge(g.s_out.h, g.s_out.v)
        p.wait_ge(g.s_st.h, g.s_st.v)
    blk.gpsimd(gp)



def kernel(**inputs):
    nc = build_nc()
    in_maps = make_in_maps(inputs)
    res = run_bass_kernel_spmd(nc, in_maps, core_ids=list(range(8)))
    return assemble([r for r in res.results])


def make_in_maps(inputs):
    i = inputs
    x = i["L0_attn_x"]
    w = {}
    for pre in ("fw", "bw"):
        w[f"in_wT_{pre}"] = np.ascontiguousarray(i[f"L0_attn_{pre}_in_w"].T)
        w[f"conv_w_{pre}"] = np.ascontiguousarray(i[f"L0_attn_{pre}_conv_w"][:, 0, :])
        w[f"conv_b_{pre}"] = np.ascontiguousarray(i[f"L0_attn_{pre}_conv_b"].reshape(DI, 1))
        w[f"xproj_wT_{pre}"] = np.ascontiguousarray(i[f"L0_attn_{pre}_xproj_w"].T)
        w[f"dt_wT_{pre}"] = np.ascontiguousarray(i[f"L0_attn_{pre}_dt_w"].T)
        w[f"dt_b_{pre}"] = np.ascontiguousarray(i[f"L0_attn_{pre}_dt_b"].reshape(DI, 1))
        w[f"Alog_{pre}"] = np.ascontiguousarray(i[f"L0_attn_{pre}_Alog"])
        w[f"Dp_{pre}"] = np.ascontiguousarray(i[f"L0_attn_{pre}_Dp"].reshape(DI, 1))
        w[f"out_wT_{pre}"] = np.ascontiguousarray(i[f"L0_attn_{pre}_out_w"].T)
    pT = np.ascontiguousarray(i["L0_attn_proj_w"].T)
    w["projT_fw"] = np.ascontiguousarray(pT[:D, :])
    w["projT_bw"] = np.ascontiguousarray(pT[D:, :])

    shared = dict(
        proj_b=np.ascontiguousarray(i["L0_attn_proj_b"].reshape(D, 1)),
        ln_w=np.ascontiguousarray(i["L0_attn_ln_w"].reshape(D, 1)),
        ln_b=np.ascontiguousarray(i["L0_attn_ln_b"].reshape(D, 1)),
        w_gate=np.ascontiguousarray(i["L0_ffn_w_gate"]),
        w_up=np.ascontiguousarray(i["L0_ffn_w_up"]),
        w_down=np.ascontiguousarray(i["L0_ffn_w_down"]),
    )
    maps = []
    for c in range(8):
        b, h = c // 2, c % 2
        xb = x[b].T
        lo, hi = h * TOK - HALO, h * TOK + TOK + HALO
        pad_l, pad_r = max(0, -lo), max(0, hi - LFULL)
        sl = xb[:, max(0, lo):min(LFULL, hi)]
        if pad_l or pad_r:
            sl = np.pad(sl, ((0, 0), (pad_l, pad_r)))
        if h == 1:
            sl = sl[:, ::-1]
        m = dict(shared)
        m["xT"] = np.ascontiguousarray(sl)
        dirA = "fw" if h == 0 else "bw"
        for key, val in w.items():
            base, pre = key.rsplit("_", 1)
            tgt = "A" if pre == dirA else "B"
            m[f"{base}_{tgt}"] = val
        maps.append(m)
    return maps


def assemble(results):
    out = np.empty((B, LFULL, D), np.float32)
    for c in range(8):
        b, h = c // 2, c % 2
        o = results[c]["out"]
        if h == 1:
            o = o[:, ::-1]
        out[b, h * TOK:(h + 1) * TOK, :] = o.T
    return out


# version 07
# speedup vs baseline: 1.2543x; 1.0034x over previous
import contextlib
import numpy as np

import concourse.bass as bass
import concourse.mybir as mybir
from concourse.bass_utils import run_bass_kernel_spmd

F32 = mybir.dt.float32
F16 = mybir.dt.float16
BF16 = mybir.dt.bfloat16
AF = mybir.ActivationFunctionType
ALU = mybir.AluOpType
AX = mybir.AxisListType

D = 1024
DI = 2048
NDT = 16
NFT = 8
DTR = 64
NS = 16
FF = 3072
NMT = 24
B = 4
LFULL = 4096
TOK = 2048
HALO = 3
TOKH = TOK + 2 * HALO
TC = 512
NCH = 4
CH5 = [(0, 512), (512, 512), (1024, 512), (1536, 512), (2048, 6)]
CH4 = [(i * TC, TC) for i in range(NCH)]

QMAX = 32767.0
WQMAX = 7.0
MAGIC = float(np.float32(1.5 * 2 ** 23))
LN_EPS = 1e-5

SC_X, SC_UP, SC_GATE, SC_GS, SC_T5 = 0, 1, 2, 3, 4
SC_X2, SC_UP2, SC_GATE2, SC_GS2, SC_T52 = 5, 6, 7, 8, 9
N_SCAL = 10
TMPC = 15


class Sem:
    def __init__(self, es, nc, name):
        self.h = es.enter_context(nc.semaphore(name))
        self.v = 0

    def inc(self, instr, n=1):
        instr.then_inc(self.h, n)
        self.v += n

    def dma(self, instr):
        self.inc(instr, 16)


class G:
    pass


def build_nc(taps=()):
    nc = bass.Bass("TRN2", target_bir_lowering=False,
                   detect_race_conditions=False)
    es = contextlib.ExitStack()
    g = G()
    g.nc = nc
    g.taps = list(taps)

    g.xT = nc.dram_tensor("xT", [D, TOKH], F32, kind="ExternalInput")
    g.inp = {}
    for d in ("A", "B"):
        for nm, shape in (("in_wT", [D, 2 * DI]), ("conv_w", [DI, 4]),
                          ("conv_b", [DI, 1]), ("xproj_wT", [DI, 96]),
                          ("dt_wT", [DTR, DI]), ("dt_b", [DI, 1]),
                          ("Alog", [DI, NS]), ("Dp", [DI, 1]),
                          ("out_wT", [DI, D]), ("projT", [D, D])):
            g.inp[f"{nm}_{d}"] = nc.dram_tensor(f"{nm}_{d}", shape, F32,
                                                kind="ExternalInput").ap()
    g.proj_b = nc.dram_tensor("proj_b", [D, 1], F32, kind="ExternalInput").ap()
    g.ln_w = nc.dram_tensor("ln_w", [D, 1], F32, kind="ExternalInput").ap()
    g.ln_b = nc.dram_tensor("ln_b", [D, 1], F32, kind="ExternalInput").ap()
    g.w_up = nc.dram_tensor("w_up", [FF, D], F32, kind="ExternalInput").ap()
    g.w_gate = nc.dram_tensor("w_gate", [FF, D], F32, kind="ExternalInput").ap()
    g.w_down = nc.dram_tensor("w_down", [D, FF], F32, kind="ExternalInput").ap()
    g.out_h = nc.dram_tensor("out", [D, TOK], F32, kind="ExternalOutput").ap()

    def dram(name, shape, dt):
        if name in g.taps:
            t = nc.dram_tensor(name, shape, dt, kind="ExternalOutput")
        else:
            t = nc.dram_tensor(name, shape, dt)
        setattr(g, name, t)
        return t

    dram("wq_upT", [D, FF], BF16)
    dram("wq_gT", [D, FF], BF16)
    dram("wq_dT", [FF, D], BF16)
    g.wb = {}
    for d in ("A", "B"):
        g.wb[f"in_wT_{d}"] = dram(f"b_in_wT_{d}", [D, 2 * DI], BF16)
        g.wb[f"out_wT_{d}"] = dram(f"b_out_wT_{d}", [DI, D], BF16)
        g.wb[f"projT_{d}"] = dram(f"b_projT_{d}", [D, D], BF16)
    dram("xln_sp", [D, TOKH], F32)
    dram("x1_sp", [D, TOKH], F32)
    dram("x1ln_sp", [D, TOKH], F32)
    dram("x1lnb_sp", [D, TOKH], BF16)
    dram("x1lnbR_sp", [D, TOKH], BF16)
    dram("x2_sp", [D, TOK], F32)
    dram("x2ln_sp", [D, TOK], F32)
    dram("up_sp", [FF, TOKH], F16)
    dram("gate_sp", [FF, TOKH], F16)
    dram("gs_sp", [FF, TOKH], F16)
    dram("t5_sp", [FF, TOKH], F16)
    dram("z_sp", [DI, TOK], F32)
    dram("projA_sp", [D, TOK], F32)
    dram("y0_sp", [DI, TOK], F32)
    g.scal_in = [dram(f"scin{i}", [1, 128], F32) for i in range(N_SCAL)]
    g.scal_out = [dram(f"scout{i}", [1, 128], F32) for i in range(N_SCAL)]
    dram("carry_io", [128, NDT * NS], F32)
    for dn in ("dbg_u", "dbg_dl", "dbg_y"):
        if dn in g.taps:
            dram(dn, [DI, TOK], F32)
    if "dbg_daw" in g.taps:
        dram("dbg_daw", [NDT * NS, 3, TC], F32)
    dram("carry_sum", [128, NDT * NS], F32)

    E = es.enter_context

    def sbuf(name, shape, dt):
        t = E(nc.sbuf_tensor(name, shape, dt))
        setattr(g, name, t)
        return t

    sbuf("ar3", [128, 16512], BF16)
    sbuf("aru", [128, 8192], F32)
    sbuf("arb", [128, 24576], F16)
    sbuf("fc8", [128, NFT, TC], F32)
    sbuf("zdt_s", [128, 2, TC], F32)
    sbuf("ubt_s", [128, 1, TC], BF16)
    sbuf("yt_s", [128, TC], F32)
    sbuf("y0t_s", [128, 2, TC], F32)
    sbuf("dup_s", [128, TC], F32)
    sbuf("ytmp_s", [128, TC], F32)
    sbuf("da_s", [128, 2, TC], F32)
    sbuf("w_sc", [128, 1, TC], F32)
    sbuf("wstr_s", [128, 2, 12, 128], BF16)
    sbuf("st_s", [128, 2, 512], F32)
    sbuf("stb_s", [128, 2, 512], F16)
    sbuf("g16_s", [128, 2, 512], F16)
    sbuf("h16_s", [128, 2, 512], F16)
    sbuf("sg_s", [128, 512], F32)
    sbuf("sg2_s", [128, 512], F32)
    sbuf("dq_s", [128, 96, 3], F32)
    sbuf("xdb_s", [96, 2, TC], F32)
    sbuf("dtb16_s", [DTR, TC], BF16)
    sbuf("bclo_s", [32, 2, TC], BF16)
    sbuf("sel_s", [32, 32, 128], BF16)
    sbuf("xicar_s", [128, NDT, HALO], F32)
    sbuf("ones_s", [1, 128], BF16)
    sbuf("onesf_s", [1, 128], F32)
    sbuf("onescol_s", [128, 1], F32)
    sbuf("ident_s", [128, 128], BF16)
    sbuf("identf_s", [128, 128], F32)
    sbuf("A_s", [128, 2, NDT * NS], F32)
    sbuf("Dp_s", [128, 2, NDT], F32)
    sbuf("cb_s", [128, 2, NDT], F32)
    sbuf("cw_s", [128, 2, NDT, 4], F32)
    sbuf("dtb_s", [128, 2, NDT], F32)
    sbuf("pb_s", [128, NFT], F32)
    sbuf("lnw_s", [128, NFT], F32)
    sbuf("lnb_s", [128, NFT], F32)
    sbuf("xprojT_s", [128, 2, NDT, 96], BF16)
    sbuf("dtwT_s", [DTR, 2, NDT, 128], BF16)
    sbuf("carry_s", [128, 2, NDT * NS], F32)
    sbuf("pcar_s", [128, NDT * NS], F32)
    sbuf("mx_s", [128, 16], F32)
    sbuf("sc_s", [128, N_SCAL], F32)
    sbuf("ss_s", [128, N_SCAL], F32)
    sbuf("t0_s", [1, 128], F32)
    sbuf("eps1_s", [1, 1], F32)

    def psum(name, shape, dt):
        t = E(nc.psum_tensor(name, shape, dt))
        setattr(g, name, t)

    psum("p_mm", [128, 2, 512], F32)
    psum("p_mm2", [128, 2, 512], F32)
    psum("p_bc", [128, 512], F32)
    psum("p_bc2", [128, 512], F32)
    psum("p_xd", [96, TC], F32)
    psum("p_tr", [128, 128], BF16)

    for s in ("s_ld", "s_st", "s_tn", "s_ve", "s_se", "s_ge", "s_cc",
              "s_out", "s_ld2"):
        setattr(g, s, Sem(es, nc, s))
    g.s_lw = [Sem(es, nc, f"s_lw{i}") for i in range(2)]
    g.s_lx = [Sem(es, nc, f"s_lx{i}") for i in range(2)]
    g.s_sw = [Sem(es, nc, f"s_sw{i}") for i in range(2)]

    with nc.Block() as block:
        g.block = block
        prep_constants(g)
        prep_weights(g)
        p1_lnq(g, src="x")
        ffn_mid(g, ver=1)
        p4_mixer(g)
        p1_lnq(g, src="x2")
        ffn_mid(g, ver=2)
        finish(g)
    es.close()
    return nc



def v_qb(g):
    return g.ar3.ap()[:, 0:NFT * TOKH].rearrange("p (f t) -> p f t", f=NFT)


def v_qb2(g):
    return g.ar3.ap()[:, 0:NFT * TOK].rearrange("p (f t) -> p f t", f=NFT)


def v_xi(g):
    return g.ar3.ap().bitcast(F32)[:, 0:NDT * (TC + HALO)].rearrange(
        "p (d t) -> p d t", d=NDT)


def v_du(g):
    return g.ar3.ap().bitcast(F32)[:, 0:NDT * TC].rearrange(
        "p (d t) -> p d t", d=NDT)


def v_gb(g, k=None):
    a = g.ar3.ap()[:, 0:NDT * TC].rearrange("p (d t) -> p d t", d=NDT)
    return a if k is None else a[:, k]


def v_u(g):
    return g.aru.ap().rearrange("p (d t) -> p d t", d=NDT)


def v_sq(g, n):
    return g.aru.ap()[:, 0:NFT * n].rearrange("p (f t) -> p f t", f=NFT)


def v_rows(g):
    return g.aru.ap()[0:1, NFT * TC:NFT * TC + 5 * 512].rearrange(
        "p (r t) -> p r t", r=5)


def v_abl(g):
    o = 2 * (NFT * TC + 5 * 512)
    return g.aru.ap().bitcast(BF16)[0:1, o:o + 4 * 512].rearrange(
        "p (r t) -> p r t", r=4)


def v_brep(g):
    return g.arb.ap().bitcast(F32)[:, 0:8 * TC].rearrange("p (n t) -> p n t", n=8)


def v_crep(g):
    return g.arb.ap().bitcast(F32)[:, 8 * TC:16 * TC]


def v_hall(g):
    return g.arb.ap().bitcast(F32)[:, 16 * TC:24 * TC]


def v_wraw(g):
    return g.arb.ap().bitcast(F32)[:, 0:4096]


def v_wqb(g):
    return g.arb.ap().bitcast(BF16)[:, 8192:8192 + 4096]


def v_lnout(g, n):
    return g.arb.ap().bitcast(F32)[:, 0:NFT * n].rearrange(
        "p (f t) -> p f t", f=NFT)


def v_pp(g):
    return g.arb.ap().bitcast(F32)[:, 16 * TC:16 * TC + NFT * TC].rearrange(
        "p (f t) -> p f t", f=NFT)


def v_x2c(g):
    return g.arb.ap().bitcast(F32)[:, 8 * TC:8 * TC + NFT * TC].rearrange(
        "p (f t) -> p f t", f=NFT)


def v_xlc(g):
    return g.arb.ap().bitcast(BF16)[:, 0:NFT * TC].rearrange(
        "p (f t) -> p f t", f=NFT)


def v_mo(g):
    return g.arb.ap().bitcast(BF16)[:, NFT * TC:2 * NFT * TC].rearrange(
        "p (f t) -> p f t", f=NFT)


def v_lnb16(g):
    return g.arb.ap().bitcast(BF16)[:, 8 * NFT * TC // 4:8 * NFT * TC // 4 + NFT * TC].rearrange(
        "p (f t) -> p f t", f=NFT)


def v_lnb16R(g):
    o = 8 * NFT * TC // 4 + NFT * TC
    return g.arb.ap().bitcast(BF16)[:, o:o + NFT * TC].rearrange(
        "p (f t) -> p f t", f=NFT)


def v_oq(g):
    return g.arb.ap().bitcast(BF16)[:, 0:NMT * 512].rearrange(
        "p (m t) -> p m t", m=NMT)


def hbmT(t):
    return t.ap().rearrange("(f p) t -> p f t", p=128)



def barrier_all(g):
    blk = g.block
    snap = [(g.s_ld, g.s_ld.v), (g.s_st, g.s_st.v), (g.s_tn, g.s_tn.v),
            (g.s_ve, g.s_ve.v), (g.s_se, g.s_se.v), (g.s_ge, g.s_ge.v),
            (g.s_cc, g.s_cc.v), (g.s_ld2, g.s_ld2.v), (g.s_out, g.s_out.v)]
    snap += [(s, s.v) for s in g.s_lw + g.s_lx + g.s_sw]
    for eng in ("sync", "vector", "scalar", "tensor", "gpsimd"):
        def body(e, snap=snap):
            for s, v in snap:
                if v:
                    e.wait_ge(s.h, v)
        getattr(blk, eng)(body)


def ln_chunk(g, src_ap, out_ap, n):
    blk = g.block
    sq = v_sq(g, n)
    ve0 = g.s_ve.v

    def se(e, ve0=ve0):
        e.wait_ge(g.s_ve.h, ve0)
        e.activation(out=sq, in_=src_ap, func=AF.Square, bias=0.0, scale=1.0)
        g.s_se.inc(e.drain())
    blk.scalar(se)
    sd = g.s_se.v

    def tn(t, sd=sd, n=n):
        t.wait_ge(g.s_se.h, sd)
        for ft in range(NFT):
            li = t.matmul(g.p_xd[0:1, 0:n], g.onescol_s[:, :], src_ap[:, ft],
                          start=(ft == 0), stop=(ft == NFT - 1))
        g.s_tn.inc(li)
        for ft in range(NFT):
            li = t.matmul(g.p_bc[0:1, 0:n], g.onescol_s[:, :], sq[:, ft],
                          start=(ft == 0), stop=(ft == NFT - 1))
        g.s_tn.inc(li)
    blk.tensor(tn)
    td = g.s_tn.v
    rows = v_rows(g)
    abl = v_abl(g)

    def ve(v, td=td, n=n):
        v.wait_ge(g.s_tn.h, td)
        v.tensor_scalar_mul(out=rows[:, 0, 0:n], in0=g.p_xd[0:1, 0:n],
                            scalar1=1.0 / D)
        v.tensor_scalar_mul(out=rows[:, 1, 0:n], in0=g.p_bc[0:1, 0:n],
                            scalar1=1.0 / D)
        v.drain()
        v.tensor_tensor(out=rows[:, 2, 0:n], in0=rows[:, 0, 0:n],
                        in1=rows[:, 0, 0:n], op=ALU.mult)
        v.drain()
        v.tensor_sub(out=rows[:, 2, 0:n], in0=rows[:, 1, 0:n],
                     in1=rows[:, 2, 0:n])
        g.s_ve.inc(v.drain())
    blk.vector(ve)
    vd = g.s_ve.v

    def se2(e, vd=vd, n=n):
        e.wait_ge(g.s_ve.h, vd)
        g.s_se.inc(e.activation(out=rows[:, 3, 0:n], in_=rows[:, 2, 0:n],
                                func=AF.Sqrt, bias=g.eps1_s[:, :], scale=1.0))
    blk.scalar(se2)
    sd2 = g.s_se.v

    def ve2(v, sd2=sd2, n=n):
        v.wait_ge(g.s_se.h, sd2)
        v.reciprocal(out=rows[:, 3, 0:n], in_=rows[:, 3, 0:n])
        v.drain()
        v.scalar_tensor_tensor(out=rows[:, 4, 0:n], in0=rows[:, 0, 0:n],
                               scalar=-1.0, in1=rows[:, 3, 0:n],
                               op0=ALU.mult, op1=ALU.mult)
        v.drain()
        v.tensor_copy(out=abl[:, 0, 0:n], in_=rows[:, 3, 0:n])
        v.tensor_copy(out=abl[:, 1, 0:n], in_=rows[:, 4, 0:n])
        v.drain()
        v.tensor_sub(out=rows[:, 0, 0:n], in0=rows[:, 3, 0:n], in1=abl[:, 0, 0:n])
        v.tensor_sub(out=rows[:, 1, 0:n], in0=rows[:, 4, 0:n], in1=abl[:, 1, 0:n])
        v.drain()
        v.tensor_copy(out=abl[:, 2, 0:n], in_=rows[:, 0, 0:n])
        v.tensor_copy(out=abl[:, 3, 0:n], in_=rows[:, 1, 0:n])
        g.s_ve.inc(v.drain())
    blk.vector(ve2)
    vd2 = g.s_ve.v

    def tn2(t, vd2=vd2, n=n):
        t.wait_ge(g.s_ve.h, vd2)
        li = t.matmul(g.p_bc[:, 0:n], g.ones_s[:, :], abl[:, 0, 0:n], start=True, stop=False)
        li = t.matmul(g.p_bc[:, 0:n], g.ones_s[:, :], abl[:, 2, 0:n], start=False, stop=True)
        g.s_tn.inc(li)
        li = t.matmul(g.p_bc2[:, 0:n], g.ones_s[:, :], abl[:, 1, 0:n], start=True, stop=False)
        li = t.matmul(g.p_bc2[:, 0:n], g.ones_s[:, :], abl[:, 3, 0:n], start=False, stop=True)
        g.s_tn.inc(li)
    blk.tensor(tn2)
    td2 = g.s_tn.v

    def ve3(v, td2=td2, n=n):
        v.wait_ge(g.s_tn.h, td2)
        v.tensor_tensor(out=out_ap, in0=src_ap,
                        in1=g.p_bc.ap()[:, None, 0:n].broadcast_to([128, NFT, n]),
                        op=ALU.mult)
        v.drain()
        v.tensor_tensor(out=out_ap, in0=out_ap,
                        in1=g.p_bc2.ap()[:, None, 0:n].broadcast_to([128, NFT, n]),
                        op=ALU.add)
        g.s_ve.inc(v.drain())
    blk.vector(ve3)
    return g.s_ve.v


def absmax_acc(g, src_ap, slot, axis=AX.X, wait_se=None):
    blk = g.block

    def ve(v, slot=slot, axis=axis):
        if wait_se is not None:
            v.wait_ge(g.s_se.h, wait_se)
        v.tensor_reduce(out=g.mx_s[:, TMPC:TMPC + 1], in_=src_ap, axis=axis,
                        op=ALU.max, apply_absolute_value=True)
        v.drain()
        v.tensor_max(out=g.mx_s[:, slot:slot + 1], in0=g.mx_s[:, slot:slot + 1],
                     in1=g.mx_s[:, TMPC:TMPC + 1])
        g.s_ve.inc(v.drain())
    blk.vector(ve)
    return g.s_ve.v


def scale_exchange(g, slot, qmax):
    blk = g.block
    ve_prev = g.s_ve.v

    def tn(t, slot=slot, ve_prev=ve_prev):
        t.wait_ge(g.s_ve.h, ve_prev)
        g.s_tn.inc(t.transpose(g.p_bc[0:1, 0:128], g.mx_s[:, slot:slot + 1],
                               g.identf_s[:]))
    blk.tensor(tn)
    td = g.s_tn.v

    def ve(v, td=td):
        v.wait_ge(g.s_tn.h, td)
        v.reduce_max(out=g.t0_s[:, 0:1], in_=g.p_bc[0:1, 0:128], axis=AX.X)
        v.drain()
        g.s_ve.inc(v.tensor_copy(
            out=g.t0_s[:, :], in_=g.t0_s.ap()[:, 0:1].broadcast_to([1, 128])))
    blk.vector(ve)
    vd = g.s_ve.v

    def gp(p, slot=slot, vd=vd, sw0=g.s_sw[0].v, sw1=g.s_sw[1].v):
        p.wait_ge(g.s_ve.h, vd)
        p.wait_ge(g.s_sw[0].h, sw0)
        p.wait_ge(g.s_sw[1].h, sw1)
        p.wait_ge(g.s_st.h, g.s_st.v)
        g.s_st.dma(p.dma_start(out=g.scal_in[slot].ap(), in_=g.t0_s[:, :]))
        p.wait_ge(g.s_st.h, g.s_st.v)
        g.s_cc.inc(p.collective_compute(
            "AllReduce", ALU.max, replica_groups=[list(range(8))],
            ins=[g.scal_in[slot].ap().opt()], outs=[g.scal_out[slot].ap().opt()]))
        p.wait_ge(g.s_cc.h, g.s_cc.v)
        p.wait_ge(g.s_st.h, g.s_st.v)
        g.s_st.dma(p.dma_start(
            out=g.ss_s[:, slot:slot + 1],
            in_=g.scal_out[slot].ap().rearrange("o p -> (o p)")[:, None]))
    blk.gpsimd(gp)
    std = g.s_st.v

    def ve2(v, slot=slot, std=std, qmax=qmax):
        v.wait_ge(g.s_st.h, std)
        v.tensor_scalar(out=g.ss_s[:, slot:slot + 1], in0=g.ss_s[:, slot:slot + 1],
                        scalar1=1.0 / qmax, scalar2=1e-12, op0=ALU.mult,
                        op1=ALU.max)
        v.drain()
        v.reciprocal(out=g.sc_s[:, slot:slot + 1], in_=g.ss_s[:, slot:slot + 1])
        g.s_ve.inc(v.drain())
    blk.vector(ve2)
    return g.s_ve.v



def prep_constants(g):
    blk = g.block
    nc = g.nc

    def q(qq):
        ctx = nc.allow_non_contiguous_dma(reason="tiny per-channel param loads")
        ctx.__enter__()
        for i, d in enumerate(("A", "B")):
            qq.wait_ge(g.s_ld.h, g.s_ld.v)
            g.s_ld.dma(qq.dma_start(
                out=g.A_s.ap()[:, i].rearrange("p (a n) -> p a n", a=NDT),
                in_=g.inp[f"Alog_{d}"].rearrange("(a p) n -> p a n", p=128)))
            qq.wait_ge(g.s_ld.h, g.s_ld.v)
            g.s_ld.dma(qq.dma_start(
                out=g.Dp_s.ap()[:, i, :, None],
                in_=g.inp[f"Dp_{d}"].rearrange("(a p) o -> p a o", p=128)))
            qq.wait_ge(g.s_ld.h, g.s_ld.v)
            g.s_ld.dma(qq.dma_start(
                out=g.cb_s.ap()[:, i, :, None],
                in_=g.inp[f"conv_b_{d}"].rearrange("(a p) o -> p a o", p=128)))
            qq.wait_ge(g.s_ld.h, g.s_ld.v)
            g.s_ld.dma(qq.dma_start(out=g.cw_s[:, i],
                                    in_=g.inp[f"conv_w_{d}"].rearrange("(a p) k -> p a k", p=128)))
            qq.wait_ge(g.s_ld.h, g.s_ld.v)
            g.s_ld.dma(qq.dma_start(
                out=g.dtb_s.ap()[:, i, :, None],
                in_=g.inp[f"dt_b_{d}"].rearrange("(a p) o -> p a o", p=128)))
        qq.wait_ge(g.s_ld.h, g.s_ld.v)
        g.s_ld.dma(qq.dma_start(out=g.pb_s.ap()[:, :, None],
                                in_=g.proj_b.rearrange("(a p) o -> p a o", p=128)))
        qq.wait_ge(g.s_ld.h, g.s_ld.v)
        g.s_ld.dma(qq.dma_start(out=g.lnw_s.ap()[:, :, None],
                                in_=g.ln_w.rearrange("(a p) o -> p a o", p=128)))
        qq.wait_ge(g.s_ld.h, g.s_ld.v)
        g.s_ld.dma(qq.dma_start(out=g.lnb_s.ap()[:, :, None],
                                in_=g.ln_b.rearrange("(a p) o -> p a o", p=128)))
        ctx.__exit__(None, None, None)
    blk.sync(q)
    ld0 = g.s_ld.v

    def ve(v):
        v.memset(g.ones_s[:], 1.0)
        v.memset(g.onesf_s[:], 1.0)
        v.memset(g.onescol_s[:], 1.0)
        v.memset(g.identf_s[:], 1.0)
        v.memset(g.ident_s[:], 1.0)
        v.memset(g.carry_s[:], 0.0)
        v.memset(g.mx_s[:], 0.0)
        v.memset(g.eps1_s[:], LN_EPS)
        v.memset(g.sel_s[:], 1.0)
        g.s_ve.inc(v.drain())
    blk.vector(ve)
    ve0 = g.s_ve.v

    def gp(p, ve0=ve0):
        p.wait_ge(g.s_ve.h, ve0)
        p.affine_select(out=g.ident_s[:], in_=g.ident_s[:], pattern=[[-1, 128]],
                        compare_op=ALU.is_equal, fill=0.0, base=0,
                        channel_multiplier=1)
        p.affine_select(out=g.identf_s[:], in_=g.identf_s[:],
                        pattern=[[-1, 128]], compare_op=ALU.is_equal,
                        fill=0.0, base=0, channel_multiplier=1)
        for n in range(32):
            p.affine_select(out=g.sel_s[:, n, :], in_=g.sel_s[:, n, :],
                            pattern=[[0, 128]], compare_op=ALU.is_equal,
                            fill=0.0, base=-n, channel_multiplier=1)
        p.sem_inc(g.s_ge.h, 1)
        g.s_ge.v += 1
    blk.gpsimd(gp)

    def se(e, ld0=ld0):
        e.wait_ge(g.s_ld.h, ld0)
        e.activation(out=g.A_s[:], in_=g.A_s[:], func=AF.Exp, bias=0.0, scale=1.0)
        e.drain()
        g.s_se.inc(e.mul(out=g.A_s[:], in_=g.A_s[:], mul=-1.0))
    blk.scalar(se)

    for i, d in enumerate(("A", "B")):
        def q2(qq, d=d, vprev=g.s_ve.v):
            qq.wait_ge(g.s_ve.h, vprev)
            qq.wait_ge(g.s_ld2.h, g.s_ld2.v)
            g.s_ld2.dma(qq.dma_start(
                out=v_wraw(g)[:, 0:NDT * 96].rearrange("p (a m) -> p a m", a=NDT),
                in_=g.inp[f"xproj_wT_{d}"].rearrange("(a p) m -> p a m", p=128)))
            qq.wait_ge(g.s_ld2.h, g.s_ld2.v)
            g.s_ld2.dma(qq.dma_start(
                out=v_wraw(g)[0:DTR, 1536:1536 + NDT * 64],
                in_=g.inp[f"dt_wT_{d}"][:, 0:NDT * 64]))
            qq.wait_ge(g.s_ld2.h, g.s_ld2.v)
            g.s_ld2.dma(qq.dma_start(
                out=v_wraw(g)[64:128, 1536:1536 + NDT * 64],
                in_=g.inp[f"dt_wT_{d}"][:, NDT * 64:NDT * 128]))
        blk.scalar(q2)
        ld = g.s_ld2.v

        def ve2(v, i=i, ld=ld):
            v.wait_ge(g.s_ld2.h, ld)
            v.tensor_copy(
                out=g.xprojT_s.ap().rearrange("p a m k -> p (a m k)")[:, i * NDT * 96:(i + 1) * NDT * 96],
                in_=v_wraw(g)[:, 0:NDT * 96])
            v.tensor_copy(
                out=g.dtwT_s.ap().rearrange("r a m k -> r (a m k)")[:, i * NDT * 128:i * NDT * 128 + NDT * 64],
                in_=v_wraw(g)[0:DTR, 1536:1536 + NDT * 64])
            g.s_ve.inc(v.tensor_copy(
                out=g.dtwT_s.ap().rearrange("r a m k -> r (a m k)")[:, i * NDT * 128 + NDT * 64:(i + 1) * NDT * 128],
                in_=v_wraw(g)[64:128, 1536:1536 + NDT * 64]))
        blk.vector(ve2)


def prep_weights_mixer(g):
    blk = g.block
    jobs = []
    for d in ("A", "B"):
        jobs.append((g.inp[f"in_wT_{d}"], g.wb[f"in_wT_{d}"], D, 2 * DI))
        jobs.append((g.inp[f"out_wT_{d}"], g.wb[f"out_wT_{d}"], DI, D))
        jobs.append((g.inp[f"projT_{d}"], g.wb[f"projT_{d}"], D, D))
    for src, dst, rows, cols in jobs:
        for i in range(rows // 128):
            def q(qq, src=src, i=i, cols=cols, stv=g.s_st.v):
                qq.wait_ge(g.s_st.h, stv)
                qq.wait_ge(g.s_ld.h, g.s_ld.v)
                g.s_ld.dma(qq.dma_start(out=v_wraw(g)[:, 0:cols],
                                        in_=src[i * 128:(i + 1) * 128, :]))
            blk.sync(q)
            ld = g.s_ld.v

            def ve(v, cols=cols, ld=ld):
                v.wait_ge(g.s_ld.h, ld)
                g.s_ve.inc(v.tensor_copy(out=v_wqb(g)[:, 0:cols],
                                         in_=v_wraw(g)[:, 0:cols]))
            blk.vector(ve)
            vd = g.s_ve.v

            def gp(p, dst=dst, i=i, cols=cols, vd=vd):
                p.wait_ge(g.s_ve.h, vd)
                p.wait_ge(g.s_st.h, g.s_st.v)
                g.s_st.dma(p.dma_start(out=dst.ap()[i * 128:(i + 1) * 128, :],
                                       in_=v_wqb(g)[:, 0:cols]))
            blk.gpsimd(gp)


def prep_weights(g):
    blk = g.block
    barrier_all(g)

    for src, dstT, O, I in ((g.w_up, g.wq_upT, FF, D),
                            (g.w_gate, g.wq_gT, FF, D),
                            (g.w_down, g.wq_dT, D, FF)):
        nblk = I // 32
        for i in range(O // 128):
            def q(qq, src=src, i=i, I=I, stv=g.s_st.v):
                qq.wait_ge(g.s_st.h, stv)
                qq.wait_ge(g.s_ld.h, g.s_ld.v)
                g.s_ld.dma(qq.dma_start(out=v_wraw(g)[:, 0:I],
                                        in_=src[i * 128:(i + 1) * 128, :]))
            blk.sync(q)
            ld = g.s_ld.v

            def ve(v, I=I, nblk=nblk, ld=ld):
                v.wait_ge(g.s_ld.h, ld)
                w = v_wraw(g)[:, 0:I]
                w3 = v_wraw(g)[:, 0:I].rearrange("p (n k) -> p n k", k=32)
                m = g.dq_s[:, 0:nblk, 0]
                s = g.dq_s[:, 0:nblk, 1]
                r = g.dq_s[:, 0:nblk, 2]
                v.tensor_reduce(out=m, in_=w3, axis=AX.X, op=ALU.max,
                                apply_absolute_value=True)
                v.drain()
                v.tensor_scalar(out=s, in0=m, scalar1=1.0 / WQMAX, scalar2=1e-12,
                                op0=ALU.mult, op1=ALU.max)
                v.drain()
                v.reciprocal(out=r, in_=s)
                v.drain()
                v.tensor_tensor(out=w3, in0=w3,
                                in1=g.dq_s.ap()[:, 0:nblk, 2:3].broadcast_to([128, nblk, 32]),
                                op=ALU.mult)
                v.drain()
                v.tensor_scalar(out=w, in0=w, scalar1=MAGIC, scalar2=MAGIC,
                                op0=ALU.add, op1=ALU.subtract)
                v.drain()
                g.s_ve.inc(v.tensor_tensor(
                    out=v_wqb(g)[:, 0:I].rearrange("p (n k) -> p n k", k=32),
                    in0=w3,
                    in1=g.dq_s.ap()[:, 0:nblk, 1:2].broadcast_to([128, nblk, 32]),
                    op=ALU.mult))
            blk.vector(ve)
            vd = g.s_ve.v

            for j in range(I // 128):
                def tn(t, j=j, vd=vd, gprev=g.s_ge.v, sprev=g.s_se.v):
                    t.wait_ge(g.s_ve.h, vd)
                    t.wait_ge(g.s_ge.h, gprev)
                    t.wait_ge(g.s_se.h, sprev)
                    g.s_tn.inc(t.transpose(g.p_tr[:],
                                           v_wqb(g)[:, j * 128:(j + 1) * 128],
                                           g.ident_s[:]))
                blk.tensor(tn)
                td = g.s_tn.v

                def se(e, td=td, stv=g.s_st.v):
                    e.wait_ge(g.s_tn.h, td)
                    e.wait_ge(g.s_st.h, stv)
                    g.s_se.inc(e.copy(out=g.stb_s.ap().rearrange("p a b -> p (a b)").bitcast(BF16)[:, 0:128],
                                      in_=g.p_tr[:]))
                blk.scalar(se)
                sd = g.s_se.v

                def gp(p, dstT=dstT, i=i, j=j, sd=sd):
                    p.wait_ge(g.s_se.h, sd)
                    p.wait_ge(g.s_st.h, g.s_st.v)
                    g.s_st.dma(p.dma_start(
                        out=dstT.ap()[j * 128:(j + 1) * 128, i * 128:(i + 1) * 128],
                        in_=g.stb_s.ap().rearrange("p a b -> p (a b)").bitcast(BF16)[:, 0:128]))
                blk.gpsimd(gp)


def p1_lnq(g, src):
    blk = g.block
    barrier_all(g)
    if src == "x":
        for c, (o, n) in enumerate(CH5):
            def q(qq, o=o, n=n, vprev=g.s_ve.v):
                qq.wait_ge(g.s_ve.h, vprev)
                qq.wait_ge(g.s_ld.h, g.s_ld.v)
                g.s_ld.dma(qq.dma_start(out=g.fc8[:, :, 0:n],
                                        in_=hbmT(g.xT)[:, :, o:o + n]))
            blk.sync(q)
            ld = g.s_ld.v

            def ve0(v, ld=ld, stv=g.s_st.v):
                v.wait_ge(g.s_ld.h, ld)
                v.wait_ge(g.s_st.h, stv)
                g.s_ve.inc(v.drain())
            blk.vector(ve0)
            ln_chunk(g, g.fc8.ap()[:, :, 0:n], v_lnout(g, n), n)
            absmax_acc(g, v_lnout(g, n), SC_X, axis=AX.XY)
            vd = g.s_ve.v

            def gp(p, o=o, n=n, vd=vd):
                p.wait_ge(g.s_ve.h, vd)
                p.wait_ge(g.s_st.h, g.s_st.v)
                g.s_st.dma(p.dma_start(out=hbmT(g.xln_sp)[:, :, o:o + n],
                                       in_=v_lnout(g, n)[:, :, 0:n]))
            blk.gpsimd(gp)
        scale_exchange(g, SC_X, QMAX)
        slot, lnsrc, chunks, qbv = SC_X, g.xln_sp, CH5, v_qb(g)
    else:
        scale_exchange(g, SC_X2, QMAX)
        slot, lnsrc, chunks, qbv = SC_X2, g.x2ln_sp, CH4, v_qb2(g)

    for c, (o, n) in enumerate(chunks):
        def q(qq, o=o, n=n, lnsrc=lnsrc, vprev=g.s_ve.v, stv=g.s_st.v):
            qq.wait_ge(g.s_ve.h, vprev)
            qq.wait_ge(g.s_st.h, stv)
            qq.wait_ge(g.s_ld.h, g.s_ld.v)
            g.s_ld.dma(qq.dma_start(out=g.fc8[:, :, 0:n],
                                    in_=hbmT(lnsrc)[:, :, o:o + n]))
        blk.sync(q)
        ld = g.s_ld.v

        def ve(v, o=o, n=n, ld=ld, slot=slot, qbv=qbv):
            v.wait_ge(g.s_ld.h, ld)
            v.tensor_scalar(out=g.fc8[:, :, 0:n], in0=g.fc8[:, :, 0:n],
                            scalar1=g.sc_s[:, slot:slot + 1], scalar2=MAGIC,
                            op0=ALU.mult, op1=ALU.add)
            v.drain()
            v.tensor_scalar(out=qbv[:, :, o:o + n], in0=g.fc8[:, :, 0:n],
                            scalar1=MAGIC, scalar2=g.ss_s[:, slot:slot + 1],
                            op0=ALU.subtract, op1=ALU.mult)
            g.s_ve.inc(v.drain())
        blk.vector(ve)


def ffn_mid(g, ver):
    blk = g.block
    barrier_all(g)
    if ver == 1:
        chunks, qbv = CH5, v_qb(g)
        s_up, s_gate, s_gs, s_t5 = SC_UP, SC_GATE, SC_GS, SC_T5
    else:
        chunks, qbv = CH4, v_qb2(g)
        s_up, s_gate, s_gs, s_t5 = SC_UP2, SC_GATE2, SC_GS2, SC_T52

    for wt, spill, slot in ((g.wq_upT, g.up_sp, s_up),
                            (g.wq_gT, g.gate_sp, s_gate)):
        for m in range(NMT):
            bi = m % 2

            def q(qq, wt=wt, m=m, bi=bi, tprev=g.s_tn.v):
                qq.wait_ge(g.s_tn.h, tprev)
                qq.wait_ge(g.s_ld.h, g.s_ld.v)
                g.s_ld.dma(qq.dma_start(
                    out=g.wstr_s[:, bi, 0:NFT, :],
                    in_=wt.ap().rearrange("(k p) f -> p k f", p=128)[:, :, m * 128:(m + 1) * 128]))
            blk.sync(q)
            ld = g.s_ld.v

            for ci, (o, n) in enumerate(chunks):
                pb = ci % 2

                def tn(t, bi=bi, o=o, n=n, pb=pb, ld=ld, sprev=g.s_se.v,
                       qbv=qbv):
                    t.wait_ge(g.s_ld.h, ld)
                    t.wait_ge(g.s_se.h, sprev)
                    for k in range(NFT):
                        li = t.matmul(g.p_mm[:, pb, 0:n], g.wstr_s[:, bi, k, :],
                                 qbv[:, k, o:o + n], start=(k == 0),
                                 stop=(k == NFT - 1))
                    g.s_tn.inc(li)
                blk.tensor(tn)
                td = g.s_tn.v

                def se(e, pb=pb, ci=ci, n=n, td=td, stv=g.s_st.v):
                    e.wait_ge(g.s_tn.h, td)
                    e.wait_ge(g.s_st.h, stv)
                    g.s_se.inc(e.copy(out=g.stb_s[:, ci % 2, 0:n],
                                      in_=g.p_mm[:, pb, 0:n]))
                blk.scalar(se)
                sd = g.s_se.v

                absmax_acc(g, g.stb_s[:, ci % 2, 0:n], slot, wait_se=sd)
                vd = g.s_ve.v

                def gp(p, spill=spill, m=m, o=o, n=n, ci=ci, vd=vd):
                    p.wait_ge(g.s_ve.h, vd)
                    p.wait_ge(g.s_st.h, g.s_st.v)
                    g.s_st.dma(p.dma_start(
                        out=spill.ap()[m * 128:(m + 1) * 128, o:o + n],
                        in_=g.stb_s[:, ci % 2, 0:n]))
                blk.gpsimd(gp)

    if ver == 1:
        prep_weights_mixer(g)
    scale_exchange(g, s_up, QMAX)
    scale_exchange(g, s_gate, QMAX)

    va_hist = {-2: g.s_ve.v, -1: g.s_ve.v}
    for m in range(NMT):
        for ci, (o, n) in enumerate(chunks):
            it = m * len(chunks) + ci
            sl = it % 2

            def q(qq, m=m, o=o, n=n, sl=sl, vprev=va_hist.get(it - 2, 0)):
                qq.wait_ge(g.s_ve.h, vprev)
                qq.wait_ge(g.s_lx[sl].h, g.s_lx[sl].v)
                g.s_lx[sl].dma(qq.dma_start(
                    out=g.g16_s[:, sl, 0:n],
                    in_=g.gate_sp.ap()[m * 128:(m + 1) * 128, o:o + n]))
            blk.sync(q)
            ld = g.s_lx[sl].v

            def ve(v, sl=sl, n=n, ld=ld, s_gate=s_gate):
                v.wait_ge(g.s_lx[sl].h, ld)
                gq = g.sg_s[:, 0:n]
                v.tensor_scalar(out=gq, in0=g.g16_s[:, sl, 0:n],
                                scalar1=g.sc_s[:, s_gate:s_gate + 1],
                                scalar2=MAGIC, op0=ALU.mult, op1=ALU.add)
                v.drain()
                v.tensor_scalar(out=gq, in0=gq, scalar1=MAGIC,
                                scalar2=g.ss_s[:, s_gate:s_gate + 1],
                                op0=ALU.subtract, op1=ALU.mult)
                g.s_ve.inc(v.drain())
            blk.vector(ve)
            vd = g.s_ve.v

            def se(e, n=n, vd=vd):
                e.wait_ge(g.s_ve.h, vd)
                g.s_se.inc(e.activation(out=g.sg2_s[:, 0:n], in_=g.sg_s[:, 0:n],
                                        func=AF.Sigmoid, bias=0.0, scale=1.0))
            blk.scalar(se)
            sd = g.s_se.v

            def ve2(v, n=n, sl=sl, sd=sd, stv=g.s_sw[sl].v):
                v.wait_ge(g.s_se.h, sd)
                v.wait_ge(g.s_sw[sl].h, stv)
                sig = g.sg2_s[:, 0:n]
                v.tensor_scalar(out=sig, in0=sig, scalar1=65536.0, scalar2=MAGIC,
                                op0=ALU.mult, op1=ALU.add)
                v.drain()
                v.tensor_scalar(out=sig, in0=sig, scalar1=MAGIC, scalar2=65535.0,
                                op0=ALU.subtract, op1=ALU.min)
                v.drain()
                v.scalar_tensor_tensor(out=g.h16_s[:, sl, 0:n], in0=sig,
                                       scalar=1.0 / 65536.0, in1=g.sg_s[:, 0:n],
                                       op0=ALU.mult, op1=ALU.mult)
                g.s_ve.inc(v.drain())
            blk.vector(ve2)
            absmax_acc(g, g.h16_s[:, sl, 0:n], s_gs)
            vd2 = g.s_ve.v

            def gp(p, m=m, o=o, n=n, sl=sl, vd2=vd2):
                p.wait_ge(g.s_ve.h, vd2)
                g.s_sw[sl].dma(p.dma_start(
                    out=g.gs_sp.ap()[m * 128:(m + 1) * 128, o:o + n],
                    in_=g.h16_s[:, sl, 0:n]))
            blk.gpsimd(gp)
            va_hist[it] = g.s_ve.v

    scale_exchange(g, s_gs, QMAX)

    vb_hist = {-2: g.s_ve.v, -1: g.s_ve.v}
    for m in range(NMT):
        for ci, (o, n) in enumerate(chunks):
            it = m * len(chunks) + ci
            sl = it % 2

            def q(qq, m=m, o=o, n=n, sl=sl, vprev=vb_hist.get(it - 2, 0),
                  swv=g.s_sw[sl].v):
                qq.wait_ge(g.s_ve.h, vprev)
                qq.wait_ge(g.s_sw[sl].h, swv)
                qq.wait_ge(g.s_lx[sl].h, g.s_lx[sl].v)
                g.s_lx[sl].dma(qq.dma_start(
                    out=g.g16_s[:, sl, 0:n],
                    in_=g.gs_sp.ap()[m * 128:(m + 1) * 128, o:o + n]))
                qq.wait_ge(g.s_lx[sl].h, g.s_lx[sl].v)
                g.s_lx[sl].dma(qq.dma_start(
                    out=g.h16_s[:, sl, 0:n],
                    in_=g.up_sp.ap()[m * 128:(m + 1) * 128, o:o + n]))
            blk.sync(q)
            ld = g.s_lx[sl].v

            def ve(v, n=n, sl=sl, ld=ld, s_gs=s_gs, s_up=s_up,
                   stv=g.s_sw[sl].v):
                v.wait_ge(g.s_lx[sl].h, ld)
                v.wait_ge(g.s_sw[sl].h, stv)
                g2 = g.sg_s[:, 0:n]
                uq = g.sg2_s[:, 0:n]
                v.tensor_scalar(out=g2, in0=g.g16_s[:, sl, 0:n],
                                scalar1=g.sc_s[:, s_gs:s_gs + 1], scalar2=MAGIC,
                                op0=ALU.mult, op1=ALU.add)
                v.tensor_scalar(out=uq, in0=g.h16_s[:, sl, 0:n],
                                scalar1=g.sc_s[:, s_up:s_up + 1], scalar2=MAGIC,
                                op0=ALU.mult, op1=ALU.add)
                v.drain()
                v.tensor_scalar(out=g2, in0=g2, scalar1=MAGIC,
                                scalar2=g.ss_s[:, s_gs:s_gs + 1],
                                op0=ALU.subtract, op1=ALU.mult)
                v.tensor_scalar(out=uq, in0=uq, scalar1=MAGIC,
                                scalar2=g.ss_s[:, s_up:s_up + 1],
                                op0=ALU.subtract, op1=ALU.mult)
                v.drain()
                v.tensor_mul(out=g.g16_s[:, sl, 0:n], in0=g2, in1=uq)
                g.s_ve.inc(v.drain())
            blk.vector(ve)
            absmax_acc(g, g.g16_s[:, sl, 0:n], s_t5)
            vd = g.s_ve.v

            def gp(p, m=m, o=o, n=n, sl=sl, vd=vd):
                p.wait_ge(g.s_ve.h, vd)
                g.s_sw[sl].dma(p.dma_start(
                    out=g.t5_sp.ap()[m * 128:(m + 1) * 128, o:o + n],
                    in_=g.g16_s[:, sl, 0:n]))
            blk.gpsimd(gp)
            vb_hist[it] = g.s_ve.v

    scale_exchange(g, s_t5, QMAX)

    oq = v_oq(g)
    vc_hist = {-2: g.s_ve.v, -1: g.s_ve.v}
    for ci, (o, n) in enumerate(chunks):
        for m in range(NMT):
            sl = m % 2
            it = ci * NMT + m

            def q(qq, m=m, o=o, n=n, sl=sl, vprev=vc_hist.get(it - 2, 0)):
                qq.wait_ge(g.s_ve.h, vprev)
                qq.wait_ge(g.s_lx[sl].h, g.s_lx[sl].v)
                g.s_lx[sl].dma(qq.dma_start(
                    out=g.g16_s[:, sl, 0:n],
                    in_=g.t5_sp.ap()[m * 128:(m + 1) * 128, o:o + n]))
            blk.sync(q)
            ld = g.s_lx[sl].v

            def ve(v, m=m, n=n, sl=sl, ld=ld, s_t5=s_t5, oq=oq, tprev=g.s_tn.v,
                   stv=g.s_st.v):
                v.wait_ge(g.s_lx[sl].h, ld)
                v.wait_ge(g.s_tn.h, tprev)
                if m < 2:
                    v.wait_ge(g.s_st.h, stv)
                v.tensor_scalar(out=g.sg_s[:, 0:n], in0=g.g16_s[:, sl, 0:n],
                                scalar1=g.sc_s[:, s_t5:s_t5 + 1], scalar2=MAGIC,
                                op0=ALU.mult, op1=ALU.add)
                v.drain()
                v.tensor_scalar(out=oq[:, m, 0:n], in0=g.sg_s[:, 0:n],
                                scalar1=MAGIC, scalar2=g.ss_s[:, s_t5:s_t5 + 1],
                                op0=ALU.subtract, op1=ALU.mult)
                g.s_ve.inc(v.drain())
            blk.vector(ve)
            vc_hist[it] = g.s_ve.v
        vd_oq = g.s_ve.v

        for ft in range(NFT):
            for half in range(2):
                def q2(qq, ft=ft, half=half, tprev=g.s_tn.v):
                    qq.wait_ge(g.s_tn.h, tprev)
                    qq.wait_ge(g.s_ld.h, g.s_ld.v)
                    g.s_ld.dma(qq.dma_start(
                        out=g.wstr_s[:, half, 0:12, :],
                        in_=g.wq_dT.ap().rearrange("(k p) f -> p k f", p=128)[:, 12 * half:12 * (half + 1), ft * 128:(ft + 1) * 128]))
                blk.sync(q2)
            ld2 = g.s_ld.v

            def tn(t, ft=ft, n=n, vd_oq=vd_oq, ld2=ld2, sprev=g.s_se.v, oq=oq):
                t.wait_ge(g.s_ve.h, vd_oq)
                t.wait_ge(g.s_ld.h, ld2)
                t.wait_ge(g.s_se.h, sprev)
                for k in range(NMT):
                    li = t.matmul(g.p_mm2[:, ft % 2, 0:n],
                             g.wstr_s[:, k // 12, k % 12, :],
                             oq[:, k, 0:n], start=(k == 0), stop=(k == NMT - 1))
                g.s_tn.inc(li)
            blk.tensor(tn)
            td = g.s_tn.v

            if ver == 1:
                def q3(qq, ft=ft, o=o, n=n, vprev=g.s_ve.v, stv=g.s_st.v):
                    qq.wait_ge(g.s_ve.h, vprev)
                    qq.wait_ge(g.s_st.h, stv)
                    qq.wait_ge(g.s_ld.h, g.s_ld.v)
                    g.s_ld.dma(qq.dma_start(out=g.fc8[:, ft, 0:n],
                                            in_=hbmT(g.xT)[:, ft, o:o + n]))
                blk.sync(q3)
                ld3 = g.s_ld.v

                def ve2(v, ft=ft, n=n, td=td, ld3=ld3):
                    v.wait_ge(g.s_tn.h, td)
                    v.wait_ge(g.s_ld.h, ld3)
                    v.scalar_tensor_tensor(out=g.fc8[:, ft, 0:n],
                                           in0=g.p_mm2[:, ft % 2, 0:n],
                                           scalar=0.5, in1=g.fc8[:, ft, 0:n],
                                           op0=ALU.mult, op1=ALU.add)
                    g.s_ve.inc(v.drain())
                blk.vector(ve2)
                vd2 = g.s_ve.v

                def se2(e, vd2=vd2):
                    e.wait_ge(g.s_ve.h, vd2)
                    e.sem_inc(g.s_se.h, 1); g.s_se.v += 1
                blk.scalar(se2)

                def gp(p, ft=ft, o=o, n=n, vd2=vd2):
                    p.wait_ge(g.s_ve.h, vd2)
                    p.wait_ge(g.s_st.h, g.s_st.v)
                    g.s_st.dma(p.dma_start(out=hbmT(g.x1_sp)[:, ft, o:o + n],
                                           in_=g.fc8[:, ft, 0:n]))
                blk.gpsimd(gp)
            else:
                def q3(qq, ft=ft, o=o, n=n, vprev=g.s_ve.v, stv=g.s_st.v):
                    qq.wait_ge(g.s_ve.h, vprev)
                    qq.wait_ge(g.s_st.h, stv)
                    qq.wait_ge(g.s_ld.h, g.s_ld.v)
                    g.s_ld.dma(qq.dma_start(out=g.fc8[:, ft, 0:n],
                                            in_=hbmT(g.x2_sp)[:, ft, o:o + n]))
                blk.sync(q3)
                ld3 = g.s_ld.v

                def ve2(v, ft=ft, n=n, td=td, ld3=ld3, ov=g.s_out.v):
                    v.wait_ge(g.s_tn.h, td)
                    v.wait_ge(g.s_ld.h, ld3)
                    v.wait_ge(g.s_out.h, ov)
                    v.scalar_tensor_tensor(out=g.st_s[:, ft % 2, 0:n],
                                           in0=g.p_mm2[:, ft % 2, 0:n],
                                           scalar=0.5, in1=g.fc8[:, ft, 0:n],
                                           op0=ALU.mult, op1=ALU.add)
                    g.s_ve.inc(v.drain())
                blk.vector(ve2)
                vd2 = g.s_ve.v

                def se2(e, vd2=vd2):
                    e.wait_ge(g.s_ve.h, vd2)
                    e.sem_inc(g.s_se.h, 1); g.s_se.v += 1
                blk.scalar(se2)

                def gp(p, ft=ft, o=o, n=n, vd2=vd2):
                    p.wait_ge(g.s_ve.h, vd2)
                    p.wait_ge(g.s_out.h, g.s_out.v)
                    g.s_out.dma(p.dma_start(
                        out=g.out_h[ft * 128:(ft + 1) * 128, o:o + n],
                        in_=g.st_s[:, ft % 2, 0:n]))
                blk.gpsimd(gp)

    if ver == 1:
        barrier_all(g)
        for c, (o, n) in enumerate(CH5):
            def q(qq, o=o, n=n, vprev=g.s_ve.v, stv=g.s_st.v):
                qq.wait_ge(g.s_ve.h, vprev)
                qq.wait_ge(g.s_st.h, stv)
                qq.wait_ge(g.s_ld.h, g.s_ld.v)
                g.s_ld.dma(qq.dma_start(out=g.fc8[:, :, 0:n],
                                        in_=hbmT(g.x1_sp)[:, :, o:o + n]))
            blk.sync(q)
            ld = g.s_ld.v

            def ve0(v, ld=ld, stv=g.s_st.v):
                v.wait_ge(g.s_ld.h, ld)
                v.wait_ge(g.s_st.h, stv)
                g.s_ve.inc(v.drain())
            blk.vector(ve0)
            ln_chunk(g, g.fc8.ap()[:, :, 0:n], v_lnout(g, n), n)
            vd = g.s_ve.v

            def ve(v, n=n, vd=vd):
                v.wait_ge(g.s_ve.h, vd)
                for ft in range(NFT):
                    v.tensor_copy(out=v_lnb16(g)[:, ft, 0:n],
                                  in_=v_lnout(g, n)[:, ft])
                v.drain()
                for ft in range(NFT):
                    v.tensor_copy(out=v_lnb16R(g)[:, ft, 0:n],
                                  in_=v_lnout(g, n)[:, ft][:, ::-1])
                g.s_ve.inc(v.drain())
            blk.vector(ve)
            vd2 = g.s_ve.v

            def gp(p, o=o, n=n, vd2=vd2):
                p.wait_ge(g.s_ve.h, vd2)
                p.wait_ge(g.s_st.h, g.s_st.v)
                g.s_st.dma(p.dma_start(out=hbmT(g.x1ln_sp)[:, :, o:o + n],
                                       in_=v_lnout(g, n)[:, :, 0:n]))
                p.wait_ge(g.s_st.h, g.s_st.v)
                g.s_st.dma(p.dma_start(out=hbmT(g.x1lnb_sp)[:, :, o:o + n],
                                       in_=v_lnb16(g)[:, :, 0:n]))
                p.wait_ge(g.s_st.h, g.s_st.v)
                g.s_st.dma(p.dma_start(
                    out=hbmT(g.x1lnbR_sp)[:, :, TOKH - o - n:TOKH - o],
                    in_=v_lnb16R(g)[:, :, 0:n]))
            blk.gpsimd(gp)


def p4_mixer(g):
    blk = g.block
    barrier_all(g)
    for di in range(2):
        mixer_direction(g, di)
        if di == 0:
            carry_exchange(g)


def carry_exchange(g):
    blk = g.block
    vd, sd = g.s_ve.v, g.s_se.v

    def gp(p, vd=vd, sd=sd):
        p.wait_ge(g.s_ve.h, vd)
        p.wait_ge(g.s_se.h, sd)
        p.wait_ge(g.s_st.h, g.s_st.v)
        g.s_st.dma(p.dma_start(out=g.carry_io.ap(), in_=g.carry_s[:, 0]))
        p.wait_ge(g.s_st.h, g.s_st.v)
        g.s_cc.inc(p.collective_compute(
            "AllReduce", ALU.add,
            replica_groups=[[0, 1], [2, 3], [4, 5], [6, 7]],
            ins=[g.carry_io.ap().opt()], outs=[g.carry_sum.ap().opt()]))
        p.wait_ge(g.s_cc.h, g.s_cc.v)
        p.wait_ge(g.s_st.h, g.s_st.v)
        g.s_st.dma(p.dma_start(out=g.pcar_s[:, :], in_=g.carry_sum.ap()))
    blk.gpsimd(gp)
    st = g.s_st.v

    def ve(v, st=st):
        v.wait_ge(g.s_st.h, st)
        v.tensor_sub(out=g.carry_s[:, 1], in0=g.pcar_s[:, :], in1=g.carry_s[:, 0])
        g.s_ve.inc(v.drain())
    blk.vector(ve)


def mixer_direction(g, di):
    blk = g.block
    xsrc = g.x1lnb_sp if di == 0 else g.x1lnbR_sp
    wname = "A" if di == 0 else "B"

    def q(qq, xsrc=xsrc, vprev=g.s_ve.v):
        qq.wait_ge(g.s_ve.h, vprev)
        qq.wait_ge(g.s_ld.h, g.s_ld.v)
        g.s_ld.dma(qq.dma_start(out=v_xlc(g)[:, :, 0:HALO],
                                in_=hbmT(xsrc)[:, :, 0:HALO]))
    blk.sync(q)
    ldh = g.s_ld.v
    for dt in range(NDT):
        def q2(qq, dt=dt, wname=wname, tprev=g.s_tn.v):
            qq.wait_ge(g.s_tn.h, tprev)
            qq.wait_ge(g.s_ld.h, g.s_ld.v)
            g.s_ld.dma(qq.dma_start(
                out=g.wstr_s[:, dt % 2, 0:NFT, :],
                in_=g.wb[f"in_wT_{wname}"].ap().rearrange("(k p) f -> p k f", p=128)[:, :, dt * 128:(dt + 1) * 128]))
        blk.sync(q2)
        ld = g.s_ld.v

        def tn(t, dt=dt, ld=ld, ldh=ldh, sprev=g.s_se.v):
            t.wait_ge(g.s_ld.h, max(ld, ldh))
            t.wait_ge(g.s_se.h, sprev)
            for k in range(NFT):
                li = t.matmul(g.p_bc[:, 0:HALO], g.wstr_s[:, dt % 2, k, :],
                         v_xlc(g)[:, k, 0:HALO], start=(k == 0),
                         stop=(k == NFT - 1))
            g.s_tn.inc(li)
        blk.tensor(tn)
        td = g.s_tn.v

        def se(e, dt=dt, td=td):
            e.wait_ge(g.s_tn.h, td)
            g.s_se.inc(e.copy(out=g.xicar_s[:, dt, :], in_=g.p_bc[:, 0:HALO]))
        blk.scalar(se)

    for c in range(NCH):
        mixer_chunk(g, di, c, wname)


def mixer_chunk(g, di, c, wname):
    blk = g.block
    o = HALO + c * TC
    xsrc = g.x1lnb_sp if di == 0 else g.x1lnbR_sp
    corig = c if di == 0 else NCH - 1 - c
    oorig = HALO + corig * TC
    carry = g.carry_s.ap()[:, di]
    xi = v_xi(g)
    u = v_u(g)
    du = v_du(g)

    def vgate(v, stv=g.s_st.v):
        v.wait_ge(g.s_st.h, stv)
        g.s_ve.inc(v.drain())
    blk.vector(vgate)

    def sgate(e, stv=g.s_st.v):
        e.wait_ge(g.s_st.h, stv)
        e.sem_inc(g.s_se.h, 1)
    blk.scalar(sgate)
    g.s_se.v += 1

    def q(qq, o=o, xsrc=xsrc, tprev=g.s_tn.v, sprev=g.s_se.v):
        qq.wait_ge(g.s_tn.h, tprev)
        qq.wait_ge(g.s_se.h, sprev)
        qq.wait_ge(g.s_ld.h, g.s_ld.v)
        g.s_ld.dma(qq.dma_start(out=v_xlc(g)[:, :, 0:TC],
                                in_=hbmT(xsrc)[:, :, o:o + TC]))
    blk.sync(q)
    ld_x = g.s_ld.v

    tn_hist = {-2: g.s_tn.v, -1: g.s_tn.v}
    xi_evac = {}
    for m in range(2 * NDT):
        def q2(qq, m=m, wname=wname, tprev=tn_hist.get(m - 2, 0)):
            qq.wait_ge(g.s_tn.h, tprev)
            qq.wait_ge(g.s_lw[m % 2].h, g.s_lw[m % 2].v)
            g.s_lw[m % 2].dma(qq.dma_start(
                out=g.wstr_s[:, m % 2, 0:NFT, :],
                in_=g.wb[f"in_wT_{wname}"].ap().rearrange("(k p) f -> p k f", p=128)[:, :, m * 128:(m + 1) * 128]))
        blk.sync(q2)
        ld = g.s_lw[m % 2].v

        def tn(t, m=m, ld=ld, ld_x=ld_x, sprev=(g.s_se.v if m == 0 else g.s_se.v - 1),
               vprev=g.s_ve.v, stprev=g.s_st.v):
            t.wait_ge(g.s_lw[m % 2].h, ld)
            t.wait_ge(g.s_ld.h, ld_x)
            t.wait_ge(g.s_se.h, max(sprev, 0))
            if m == 0:
                t.wait_ge(g.s_ve.h, vprev)
                t.wait_ge(g.s_st.h, stprev)
            for k in range(NFT):
                li = t.matmul(g.p_mm[:, m % 2, :], g.wstr_s[:, m % 2, k, :],
                         v_xlc(g)[:, k, 0:TC], start=(k == 0), stop=(k == NFT - 1))
            g.s_tn.inc(li)
        blk.tensor(tn)
        td = g.s_tn.v

        if m < NDT:
            def se(e, m=m, td=td):
                e.wait_ge(g.s_tn.h, td)
                g.s_se.inc(e.copy(out=xi[:, m, HALO:HALO + TC],
                                  in_=g.p_mm[:, m % 2, :]))
            blk.scalar(se)
            xi_evac[m] = g.s_se.v
        else:
            def se(e, m=m, td=td, stprev=g.s_sw[m % 2].v):
                e.wait_ge(g.s_tn.h, td)
                e.wait_ge(g.s_sw[m % 2].h, stprev)
                g.s_se.inc(e.copy(out=g.st_s[:, m % 2, :],
                                  in_=g.p_mm[:, m % 2, :]))
            blk.scalar(se)
            sd = g.s_se.v

            def gp(p, m=m, corig=corig, sd=sd):
                p.wait_ge(g.s_se.h, sd)
                g.s_sw[m % 2].dma(p.dma_start(
                    out=g.z_sp.ap()[(m - NDT) * 128:(m - NDT + 1) * 128,
                                    corig * TC:(corig + 1) * TC],
                    in_=g.st_s[:, m % 2, :]))
            blk.gpsimd(gp)
        tn_hist[m] = g.s_tn.v

    sd_xi = g.s_se.v

    def ve_h(v, sd_xi=sd_xi):
        v.wait_ge(g.s_se.h, sd_xi)
        v.tensor_copy(out=xi[:, :, 0:HALO], in_=g.xicar_s[:, :, :])
        g.s_ve.inc(v.drain())
    blk.vector(ve_h)

    for dt in range(NDT):
        def ve(v, dt=dt, di=di):
            v.tensor_scalar_mul(out=u[:, dt], in0=xi[:, dt, 0:TC],
                                scalar1=g.cw_s[:, di, dt, 0:1])
            v.drain()
            for k in range(1, 4):
                v.scalar_tensor_tensor(out=u[:, dt], in0=xi[:, dt, k:k + TC],
                                       scalar=g.cw_s[:, di, dt, k:k + 1],
                                       in1=u[:, dt], op0=ALU.mult, op1=ALU.add)
                v.drain()
            v.sem_inc(g.s_ve.h, 1); g.s_ve.v += 1
        blk.vector(ve)

    def ve_t(v):
        v.tensor_copy(out=g.xicar_s[:, :, :], in_=xi[:, :, TC:TC + HALO])
        g.s_ve.inc(v.drain())
    blk.vector(ve_t)
    ve_tail = g.s_ve.v

    def se_u(e, ve_tail=ve_tail, di=di):
        e.wait_ge(g.s_ve.h, ve_tail)
        for dt in range(NDT):
            e.activation(out=u[:, dt], in_=u[:, dt], func=AF.Silu,
                         bias=g.cb_s[:, di, dt:dt + 1], scale=1.0)
        g.s_se.inc(e.drain())
    blk.scalar(se_u)
    se_silu = g.s_se.v

    for dt in range(NDT):
        def ve(v, dt=dt, se_silu=se_silu, tprev=g.s_tn.v):
            v.wait_ge(g.s_se.h, se_silu)
            v.wait_ge(g.s_tn.h, tprev)
            g.s_ve.inc(v.tensor_copy(out=g.ubt_s[:, 0, :], in_=u[:, dt]))
        blk.vector(ve)
        vd = g.s_ve.v

        def tn(t, dt=dt, vd=vd, di=di, sprev=g.s_se.v):
            t.wait_ge(g.s_ve.h, vd)
            if dt == 0:
                t.wait_ge(g.s_se.h, sprev)
            li = t.matmul(g.p_xd[:, :], g.xprojT_s[:, di, dt, :],
                     g.ubt_s[:, 0, :],
                     start=(dt == 0), stop=(dt == NDT - 1))
            g.s_tn.inc(li)
        blk.tensor(tn)
    td_xd = g.s_tn.v

    def se_xd(e, td_xd=td_xd):
        e.wait_ge(g.s_tn.h, td_xd)
        e.copy(out=g.dtb16_s[:, :], in_=g.p_xd[0:DTR, :])
        g.s_se.inc(e.copy(out=g.xdb_s[0:32, 0, :], in_=g.p_xd[DTR:96, :]))
    blk.scalar(se_xd)
    sd_xd = g.s_se.v

    for dt in range(NDT):
        def tn(t, dt=dt, sd_xd=sd_xd, di=di, sprev=g.s_se.v):
            t.wait_ge(g.s_se.h, max(sd_xd, sprev if dt >= 2 else 0))
            li = t.matmul(g.p_mm[:, dt % 2, :], g.dtwT_s[:, di, dt, :],
                     g.dtb16_s[:, :], start=True, stop=True)
            g.s_tn.inc(li)
        blk.tensor(tn)
        td = g.s_tn.v

        def se(e, dt=dt, td=td, di=di):
            e.wait_ge(g.s_tn.h, td)
            e.activation(out=du[:, dt], in_=g.p_mm[:, dt % 2, :], func=AF.Exp,
                         bias=g.dtb_s[:, di, dt:dt + 1], scale=1.0)
            e.drain()
            g.s_se.inc(e.activation(out=du[:, dt], in_=du[:, dt], func=AF.Ln,
                                    bias=g.onescol_s[:, :], scale=1.0))
        blk.scalar(se)
    sd_delta = g.s_se.v

    ve_duv = g.s_ve.v

    def ve_bc(v, sd_xd=sd_xd):
        v.wait_ge(g.s_se.h, sd_xd)
        v.tensor_copy(out=g.bclo_s[:, 0, :], in_=g.xdb_s[0:32, 0, :])
        v.drain()
        v.tensor_sub(out=g.xdb_s[0:32, 1, :], in0=g.xdb_s[0:32, 0, :],
                     in1=g.bclo_s[:, 0, :])
        v.drain()
        v.tensor_copy(out=g.bclo_s[:, 1, :], in_=g.xdb_s[0:32, 1, :])
        g.s_ve.inc(v.drain())
    blk.vector(ve_bc)
    ve_bcv = g.s_ve.v

    hall = v_hall(g)
    crep = v_crep(g)
    brep = v_brep(g)
    scan_hist = {"base": g.s_ve.v}
    for gi in range(2):
        for n in range(8):
            def tn(t, gi=gi, n=n, ve_bcv=ve_bcv, sprev=g.s_se.v):
                t.wait_ge(g.s_ve.h, ve_bcv)
                t.wait_ge(g.s_se.h, sprev)
                sn = gi * 8 + n
                li = t.matmul(g.p_bc[:, :], g.sel_s[:, sn, :], g.bclo_s[:, 0, :],
                         start=True, stop=False)
                li = t.matmul(g.p_bc[:, :], g.sel_s[:, sn, :], g.bclo_s[:, 1, :],
                              start=False, stop=True)
                g.s_tn.inc(li)
                li = t.matmul(g.p_bc2[:, :], g.sel_s[:, 16 + sn, :], g.bclo_s[:, 0, :],
                         start=True, stop=False)
                li = t.matmul(g.p_bc2[:, :], g.sel_s[:, 16 + sn, :],
                              g.bclo_s[:, 1, :], start=False, stop=True)
                g.s_tn.inc(li)
            blk.tensor(tn)
            td = g.s_tn.v

            def se(e, n=n, td=td, vprev=g.s_ve.v):
                e.wait_ge(g.s_tn.h, td)
                e.wait_ge(g.s_ve.h, vprev)
                e.copy(out=brep[:, n, :], in_=g.p_bc[:, :])
                g.s_se.inc(e.copy(
                    out=crep.rearrange("p (t n) -> p t n", n=8)[:, :, n],
                    in_=g.p_bc2[:, :]))
            blk.scalar(se)
        sd_bc = g.s_se.v

        for dt in range(NDT):
            def vdup(v, dt=dt, sd_delta=sd_delta):
                v.wait_ge(g.s_se.h, sd_delta)
                v.tensor_mul(out=g.dup_s[:, :], in0=v_du(g)[:, dt],
                             in1=v_u(g)[:, dt])
                g.s_ve.inc(v.drain())
            blk.vector(vdup)

            for n in range(8):
                sn = gi * 8 + n
                slot = n % 2

                def se(e, dt=dt, sn=sn, slot=slot, sd_bc=sd_bc, di=di,
                       vprev=scan_hist.get((gi, dt, n - 2),
                                           scan_hist.get((gi, dt - 1, 7),
                                           scan_hist.get((gi - 1, NDT - 1, 7),
                                           scan_hist["base"])))):
                    e.wait_ge(g.s_se.h, sd_bc)
                    e.wait_ge(g.s_ve.h, vprev)
                    g.s_se.inc(e.activation(
                        out=g.da_s[:, slot, :], in_=v_du(g)[:, dt],
                        func=AF.Exp, bias=0.0,
                        scale=g.A_s[:, di, dt * NS + sn:dt * NS + sn + 1]))
                blk.scalar(se)
                sd = g.s_se.v

                def ve(v, dt=dt, n=n, sn=sn, slot=slot, sd=sd, carry=carry,
                       ve_duv=ve_duv):
                    v.wait_ge(g.s_se.h, sd)
                    v.wait_ge(g.s_ve.h, ve_duv)
                    v.tensor_mul(out=g.w_sc[:, 0, :], in0=g.dup_s[:, :],
                                 in1=brep[:, n, :])
                    v.drain()
                    v.tensor_tensor_scan(
                        out=hall.rearrange("p (t n) -> p t n", n=8)[:, :, n],
                        data0=g.da_s[:, slot, :], data1=g.w_sc[:, 0, :],
                        initial=carry[:, dt * NS + sn:dt * NS + sn + 1],
                        op0=ALU.mult, op1=ALU.add)
                    g.s_ve.inc(v.drain())
                blk.vector(ve)

                if di == 0 and c == 0 and hasattr(g, "dbg_daw") and dt in (0, 1, 8):
                    def gpw(p, dt=dt, sn=sn, slot=slot, n=n, vprev=g.s_ve.v):
                        p.wait_ge(g.s_ve.h, vprev)
                        p.wait_ge(g.s_st.h, g.s_st.v)
                        row = dt * NS + sn
                        g.s_st.dma(p.dma_start(
                            out=g.dbg_daw.ap()[row, 0, :][None, :],
                            in_=g.da_s[0:1, slot, :]))
                        p.wait_ge(g.s_st.h, g.s_st.v)
                        g.s_st.dma(p.dma_start(
                            out=g.dbg_daw.ap()[row, 1, :][None, :],
                            in_=g.w_sc[0:1, 0, :]))
                        p.wait_ge(g.s_st.h, g.s_st.v)
                        ctx = g.nc.allow_non_contiguous_dma(reason="debug")
                        ctx.__enter__()
                        g.s_st.dma(p.dma_start(
                            out=g.dbg_daw.ap()[row, 2, :][None, :],
                            in_=hall.rearrange("p (t n) -> p t n", n=8)[0:1, :, n]))
                        ctx.__exit__(None, None, None)
                    blk.gpsimd(gpw)
                    def vg(v, stv=g.s_st.v):
                        v.wait_ge(g.s_st.h, stv)
                    blk.vector(vg)

            def ve2(v, dt=dt, gi=gi, carry=carry, stv=g.s_st.v):
                v.wait_ge(g.s_st.h, stv)
                v.tensor_copy(out=carry[:, dt * NS + gi * 8:dt * NS + gi * 8 + 8],
                              in_=hall[:, (TC - 1) * 8:TC * 8])
                v.drain()
                v.tensor_mul(out=hall[:, :], in0=hall[:, :], in1=crep[:, :])
                v.drain()
                if gi == 0:
                    v.tensor_reduce(out=g.yt_s[:, :],
                                    in_=hall.rearrange("p (t n) -> p t n", n=8),
                                    axis=AX.X, op=ALU.add)
                    g.s_ve.inc(v.drain())
                else:
                    v.tensor_reduce(out=g.ytmp_s[:, :],
                                    in_=hall.rearrange("p (t n) -> p t n", n=8),
                                    axis=AX.X, op=ALU.add)
                    g.s_ve.inc(v.drain())
            blk.vector(ve2)

            if gi == 0:
                def gp0(p, dt=dt, corig=corig, vprev=g.s_ve.v):
                    p.wait_ge(g.s_ve.h, vprev)
                    p.wait_ge(g.s_st.h, g.s_st.v)
                    g.s_st.dma(p.dma_start(
                        out=g.y0_sp.ap()[dt * 128:(dt + 1) * 128,
                                         corig * TC:(corig + 1) * TC],
                        in_=g.yt_s[:, :]))
                blk.gpsimd(gp0)

            if gi == 1:
                def q3(qq, dt=dt, corig=corig, vprev=g.s_ve.v, stv=g.s_st.v,
                       sw0=g.s_sw[0].v, sw1=g.s_sw[1].v):
                    qq.wait_ge(g.s_ve.h, vprev)
                    qq.wait_ge(g.s_st.h, stv)
                    qq.wait_ge(g.s_sw[0].h, sw0)
                    qq.wait_ge(g.s_sw[1].h, sw1)
                    qq.wait_ge(g.s_ld.h, g.s_ld.v)
                    g.s_ld.dma(qq.dma_start(
                        out=g.zdt_s[:, dt % 2, :],
                        in_=g.z_sp.ap()[dt * 128:(dt + 1) * 128,
                                        corig * TC:(corig + 1) * TC]))
                    qq.wait_ge(g.s_ld.h, g.s_ld.v)
                    g.s_ld.dma(qq.dma_start(
                        out=g.y0t_s[:, dt % 2, :],
                        in_=g.y0_sp.ap()[dt * 128:(dt + 1) * 128,
                                         corig * TC:(corig + 1) * TC]))
                blk.sync(q3)
                ldz = g.s_ld.v

                def se2(e, dt=dt, ldz=ldz, vprev=g.s_ve.v):
                    e.wait_ge(g.s_ld.h, ldz)
                    e.wait_ge(g.s_ve.h, vprev)
                    g.s_se.inc(e.activation(out=g.zdt_s[:, dt % 2, :],
                                            in_=g.zdt_s[:, dt % 2, :],
                                            func=AF.Silu, bias=0.0, scale=1.0))
                blk.scalar(se2)
                sdz = g.s_se.v

                def ve3(v, dt=dt, di=di, sdz=sdz, ldz=ldz):
                    v.wait_ge(g.s_se.h, sdz)
                    v.wait_ge(g.s_ld.h, ldz)
                    v.tensor_add(out=g.yt_s[:, :], in0=g.ytmp_s[:, :],
                                 in1=g.y0t_s[:, dt % 2, :])
                    v.drain()
                    v.scalar_tensor_tensor(out=g.yt_s[:, :], in0=v_u(g)[:, dt],
                                           scalar=g.Dp_s[:, di, dt:dt + 1],
                                           in1=g.yt_s[:, :], op0=ALU.mult,
                                           op1=ALU.add)
                    v.drain()
                    v.tensor_mul(out=v_gb(g, dt), in0=g.yt_s[:, :],
                                 in1=g.zdt_s[:, dt % 2, :])
                    g.s_ve.inc(v.drain())
                blk.vector(ve3)

                if di == 0 and hasattr(g, "dbg_y"):
                    def gpy(p, dt=dt, corig=corig, vprev=g.s_ve.v):
                        p.wait_ge(g.s_ve.h, vprev)
                        p.wait_ge(g.s_st.h, g.s_st.v)
                        g.s_st.dma(p.dma_start(
                            out=g.dbg_y.ap()[dt * 128:(dt + 1) * 128,
                                             corig * TC:(corig + 1) * TC],
                            in_=g.yt_s[:, :]))
                    blk.gpsimd(gpy)

    ve_gball = g.s_ve.v

    if di == 0:
        for dn, view in (("dbg_u", v_u(g)), ("dbg_dl", v_du(g))):
            if hasattr(g, dn):
                def gpd(p, dn=dn, view=view, corig=corig, vprev=g.s_ve.v):
                    p.wait_ge(g.s_ve.h, vprev)
                    p.wait_ge(g.s_st.h, g.s_st.v)
                    g.s_st.dma(p.dma_start(
                        out=getattr(g, dn).ap().rearrange("(d p) t -> p d t", p=128)[:, :, corig * TC:(corig + 1) * TC],
                        in_=view))
                blk.gpsimd(gpd)

    def q4(qq, oorig=oorig, vprev=g.s_ve.v):
        qq.wait_ge(g.s_ve.h, vprev)
        qq.wait_ge(g.s_ld.h, g.s_ld.v)
        g.s_ld.dma(qq.dma_start(out=g.fc8[:, :, :],
                                in_=hbmT(g.x1ln_sp)[:, :, oorig:oorig + TC]))
    blk.sync(q4)
    ld_xl = g.s_ld.v

    for ft in range(NFT):
        def q5(qq, ft=ft, wname=wname, tprev=g.s_tn.v):
            qq.wait_ge(g.s_tn.h, tprev)
            qq.wait_ge(g.s_ld.h, g.s_ld.v)
            g.s_ld.dma(qq.dma_start(
                out=g.wstr_s[:, ft % 2, 0:12, :],
                in_=g.wb[f"out_wT_{wname}"].ap().rearrange("(k p) f -> p k f", p=128)[:, 0:12, ft * 128:(ft + 1) * 128]))
            qq.wait_ge(g.s_ld.h, g.s_ld.v)
            g.s_ld.dma(qq.dma_start(
                out=g.wstr_s[:, (ft + 1) % 2, 0:4, :],
                in_=g.wb[f"out_wT_{wname}"].ap().rearrange("(k p) f -> p k f", p=128)[:, 12:16, ft * 128:(ft + 1) * 128]))
        blk.sync(q5)
        ld = g.s_ld.v

        def tn(t, ft=ft, ld=ld, ve_gball=ve_gball, vprev=g.s_ve.v):
            t.wait_ge(g.s_ld.h, ld)
            t.wait_ge(g.s_ve.h, ve_gball)
            if ft >= 2:
                t.wait_ge(g.s_ve.h, vprev)
            for k in range(12):
                li = t.matmul(g.p_mm[:, ft % 2, :], g.wstr_s[:, ft % 2, k, :],
                         v_gb(g, k), start=(k == 0), stop=False)
            for k in range(12, 16):
                li = t.matmul(g.p_mm[:, ft % 2, :],
                         g.wstr_s[:, (ft + 1) % 2, k - 12, :],
                         v_gb(g, k), start=False, stop=(k == 15))
            g.s_tn.inc(li)
        blk.tensor(tn)
        td = g.s_tn.v

        def ve(v, ft=ft, di=di, td=td, ld_xl=ld_xl):
            v.wait_ge(g.s_tn.h, td)
            v.wait_ge(g.s_ld.h, ld_xl)
            src = g.p_mm[:, ft % 2, :] if di == 0 else g.p_mm[:, ft % 2, ::-1]
            v.scalar_tensor_tensor(out=v_mo(g)[:, ft, :], in0=src, scalar=1.0,
                                   in1=g.fc8[:, ft, :], op0=ALU.mult,
                                   op1=ALU.add)
            g.s_ve.inc(v.drain())
        blk.vector(ve)
    ve_mo = g.s_ve.v

    for ft in range(NFT):
        def q6(qq, ft=ft, wname=wname, tprev=g.s_tn.v):
            qq.wait_ge(g.s_tn.h, tprev)
            qq.wait_ge(g.s_ld.h, g.s_ld.v)
            g.s_ld.dma(qq.dma_start(
                out=g.wstr_s[:, ft % 2, 0:NFT, :],
                in_=g.wb[f"projT_{wname}"].ap().rearrange("(k p) f -> p k f", p=128)[:, :, ft * 128:(ft + 1) * 128]))
        blk.sync(q6)
        ld = g.s_ld.v

        def tn(t, ft=ft, ld=ld, ve_mo=ve_mo, sprev=g.s_se.v, vprev=g.s_ve.v):
            t.wait_ge(g.s_ld.h, ld)
            t.wait_ge(g.s_ve.h, ve_mo)
            if ft >= 2:
                t.wait_ge(g.s_se.h, sprev)
                t.wait_ge(g.s_ve.h, vprev)
            for k in range(NFT):
                li = t.matmul(g.p_mm2[:, ft % 2, :], g.wstr_s[:, ft % 2, k, :],
                         v_mo(g)[:, k, :], start=(k == 0), stop=(k == NFT - 1))
            g.s_tn.inc(li)
        blk.tensor(tn)
        td = g.s_tn.v

        if di == 0:
            def se(e, ft=ft, td=td, stprev=g.s_st.v):
                e.wait_ge(g.s_tn.h, td)
                e.wait_ge(g.s_st.h, stprev)
                g.s_se.inc(e.copy(out=g.st_s[:, ft % 2, :],
                                  in_=g.p_mm2[:, ft % 2, :]))
            blk.scalar(se)
            sd = g.s_se.v

            def gp(p, ft=ft, corig=corig, sd=sd):
                p.wait_ge(g.s_se.h, sd)
                p.wait_ge(g.s_st.h, g.s_st.v)
                g.s_st.dma(p.dma_start(
                    out=hbmT(g.projA_sp)[:, ft, corig * TC:(corig + 1) * TC],
                    in_=g.st_s[:, ft % 2, :]))
            blk.gpsimd(gp)
        else:
            def q7(qq, ft=ft, corig=corig, vprev=g.s_ve.v):
                qq.wait_ge(g.s_ve.h, vprev)
                qq.wait_ge(g.s_ld.h, g.s_ld.v)
                g.s_ld.dma(qq.dma_start(
                    out=g.st_s[:, ft % 2, :],
                    in_=hbmT(g.projA_sp)[:, ft, corig * TC:(corig + 1) * TC]))
            blk.sync(q7)
            ld2 = g.s_ld.v

            def ve(v, ft=ft, td=td, ld2=ld2):
                v.wait_ge(g.s_tn.h, td)
                v.wait_ge(g.s_ld.h, ld2)
                v.scalar_tensor_tensor(out=v_pp(g)[:, ft, :],
                                       in0=g.p_mm2[:, ft % 2, :],
                                       scalar=g.pb_s[:, ft:ft + 1],
                                       in1=g.st_s[:, ft % 2, :],
                                       op0=ALU.add, op1=ALU.add)
                g.s_ve.inc(v.drain())
            blk.vector(ve)

    if di == 1:
        mixer_epilogue(g, corig, oorig)


def mixer_epilogue(g, corig, oorig):
    blk = g.block
    pp = v_pp(g)
    x2 = v_x2c(g)
    ln_chunk(g, pp, pp, TC)

    def q(qq, oorig=oorig, vprev=g.s_ve.v):
        qq.wait_ge(g.s_ve.h, vprev)
        qq.wait_ge(g.s_ld.h, g.s_ld.v)
        g.s_ld.dma(qq.dma_start(out=g.fc8[:, :, :],
                                in_=hbmT(g.x1_sp)[:, :, oorig:oorig + TC]))
    blk.sync(q)
    ld = g.s_ld.v

    def ve(v, ld=ld):
        v.wait_ge(g.s_ld.h, ld)
        for ft in range(NFT):
            v.tensor_scalar(out=pp[:, ft, :], in0=pp[:, ft, :],
                            scalar1=g.lnw_s[:, ft:ft + 1],
                            scalar2=g.lnb_s[:, ft:ft + 1],
                            op0=ALU.mult, op1=ALU.add)
        v.drain()
        v.tensor_tensor(out=x2, in0=pp, in1=g.fc8.ap()[:, :, :],
                        op=ALU.add)
        g.s_ve.inc(v.drain())
    blk.vector(ve)
    vd = g.s_ve.v

    def gp(p, corig=corig, vd=vd):
        p.wait_ge(g.s_ve.h, vd)
        p.wait_ge(g.s_st.h, g.s_st.v)
        g.s_st.dma(p.dma_start(
            out=hbmT(g.x2_sp)[:, :, corig * TC:(corig + 1) * TC],
            in_=x2[:, :, :]))
    blk.gpsimd(gp)

    ln_chunk(g, x2, pp, TC)
    absmax_acc(g, pp, SC_X2, axis=AX.XY)
    vd2 = g.s_ve.v

    def gp2(p, corig=corig, vd2=vd2):
        p.wait_ge(g.s_ve.h, vd2)
        p.wait_ge(g.s_st.h, g.s_st.v)
        g.s_st.dma(p.dma_start(
            out=hbmT(g.x2ln_sp)[:, :, corig * TC:(corig + 1) * TC],
            in_=pp[:, :, :]))
    blk.gpsimd(gp2)


def finish(g):
    blk = g.block

    def gp(p):
        p.wait_ge(g.s_out.h, g.s_out.v)
        p.wait_ge(g.s_st.h, g.s_st.v)
    blk.gpsimd(gp)



def kernel(**inputs):
    nc = build_nc()
    in_maps = make_in_maps(inputs)
    res = run_bass_kernel_spmd(nc, in_maps, core_ids=list(range(8)))
    return assemble([r for r in res.results])


def make_in_maps(inputs):
    i = inputs
    x = i["L0_attn_x"]
    w = {}
    for pre in ("fw", "bw"):
        w[f"in_wT_{pre}"] = np.ascontiguousarray(i[f"L0_attn_{pre}_in_w"].T)
        w[f"conv_w_{pre}"] = np.ascontiguousarray(i[f"L0_attn_{pre}_conv_w"][:, 0, :])
        w[f"conv_b_{pre}"] = np.ascontiguousarray(i[f"L0_attn_{pre}_conv_b"].reshape(DI, 1))
        w[f"xproj_wT_{pre}"] = np.ascontiguousarray(i[f"L0_attn_{pre}_xproj_w"].T)
        w[f"dt_wT_{pre}"] = np.ascontiguousarray(i[f"L0_attn_{pre}_dt_w"].T)
        w[f"dt_b_{pre}"] = np.ascontiguousarray(i[f"L0_attn_{pre}_dt_b"].reshape(DI, 1))
        w[f"Alog_{pre}"] = np.ascontiguousarray(i[f"L0_attn_{pre}_Alog"])
        w[f"Dp_{pre}"] = np.ascontiguousarray(i[f"L0_attn_{pre}_Dp"].reshape(DI, 1))
        w[f"out_wT_{pre}"] = np.ascontiguousarray(i[f"L0_attn_{pre}_out_w"].T)
    pT = np.ascontiguousarray(i["L0_attn_proj_w"].T)
    w["projT_fw"] = np.ascontiguousarray(pT[:D, :])
    w["projT_bw"] = np.ascontiguousarray(pT[D:, :])

    shared = dict(
        proj_b=np.ascontiguousarray(i["L0_attn_proj_b"].reshape(D, 1)),
        ln_w=np.ascontiguousarray(i["L0_attn_ln_w"].reshape(D, 1)),
        ln_b=np.ascontiguousarray(i["L0_attn_ln_b"].reshape(D, 1)),
        w_gate=np.ascontiguousarray(i["L0_ffn_w_gate"]),
        w_up=np.ascontiguousarray(i["L0_ffn_w_up"]),
        w_down=np.ascontiguousarray(i["L0_ffn_w_down"]),
    )
    maps = []
    for c in range(8):
        b, h = c // 2, c % 2
        xb = x[b].T
        lo, hi = h * TOK - HALO, h * TOK + TOK + HALO
        pad_l, pad_r = max(0, -lo), max(0, hi - LFULL)
        sl = xb[:, max(0, lo):min(LFULL, hi)]
        if pad_l or pad_r:
            sl = np.pad(sl, ((0, 0), (pad_l, pad_r)))
        if h == 1:
            sl = sl[:, ::-1]
        m = dict(shared)
        m["xT"] = np.ascontiguousarray(sl)
        dirA = "fw" if h == 0 else "bw"
        for key, val in w.items():
            base, pre = key.rsplit("_", 1)
            tgt = "A" if pre == dirA else "B"
            m[f"{base}_{tgt}"] = val
        maps.append(m)
    return maps


def assemble(results):
    out = np.empty((B, LFULL, D), np.float32)
    for c in range(8):
        b, h = c // 2, c % 2
        o = results[c]["out"]
        if h == 1:
            o = o[:, ::-1]
        out[b, h * TOK:(h + 1) * TOK, :] = o.T
    return out
